# version 21
# baseline (speedup 1.0000x reference)
import contextlib

import numpy as np

import concourse.bass as bass
import concourse.mybir as mybir
from concourse.ap import AP
from concourse.bass_utils import run_bass_kernel_spmd

F32 = mybir.dt.float32
BF16 = mybir.dt.bfloat16
I32 = mybir.dt.int32
AF = mybir.ActivationFunctionType
ALU = mybir.AluOpType
AX = mybir.AxisListType

ENGINES = ("sync", "gpsimd", "vector", "scalar", "tensor")

MAGIC = 12582912.0
CQ = 256.49996948242188


def W(*items):
    out = []
    for it in items:
        if it is None:
            continue
        if isinstance(it, tuple) and it and all(
                sub is None or isinstance(sub, tuple) for sub in it):
            for sub in it:
                if sub is not None and sub[1] > 0:
                    out.append(sub)
            continue
        s, v = it
        if v > 0:
            out.append((s, v))
    return tuple(out)


TRUNCATE_AT = None


class Sched:
    def __init__(self):
        self.streams = {e: [] for e in ENGINES}
        self.cnt = {}
        self.seq = 0
        self.sections = {}

    def mark(self, name):
        self.sections[name] = (self.seq, dict(self.cnt))

    def n(self, sem):
        return self.cnt.get(sem, 0)

    def cur(self, sem):
        return (sem, self.cnt.get(sem, 0))

    def op(self, eng, emit, waits=(), inc=None, inc_n=1):
        if inc is not None:
            self.cnt[inc] = self.cnt.get(inc, 0) + inc_n
        self.streams[eng].append(
            (tuple(waits), emit, (inc, inc_n) if inc else None, self.seq))
        self.seq += 1
        return (inc, self.cnt[inc]) if inc else None

    def barrier(self):
        f = W(*[(s, v) for s, v in self.cnt.items()])
        for e in ENGINES:
            self.streams[e].append((f, None, None, self.seq))
        self.seq += 1

    def emit(self, block, sems):
        cutoff, fence = None, None
        if TRUNCATE_AT is not None:
            cutoff, fcnt = self.sections[TRUNCATE_AT]
            fence = W(*[(s, v) for s, v in fcnt.items()])

        def run(eng_obj, stream):
            waited = {}
            for waits, emitf, inc, seq in stream:
                if cutoff is not None and seq >= cutoff:
                    continue
                for s, v in waits:
                    if waited.get(s, -1) < v:
                        eng_obj.wait_ge(sems[s], v)
                        waited[s] = v
                if emitf is not None:
                    instr = emitf(eng_obj)
                    if inc is not None:
                        instr.then_inc(sems[inc[0]], inc[1])
            if fence is not None:
                for s, v in fence:
                    if waited.get(s, -1) < v:
                        eng_obj.wait_ge(sems[s], v)

        for name in ENGINES:
            getattr(block, name)((lambda st: (lambda e: run(e, st)))(self.streams[name]))


def build_core(D=1024, S=4096, H=4096, L=2048, Bc=2, dbg=False):
    TA = 256
    TW = 256
    T = Bc * L
    d8, s32, h32 = D // 128, S // 128, H // 128
    nA = T // TA
    nW = T // TW
    nWq = L // TW
    WM = max(D, H)
    sqrtS = float(np.float32(np.sqrt(S)))
    ndh = max(1, D // 512)
    dw = D // ndh
    nbn = max(1, D // 512)
    nqg = 1
    assert TA == 256 and TW == 256 and L % TW == 0 and T % TA == 0
    assert (H * D) & (H * D - 1) == 0
    assert h32 % nqg == 0

    nc = bass.Bass("TRN2", target_bir_lowering=False, debug=False)

    x_in = nc.dram_tensor("x", [T, D], F32, kind="ExternalInput").ap()
    k_in = nc.dram_tensor("k", [S, D], F32, kind="ExternalInput").ap()
    v_in = nc.dram_tensor("v", [S, D], F32, kind="ExternalInput").ap()
    w1_in = nc.dram_tensor("w1", [H, D], F32, kind="ExternalInput").ap()
    w2_in = nc.dram_tensor("w2", [D, H], F32, kind="ExternalInput").ap()
    cwb_in = nc.dram_tensor("cwb", [H, 4], F32, kind="ExternalInput").ap()
    out_hbm = nc.dram_tensor("out", [T, D], F32, kind="ExternalOutput").ap()
    ikind = "ExternalOutput" if dbg else "Internal"
    y_sc = nc.dram_tensor("y_sc", [T, D], F32, kind=ikind).ap()
    yqT_sc = nc.dram_tensor("yqT_sc", [D, T], BF16, kind=ikind).ap()
    w1T_sc = nc.dram_tensor("w1T_sc", [D, H], BF16, kind=ikind).ap()
    w2T_sc = nc.dram_tensor("w2T_sc", [H, D], BF16, kind=ikind).ap()
    ry_sc = nc.dram_tensor("ry_sc", [1, T], F32, kind=ikind).ap()
    if dbg:
        dbg_m = nc.dram_tensor("dbg_m", [128, 512], F32, kind="ExternalOutput").ap()
        dbg_r = nc.dram_tensor("dbg_r", [4, 512], F32, kind="ExternalOutput").ap()
        dbg_h = nc.dram_tensor("dbg_h", [128, 8192], F32, kind="ExternalOutput").ap()

    outer = contextlib.ExitStack()
    sbuf = lambda name, shape, dt=F32: outer.enter_context(nc.sbuf_tensor(name, shape, dt))
    sems = {s: outer.enter_context(nc.semaphore(s))
            for s in ("pe", "ve", "ac", "gp", "qc", "qw0", "qw1", "qs",
                      "qk0", "qk1", "qv", "qx0", "qx1", "qy", "qq",
                      "qw1t", "qw2t", "qt0", "qt1", "qyl", "qo",
                      "qr0", "qr1", "qrl0", "qrl1",
                      "qg0", "qg1", "qg2", "qg3")}
    sc = Sched()
    O = sc.op

    pA = outer.enter_context(nc.psum_tensor("pA", [128, 2048], F32))
    pB = outer.enter_context(nc.psum_tensor("pB", [128, 1024], F32))
    pC = outer.enter_context(nc.psum_tensor("pC", [128, 512], F32))
    pD = outer.enter_context(nc.psum_tensor("pD", [128, 1024], BF16))

    identb = sbuf("identb", [128, 128], BF16)
    identf = sbuf("identf", [128, 128], F32)
    ones_f1 = sbuf("ones_f1", [1, 128], F32)
    one11 = sbuf("one11", [1, 1], F32)
    ones_bf = sbuf("ones_bf", [128, 1], BF16)
    b256 = sbuf("b256", [128, 1], F32)
    cqc = sbuf("cqc", [128, 1], F32)
    m256c = sbuf("m256c", [128, 1], F32)
    epsc = sbuf("epsc", [128, 1], F32)
    cwb_sb = sbuf("cwb_sb", [128, h32 * 4], F32)
    mrow_s = sbuf("mrow_s", [1, 2], F32)
    mcl = sbuf("mcl", [1, 2], F32)
    wsc = sbuf("wsc", [1, 2], F32)
    mwv = sbuf("mwv", [1, 2], F32)
    m2s = sbuf("m2s", [1, 1], F32)
    wq_col = sbuf("wq_col", [128, 2], F32)
    m2s_col = sbuf("m2s_col", [128, 1], F32)

    def ident_init(g, buf):
        return g.affine_select(out=buf[:], in_=buf[:], compare_op=ALU.not_equal,
                               fill=1.0, base=0, pattern=[[-1, 128]],
                               channel_multiplier=1)

    O("gpsimd", lambda g: g.memset(identb[:], 0.0), (), "gp")
    O("gpsimd", lambda g: ident_init(g, identb), W(sc.cur("gp")), "gp")
    O("gpsimd", lambda g: g.memset(identf[:], 0.0), (), "gp")
    O("gpsimd", lambda g: ident_init(g, identf), W(sc.cur("gp")), "gp")
    for buf, val in ((ones_f1, 1.0), (one11, 1.0), (ones_bf, 1.0),
                     (b256, 256.5), (epsc, 1e-5), (cqc, CQ), (m256c, -256.0)):
        O("gpsimd", lambda g, b=buf, v=val: g.memset(b[:], v), (), "gp")
    consts_ok = sc.cur("gp")

    O("sync", lambda s: s.dma_start(
        out=cwb_sb[:].rearrange("p (c k) -> p c k", c=h32),
        in_=cwb_in.rearrange("(c p) k -> p c k", p=128)), (), "qc", 16)
    cwb_ok = sc.cur("qc")
    sc.mark("consts")

    pa = contextlib.ExitStack()
    asb = lambda name, shape, dt=F32: pa.enter_context(nc.sbuf_tensor(name, shape, dt))
    KT = asb("KT", [128, d8 * S], BF16)
    V_sb = asb("V_sb", [128, s32 * D], BF16)
    kbts = [asb(f"kbt{i}", [128, D], BF16) for i in range(2)]
    buf_free = [None, None]
    for c in range(S // 128):
        b = c % 2
        ld = O("gpsimd", lambda g, c=c, b=b: g.dma_start(
            out=kbts[b][:], in_=k_in[128 * c:128 * (c + 1), :]),
            W(buf_free[b]), f"qk{b}", 16)
        for g0 in range(0, d8, 8):
            gn = min(8, d8 - g0)
            for jj in range(gn):
                j = g0 + jj
                O("tensor", lambda t, j=j, jj=jj, b=b: t.transpose(
                    pD[:, 128 * jj:128 * (jj + 1)],
                    kbts[b][:, 128 * j:128 * (j + 1)], identb[:]),
                    W(ld, sc.cur("ve"), consts_ok) if jj == 0 else (),
                    "pe" if jj == gn - 1 else None)
            O("vector", lambda v, c=c, g0=g0, gn=gn: v.tensor_copy(
                KT[:].rearrange("p (j s) -> p j s", j=d8)[:, g0:g0 + gn,
                                                          128 * c:128 * (c + 1)],
                pD[:, 0:128 * gn].rearrange("p (j q) -> p j q", j=gn)),
                W(sc.cur("pe")), "ve")
        buf_free[b] = sc.cur("pe")
    kt_ok = sc.cur("ve")
    sc.mark("ktv")
    v_ok = O("gpsimd", lambda g: g.dma_start(
        out=V_sb[:].rearrange("p (c d) -> p c d", c=s32),
        in_=v_in.rearrange("(c p) d -> p c d", p=128)), (), "qv", 16)

    UW = min(512, min(D, H))
    wtmp = [asb(f"wtmp{i}", [128, UW], F32) for i in range(3)]
    wred = asb("wred", [128, 2], F32)
    wacc = asb("wacc", [128, 2], F32)
    wi32 = asb("wi32", [128, UW], I32)
    wqb = [asb(f"wqb{i}", [128, UW], BF16) for i in range(2)]
    wstg = asb("wstg", [128, (UW // 128) * 128], BF16)

    O("vector", lambda v: v.memset(wacc[:], 0.0), (), "ve")

    wst = {"buf_free": [None] * 3, "stg_free": None, "n": 0, "nq": 0, "np": 0,
           "wq_col_ok": None, "mw_ok": None, "wqb_free": [None, None],
           "wqb_ok": [None, None], "loads": [], "ld_next": 0, "ld_h": {}}

    def issue_wload():
        u = wst["ld_next"]
        if u >= len(wst["loads"]):
            return
        wst["ld_next"] += 1
        m, r, cc = wst["loads"][u]
        b = u % 3
        wst["ld_h"][u] = O("gpsimd", lambda g, m=m, r=r, cc=cc, b=b: g.dma_start(
            out=wtmp[b][:], in_=m[128 * r:128 * (r + 1), UW * cc:UW * (cc + 1)]),
            W(wst["buf_free"][b]), f"qg{b}", 16)

    def units_of(rows, width):
        return [(r, cc) for r in range(rows // 128) for cc in range(width // UW)]

    def mean_unit(m_in, r, cc, col):
        if wst["n"] == 0:
            for _ in range(3):
                issue_wload()
        u = wst["n"]
        b = u % 3
        wst["n"] += 1
        O("vector", lambda v, b=b: v.reduce_sum(
            wred[:, 0:1], wtmp[b][:], axis=AX.X, apply_absolute_value=True),
            W(wst["ld_h"].pop(u), sc.cur("ve")), "ve")
        wst["buf_free"][b] = sc.cur("ve")
        O("vector", lambda v, col=col: v.tensor_tensor(
            wacc[:, col:col + 1], wacc[:, col:col + 1], wred[:, 0:1], ALU.add),
            W(sc.cur("ve")), "ve")
        issue_wload()

    def mean_finalize():
        O("gpsimd", lambda g: g.tensor_reduce(mrow_s[:], wacc[:], axis=AX.C,
                                              op=ALU.add),
          W(sc.cur("ve"), consts_ok), "gp")
        O("vector", lambda v: v.tensor_scalar(mcl[:], mrow_s[:], 1.0 / (H * D), 1e-5,
                                              ALU.mult, ALU.max),
          W(sc.cur("gp")), "ve")
        O("vector", lambda v: v.reciprocal(wsc[:], mcl[:]), W(sc.cur("ve")), "ve")
        O("vector", lambda v: v.reciprocal(mwv[:], wsc[:]), W(sc.cur("ve")), "ve")
        O("vector", lambda v: v.tensor_scalar(m2s[:], mwv[0:1, 1:2], 1.0 / 127.0,
                                              None, ALU.mult),
          W(sc.cur("ve")), "ve")
        wst["mw_ok"] = sc.cur("ve")
        O("tensor", lambda t: t.matmul(pA[:, 0:2], ones_f1[:], wsc[:],
                                       start=True, stop=True),
          W(wst["mw_ok"], consts_ok, last["pA0"]), "pe")
        O("tensor", lambda t: t.matmul(pA[:, 2:3], ones_f1[:], m2s[:],
                                       start=True, stop=True), (), "pe")
        O("vector", lambda v: v.tensor_copy(wq_col[:], pA[:, 0:2]),
          W(sc.cur("pe"), sc.cur("ve")), "ve")
        O("vector", lambda v: v.tensor_copy(m2s_col[:], pA[:, 2:3]),
          W(sc.cur("ve")), "ve")
        last["pA0"] = sc.cur("ve")
        wst["wq_col_ok"] = sc.cur("ve")

    def tern_unit_dve(m_in, r, cc, col):
        u = wst["n"]
        b = u % 3
        qb = wst["nq"] % 2
        wst["n"] += 1
        wst["nq"] += 1
        O("vector", lambda v, b=b, col=col: v.tensor_scalar(
            wtmp[b][:], wtmp[b][:], wq_col[:, col:col + 1], 1.0, ALU.mult, ALU.min),
            W(wst["ld_h"].pop(u), wst["wq_col_ok"], sc.cur("ve")), "ve")
        O("vector", lambda v, b=b: v.tensor_scalar(
            wtmp[b][:], wtmp[b][:], -1.0, MAGIC, ALU.max, ALU.add),
            W(sc.cur("ve")), "ve")
        O("vector", lambda v, b=b: v.tensor_scalar(
            wi32[:], wtmp[b][:], MAGIC, CQ, ALU.subtract, ALU.add),
            W(sc.cur("ve")), "ve")
        wst["buf_free"][b] = sc.cur("ve")
        issue_wload()
        O("vector", lambda v, qb=qb: v.tensor_scalar(
            wqb[qb][:], wi32[:], -256.0, None, ALU.add),
            W(sc.cur("ve"), wst["wqb_free"][qb]), "ve")
        wst["wqb_ok"][qb] = sc.cur("ve")

    def tern_unit_pe(m_sc, r, cc):
        qb = wst["np"] % 2
        wst["np"] += 1
        nj = UW // 128
        for jj in range(nj):
            O("tensor", lambda t, jj=jj, qb=qb: t.transpose(
                pD[:, 128 * jj:128 * (jj + 1)],
                wqb[qb][:, 128 * jj:128 * (jj + 1)], identb[:]),
                W(wst["wqb_ok"][qb], last["pD"], consts_ok) if jj == 0 else (),
                "pe" if jj == nj - 1 else None)
        wst["wqb_free"][qb] = sc.cur("pe")
        O("vector", lambda v, nj=nj: v.tensor_copy(
            wstg[:, 0:128 * nj], pD[:, 0:128 * nj]),
            W(sc.cur("pe"), wst["stg_free"], sc.cur("ve")), "ve")
        last["pD"] = sc.cur("ve")
        wst["stg_free"] = O("sync", lambda s, m_sc=m_sc, r=r, cc=cc, nj=nj:
                            s.dma_start(
            out=m_sc.rearrange("(j p) r -> p j r", p=128)[
                :, (UW * cc) // 128:(UW * cc) // 128 + nj,
                128 * r:128 * (r + 1)],
            in_=wstg[:, 0:128 * nj].rearrange("p (j q) -> p j q", j=nj)),
            W(sc.cur("ve")), "qs", 16)

    wprep_steps = []
    for r, cc in units_of(H, D):
        wst["loads"].append((w1_in, r, cc))
    for r, cc in units_of(D, H):
        wst["loads"].append((w2_in, r, cc))
    for r, cc in units_of(H, D):
        wst["loads"].append((w1_in, r, cc))
    for r, cc in units_of(D, H):
        wst["loads"].append((w2_in, r, cc))
    for r, cc in units_of(H, D):
        wprep_steps.append(lambda r=r, cc=cc: mean_unit(w1_in, r, cc, 0))
    for r, cc in units_of(D, H):
        wprep_steps.append(lambda r=r, cc=cc: mean_unit(w2_in, r, cc, 1))
    wprep_steps.append(mean_finalize)
    tern_list = ([(w1_in, w1T_sc, r, cc, 0) for r, cc in units_of(H, D)]
                 + [(w2_in, w2T_sc, r, cc, 1) for r, cc in units_of(D, H)])
    for idx, (tm_in, tm_sc, tr, tcc, tcol) in enumerate(tern_list):
        def step(idx=idx, m_in=tm_in, m_sc=tm_sc, r=tr, cc=tcc, col=tcol):
            if idx > 0:
                pm = tern_list[idx - 1]
                tern_unit_pe(pm[1], pm[2], pm[3])
            tern_unit_dve(m_in, r, cc, col)
            if idx == len(tern_list) - 1:
                tern_unit_pe(m_sc, r, cc)
        wprep_steps.append(step)

    x_sb = [asb(f"x_sb{i}", [128, 2 * D], F32) for i in range(2)]
    xb_sb = asb("xb_sb", [128, D], BF16)
    xT_sb = asb("xT_sb", [128, d8 * TA], BF16)
    g_sb = asb("g_sb", [128, s32 * TA], BF16)
    g2_sb = [asb(f"g2_sb{i}", [128, TA], BF16) for i in range(2)]
    l2row = asb("l2row", [1, TA], F32)
    zh = asb("zh", [128, D], F32)
    y_sb = asb("y_sb", [128, 2 * D], F32)
    yqi = asb("yqi", [128, D // 2], I32)
    yqb = asb("yqb", [128, 2 * D], BF16)
    yqT_sb = asb("yqT_sb", [128, d8 * 128], BF16)
    bnst = asb("bnst", [128, nbn * 6], F32)
    smalls = asb("smalls", [128, 16], F32)
    ry_row = [asb(f"ry_row{i}", [1, TA], F32) for i in range(2)]

    last = {k: None for k in ("x0", "x1", "xb", "xT", "g", "pD", "pC",
                              "zh", "y", "yqi", "yqb", "yqT", "smalls_ac",
                              "pS0", "pS1", "g2_0", "g2_1", "ry",
                              "yqb_rdy", "qsc_ok", "rysm", "ryb0", "ryb1")}
    last.update({f"pA{i}": None for i in range(4)})

    nmean = len(units_of(H, D)) + len(units_of(D, H)) + 1
    ntern = len(units_of(H, D)) + len(units_of(D, H))
    mtiles = max(1, min(4, nA // 4))
    wsteps_per_tile = []
    for i in range(nA):
        if i < mtiles:
            k = -(-nmean // mtiles)
        else:
            k = -(-ntern // (nA - mtiles))
        wsteps_per_tile.append(k)

    xloads = [O("sync", lambda s: s.dma_start(
        out=x_sb[0][:].rearrange("p (c d) -> p c d", c=2),
        in_=x_in[0:TA, :].rearrange("(c p) d -> p c d", p=128)), (), "qx0", 16)]

    def emit_head(i):
        xload = xloads[i]
        for c in range(2):
            O("vector", lambda v, c=c, xs=x_sb[i % 2]: v.tensor_copy(
                xb_sb[:], xs[:, c * D:(c + 1) * D]),
                W(xload, last["xb"], sc.cur("ve")), "ve")
            xb_ok = sc.cur("ve")
            for j in range(d8):
                O("tensor", lambda t, c=c, j=j: t.transpose(
                    pD[:, 128 * j:128 * (j + 1)],
                    xb_sb[:, 128 * j:128 * (j + 1)], identb[:]),
                    W(xb_ok, last["pD"], consts_ok) if j == 0 else (),
                    "pe" if j == d8 - 1 else None)
            O("vector", lambda v, c=c: v.tensor_copy(
                xT_sb[:].rearrange("p (j t) -> p j t", j=d8)[:, :, 128 * c:128 * (c + 1)],
                pD[:, 0:128 * d8].rearrange("p (j q) -> p j q", j=d8)),
                W(sc.cur("pe"), last["xT"], sc.cur("ve")), "ve")
            last["pD"] = sc.cur("ve")
            last["xb"] = sc.cur("pe")
        return sc.cur("ve")

    def emit_yq_lag(i):
        for c in range(2):
            for j in range(d8):
                O("tensor", lambda t, c=c, j=j: t.transpose(
                    pD[:, 128 * j:128 * (j + 1)],
                    yqb[:, c * D + 128 * j:c * D + 128 * (j + 1)], identb[:]),
                    W(last["yqb_rdy"], last["pD"]) if j == 0 else (),
                    "pe" if j == d8 - 1 else None)
            last["yqb"] = sc.cur("pe")
            O("vector", lambda v, c=c: v.tensor_copy(
                yqT_sb[:], pD[:, 0:128 * d8]),
                W(sc.cur("pe"), last["yqT"], sc.cur("ve")), "ve")
            last["pD"] = sc.cur("ve")
            O("sync", lambda s, i=i, c=c: s.dma_start(
                out=yqT_sc.rearrange("(j p) t -> p j t", p=128)[
                    :, :, TA * i + 128 * c:TA * i + 128 * (c + 1)],
                in_=yqT_sb[:].rearrange("p (j t) -> p j t", j=d8)),
                W(sc.cur("ve")), "qq", 16)
            last["yqT"] = sc.cur("qq")
        rb = i % 2
        for c in range(2):
            O("tensor", lambda t, c=c: t.transpose(
                pC[0:1, 256:384], smalls[:, 14 + c:15 + c], identf[:]),
                W(last["qsc_ok"], last["pC"], last[f"ryb{rb}"]), "pe")
            O("vector", lambda v, c=c, rb=rb: v.tensor_copy(
                ry_row[rb][0:1, 128 * c:128 * (c + 1)], pC[0:1, 256:384]),
                W(sc.cur("pe"), sc.cur("ve")), "ve")
            last["pC"] = sc.cur("ve")
        last["rysm"] = sc.cur("pe")
        O("sync", lambda s, i=i, rb=rb: s.dma_start(
            out=ry_sc[0:1, TA * i:TA * (i + 1)], in_=ry_row[rb][:]),
            W(sc.cur("ve")), f"qr{rb}", 16)
        last[f"ryb{rb}"] = sc.cur(f"qr{rb}")
        last["ry"] = sc.cur(f"qr{rb}")

    xT_ok = None
    for i in range(nA):
        sc.mark(f"tile{i}")
        wquota = [wsteps_per_tile[i] if i < len(wsteps_per_tile) else 0]
        xb_i = i % 2
        xload = xloads[i]

        if i + 1 < nA:
            xloads.append(O("sync", lambda s, i=i: s.dma_start(
                out=x_sb[(i + 1) % 2][:].rearrange("p (c d) -> p c d", c=2),
                in_=x_in[TA * (i + 1):TA * (i + 2), :]
                .rearrange("(c p) d -> p c d", p=128)),
                W(last[f"x{(i + 1) % 2}"]), f"qx{(i + 1) % 2}", 16))

        if i == 0:
            xT_ok = emit_head(0)

        ps_lohi = [(0, TA), (512, 512 + TA)]
        gelu_cnt = {}
        g2_cnt = {}

        def emit_ones(ss, i=i):
            O("tensor", lambda t, ss=ss: t.matmul(
                pC[0:1, 0:TA], ones_bf[:], g2_sb[ss % 2][:],
                start=(ss == 0), stop=(ss == s32 - 1)),
                W(g2_cnt[ss], last["pC"] if ss == 0 else None), "pe")
            last[f"g2_{ss % 2}"] = sc.cur("pe")

        for ss in range(s32):
            lo, hi = ps_lohi[ss % 2]
            w0 = [xT_ok, kt_ok]
            if ss >= 2:
                w0.append(gelu_cnt[ss - 2])
            else:
                w0.append(last[f"pS{ss}"])
            for j in range(d8):
                O("tensor", lambda t, ss=ss, j=j, lo=lo, hi=hi: t.matmul(
                    pB[:, lo:hi],
                    KT[:, j * S + 128 * ss: j * S + 128 * (ss + 1)],
                    xT_sb[:, j * TA:(j + 1) * TA],
                    start=(j == 0), stop=(j == d8 - 1)),
                    W(*w0) if j == 0 else (), "pe" if j == d8 - 1 else None)
            mm1_done = sc.cur("pe")
            if ss >= 2:
                emit_ones(ss - 2)
            O("scalar", lambda s, ss=ss, lo=lo, hi=hi: s.activation(
                g_sb[:, ss * TA:(ss + 1) * TA], pB[:, lo:hi], AF.Gelu),
                W(mm1_done), "ac")
            gelu_cnt[ss] = sc.cur("ac")
            O("vector", lambda v, ss=ss: v.tensor_mul(
                g2_sb[ss % 2][:], g_sb[:, ss * TA:(ss + 1) * TA],
                g_sb[:, ss * TA:(ss + 1) * TA]),
                W(gelu_cnt[ss], last[f"g2_{ss % 2}"], sc.cur("ve")), "ve")
            g2_cnt[ss] = sc.cur("ve")
            if ss % 3 == 2 and wquota[0] > 0 and wprep_steps:
                nper = -(-wquota[0] // max(1, (s32 - ss) // 3))
                for _ in range(nper):
                    if wprep_steps:
                        wprep_steps.pop(0)()
                        wquota[0] -= 1
        emit_ones(s32 - 2)
        emit_ones(s32 - 1)
        last["pS0"] = gelu_cnt[s32 - 2]
        last["pS1"] = gelu_cnt[s32 - 1]
        l2mm_done = sc.cur("pe")

        last["xT"] = l2mm_done
        if i > 0:
            emit_yq_lag(i - 1)

        O("vector", lambda v: v.tensor_copy(l2row[:], pC[0:1, 0:TA]),
          W(l2mm_done, sc.cur("ve")), "ve")
        l2row_ok = sc.cur("ve")
        for c in range(2):
            O("tensor", lambda t, c=c: t.matmul(
                pC[:, 256 + c:257 + c], l2row[0:1, 128 * c:128 * (c + 1)], one11[:],
                start=True, stop=True),
                W(l2row_ok) if c == 0 else (), "pe" if c == 1 else None)
        O("vector", lambda v: v.tensor_copy(smalls[:, 0:2], pC[:, 256:258]),
          W(sc.cur("pe"), sc.cur("ve")), "ve")
        last["pC"] = sc.cur("ve")
        O("scalar", lambda s: s.activation(smalls[:, 0:2], smalls[:, 0:2], AF.Sqrt),
          W(sc.cur("ve"), last["smalls_ac"]), "ac")
        sqrt_ok = sc.cur("ac")
        O("vector", lambda v: v.reciprocal(smalls[:, 2:4], smalls[:, 0:2]),
          W(sqrt_ok, sc.cur("ve")), "ve")
        O("vector", lambda v: v.tensor_scalar(smalls[:, 2:4], smalls[:, 2:4],
                                              sqrtS, None, ALU.mult),
          W(sc.cur("ve")), "ve")
        rl2_ok = sc.cur("ve")

        if i + 1 < nA:
            next_xT_ok = emit_head(i + 1)

        for c in range(2):
            for n in range(ndh):
                gidx = c * ndh + n
                for k in range(s32):
                    w0 = W(gelu_cnt[k], v_ok, last[f"pA{gidx}"] if k == 0 else None)
                    O("tensor", lambda t, c=c, n=n, gidx=gidx, k=k: t.matmul(
                        pA[:, 512 * gidx:512 * gidx + dw],
                        g_sb[:, k * TA + 128 * c: k * TA + 128 * (c + 1)],
                        V_sb[:, k * D + dw * n: k * D + dw * (n + 1)],
                        start=(k == 0), stop=(k == s32 - 1)),
                        w0, "pe" if k == s32 - 1 else None)
                mm2_done = sc.cur("pe")
                O("vector", lambda v, c=c, n=n, gidx=gidx, xs=x_sb[xb_i]:
                  v.scalar_tensor_tensor(
                    zh[:, dw * n:dw * (n + 1)], pA[:, 512 * gidx:512 * gidx + dw],
                    smalls[:, 2 + c:3 + c],
                    xs[:, c * D + dw * n:c * D + dw * (n + 1)],
                    ALU.mult, ALU.add),
                    W(mm2_done, rl2_ok, xload,
                      last["zh"] if n == 0 else None, sc.cur("ve")), "ve")
                last[f"pA{gidx}"] = sc.cur("ve")
            z_ok = sc.cur("ve")
            for bb in range(nbn):
                O("vector", lambda v, bb=bb: v.bn_stats(
                    bnst[:, 6 * bb:6 * (bb + 1)],
                    zh[:, 512 * bb:512 * bb + min(512, D)]),
                    W(z_ok, sc.cur("ve")), "ve")
            O("vector", lambda v: v.bn_aggr(smalls[:, 4:6], bnst[:, 0:6 * nbn]),
              W(sc.cur("ve")), "ve")
            bn_ok = sc.cur("ve")
            O("scalar", lambda s: s.activation(smalls[:, 6:7], smalls[:, 5:6],
                                               AF.Sqrt, bias=epsc[:]),
              W(bn_ok, consts_ok), "ac")
            O("vector", lambda v: v.reciprocal(smalls[:, 8:9], smalls[:, 6:7]),
              W(sc.cur("ac"), sc.cur("ve")), "ve")
            O("vector", lambda v: v.tensor_mul(smalls[:, 10:11], smalls[:, 4:5],
                                               smalls[:, 8:9]),
              W(sc.cur("ve")), "ve")
            O("vector", lambda v, c=c: v.tensor_scalar(
                y_sb[:, c * D:(c + 1) * D], zh[:], smalls[:, 8:9], smalls[:, 10:11],
                ALU.mult, ALU.subtract),
                W(sc.cur("ve"), last["y"] if c == 0 else None), "ve")
            last["zh"] = sc.cur("ve")
            y_ok = sc.cur("ve")
            O("vector", lambda v, c=c: v.reduce_max(
                smalls[:, 12:13], y_sb[:, c * D:(c + 1) * D], axis=AX.X,
                apply_absolute_value=True), W(y_ok), "ve")
            O("vector", lambda v: v.tensor_scalar(smalls[:, 12:13], smalls[:, 12:13],
                                                  1e-5, None, ALU.max),
              W(sc.cur("ve")), "ve")
            O("vector", lambda v: v.reciprocal(smalls[:, 13:14], smalls[:, 12:13]),
              W(sc.cur("ve")), "ve")
            O("vector", lambda v: v.tensor_scalar(smalls[:, 13:14], smalls[:, 13:14],
                                                  127.0, None, ALU.mult),
              W(sc.cur("ve")), "ve")
            O("vector", lambda v, c=c: v.tensor_scalar(
                smalls[:, 14 + c:15 + c], smalls[:, 12:13], 1.0 / 127.0, None,
                ALU.mult),
                W(sc.cur("ve"), last["rysm"] if c == 0 else None), "ve")
            qsc_ok = sc.cur("ve")
            for hh in range(2):
                hw2 = D // 2
                O("scalar", lambda s, c=c, hh=hh, hw2=hw2: s.activation(
                    zh[:, hh * hw2:(hh + 1) * hw2],
                    y_sb[:, c * D + hh * hw2:c * D + (hh + 1) * hw2],
                    AF.Identity, scale=smalls[:, 13:14]),
                    W(qsc_ok, last["zh"], consts_ok), "ac")
                last["smalls_ac"] = sc.cur("ac")
                O("vector", lambda v, hh=hh, hw2=hw2: v.tensor_scalar(
                    zh[:, hh * hw2:(hh + 1) * hw2], zh[:, hh * hw2:(hh + 1) * hw2],
                    MAGIC, MAGIC, ALU.add, ALU.subtract),
                    W(sc.cur("ac"), sc.cur("ve")), "ve")
                O("vector", lambda v, hh=hh, hw2=hw2: v.tensor_scalar(
                    yqi[:], zh[:, hh * hw2:(hh + 1) * hw2], CQ, None, ALU.add),
                    W(last["yqi"], sc.cur("ve")), "ve")
                O("vector", lambda v, c=c, hh=hh, hw2=hw2: v.tensor_scalar(
                    yqb[:, c * D + hh * hw2:c * D + (hh + 1) * hw2],
                    yqi[:], -256.0, None, ALU.add),
                    W(last["yqb"] if (c == 0 and hh == 0) else None,
                      sc.cur("ve")), "ve")
                last["yqi"] = sc.cur("ve")
                last["zh"] = sc.cur("ve")
        last["g"] = sc.cur("pe")
        last["yqb_rdy"] = sc.cur("ve")
        last["qsc_ok"] = sc.cur("ve")

        O("sync", lambda s, i=i: s.dma_start(
            out=y_sc[TA * i:TA * (i + 1), :].rearrange("(c p) d -> p c d", p=128),
            in_=y_sb[:].rearrange("p (c d) -> p c d", c=2)),
            W(sc.cur("ve")), "qy", 16)
        last["y"] = sc.cur("qy")
        last[f"x{xb_i}"] = sc.cur("ve")
        if i + 1 < nA:
            xT_ok = next_xT_ok

    while wprep_steps:
        wprep_steps.pop(0)()
    emit_yq_lag(nA - 1)
    sc.mark("phaseA")
    pa.close()
    sc.barrier()

    W1T = sbuf("W1T", [128, d8 * H], BF16)
    W2T = sbuf("W2T", [128, h32 * D], BF16)
    yqTw = [sbuf(f"yqTw{i}", [128, d8 * (TW + 2)], BF16) for i in range(2)]
    y_win = sbuf("y_win", [128, D], F32)
    Srow_b = sbuf("Srow_b", [1, TW + 2], F32)
    Srow = [Srow_b, Srow_b]
    S_sb = [sbuf(f"S_sb{i}", [128, TW + 2], F32) for i in range(2)]
    htr = [sbuf(f"htr{i}", [128, TW + 2], F32) for i in range(3)]
    ca = [sbuf(f"ca{i}", [128, TW], F32) for i in range(2)]
    ca3 = [sbuf(f"ca3{i}", [128, TW], F32) for i in range(2)]
    hc_buf = sbuf("hc_buf", [128, h32 * TW], F32)
    mbuf = sbuf("mbuf", [128, TW], F32)
    mg = sbuf("mg", [128, 2 * TW], F32)
    i32tr = sbuf("i32tr", [128, 2 * TW], I32)
    hq_sb = sbuf("hq_sb", [128, h32 * TW], BF16)
    row2 = sbuf("row2", [1, TW], F32)
    S2_sb = sbuf("S2_sb", [128, TW], F32)
    bsml = sbuf("bsml", [128, 16], F32)
    bnst2 = sbuf("bnst2", [128, nbn * 6], F32)

    w1t_ok = O("sync", lambda s: s.dma_start(
        out=W1T[:].rearrange("p (j h) -> p j h", j=d8),
        in_=w1T_sc.rearrange("(j p) h -> p j h", p=128)), (), "qw1t", 16)
    w2t_ok = O("sync", lambda s: s.dma_start(
        out=W2T[:].rearrange("p (c d) -> p c d", c=h32),
        in_=w2T_sc.rearrange("(c p) d -> p c d", p=128)), (), "qw2t", 16)

    lastB = {k: None for k in ("pC", "htr0", "htr1", "htr2",
                               "htr0ac", "htr1ac", "htr2ac",
                               "ca0", "ca1", "ca30", "ca31",
                               "hc", "m8", "hq", "S2",
                               "row2", "ywin", "bsml_ac",
                               "Srow0", "Srow1", "S0", "S1", "pA2",
                               "mslot0", "mslot1", "mslot2", "mslot3")}
    lastB.update({f"pA{i}": None for i in range(4)})

    def emit_win_loads(w):
        q, wq = divmod(w, nWq)
        go = q * L + wq * TW
        lp = wq == 0
        rp = wq == nWq - 1
        c0 = go - (0 if lp else 1)
        c1 = min(go + TW + 1, q * L + L)
        ncols = c1 - c0
        dst0 = 1 if lp else 0
        wb = w % 2
        buf = yqTw[wb]
        ld = O("sync", lambda s, buf=buf, c0=c0, ncols=ncols, dst0=dst0: s.dma_start(
            out=buf[:].rearrange("p (j t) -> p j t", j=d8)[:, :, dst0:dst0 + ncols],
            in_=yqT_sc.rearrange("(j p) t -> p j t", p=128)[:, :, c0:c0 + ncols]),
            W(lastB.get(f"yqTw{wb}"), last["yqT"]), f"qt{wb}", 16)
        pads = []
        if lp:
            pads.append(O("vector", lambda v, buf=buf: v.memset(
                buf[:].rearrange("p (j t) -> p j t", j=d8)[:, :, 0:1], 0.0),
                W(lastB.get(f"yqTw{wb}"), ld, sc.cur("ve")), "ve"))
        if rp:
            pads.append(O("vector", lambda v, buf=buf: v.memset(
                buf[:].rearrange("p (j t) -> p j t", j=d8)[:, :, TW + 1:TW + 2], 0.0),
                W(lastB.get(f"yqTw{wb}"), ld, sc.cur("ve")), "ve"))
        return (ld, pads, go, lp, rp)

    def emit_sprep(w):
        wb = w % 2
        q, wq = divmod(w, nWq)
        go = q * L + wq * TW
        lp = wq == 0
        rp = wq == nWq - 1
        sr0 = go - (0 if lp else 1)
        srcols = min(go + TW + 1, q * L + L) - sr0
        sdst = 1 if lp else 0
        ryld = O("sync", lambda s, sr0=sr0, srcols=srcols, sdst=sdst, wb=wb:
                 s.dma_start(out=Srow[wb][0:1, sdst:sdst + srcols],
                             in_=ry_sc[0:1, sr0:sr0 + srcols]),
                 W(lastB["Srow0"], lastB["Srow1"], last["ry"]), f"qrl{wb}", 16)
        O("vector", lambda v, sr0=sr0, srcols=srcols, sdst=sdst, wb=wb:
          v.tensor_scalar(
            Srow[wb][0:1, sdst:sdst + srcols], Srow[wb][0:1, sdst:sdst + srcols],
            mwv[0:1, 0:1], None, ALU.mult),
            W(ryld, lastB[f"Srow{wb}"], wst["mw_ok"], sc.cur("ve")), "ve")
        if lp:
            O("vector", lambda v, wb=wb: v.memset(Srow[wb][0:1, 0:1], 0.0),
              W(sc.cur("ve")), "ve")
        if rp:
            O("vector", lambda v, wb=wb: v.memset(Srow[wb][0:1, TW + 1:TW + 2], 0.0),
              W(sc.cur("ve")), "ve")
        srow_ok = sc.cur("ve")
        O("tensor", lambda t, wb=wb: t.matmul(pC[:, 0:TW + 2], ones_f1[:],
                                              Srow[wb][:], start=True, stop=True),
          W(srow_ok, lastB["pC"]), "pe")
        O("vector", lambda v, wb=wb: v.tensor_copy(S_sb[wb][:], pC[:, 0:TW + 2]),
          W(sc.cur("pe"), lastB[f"S{wb}"], sc.cur("ve")), "ve")
        lastB["pC"] = sc.cur("ve")
        lastB[f"Srow{wb}"] = sc.cur("ve")
        return sc.cur("ve")

    win_loads = {0: emit_win_loads(0)}
    s_oks = {0: emit_sprep(0)}

    mslots = [(pB, 0), (pB, 512), (pA, 1536)]
    gsz = 16 if h32 % 16 == 0 else h32
    ng = h32 // gsz
    octets = [list(range(o, min(o + 8, h32))) for o in range(0, h32, 8)]
    st = {"hq_prev": None, "dbg_gate": None, "pend_silu": [], "i32ac": [None, None]}

    def emit_conv_chunk(w, hc):
        ld, pads, go, lp, rp = win_loads[w]
        s_ok = s_oks[w]
        s_i = hc % 3
        mp, lo = mslots[s_i]
        hi = lo + TW + 2

        def mm3(t, hc=hc, mp=mp, lo=lo, hi=hi, wb=w % 2):
            r = None
            for j in range(d8):
                r = t.matmul(
                    mp[:, lo:hi],
                    W1T[:, j * H + 128 * hc: j * H + 128 * (hc + 1)],
                    yqTw[wb][:, j * (TW + 2):(j + 1) * (TW + 2)],
                    start=(j == 0), stop=(j == d8 - 1))
            return r
        O("tensor", mm3,
          W(ld, *pads, w1t_ok, lastB[f"mslot{s_i}"]), "pe")
        mm3_done = sc.cur("pe")
        hb = hc % 3
        O("vector", lambda v, hb=hb, mp=mp, lo=lo, hi=hi, wb=w % 2:
          v.tensor_tensor(
            htr[hb][:], mp[:, lo:hi], S_sb[wb][:], ALU.mult),
            W(mm3_done, s_ok, lastB[f"htr{hb}"],
              lastB[f"htr{hb}ac"], sc.cur("ve")), "ve")
        lastB[f"mslot{s_i}"] = sc.cur("ve")
        htr_ok = sc.cur("ve")
        O("scalar", lambda s, hc=hc, hb=hb: s.activation(
            ca[hc % 2][:], htr[hb][:, 0:TW], AF.Identity,
            scale=cwb_sb[:, 4 * hc:4 * hc + 1],
            bias=cwb_sb[:, 4 * hc + 3:4 * hc + 4]),
            W(htr_ok, cwb_ok, lastB[f"ca{hc % 2}"]), "ac")
        c1_ok = sc.cur("ac")
        O("scalar", lambda s, hc=hc, hb=hb: s.activation(
            ca3[hc % 2][:], htr[hb][:, 2:TW + 2], AF.Identity,
            scale=cwb_sb[:, 4 * hc + 2:4 * hc + 3], bias=0.0),
            W(lastB[f"ca3{hc % 2}"], sc.cur("ac")), "ac")
        lastB[f"htr{hb}ac"] = sc.cur("ac")
        c3_ok = sc.cur("ac")
        O("vector", lambda v, hc=hc, hb=hb: v.scalar_tensor_tensor(
            ca[hc % 2][:], htr[hb][:, 1:TW + 1],
            cwb_sb[:, 4 * hc + 1:4 * hc + 2], ca[hc % 2][:],
            ALU.mult, ALU.add),
            W(c1_ok, sc.cur("ve")), "ve")
        lastB[f"htr{hb}"] = sc.cur("ve")
        c2_ok = sc.cur("ve")
        O("gpsimd", lambda g, hc=hc: g.tensor_tensor(
            hc_buf[:, hc * TW:(hc + 1) * TW], ca[hc % 2][:],
            ca3[hc % 2][:], ALU.add),
            W(c2_ok, c3_ok,
              st["hc_last"][hc] if st["hc_last"] is not None else None,
              st["dbg_gate"] if w == 1 else None,
              sc.cur("gp")), "gp")
        lastB[f"ca{hc % 2}"] = sc.cur("gp")
        lastB[f"ca3{hc % 2}"] = sc.cur("gp")
        if hc % gsz == gsz - 1:
            st["pend_silu"].append((w, hc // gsz))

    def emit_silu_group(w, gq):
        hc = gq * gsz + gsz - 1
        O("scalar", lambda s, hc=hc: s.activation(
            hc_buf[:, (hc - gsz + 1) * TW:(hc + 1) * TW],
            hc_buf[:, (hc - gsz + 1) * TW:(hc + 1) * TW], AF.Silu),
            W(sc.cur("gp"), sc.cur("ac")), "ac")
        O("vector", lambda v, hc=hc, gq=gq: v.reduce_max(
            mg[:, gq * TW:(gq + 1) * TW],
            hc_buf[:, (hc - gsz + 1) * TW:(hc + 1) * TW]
            .rearrange("p (c t) -> p t c", c=gsz),
            axis=AX.X, apply_absolute_value=True),
            W(sc.cur("ac"), sc.cur("ve")), "ve")

    def emit_finals(w):
        O("vector", lambda v: v.reduce_max(
            mbuf[:], mg[:, 0:ng * TW].rearrange("p (c t) -> p t c", c=ng),
            axis=AX.X), W(sc.cur("ve"), lastB["m8"]), "ve")
        mbuf_ok = sc.cur("ve")
        for c in range(2):
            O("tensor", lambda t, c=c: t.transpose(
                pC[:, 128 * c:128 * (c + 1)], mbuf[:, 128 * c:128 * (c + 1)],
                identf[:]),
                W(mbuf_ok, lastB["pC"]) if c == 0 else (),
                "pe" if c == 1 else None)
        lastB["m8"] = sc.cur("pe")
        O("vector", lambda v: v.reduce_max(
            bsml[:, 0:2], pC[:, 0:256].rearrange("p (c t) -> p c t", c=2),
            axis=AX.X), W(sc.cur("pe"), lastB["bsml_ac"], sc.cur("ve")), "ve")
        lastB["pC"] = sc.cur("ve")
        O("vector", lambda v: v.tensor_scalar(bsml[:, 0:2], bsml[:, 0:2],
                                              1e-5, None, ALU.max),
          W(sc.cur("ve")), "ve")
        O("vector", lambda v: v.tensor_scalar(bsml[:, 2:4], bsml[:, 0:2],
                                              m2s_col[:], None, ALU.mult),
          W(sc.cur("ve"), wst["wq_col_ok"]), "ve")
        rh2_ok = sc.cur("ve")
        O("vector", lambda v: v.reciprocal(bsml[:, 4:6], bsml[:, 0:2]),
          W(sc.cur("ve")), "ve")
        O("vector", lambda v: v.tensor_scalar(bsml[:, 4:6], bsml[:, 4:6],
                                              127.0, None, ALU.mult),
          W(sc.cur("ve")), "ve")
        sc2_ok = sc.cur("ve")
        for c in range(2):
            O("tensor", lambda t, c=c: t.transpose(
                pC[0:1, 256:384], bsml[:, 4 + c:5 + c], identf[:]),
                W(sc2_ok, lastB["pC"]), "pe")
            O("vector", lambda v, c=c: v.tensor_copy(
                row2[0:1, 128 * c:128 * (c + 1)], pC[0:1, 256:384]),
                W(sc.cur("pe"), lastB["row2"] if c == 0 else None,
                  sc.cur("ve")), "ve")
            lastB["pC"] = sc.cur("ve")
        O("tensor", lambda t: t.matmul(pC[:, 0:TW], ones_f1[:], row2[:],
                                       start=True, stop=True),
          W(sc.cur("ve"), lastB["pC"]), "pe")
        lastB["row2"] = sc.cur("pe")
        O("vector", lambda v: v.tensor_copy(S2_sb[:], pC[:, 0:TW]),
          W(sc.cur("pe"), lastB["S2"], sc.cur("ve")), "ve")
        lastB["pC"] = sc.cur("ve")
        s2_ok = sc.cur("ve")
        dbg_ok = None
        if dbg and w == 0:
            O("sync", lambda s: s.dma_start(out=dbg_m[:, 0:TW], in_=mbuf[:]),
              W(s2_ok), "qc", 16)
            O("sync", lambda s: s.dma_start(out=dbg_h[:, 0:h32 * TW], in_=hc_buf[:]),
              W(s2_ok), "qc", 16)
            dbg_ok = sc.cur("qc")
            st["dbg_gate"] = dbg_ok
        return s2_ok, rh2_ok, dbg_ok

    def emit_quant_chunk(w, k, s2_ok, dbg_ok, hq_cnt, hc_last):
        sl = slice(k * TW, (k + 1) * TW)
        i2 = slice((k % 2) * TW, (k % 2) * TW + TW)
        O("gpsimd", lambda g, sl=sl: g.tensor_tensor(
            hc_buf[:, sl], hc_buf[:, sl], S2_sb[:], ALU.mult),
            W(s2_ok, dbg_ok if k == 0 else None, sc.cur("gp")), "gp")
        O("vector", lambda v, sl=sl: v.tensor_scalar(
            hc_buf[:, sl], hc_buf[:, sl], MAGIC, MAGIC, ALU.add, ALU.subtract),
            W(sc.cur("gp"), sc.cur("ve")), "ve")
        O("vector", lambda v, sl=sl, i2=i2: v.tensor_scalar(
            i32tr[:, i2], hc_buf[:, sl], CQ, None, ALU.add),
            W(st["i32ac"][k % 2], sc.cur("ve")), "ve")
        hc_last[k] = sc.cur("ve")
        if k % 2 == 1:
            O("scalar", lambda s, k=k: s.activation(
                hq_sb[:, (k - 1) * TW:(k + 1) * TW], i32tr[:],
                AF.Identity, bias=m256c[:]),
                W(lastB["hq"] if k == 1 else None, consts_ok,
                  sc.cur("ve"), sc.cur("ac")), "ac")
            st["i32ac"][0] = sc.cur("ac")
            st["i32ac"][1] = sc.cur("ac")
            hq_cnt[k] = sc.cur("ac")

    def emit_mm4_octet(w, ks, hq_cnt, mm4_done):
        for k in ks:
            for n in range(ndh):
                w0 = W(hq_cnt[ks[-1]] if (k == ks[0] and n == 0) else None,
                       w2t_ok if k == 0 and n == 0 else None,
                       lastB[f"pA{n}"] if k == 0 else None)
                is_stop = k == h32 - 1
                O("tensor", lambda t, n=n, k=k, is_stop=is_stop: t.matmul(
                    pA[:, 512 * n:512 * n + dw],
                    hq_sb[:, k * TW: k * TW + 128],
                    W2T[:, k * D + dw * n: k * D + dw * (n + 1)],
                    start=(k == 0), stop=is_stop),
                    w0, "pe" if is_stop else None)
                if is_stop:
                    mm4_done[(0, n)] = sc.cur("pe")
        for k in ks:
            w0 = W(lastB["pA2"] if k == 0 else None)
            is_stop = k == h32 - 1
            O("tensor", lambda t, k=k, is_stop=is_stop: t.matmul(
                pA[:, 1024:1024 + dw],
                hq_sb[:, k * TW + 128: k * TW + 256],
                W2T[:, k * D: k * D + dw],
                start=(k == 0), stop=is_stop),
                w0, "pe" if is_stop else None)
            if is_stop:
                mm4_done[(1, 0)] = sc.cur("pe")

    def emit_mm4_c1n1(w, mm4_done):
        if ndh < 2:
            return
        for k in range(h32):
            w0 = W(lastB["pC"] if k == 0 else None)
            is_stop = k == h32 - 1
            O("tensor", lambda t, k=k, is_stop=is_stop: t.matmul(
                pC[:, 0:dw],
                hq_sb[:, k * TW + 128: k * TW + 256],
                W2T[:, k * D + dw: k * D + 2 * dw],
                start=(k == 0), stop=is_stop),
                w0, "pe" if is_stop else None)
            if is_stop:
                mm4_done[(1, 1)] = sc.cur("pe")

    def emit_ln2_store(w, c):
        for bb in range(nbn):
            O("vector", lambda v, bb=bb: v.bn_stats(
                bnst2[:, 6 * bb:6 * (bb + 1)],
                y_win[:, 512 * bb:512 * bb + min(512, D)]),
                W(sc.cur("ve")), "ve")
        O("vector", lambda v: v.bn_aggr(bsml[:, 4:6], bnst2[:, 0:6 * nbn]),
          W(sc.cur("ve")), "ve")
        O("scalar", lambda s: s.activation(bsml[:, 6:7], bsml[:, 5:6],
                                           AF.Sqrt, bias=epsc[:]),
          W(sc.cur("ve"), consts_ok, sc.cur("ac")), "ac")
        lastB["bsml_ac"] = sc.cur("ac")
        O("vector", lambda v: v.reciprocal(bsml[:, 8:9], bsml[:, 6:7]),
          W(sc.cur("ac"), sc.cur("ve")), "ve")
        O("vector", lambda v: v.tensor_mul(bsml[:, 10:11], bsml[:, 4:5],
                                           bsml[:, 8:9]),
          W(sc.cur("ve")), "ve")
        O("vector", lambda v: v.tensor_scalar(
            y_win[:], y_win[:],
            bsml[:, 8:9], bsml[:, 10:11], ALU.mult, ALU.subtract),
            W(sc.cur("ve")), "ve")
        out_ok = sc.cur("ve")
        O("sync", lambda s, w=w, c=c: s.dma_start(
            out=out_hbm[TW * w + 128 * c:TW * w + 128 * (c + 1), :],
            in_=y_win[:]),
            W(out_ok), "qo", 16)
        lastB["ywin"] = sc.cur("qo")

    def emit_tail_c0(w, yld, rh2_ok, mm4_done):
        for n in range(ndh):
            O("vector", lambda v, n=n: v.scalar_tensor_tensor(
                y_win[:, dw * n:dw * (n + 1)],
                pA[:, 512 * n:512 * n + dw],
                bsml[:, 2:3],
                y_win[:, dw * n:dw * (n + 1)],
                ALU.mult, ALU.add),
                W(mm4_done[(0, n)], rh2_ok, yld, sc.cur("ve")), "ve")
            lastB[f"pA{n}"] = sc.cur("ve")
        emit_ln2_store(w, 0)

    def emit_tail_c1(w, yld, rh2_ok, mm4_done):
        O("vector", lambda v: v.scalar_tensor_tensor(
            y_win[:, 0:dw], pA[:, 1024:1024 + dw], bsml[:, 3:4],
            y_win[:, 0:dw], ALU.mult, ALU.add),
            W(mm4_done[(1, 0)], rh2_ok, yld, sc.cur("ve")), "ve")
        lastB["pA2"] = sc.cur("ve")
        if ndh >= 2:
            O("vector", lambda v: v.scalar_tensor_tensor(
                y_win[:, dw:2 * dw], pC[:, 0:dw], bsml[:, 3:4],
                y_win[:, dw:2 * dw], ALU.mult, ALU.add),
                W(mm4_done[(1, 1)], rh2_ok, yld, sc.cur("ve")), "ve")
            lastB["pC"] = sc.cur("ve")
        emit_ln2_store(w, 1)

    def flush_silu():
        for (wc, gq) in st["pend_silu"]:
            emit_silu_group(wc, gq)
        st["pend_silu"] = []

    st["hc_last"] = None
    for hc in range(h32):
        emit_conv_chunk(0, hc)
    flush_silu()
    lastB["S0"] = sc.cur("ve")
    lastB["yqTw0"] = sc.cur("pe")
    if nW > 1:
        s_oks[1] = emit_sprep(1)
        win_loads[1] = emit_win_loads(1)
    sc.mark("w_head")

    for w in range(nW):
        sc.mark(f"win{w}")
        s2_ok, rh2_ok, dbg_ok = emit_finals(w)
        hq_cnt = {}
        hc_last = {}
        st["hc_last"] = hc_last
        yld = O("sync", lambda s, w=w: s.dma_start(
            out=y_win[:],
            in_=y_sc[TW * w:TW * w + 128, :]),
            W(lastB["ywin"], last["y"]), "qyl", 16)
        mm4_done = {}
        for oi, ks in enumerate(octets):
            flush_silu()
            for k in ks:
                emit_quant_chunk(w, k, s2_ok, dbg_ok, hq_cnt, hc_last)
            if w + 1 < nW:
                for hc in ks:
                    emit_conv_chunk(w + 1, hc)
            if oi > 0:
                emit_mm4_octet(w, octets[oi - 1], hq_cnt, mm4_done)
        emit_mm4_octet(w, octets[-1], hq_cnt, mm4_done)
        if w + 1 < nW:
            lastB[f"S{(w + 1) % 2}"] = sc.cur("ve")
            lastB[f"yqTw{(w + 1) % 2}"] = sc.cur("pe")
        emit_mm4_c1n1(w, mm4_done)
        emit_tail_c0(w, yld, rh2_ok, mm4_done)
        yld = O("sync", lambda s, w=w: s.dma_start(
            out=y_win[:],
            in_=y_sc[TW * w + 128:TW * w + 256, :]),
            W(lastB["ywin"], last["y"]), "qyl", 16)
        emit_tail_c1(w, yld, rh2_ok, mm4_done)
        lastB["hq"] = sc.cur("pe")
        lastB["S2"] = sc.cur("ve")
        flush_silu()
        if w + 2 < nW:
            s_oks[w + 2] = emit_sprep(w + 2)
            win_loads[w + 2] = emit_win_loads(w + 2)
        win_loads.pop(w)
        s_oks.pop(w)

    sc.streams["sync"].append((W(sc.cur("qo"), sc.cur("qy"), sc.cur("qq")), None, None, sc.seq))
    sc.streams["gpsimd"].append((W(sc.cur("qk0"), sc.cur("qk1"), sc.cur("qv")), None, None, sc.seq))
    sc.seq += 1

    with nc.Block() as block:
        sc.emit(block, sems)

    outer.close()
    return nc


_NC_CACHE = {}


def get_program(key=(1024, 4096, 4096, 2048, 2)):
    if key not in _NC_CACHE:
        _NC_CACHE[key] = build_core(*key)
    return _NC_CACHE[key]


def kernel(L0_attn_inputs, L0_attn_key_param_tokens, L0_attn_value_param_tokens,
           L0_ffn_w1, L0_ffn_conv_w, L0_ffn_conv_b, L0_ffn_w2):
    x = np.ascontiguousarray(np.asarray(L0_attn_inputs, dtype=np.float32))
    k = np.ascontiguousarray(np.asarray(L0_attn_key_param_tokens, dtype=np.float32))
    v = np.ascontiguousarray(np.asarray(L0_attn_value_param_tokens, dtype=np.float32))
    w1 = np.ascontiguousarray(np.asarray(L0_ffn_w1, dtype=np.float32))
    w2 = np.ascontiguousarray(np.asarray(L0_ffn_w2, dtype=np.float32))
    cw = np.asarray(L0_ffn_conv_w, dtype=np.float32).reshape(-1, 3)
    cb = np.asarray(L0_ffn_conv_b, dtype=np.float32).reshape(-1, 1)
    cwb = np.ascontiguousarray(np.concatenate([cw, cb], axis=1))

    B, L, D = x.shape
    ncores = 8
    bc = B // ncores
    nc_prog = get_program((D, k.shape[0], w1.shape[0], L, bc))
    in_maps = []
    for c in range(ncores):
        in_maps.append({
            "x": np.ascontiguousarray(x[bc * c:bc * (c + 1)].reshape(bc * L, D)),
            "k": k, "v": v, "w1": w1, "w2": w2, "cwb": cwb,
        })
    res = run_bass_kernel_spmd(nc_prog, in_maps, core_ids=list(range(ncores)))
    out = np.concatenate(
        [r["out"].reshape(bc, L, D) for r in res.results], axis=0)
    return out


# version 23
# speedup vs baseline: 1.0017x; 1.0017x over previous
import contextlib

import numpy as np

import concourse.bass as bass
import concourse.mybir as mybir
from concourse.ap import AP
from concourse.bass_utils import run_bass_kernel_spmd

F32 = mybir.dt.float32
BF16 = mybir.dt.bfloat16
I32 = mybir.dt.int32
AF = mybir.ActivationFunctionType
ALU = mybir.AluOpType
AX = mybir.AxisListType

ENGINES = ("sync", "gpsimd", "vector", "scalar", "tensor")

MAGIC = 12582912.0
CQ = 256.49996948242188


def W(*items):
    out = []
    for it in items:
        if it is None:
            continue
        if isinstance(it, tuple) and it and all(
                sub is None or isinstance(sub, tuple) for sub in it):
            for sub in it:
                if sub is not None and sub[1] > 0:
                    out.append(sub)
            continue
        s, v = it
        if v > 0:
            out.append((s, v))
    return tuple(out)


TRUNCATE_AT = None


class Sched:
    def __init__(self):
        self.streams = {e: [] for e in ENGINES}
        self.cnt = {}
        self.seq = 0
        self.sections = {}

    def mark(self, name):
        self.sections[name] = (self.seq, dict(self.cnt))

    def n(self, sem):
        return self.cnt.get(sem, 0)

    def cur(self, sem):
        return (sem, self.cnt.get(sem, 0))

    def op(self, eng, emit, waits=(), inc=None, inc_n=1):
        if inc is not None:
            self.cnt[inc] = self.cnt.get(inc, 0) + inc_n
        self.streams[eng].append(
            (tuple(waits), emit, (inc, inc_n) if inc else None, self.seq))
        self.seq += 1
        return (inc, self.cnt[inc]) if inc else None

    def barrier(self):
        f = W(*[(s, v) for s, v in self.cnt.items()])
        for e in ENGINES:
            self.streams[e].append((f, None, None, self.seq))
        self.seq += 1

    def emit(self, block, sems):
        cutoff, fence = None, None
        if TRUNCATE_AT is not None:
            cutoff, fcnt = self.sections[TRUNCATE_AT]
            fence = W(*[(s, v) for s, v in fcnt.items()])

        def run(eng_obj, stream):
            waited = {}
            for waits, emitf, inc, seq in stream:
                if cutoff is not None and seq >= cutoff:
                    continue
                for s, v in waits:
                    if waited.get(s, -1) < v:
                        eng_obj.wait_ge(sems[s], v)
                        waited[s] = v
                if emitf is not None:
                    instr = emitf(eng_obj)
                    if inc is not None:
                        instr.then_inc(sems[inc[0]], inc[1])
            if fence is not None:
                for s, v in fence:
                    if waited.get(s, -1) < v:
                        eng_obj.wait_ge(sems[s], v)

        for name in ENGINES:
            getattr(block, name)((lambda st: (lambda e: run(e, st)))(self.streams[name]))


def build_core(D=1024, S=4096, H=4096, L=2048, Bc=2, dbg=False):
    TA = 256
    TW = 256
    T = Bc * L
    d8, s32, h32 = D // 128, S // 128, H // 128
    nA = T // TA
    nW = T // TW
    nWq = L // TW
    WM = max(D, H)
    sqrtS = float(np.float32(np.sqrt(S)))
    ndh = max(1, D // 512)
    dw = D // ndh
    nbn = max(1, D // 512)
    nqg = 1
    assert TA == 256 and TW == 256 and L % TW == 0 and T % TA == 0
    assert (H * D) & (H * D - 1) == 0
    assert h32 % nqg == 0

    nc = bass.Bass("TRN2", target_bir_lowering=False, debug=False)

    x_in = nc.dram_tensor("x", [T, D], F32, kind="ExternalInput").ap()
    k_in = nc.dram_tensor("k", [S, D], F32, kind="ExternalInput").ap()
    v_in = nc.dram_tensor("v", [S, D], F32, kind="ExternalInput").ap()
    w1_in = nc.dram_tensor("w1", [H, D], F32, kind="ExternalInput").ap()
    w2_in = nc.dram_tensor("w2", [D, H], F32, kind="ExternalInput").ap()
    cwb_in = nc.dram_tensor("cwb", [H, 4], F32, kind="ExternalInput").ap()
    out_hbm = nc.dram_tensor("out", [T, D], F32, kind="ExternalOutput").ap()
    ikind = "ExternalOutput" if dbg else "Internal"
    y_sc = nc.dram_tensor("y_sc", [T, D], F32, kind=ikind).ap()
    yqT_sc = nc.dram_tensor("yqT_sc", [D, T], BF16, kind=ikind).ap()
    w1T_sc = nc.dram_tensor("w1T_sc", [D, H], BF16, kind=ikind).ap()
    w2T_sc = nc.dram_tensor("w2T_sc", [H, D], BF16, kind=ikind).ap()
    ry_sc = nc.dram_tensor("ry_sc", [1, T], F32, kind=ikind).ap()
    if dbg:
        dbg_m = nc.dram_tensor("dbg_m", [128, 512], F32, kind="ExternalOutput").ap()
        dbg_r = nc.dram_tensor("dbg_r", [4, 512], F32, kind="ExternalOutput").ap()
        dbg_h = nc.dram_tensor("dbg_h", [128, 8192], F32, kind="ExternalOutput").ap()

    outer = contextlib.ExitStack()
    sbuf = lambda name, shape, dt=F32: outer.enter_context(nc.sbuf_tensor(name, shape, dt))
    sems = {s: outer.enter_context(nc.semaphore(s))
            for s in ("pe", "ve", "ac", "gp", "qc", "qw0", "qw1", "qs",
                      "qk0", "qk1", "qv", "qx0", "qx1", "qy", "qq",
                      "qw1t", "qw2t", "qt0", "qt1", "qyl", "qo",
                      "qr0", "qr1", "qrl0", "qrl1",
                      "qg0", "qg1", "qg2", "qg3")}
    sc = Sched()
    O = sc.op

    pA = outer.enter_context(nc.psum_tensor("pA", [128, 2048], F32))
    pB = outer.enter_context(nc.psum_tensor("pB", [128, 1024], F32))
    pC = outer.enter_context(nc.psum_tensor("pC", [128, 512], F32))
    pD = outer.enter_context(nc.psum_tensor("pD", [128, 1024], BF16))

    identb = sbuf("identb", [128, 128], BF16)
    identf = sbuf("identf", [128, 128], F32)
    ones_f1 = sbuf("ones_f1", [1, 128], F32)
    one11 = sbuf("one11", [1, 1], F32)
    ones_bf = sbuf("ones_bf", [128, 1], BF16)
    b256 = sbuf("b256", [128, 1], F32)
    cqc = sbuf("cqc", [128, 1], F32)
    m256c = sbuf("m256c", [128, 1], F32)
    epsc = sbuf("epsc", [128, 1], F32)
    cwb_sb = sbuf("cwb_sb", [128, h32 * 4], F32)
    mrow_s = sbuf("mrow_s", [1, 2], F32)
    mcl = sbuf("mcl", [1, 2], F32)
    wsc = sbuf("wsc", [1, 2], F32)
    mwv = sbuf("mwv", [1, 2], F32)
    m2s = sbuf("m2s", [1, 1], F32)
    wq_col = sbuf("wq_col", [128, 2], F32)
    m2s_col = sbuf("m2s_col", [128, 1], F32)

    def ident_init(g, buf):
        return g.affine_select(out=buf[:], in_=buf[:], compare_op=ALU.not_equal,
                               fill=1.0, base=0, pattern=[[-1, 128]],
                               channel_multiplier=1)

    O("gpsimd", lambda g: g.memset(identb[:], 0.0), (), "gp")
    O("gpsimd", lambda g: ident_init(g, identb), W(sc.cur("gp")), "gp")
    O("gpsimd", lambda g: g.memset(identf[:], 0.0), (), "gp")
    O("gpsimd", lambda g: ident_init(g, identf), W(sc.cur("gp")), "gp")
    for buf, val in ((ones_f1, 1.0), (one11, 1.0), (ones_bf, 1.0),
                     (b256, 256.5), (epsc, 1e-5), (cqc, CQ), (m256c, -256.0)):
        O("gpsimd", lambda g, b=buf, v=val: g.memset(b[:], v), (), "gp")
    consts_ok = sc.cur("gp")

    O("sync", lambda s: s.dma_start(
        out=cwb_sb[:].rearrange("p (c k) -> p c k", c=h32),
        in_=cwb_in.rearrange("(c p) k -> p c k", p=128)), (), "qc", 16)
    cwb_ok = sc.cur("qc")
    sc.mark("consts")

    pa = contextlib.ExitStack()
    asb = lambda name, shape, dt=F32: pa.enter_context(nc.sbuf_tensor(name, shape, dt))
    KT = asb("KT", [128, d8 * S], BF16)
    V_sb = asb("V_sb", [128, s32 * D], BF16)
    kbts = [asb(f"kbt{i}", [128, D], BF16) for i in range(2)]
    buf_free = [None, None]
    for c in range(S // 128):
        b = c % 2
        ld = O("gpsimd", lambda g, c=c, b=b: g.dma_start(
            out=kbts[b][:], in_=k_in[128 * c:128 * (c + 1), :]),
            W(buf_free[b]), f"qk{b}", 16)
        for g0 in range(0, d8, 8):
            gn = min(8, d8 - g0)
            for jj in range(gn):
                j = g0 + jj
                O("tensor", lambda t, j=j, jj=jj, b=b: t.transpose(
                    pD[:, 128 * jj:128 * (jj + 1)],
                    kbts[b][:, 128 * j:128 * (j + 1)], identb[:]),
                    W(ld, sc.cur("ve"), consts_ok) if jj == 0 else (),
                    "pe" if jj == gn - 1 else None)
            O("vector", lambda v, c=c, g0=g0, gn=gn: v.tensor_copy(
                KT[:].rearrange("p (j s) -> p j s", j=d8)[:, g0:g0 + gn,
                                                          128 * c:128 * (c + 1)],
                pD[:, 0:128 * gn].rearrange("p (j q) -> p j q", j=gn)),
                W(sc.cur("pe")), "ve")
        buf_free[b] = sc.cur("pe")
    kt_ok = sc.cur("ve")
    sc.mark("ktv")
    v_ok = O("gpsimd", lambda g: g.dma_start(
        out=V_sb[:].rearrange("p (c d) -> p c d", c=s32),
        in_=v_in.rearrange("(c p) d -> p c d", p=128)), (), "qv", 16)

    UW = min(512, min(D, H))
    wtmp = [asb(f"wtmp{i}", [128, UW], F32) for i in range(3)]
    wred = asb("wred", [128, 2], F32)
    wacc = asb("wacc", [128, 2], F32)
    wi32 = asb("wi32", [128, UW], I32)
    wqb = [asb(f"wqb{i}", [128, UW], BF16) for i in range(2)]
    wstg = asb("wstg", [128, (UW // 128) * 128], BF16)

    O("vector", lambda v: v.memset(wacc[:], 0.0), (), "ve")

    wst = {"buf_free": [None] * 3, "stg_free": None, "n": 0, "nq": 0, "np": 0,
           "wq_col_ok": None, "mw_ok": None, "wqb_free": [None, None],
           "wqb_ok": [None, None], "loads": [], "ld_next": 0, "ld_h": {}}

    def issue_wload():
        u = wst["ld_next"]
        if u >= len(wst["loads"]):
            return
        wst["ld_next"] += 1
        m, r, cc = wst["loads"][u]
        b = u % 3
        wst["ld_h"][u] = O("gpsimd", lambda g, m=m, r=r, cc=cc, b=b: g.dma_start(
            out=wtmp[b][:], in_=m[128 * r:128 * (r + 1), UW * cc:UW * (cc + 1)]),
            W(wst["buf_free"][b]), f"qg{b}", 16)

    def units_of(rows, width):
        return [(r, cc) for r in range(rows // 128) for cc in range(width // UW)]

    def mean_unit(m_in, r, cc, col):
        if wst["n"] == 0:
            for _ in range(3):
                issue_wload()
        u = wst["n"]
        b = u % 3
        wst["n"] += 1
        O("vector", lambda v, b=b: v.reduce_sum(
            wred[:, 0:1], wtmp[b][:], axis=AX.X, apply_absolute_value=True),
            W(wst["ld_h"].pop(u), sc.cur("ve")), "ve")
        wst["buf_free"][b] = sc.cur("ve")
        O("vector", lambda v, col=col: v.tensor_tensor(
            wacc[:, col:col + 1], wacc[:, col:col + 1], wred[:, 0:1], ALU.add),
            W(sc.cur("ve")), "ve")
        issue_wload()

    def mean_finalize():
        O("gpsimd", lambda g: g.tensor_reduce(mrow_s[:], wacc[:], axis=AX.C,
                                              op=ALU.add),
          W(sc.cur("ve"), consts_ok), "gp")
        O("vector", lambda v: v.tensor_scalar(mcl[:], mrow_s[:], 1.0 / (H * D), 1e-5,
                                              ALU.mult, ALU.max),
          W(sc.cur("gp")), "ve")
        O("vector", lambda v: v.reciprocal(wsc[:], mcl[:]), W(sc.cur("ve")), "ve")
        O("vector", lambda v: v.reciprocal(mwv[:], wsc[:]), W(sc.cur("ve")), "ve")
        O("vector", lambda v: v.tensor_scalar(m2s[:], mwv[0:1, 1:2], 1.0 / 127.0,
                                              None, ALU.mult),
          W(sc.cur("ve")), "ve")
        wst["mw_ok"] = sc.cur("ve")
        O("tensor", lambda t: t.matmul(pA[:, 0:2], ones_f1[:], wsc[:],
                                       start=True, stop=True),
          W(wst["mw_ok"], consts_ok, last["pA0"]), "pe")
        O("tensor", lambda t: t.matmul(pA[:, 2:3], ones_f1[:], m2s[:],
                                       start=True, stop=True), (), "pe")
        O("vector", lambda v: v.tensor_copy(wq_col[:], pA[:, 0:2]),
          W(sc.cur("pe"), sc.cur("ve")), "ve")
        O("vector", lambda v: v.tensor_copy(m2s_col[:], pA[:, 2:3]),
          W(sc.cur("ve")), "ve")
        last["pA0"] = sc.cur("ve")
        wst["wq_col_ok"] = sc.cur("ve")

    def tern_unit_dve(m_in, r, cc, col):
        u = wst["n"]
        b = u % 3
        qb = wst["nq"] % 2
        wst["n"] += 1
        wst["nq"] += 1
        O("vector", lambda v, b=b, col=col: v.tensor_scalar(
            wtmp[b][:], wtmp[b][:], wq_col[:, col:col + 1], 1.0, ALU.mult, ALU.min),
            W(wst["ld_h"].pop(u), wst["wq_col_ok"], sc.cur("ve")), "ve")
        O("vector", lambda v, b=b: v.tensor_scalar(
            wtmp[b][:], wtmp[b][:], -1.0, MAGIC, ALU.max, ALU.add),
            W(sc.cur("ve")), "ve")
        O("vector", lambda v, b=b: v.tensor_scalar(
            wi32[:], wtmp[b][:], MAGIC, CQ, ALU.subtract, ALU.add),
            W(sc.cur("ve")), "ve")
        wst["buf_free"][b] = sc.cur("ve")
        issue_wload()
        O("vector", lambda v, qb=qb: v.tensor_scalar(
            wqb[qb][:], wi32[:], -256.0, None, ALU.add),
            W(sc.cur("ve"), wst["wqb_free"][qb]), "ve")
        wst["wqb_ok"][qb] = sc.cur("ve")

    def tern_unit_pe(m_sc, r, cc):
        qb = wst["np"] % 2
        wst["np"] += 1
        nj = UW // 128
        for jj in range(nj):
            O("tensor", lambda t, jj=jj, qb=qb: t.transpose(
                pD[:, 128 * jj:128 * (jj + 1)],
                wqb[qb][:, 128 * jj:128 * (jj + 1)], identb[:]),
                W(wst["wqb_ok"][qb], last["pD"], consts_ok) if jj == 0 else (),
                "pe" if jj == nj - 1 else None)
        wst["wqb_free"][qb] = sc.cur("pe")
        O("vector", lambda v, nj=nj: v.tensor_copy(
            wstg[:, 0:128 * nj], pD[:, 0:128 * nj]),
            W(sc.cur("pe"), wst["stg_free"], sc.cur("ve")), "ve")
        last["pD"] = sc.cur("ve")
        wst["stg_free"] = O("sync", lambda s, m_sc=m_sc, r=r, cc=cc, nj=nj:
                            s.dma_start(
            out=m_sc.rearrange("(j p) r -> p j r", p=128)[
                :, (UW * cc) // 128:(UW * cc) // 128 + nj,
                128 * r:128 * (r + 1)],
            in_=wstg[:, 0:128 * nj].rearrange("p (j q) -> p j q", j=nj)),
            W(sc.cur("ve")), "qs", 16)

    wprep_steps = []
    for r, cc in units_of(H, D):
        wst["loads"].append((w1_in, r, cc))
    for r, cc in units_of(D, H):
        wst["loads"].append((w2_in, r, cc))
    for r, cc in units_of(H, D):
        wst["loads"].append((w1_in, r, cc))
    for r, cc in units_of(D, H):
        wst["loads"].append((w2_in, r, cc))
    for r, cc in units_of(H, D):
        wprep_steps.append(lambda r=r, cc=cc: mean_unit(w1_in, r, cc, 0))
    for r, cc in units_of(D, H):
        wprep_steps.append(lambda r=r, cc=cc: mean_unit(w2_in, r, cc, 1))
    wprep_steps.append(mean_finalize)
    tern_list = ([(w1_in, w1T_sc, r, cc, 0) for r, cc in units_of(H, D)]
                 + [(w2_in, w2T_sc, r, cc, 1) for r, cc in units_of(D, H)])
    for idx, (tm_in, tm_sc, tr, tcc, tcol) in enumerate(tern_list):
        def step(idx=idx, m_in=tm_in, m_sc=tm_sc, r=tr, cc=tcc, col=tcol):
            if idx > 0:
                pm = tern_list[idx - 1]
                tern_unit_pe(pm[1], pm[2], pm[3])
            tern_unit_dve(m_in, r, cc, col)
            if idx == len(tern_list) - 1:
                tern_unit_pe(m_sc, r, cc)
        wprep_steps.append(step)

    x_sb = [asb(f"x_sb{i}", [128, 2 * D], F32) for i in range(2)]
    xb_sb = asb("xb_sb", [128, D], BF16)
    xT_sb = asb("xT_sb", [128, d8 * TA], BF16)
    g_sb = asb("g_sb", [128, s32 * TA], BF16)
    g2_sb = [asb(f"g2_sb{i}", [128, TA], BF16) for i in range(3)]
    l2row = asb("l2row", [1, TA], F32)
    zh = asb("zh", [128, D], F32)
    y_sb = asb("y_sb", [128, 2 * D], F32)
    yqi = asb("yqi", [128, D // 2], I32)
    yqb = asb("yqb", [128, 2 * D], BF16)
    yqT_sb = asb("yqT_sb", [128, d8 * 128], BF16)
    bnst = asb("bnst", [128, nbn * 6], F32)
    smalls = asb("smalls", [128, 16], F32)
    ry_row = [asb(f"ry_row{i}", [1, TA], F32) for i in range(2)]

    last = {k: None for k in ("x0", "x1", "xb", "xT", "g", "pD", "pC",
                              "zh", "y", "yqi", "yqb", "yqT", "smalls_ac",
                              "pS0", "pS1", "g2_0", "g2_1", "g2_2", "g2_3",
                              "ry", "yqb_rdy", "qsc_ok", "rysm",
                              "ryb0", "ryb1")}
    last.update({f"pA{i}": None for i in range(4)})

    nmean = len(units_of(H, D)) + len(units_of(D, H)) + 1
    ntern = len(units_of(H, D)) + len(units_of(D, H))
    mtiles = max(1, min(4, nA // 4))
    wsteps_per_tile = []
    for i in range(nA):
        if i < mtiles:
            k = -(-nmean // mtiles)
        else:
            k = -(-ntern // (nA - mtiles))
        wsteps_per_tile.append(k)

    xloads = [O("sync", lambda s: s.dma_start(
        out=x_sb[0][:].rearrange("p (c d) -> p c d", c=2),
        in_=x_in[0:TA, :].rearrange("(c p) d -> p c d", p=128)), (), "qx0", 16)]

    def emit_head(i):
        xload = xloads[i]
        for c in range(2):
            O("vector", lambda v, c=c, xs=x_sb[i % 2]: v.tensor_copy(
                xb_sb[:], xs[:, c * D:(c + 1) * D]),
                W(xload, last["xb"], sc.cur("ve")), "ve")
            xb_ok = sc.cur("ve")
            for j in range(d8):
                O("tensor", lambda t, c=c, j=j: t.transpose(
                    pD[:, 128 * j:128 * (j + 1)],
                    xb_sb[:, 128 * j:128 * (j + 1)], identb[:]),
                    W(xb_ok, last["pD"], consts_ok) if j == 0 else (),
                    "pe" if j == d8 - 1 else None)
            O("vector", lambda v, c=c: v.tensor_copy(
                xT_sb[:].rearrange("p (j t) -> p j t", j=d8)[:, :, 128 * c:128 * (c + 1)],
                pD[:, 0:128 * d8].rearrange("p (j q) -> p j q", j=d8)),
                W(sc.cur("pe"), last["xT"], sc.cur("ve")), "ve")
            last["pD"] = sc.cur("ve")
            last["xb"] = sc.cur("pe")
        return sc.cur("ve")

    def emit_yq_lag(i):
        for c in range(2):
            for j in range(d8):
                O("tensor", lambda t, c=c, j=j: t.transpose(
                    pD[:, 128 * j:128 * (j + 1)],
                    yqb[:, c * D + 128 * j:c * D + 128 * (j + 1)], identb[:]),
                    W(last["yqb_rdy"], last["pD"]) if j == 0 else (),
                    "pe" if j == d8 - 1 else None)
            last["yqb"] = sc.cur("pe")
            O("vector", lambda v, c=c: v.tensor_copy(
                yqT_sb[:], pD[:, 0:128 * d8]),
                W(sc.cur("pe"), last["yqT"], sc.cur("ve")), "ve")
            last["pD"] = sc.cur("ve")
            O("sync", lambda s, i=i, c=c: s.dma_start(
                out=yqT_sc.rearrange("(j p) t -> p j t", p=128)[
                    :, :, TA * i + 128 * c:TA * i + 128 * (c + 1)],
                in_=yqT_sb[:].rearrange("p (j t) -> p j t", j=d8)),
                W(sc.cur("ve")), "qq", 16)
            last["yqT"] = sc.cur("qq")
        rb = i % 2
        for c in range(2):
            O("tensor", lambda t, c=c: t.transpose(
                pC[0:1, 256:384], smalls[:, 14 + c:15 + c], identf[:]),
                W(last["qsc_ok"], last["pC"], last[f"ryb{rb}"]), "pe")
            O("vector", lambda v, c=c, rb=rb: v.tensor_copy(
                ry_row[rb][0:1, 128 * c:128 * (c + 1)], pC[0:1, 256:384]),
                W(sc.cur("pe"), sc.cur("ve")), "ve")
            last["pC"] = sc.cur("ve")
        last["rysm"] = sc.cur("pe")
        O("sync", lambda s, i=i, rb=rb: s.dma_start(
            out=ry_sc[0:1, TA * i:TA * (i + 1)], in_=ry_row[rb][:]),
            W(sc.cur("ve")), f"qr{rb}", 16)
        last[f"ryb{rb}"] = sc.cur(f"qr{rb}")
        last["ry"] = sc.cur(f"qr{rb}")

    xT_ok = None
    for i in range(nA):
        sc.mark(f"tile{i}")
        wquota = [wsteps_per_tile[i] if i < len(wsteps_per_tile) else 0]
        xb_i = i % 2
        xload = xloads[i]

        if i + 1 < nA:
            xloads.append(O("sync", lambda s, i=i: s.dma_start(
                out=x_sb[(i + 1) % 2][:].rearrange("p (c d) -> p c d", c=2),
                in_=x_in[TA * (i + 1):TA * (i + 2), :]
                .rearrange("(c p) d -> p c d", p=128)),
                W(last[f"x{(i + 1) % 2}"]), f"qx{(i + 1) % 2}", 16))

        if i == 0:
            xT_ok = emit_head(0)

        ps_lohi = [(0, TA), (512, 512 + TA)]
        gelu_cnt = {}
        g2_cnt = {}

        def emit_ones(ss, i=i):
            O("tensor", lambda t, ss=ss: t.matmul(
                pC[0:1, 0:TA], ones_bf[:], g2_sb[ss % 3][:],
                start=(ss == 0), stop=(ss == s32 - 1)),
                W(g2_cnt[ss], last["pC"] if ss == 0 else None), "pe")
            last[f"g2_{ss % 3}"] = sc.cur("pe")

        for ss in range(s32):
            lo, hi = ps_lohi[ss % 2]
            w0 = [xT_ok, kt_ok]
            if ss >= 2:
                w0.append(gelu_cnt[ss - 2])
            else:
                w0.append(last[f"pS{ss}"])
            for j in range(d8):
                O("tensor", lambda t, ss=ss, j=j, lo=lo, hi=hi: t.matmul(
                    pB[:, lo:hi],
                    KT[:, j * S + 128 * ss: j * S + 128 * (ss + 1)],
                    xT_sb[:, j * TA:(j + 1) * TA],
                    start=(j == 0), stop=(j == d8 - 1)),
                    W(*w0) if j == 0 else (), "pe" if j == d8 - 1 else None)
            mm1_done = sc.cur("pe")
            if ss >= 3:
                emit_ones(ss - 3)
            O("scalar", lambda s, ss=ss, lo=lo, hi=hi: s.activation(
                g_sb[:, ss * TA:(ss + 1) * TA], pB[:, lo:hi], AF.Gelu),
                W(mm1_done), "ac")
            gelu_cnt[ss] = sc.cur("ac")
            O("vector", lambda v, ss=ss: v.tensor_mul(
                g2_sb[ss % 3][:], g_sb[:, ss * TA:(ss + 1) * TA],
                g_sb[:, ss * TA:(ss + 1) * TA]),
                W(gelu_cnt[ss], last[f"g2_{ss % 3}"], sc.cur("ve")), "ve")
            g2_cnt[ss] = sc.cur("ve")
            if ss % 3 == 2 and wquota[0] > 0 and wprep_steps:
                nper = -(-wquota[0] // max(1, (s32 - ss) // 3))
                for _ in range(nper):
                    if wprep_steps:
                        wprep_steps.pop(0)()
                        wquota[0] -= 1
        for sss in range(max(0, s32 - 3), s32):
            emit_ones(sss)
        last["pS0"] = gelu_cnt[s32 - 2]
        last["pS1"] = gelu_cnt[s32 - 1]
        l2mm_done = sc.cur("pe")

        last["xT"] = l2mm_done
        if i + 1 < nA:
            next_xT_ok = emit_head(i + 1)
        if i > 0:
            emit_yq_lag(i - 1)

        O("vector", lambda v: v.tensor_copy(l2row[:], pC[0:1, 0:TA]),
          W(l2mm_done, sc.cur("ve")), "ve")
        l2row_ok = sc.cur("ve")
        for c in range(2):
            O("tensor", lambda t, c=c: t.matmul(
                pC[:, 256 + c:257 + c], l2row[0:1, 128 * c:128 * (c + 1)], one11[:],
                start=True, stop=True),
                W(l2row_ok) if c == 0 else (), "pe" if c == 1 else None)
        O("vector", lambda v: v.tensor_copy(smalls[:, 0:2], pC[:, 256:258]),
          W(sc.cur("pe"), sc.cur("ve")), "ve")
        last["pC"] = sc.cur("ve")
        O("scalar", lambda s: s.activation(smalls[:, 0:2], smalls[:, 0:2], AF.Sqrt),
          W(sc.cur("ve"), last["smalls_ac"]), "ac")
        sqrt_ok = sc.cur("ac")
        O("vector", lambda v: v.reciprocal(smalls[:, 2:4], smalls[:, 0:2]),
          W(sqrt_ok, sc.cur("ve")), "ve")
        O("vector", lambda v: v.tensor_scalar(smalls[:, 2:4], smalls[:, 2:4],
                                              sqrtS, None, ALU.mult),
          W(sc.cur("ve")), "ve")
        rl2_ok = sc.cur("ve")

        for c in range(2):
            for n in range(ndh):
                gidx = c * ndh + n
                for k in range(s32):
                    w0 = W(gelu_cnt[k], v_ok, last[f"pA{gidx}"] if k == 0 else None)
                    O("tensor", lambda t, c=c, n=n, gidx=gidx, k=k: t.matmul(
                        pA[:, 512 * gidx:512 * gidx + dw],
                        g_sb[:, k * TA + 128 * c: k * TA + 128 * (c + 1)],
                        V_sb[:, k * D + dw * n: k * D + dw * (n + 1)],
                        start=(k == 0), stop=(k == s32 - 1)),
                        w0, "pe" if k == s32 - 1 else None)
                mm2_done = sc.cur("pe")
                O("vector", lambda v, c=c, n=n, gidx=gidx, xs=x_sb[xb_i]:
                  v.scalar_tensor_tensor(
                    zh[:, dw * n:dw * (n + 1)], pA[:, 512 * gidx:512 * gidx + dw],
                    smalls[:, 2 + c:3 + c],
                    xs[:, c * D + dw * n:c * D + dw * (n + 1)],
                    ALU.mult, ALU.add),
                    W(mm2_done, rl2_ok, xload,
                      last["zh"] if n == 0 else None, sc.cur("ve")), "ve")
                last[f"pA{gidx}"] = sc.cur("ve")
            z_ok = sc.cur("ve")
            for bb in range(nbn):
                O("vector", lambda v, bb=bb: v.bn_stats(
                    bnst[:, 6 * bb:6 * (bb + 1)],
                    zh[:, 512 * bb:512 * bb + min(512, D)]),
                    W(z_ok, sc.cur("ve")), "ve")
            O("vector", lambda v: v.bn_aggr(smalls[:, 4:6], bnst[:, 0:6 * nbn]),
              W(sc.cur("ve")), "ve")
            bn_ok = sc.cur("ve")
            O("scalar", lambda s: s.activation(smalls[:, 6:7], smalls[:, 5:6],
                                               AF.Sqrt, bias=epsc[:]),
              W(bn_ok, consts_ok), "ac")
            last["smalls_ac"] = sc.cur("ac")
            O("vector", lambda v: v.reciprocal(smalls[:, 8:9], smalls[:, 6:7]),
              W(sc.cur("ac"), sc.cur("ve")), "ve")
            O("vector", lambda v: v.tensor_mul(smalls[:, 10:11], smalls[:, 4:5],
                                               smalls[:, 8:9]),
              W(sc.cur("ve")), "ve")
            O("vector", lambda v, c=c: v.tensor_scalar(
                y_sb[:, c * D:(c + 1) * D], zh[:], smalls[:, 8:9], smalls[:, 10:11],
                ALU.mult, ALU.subtract),
                W(sc.cur("ve"), last["y"] if c == 0 else None), "ve")
            last["zh"] = sc.cur("ve")
            y_ok = sc.cur("ve")
            O("vector", lambda v, c=c: v.reduce_max(
                smalls[:, 12:13], y_sb[:, c * D:(c + 1) * D], axis=AX.X,
                apply_absolute_value=True), W(y_ok), "ve")
            O("vector", lambda v: v.tensor_scalar(smalls[:, 12:13], smalls[:, 12:13],
                                                  1e-5, None, ALU.max),
              W(sc.cur("ve")), "ve")
            O("vector", lambda v: v.reciprocal(smalls[:, 13:14], smalls[:, 12:13]),
              W(sc.cur("ve")), "ve")
            O("vector", lambda v: v.tensor_scalar(smalls[:, 13:14], smalls[:, 13:14],
                                                  127.0, None, ALU.mult),
              W(sc.cur("ve")), "ve")
            O("vector", lambda v, c=c: v.tensor_scalar(
                smalls[:, 14 + c:15 + c], smalls[:, 12:13], 1.0 / 127.0, None,
                ALU.mult),
                W(sc.cur("ve"), last["rysm"] if c == 0 else None), "ve")
            qsc_ok = sc.cur("ve")
            for hh in range(2):
                hw2 = D // 2
                O("vector", lambda v, c=c, hh=hh, hw2=hw2: v.tensor_scalar(
                    zh[:, hh * hw2:(hh + 1) * hw2],
                    y_sb[:, c * D + hh * hw2:c * D + (hh + 1) * hw2],
                    smalls[:, 13:14], MAGIC, ALU.mult, ALU.add),
                    W(qsc_ok, last["zh"], sc.cur("ve")), "ve")
                O("vector", lambda v, hh=hh, hw2=hw2: v.tensor_scalar(
                    yqi[:], zh[:, hh * hw2:(hh + 1) * hw2],
                    MAGIC, CQ, ALU.subtract, ALU.add),
                    W(last["yqi"], sc.cur("ve")), "ve")
                O("vector", lambda v, c=c, hh=hh, hw2=hw2: v.tensor_scalar(
                    yqb[:, c * D + hh * hw2:c * D + (hh + 1) * hw2],
                    yqi[:], -256.0, None, ALU.add),
                    W(last["yqb"] if (c == 0 and hh == 0) else None,
                      sc.cur("ve")), "ve")
                last["yqi"] = sc.cur("ve")
                last["zh"] = sc.cur("ve")
        last["g"] = sc.cur("pe")
        last["yqb_rdy"] = sc.cur("ve")
        last["qsc_ok"] = sc.cur("ve")

        O("sync", lambda s, i=i: s.dma_start(
            out=y_sc[TA * i:TA * (i + 1), :].rearrange("(c p) d -> p c d", p=128),
            in_=y_sb[:].rearrange("p (c d) -> p c d", c=2)),
            W(sc.cur("ve")), "qy", 16)
        last["y"] = sc.cur("qy")
        last[f"x{xb_i}"] = sc.cur("ve")
        if i + 1 < nA:
            xT_ok = next_xT_ok

    while wprep_steps:
        wprep_steps.pop(0)()
    emit_yq_lag(nA - 1)
    sc.mark("phaseA")
    pa.close()
    sc.barrier()

    W1T = sbuf("W1T", [128, d8 * H], BF16)
    W2T = sbuf("W2T", [128, h32 * D], BF16)
    yqTw = [sbuf(f"yqTw{i}", [128, d8 * (TW + 2)], BF16) for i in range(2)]
    y_win = sbuf("y_win", [128, D], F32)
    Srow_b = sbuf("Srow_b", [1, TW + 2], F32)
    Srow = [Srow_b, Srow_b]
    S_sb = [sbuf(f"S_sb{i}", [128, TW + 2], F32) for i in range(2)]
    htr = [sbuf(f"htr{i}", [128, TW + 2], F32) for i in range(3)]
    ca = [sbuf(f"ca{i}", [128, TW], F32) for i in range(2)]
    ca3 = [sbuf(f"ca3{i}", [128, TW], F32) for i in range(2)]
    hc_buf = sbuf("hc_buf", [128, h32 * TW], F32)
    mbuf = sbuf("mbuf", [128, TW], F32)
    mg = sbuf("mg", [128, 2 * TW], F32)
    i32tr = sbuf("i32tr", [128, 2 * TW], I32)
    hq_sb = sbuf("hq_sb", [128, h32 * TW], BF16)
    row2 = sbuf("row2", [1, TW], F32)
    S2_sb = sbuf("S2_sb", [128, TW], F32)
    bsml = sbuf("bsml", [128, 16], F32)
    bnst2 = sbuf("bnst2", [128, nbn * 6], F32)

    w1t_ok = O("sync", lambda s: s.dma_start(
        out=W1T[:].rearrange("p (j h) -> p j h", j=d8),
        in_=w1T_sc.rearrange("(j p) h -> p j h", p=128)), (), "qw1t", 16)
    w2t_ok = O("sync", lambda s: s.dma_start(
        out=W2T[:].rearrange("p (c d) -> p c d", c=h32),
        in_=w2T_sc.rearrange("(c p) d -> p c d", p=128)), (), "qw2t", 16)

    lastB = {k: None for k in ("pC", "htr0", "htr1", "htr2",
                               "htr0ac", "htr1ac", "htr2ac",
                               "ca0", "ca1", "ca30", "ca31",
                               "hc", "m8", "hq", "S2",
                               "row2", "ywin", "bsml_ac",
                               "Srow0", "Srow1", "S0", "S1", "pA2",
                               "mslot0", "mslot1", "mslot2", "mslot3")}
    lastB.update({f"pA{i}": None for i in range(4)})

    def emit_win_loads(w):
        q, wq = divmod(w, nWq)
        go = q * L + wq * TW
        lp = wq == 0
        rp = wq == nWq - 1
        c0 = go - (0 if lp else 1)
        c1 = min(go + TW + 1, q * L + L)
        ncols = c1 - c0
        dst0 = 1 if lp else 0
        wb = w % 2
        buf = yqTw[wb]
        ld = O("sync", lambda s, buf=buf, c0=c0, ncols=ncols, dst0=dst0: s.dma_start(
            out=buf[:].rearrange("p (j t) -> p j t", j=d8)[:, :, dst0:dst0 + ncols],
            in_=yqT_sc.rearrange("(j p) t -> p j t", p=128)[:, :, c0:c0 + ncols]),
            W(lastB.get(f"yqTw{wb}"), last["yqT"]), f"qt{wb}", 16)
        pads = []
        if lp:
            pads.append(O("vector", lambda v, buf=buf: v.memset(
                buf[:].rearrange("p (j t) -> p j t", j=d8)[:, :, 0:1], 0.0),
                W(lastB.get(f"yqTw{wb}"), ld, sc.cur("ve")), "ve"))
        if rp:
            pads.append(O("vector", lambda v, buf=buf: v.memset(
                buf[:].rearrange("p (j t) -> p j t", j=d8)[:, :, TW + 1:TW + 2], 0.0),
                W(lastB.get(f"yqTw{wb}"), ld, sc.cur("ve")), "ve"))
        return (ld, pads, go, lp, rp)

    def emit_sprep(w):
        wb = w % 2
        q, wq = divmod(w, nWq)
        go = q * L + wq * TW
        lp = wq == 0
        rp = wq == nWq - 1
        sr0 = go - (0 if lp else 1)
        srcols = min(go + TW + 1, q * L + L) - sr0
        sdst = 1 if lp else 0
        ryld = O("sync", lambda s, sr0=sr0, srcols=srcols, sdst=sdst, wb=wb:
                 s.dma_start(out=Srow[wb][0:1, sdst:sdst + srcols],
                             in_=ry_sc[0:1, sr0:sr0 + srcols]),
                 W(lastB["Srow0"], lastB["Srow1"], last["ry"]), f"qrl{wb}", 16)
        O("vector", lambda v, sr0=sr0, srcols=srcols, sdst=sdst, wb=wb:
          v.tensor_scalar(
            Srow[wb][0:1, sdst:sdst + srcols], Srow[wb][0:1, sdst:sdst + srcols],
            mwv[0:1, 0:1], None, ALU.mult),
            W(ryld, lastB[f"Srow{wb}"], wst["mw_ok"], sc.cur("ve")), "ve")
        if lp:
            O("vector", lambda v, wb=wb: v.memset(Srow[wb][0:1, 0:1], 0.0),
              W(sc.cur("ve")), "ve")
        if rp:
            O("vector", lambda v, wb=wb: v.memset(Srow[wb][0:1, TW + 1:TW + 2], 0.0),
              W(sc.cur("ve")), "ve")
        srow_ok = sc.cur("ve")
        O("tensor", lambda t, wb=wb: t.matmul(pC[:, 0:TW + 2], ones_f1[:],
                                              Srow[wb][:], start=True, stop=True),
          W(srow_ok, lastB["pC"]), "pe")
        O("vector", lambda v, wb=wb: v.tensor_copy(S_sb[wb][:], pC[:, 0:TW + 2]),
          W(sc.cur("pe"), lastB[f"S{wb}"], sc.cur("ve")), "ve")
        lastB["pC"] = sc.cur("ve")
        lastB[f"Srow{wb}"] = sc.cur("ve")
        return sc.cur("ve")

    win_loads = {0: emit_win_loads(0)}
    s_oks = {0: emit_sprep(0)}

    mslots = [(pB, 0), (pB, 512), (pA, 1536)]
    gsz = 16 if h32 % 16 == 0 else h32
    ng = h32 // gsz
    octets = [list(range(o, min(o + 8, h32))) for o in range(0, h32, 8)]
    st = {"hq_prev": None, "dbg_gate": None, "pend_silu": [], "i32ac": [None, None]}

    def emit_conv_chunk(w, hc):
        ld, pads, go, lp, rp = win_loads[w]
        s_ok = s_oks[w]
        s_i = hc % 3
        mp, lo = mslots[s_i]
        hi = lo + TW + 2

        def mm3(t, hc=hc, mp=mp, lo=lo, hi=hi, wb=w % 2):
            r = None
            for j in range(d8):
                r = t.matmul(
                    mp[:, lo:hi],
                    W1T[:, j * H + 128 * hc: j * H + 128 * (hc + 1)],
                    yqTw[wb][:, j * (TW + 2):(j + 1) * (TW + 2)],
                    start=(j == 0), stop=(j == d8 - 1))
            return r
        O("tensor", mm3,
          W(ld, *pads, w1t_ok, lastB[f"mslot{s_i}"]), "pe")
        mm3_done = sc.cur("pe")
        hb = hc % 3
        O("vector", lambda v, hb=hb, mp=mp, lo=lo, hi=hi, wb=w % 2:
          v.tensor_tensor(
            htr[hb][:], mp[:, lo:hi], S_sb[wb][:], ALU.mult),
            W(mm3_done, s_ok, lastB[f"htr{hb}"],
              lastB[f"htr{hb}ac"], sc.cur("ve")), "ve")
        lastB[f"mslot{s_i}"] = sc.cur("ve")
        htr_ok = sc.cur("ve")
        O("scalar", lambda s, hc=hc, hb=hb: s.activation(
            ca[hc % 2][:], htr[hb][:, 0:TW], AF.Identity,
            scale=cwb_sb[:, 4 * hc:4 * hc + 1],
            bias=cwb_sb[:, 4 * hc + 3:4 * hc + 4]),
            W(htr_ok, cwb_ok, lastB[f"ca{hc % 2}"]), "ac")
        c1_ok = sc.cur("ac")
        O("scalar", lambda s, hc=hc, hb=hb: s.activation(
            ca3[hc % 2][:], htr[hb][:, 2:TW + 2], AF.Identity,
            scale=cwb_sb[:, 4 * hc + 2:4 * hc + 3], bias=0.0),
            W(lastB[f"ca3{hc % 2}"], sc.cur("ac")), "ac")
        lastB[f"htr{hb}ac"] = sc.cur("ac")
        c3_ok = sc.cur("ac")
        O("vector", lambda v, hc=hc, hb=hb: v.scalar_tensor_tensor(
            ca[hc % 2][:], htr[hb][:, 1:TW + 1],
            cwb_sb[:, 4 * hc + 1:4 * hc + 2], ca[hc % 2][:],
            ALU.mult, ALU.add),
            W(c1_ok, sc.cur("ve")), "ve")
        lastB[f"htr{hb}"] = sc.cur("ve")
        c2_ok = sc.cur("ve")
        O("gpsimd", lambda g, hc=hc: g.tensor_tensor(
            hc_buf[:, hc * TW:(hc + 1) * TW], ca[hc % 2][:],
            ca3[hc % 2][:], ALU.add),
            W(c2_ok, c3_ok,
              st["hc_last"][hc] if st["hc_last"] is not None else None,
              st["dbg_gate"] if w == 1 else None,
              sc.cur("gp")), "gp")
        lastB[f"ca{hc % 2}"] = sc.cur("gp")
        lastB[f"ca3{hc % 2}"] = sc.cur("gp")
        if hc % gsz == gsz - 1:
            st["pend_silu"].append((w, hc // gsz))

    def emit_silu_group(w, gq):
        hc = gq * gsz + gsz - 1
        O("scalar", lambda s, hc=hc: s.activation(
            hc_buf[:, (hc - gsz + 1) * TW:(hc + 1) * TW],
            hc_buf[:, (hc - gsz + 1) * TW:(hc + 1) * TW], AF.Silu),
            W(sc.cur("gp"), sc.cur("ac")), "ac")
        O("vector", lambda v, hc=hc, gq=gq: v.reduce_max(
            mg[:, gq * TW:(gq + 1) * TW],
            hc_buf[:, (hc - gsz + 1) * TW:(hc + 1) * TW]
            .rearrange("p (c t) -> p t c", c=gsz),
            axis=AX.X, apply_absolute_value=True),
            W(sc.cur("ac"), sc.cur("ve")), "ve")

    def emit_finals(w):
        O("vector", lambda v: v.reduce_max(
            mbuf[:], mg[:, 0:ng * TW].rearrange("p (c t) -> p t c", c=ng),
            axis=AX.X), W(sc.cur("ve"), lastB["m8"]), "ve")
        mbuf_ok = sc.cur("ve")
        for c in range(2):
            O("tensor", lambda t, c=c: t.transpose(
                pC[:, 128 * c:128 * (c + 1)], mbuf[:, 128 * c:128 * (c + 1)],
                identf[:]),
                W(mbuf_ok, lastB["pC"]) if c == 0 else (),
                "pe" if c == 1 else None)
        lastB["m8"] = sc.cur("pe")
        O("vector", lambda v: v.reduce_max(
            bsml[:, 0:2], pC[:, 0:256].rearrange("p (c t) -> p c t", c=2),
            axis=AX.X), W(sc.cur("pe"), lastB["bsml_ac"], sc.cur("ve")), "ve")
        lastB["pC"] = sc.cur("ve")
        O("vector", lambda v: v.tensor_scalar(bsml[:, 0:2], bsml[:, 0:2],
                                              1e-5, None, ALU.max),
          W(sc.cur("ve")), "ve")
        O("vector", lambda v: v.tensor_scalar(bsml[:, 2:4], bsml[:, 0:2],
                                              m2s_col[:], None, ALU.mult),
          W(sc.cur("ve"), wst["wq_col_ok"]), "ve")
        rh2_ok = sc.cur("ve")
        O("vector", lambda v: v.reciprocal(bsml[:, 4:6], bsml[:, 0:2]),
          W(sc.cur("ve")), "ve")
        O("vector", lambda v: v.tensor_scalar(bsml[:, 4:6], bsml[:, 4:6],
                                              127.0, None, ALU.mult),
          W(sc.cur("ve")), "ve")
        sc2_ok = sc.cur("ve")
        for c in range(2):
            O("tensor", lambda t, c=c: t.transpose(
                pC[0:1, 256:384], bsml[:, 4 + c:5 + c], identf[:]),
                W(sc2_ok, lastB["pC"]), "pe")
            O("vector", lambda v, c=c: v.tensor_copy(
                row2[0:1, 128 * c:128 * (c + 1)], pC[0:1, 256:384]),
                W(sc.cur("pe"), lastB["row2"] if c == 0 else None,
                  sc.cur("ve")), "ve")
            lastB["pC"] = sc.cur("ve")
        O("tensor", lambda t: t.matmul(pC[:, 0:TW], ones_f1[:], row2[:],
                                       start=True, stop=True),
          W(sc.cur("ve"), lastB["pC"]), "pe")
        lastB["row2"] = sc.cur("pe")
        O("vector", lambda v: v.tensor_copy(S2_sb[:], pC[:, 0:TW]),
          W(sc.cur("pe"), lastB["S2"], sc.cur("ve")), "ve")
        lastB["pC"] = sc.cur("ve")
        s2_ok = sc.cur("ve")
        dbg_ok = None
        if dbg and w == 0:
            O("sync", lambda s: s.dma_start(out=dbg_m[:, 0:TW], in_=mbuf[:]),
              W(s2_ok), "qc", 16)
            O("sync", lambda s: s.dma_start(out=dbg_h[:, 0:h32 * TW], in_=hc_buf[:]),
              W(s2_ok), "qc", 16)
            dbg_ok = sc.cur("qc")
            st["dbg_gate"] = dbg_ok
        return s2_ok, rh2_ok, dbg_ok

    def emit_quant_chunk(w, k, s2_ok, dbg_ok, hq_cnt, hc_last):
        sl = slice(k * TW, (k + 1) * TW)
        i2 = slice((k % 2) * TW, (k % 2) * TW + TW)
        O("gpsimd", lambda g, sl=sl: g.tensor_tensor(
            hc_buf[:, sl], hc_buf[:, sl], S2_sb[:], ALU.mult),
            W(s2_ok, dbg_ok if k == 0 else None, sc.cur("gp")), "gp")
        O("vector", lambda v, sl=sl: v.tensor_scalar(
            hc_buf[:, sl], hc_buf[:, sl], MAGIC, MAGIC, ALU.add, ALU.subtract),
            W(sc.cur("gp"), sc.cur("ve")), "ve")
        O("vector", lambda v, sl=sl, i2=i2: v.tensor_scalar(
            i32tr[:, i2], hc_buf[:, sl], CQ, None, ALU.add),
            W(st["i32ac"][k % 2], sc.cur("ve")), "ve")
        hc_last[k] = sc.cur("ve")
        if k % 2 == 1:
            O("scalar", lambda s, k=k: s.activation(
                hq_sb[:, (k - 1) * TW:(k + 1) * TW], i32tr[:],
                AF.Identity, bias=m256c[:]),
                W(lastB["hq"] if k == 1 else None, consts_ok,
                  sc.cur("ve"), sc.cur("ac")), "ac")
            st["i32ac"][0] = sc.cur("ac")
            st["i32ac"][1] = sc.cur("ac")
            hq_cnt[k] = sc.cur("ac")

    def emit_mm4_octet(w, ks, hq_cnt, mm4_done):
        for k in ks:
            for n in range(ndh):
                w0 = W(hq_cnt[ks[-1]] if (k == ks[0] and n == 0) else None,
                       w2t_ok if k == 0 and n == 0 else None,
                       lastB[f"pA{n}"] if k == 0 else None)
                is_stop = k == h32 - 1
                O("tensor", lambda t, n=n, k=k, is_stop=is_stop: t.matmul(
                    pA[:, 512 * n:512 * n + dw],
                    hq_sb[:, k * TW: k * TW + 128],
                    W2T[:, k * D + dw * n: k * D + dw * (n + 1)],
                    start=(k == 0), stop=is_stop),
                    w0, "pe" if is_stop else None)
                if is_stop:
                    mm4_done[(0, n)] = sc.cur("pe")
        for k in ks:
            w0 = W(lastB["pA2"] if k == 0 else None)
            is_stop = k == h32 - 1
            O("tensor", lambda t, k=k, is_stop=is_stop: t.matmul(
                pA[:, 1024:1024 + dw],
                hq_sb[:, k * TW + 128: k * TW + 256],
                W2T[:, k * D: k * D + dw],
                start=(k == 0), stop=is_stop),
                w0, "pe" if is_stop else None)
            if is_stop:
                mm4_done[(1, 0)] = sc.cur("pe")

    def emit_mm4_c1n1(w, mm4_done):
        if ndh < 2:
            return
        for k in range(h32):
            w0 = W(lastB["pC"] if k == 0 else None)
            is_stop = k == h32 - 1
            O("tensor", lambda t, k=k, is_stop=is_stop: t.matmul(
                pC[:, 0:dw],
                hq_sb[:, k * TW + 128: k * TW + 256],
                W2T[:, k * D + dw: k * D + 2 * dw],
                start=(k == 0), stop=is_stop),
                w0, "pe" if is_stop else None)
            if is_stop:
                mm4_done[(1, 1)] = sc.cur("pe")

    def emit_ln2_store(w, c):
        for bb in range(nbn):
            O("vector", lambda v, bb=bb: v.bn_stats(
                bnst2[:, 6 * bb:6 * (bb + 1)],
                y_win[:, 512 * bb:512 * bb + min(512, D)]),
                W(sc.cur("ve")), "ve")
        O("vector", lambda v: v.bn_aggr(bsml[:, 4:6], bnst2[:, 0:6 * nbn]),
          W(sc.cur("ve")), "ve")
        O("scalar", lambda s: s.activation(bsml[:, 6:7], bsml[:, 5:6],
                                           AF.Sqrt, bias=epsc[:]),
          W(sc.cur("ve"), consts_ok, sc.cur("ac")), "ac")
        lastB["bsml_ac"] = sc.cur("ac")
        O("vector", lambda v: v.reciprocal(bsml[:, 8:9], bsml[:, 6:7]),
          W(sc.cur("ac"), sc.cur("ve")), "ve")
        O("vector", lambda v: v.tensor_mul(bsml[:, 10:11], bsml[:, 4:5],
                                           bsml[:, 8:9]),
          W(sc.cur("ve")), "ve")
        O("vector", lambda v: v.tensor_scalar(
            y_win[:], y_win[:],
            bsml[:, 8:9], bsml[:, 10:11], ALU.mult, ALU.subtract),
            W(sc.cur("ve")), "ve")
        out_ok = sc.cur("ve")
        O("sync", lambda s, w=w, c=c: s.dma_start(
            out=out_hbm[TW * w + 128 * c:TW * w + 128 * (c + 1), :],
            in_=y_win[:]),
            W(out_ok), "qo", 16)
        lastB["ywin"] = sc.cur("qo")

    def emit_tail_c0(w, yld, rh2_ok, mm4_done):
        for n in range(ndh):
            O("vector", lambda v, n=n: v.scalar_tensor_tensor(
                y_win[:, dw * n:dw * (n + 1)],
                pA[:, 512 * n:512 * n + dw],
                bsml[:, 2:3],
                y_win[:, dw * n:dw * (n + 1)],
                ALU.mult, ALU.add),
                W(mm4_done[(0, n)], rh2_ok, yld, sc.cur("ve")), "ve")
            lastB[f"pA{n}"] = sc.cur("ve")
        emit_ln2_store(w, 0)

    def emit_tail_c1(w, yld, rh2_ok, mm4_done):
        O("vector", lambda v: v.scalar_tensor_tensor(
            y_win[:, 0:dw], pA[:, 1024:1024 + dw], bsml[:, 3:4],
            y_win[:, 0:dw], ALU.mult, ALU.add),
            W(mm4_done[(1, 0)], rh2_ok, yld, sc.cur("ve")), "ve")
        lastB["pA2"] = sc.cur("ve")
        if ndh >= 2:
            O("vector", lambda v: v.scalar_tensor_tensor(
                y_win[:, dw:2 * dw], pC[:, 0:dw], bsml[:, 3:4],
                y_win[:, dw:2 * dw], ALU.mult, ALU.add),
                W(mm4_done[(1, 1)], rh2_ok, yld, sc.cur("ve")), "ve")
            lastB["pC"] = sc.cur("ve")
        emit_ln2_store(w, 1)

    def flush_silu():
        for (wc, gq) in st["pend_silu"]:
            emit_silu_group(wc, gq)
        st["pend_silu"] = []

    st["hc_last"] = None
    for hc in range(h32):
        emit_conv_chunk(0, hc)
    flush_silu()
    lastB["S0"] = sc.cur("ve")
    lastB["yqTw0"] = sc.cur("pe")
    if nW > 1:
        s_oks[1] = emit_sprep(1)
        win_loads[1] = emit_win_loads(1)
    sc.mark("w_head")

    for w in range(nW):
        sc.mark(f"win{w}")
        s2_ok, rh2_ok, dbg_ok = emit_finals(w)
        hq_cnt = {}
        hc_last = {}
        st["hc_last"] = hc_last
        yld = O("sync", lambda s, w=w: s.dma_start(
            out=y_win[:],
            in_=y_sc[TW * w:TW * w + 128, :]),
            W(lastB["ywin"], last["y"]), "qyl", 16)
        mm4_done = {}
        for oi, ks in enumerate(octets):
            flush_silu()
            for k in ks:
                emit_quant_chunk(w, k, s2_ok, dbg_ok, hq_cnt, hc_last)
            if w + 1 < nW:
                for hc in ks:
                    emit_conv_chunk(w + 1, hc)
            if oi > 0:
                emit_mm4_octet(w, octets[oi - 1], hq_cnt, mm4_done)
        emit_mm4_octet(w, octets[-1], hq_cnt, mm4_done)
        if w + 1 < nW:
            lastB[f"S{(w + 1) % 2}"] = sc.cur("ve")
            lastB[f"yqTw{(w + 1) % 2}"] = sc.cur("pe")
        emit_mm4_c1n1(w, mm4_done)
        emit_tail_c0(w, yld, rh2_ok, mm4_done)
        yld = O("sync", lambda s, w=w: s.dma_start(
            out=y_win[:],
            in_=y_sc[TW * w + 128:TW * w + 256, :]),
            W(lastB["ywin"], last["y"]), "qyl", 16)
        emit_tail_c1(w, yld, rh2_ok, mm4_done)
        lastB["hq"] = sc.cur("pe")
        lastB["S2"] = sc.cur("ve")
        flush_silu()
        if w + 2 < nW:
            s_oks[w + 2] = emit_sprep(w + 2)
            win_loads[w + 2] = emit_win_loads(w + 2)
        win_loads.pop(w)
        s_oks.pop(w)

    sc.streams["sync"].append((W(sc.cur("qo"), sc.cur("qy"), sc.cur("qq")), None, None, sc.seq))
    sc.streams["gpsimd"].append((W(sc.cur("qk0"), sc.cur("qk1"), sc.cur("qv")), None, None, sc.seq))
    sc.seq += 1

    with nc.Block() as block:
        sc.emit(block, sems)

    outer.close()
    return nc


_NC_CACHE = {}


def get_program(key=(1024, 4096, 4096, 2048, 2)):
    if key not in _NC_CACHE:
        _NC_CACHE[key] = build_core(*key)
    return _NC_CACHE[key]


def kernel(L0_attn_inputs, L0_attn_key_param_tokens, L0_attn_value_param_tokens,
           L0_ffn_w1, L0_ffn_conv_w, L0_ffn_conv_b, L0_ffn_w2):
    x = np.ascontiguousarray(np.asarray(L0_attn_inputs, dtype=np.float32))
    k = np.ascontiguousarray(np.asarray(L0_attn_key_param_tokens, dtype=np.float32))
    v = np.ascontiguousarray(np.asarray(L0_attn_value_param_tokens, dtype=np.float32))
    w1 = np.ascontiguousarray(np.asarray(L0_ffn_w1, dtype=np.float32))
    w2 = np.ascontiguousarray(np.asarray(L0_ffn_w2, dtype=np.float32))
    cw = np.asarray(L0_ffn_conv_w, dtype=np.float32).reshape(-1, 3)
    cb = np.asarray(L0_ffn_conv_b, dtype=np.float32).reshape(-1, 1)
    cwb = np.ascontiguousarray(np.concatenate([cw, cb], axis=1))

    B, L, D = x.shape
    ncores = 8
    bc = B // ncores
    nc_prog = get_program((D, k.shape[0], w1.shape[0], L, bc))
    in_maps = []
    for c in range(ncores):
        in_maps.append({
            "x": np.ascontiguousarray(x[bc * c:bc * (c + 1)].reshape(bc * L, D)),
            "k": k, "v": v, "w1": w1, "w2": w2, "cwb": cwb,
        })
    res = run_bass_kernel_spmd(nc_prog, in_maps, core_ids=list(range(ncores)))
    out = np.concatenate(
        [r["out"].reshape(bc, L, D) for r in res.results], axis=0)
    return out


# version 24
# speedup vs baseline: 1.0132x; 1.0115x over previous
import contextlib

import numpy as np

import concourse.bass as bass
import concourse.mybir as mybir
from concourse.ap import AP
from concourse.bass_utils import run_bass_kernel_spmd

F32 = mybir.dt.float32
BF16 = mybir.dt.bfloat16
I32 = mybir.dt.int32
AF = mybir.ActivationFunctionType
ALU = mybir.AluOpType
AX = mybir.AxisListType

ENGINES = ("sync", "gpsimd", "vector", "scalar", "tensor")

MAGIC = 12582912.0
CQ = 256.49996948242188


def W(*items):
    out = []
    for it in items:
        if it is None:
            continue
        if isinstance(it, tuple) and it and all(
                sub is None or isinstance(sub, tuple) for sub in it):
            for sub in it:
                if sub is not None and sub[1] > 0:
                    out.append(sub)
            continue
        s, v = it
        if v > 0:
            out.append((s, v))
    return tuple(out)


TRUNCATE_AT = None


class Sched:
    def __init__(self):
        self.streams = {e: [] for e in ENGINES}
        self.cnt = {}
        self.seq = 0
        self.sections = {}

    def mark(self, name):
        self.sections[name] = (self.seq, dict(self.cnt))

    def n(self, sem):
        return self.cnt.get(sem, 0)

    def cur(self, sem):
        return (sem, self.cnt.get(sem, 0))

    def op(self, eng, emit, waits=(), inc=None, inc_n=1):
        if inc is not None:
            self.cnt[inc] = self.cnt.get(inc, 0) + inc_n
        co = {}
        for s, v in waits:
            if co.get(s, -1) < v:
                co[s] = v
        self.streams[eng].append(
            (tuple(co.items()), emit, (inc, inc_n) if inc else None, self.seq))
        self.seq += 1
        return (inc, self.cnt[inc]) if inc else None

    def barrier(self):
        f = W(*[(s, v) for s, v in self.cnt.items()])
        for e in ENGINES:
            self.streams[e].append((f, None, None, self.seq))
        self.seq += 1

    def emit(self, block, sems):
        cutoff, fence = None, None
        if TRUNCATE_AT is not None:
            cutoff, fcnt = self.sections[TRUNCATE_AT]
            fence = W(*[(s, v) for s, v in fcnt.items()])

        def run(eng_obj, stream):
            waited = {}
            for waits, emitf, inc, seq in stream:
                if cutoff is not None and seq >= cutoff:
                    continue
                for s, v in waits:
                    if waited.get(s, -1) < v:
                        eng_obj.wait_ge(sems[s], v)
                        waited[s] = v
                if emitf is not None:
                    instr = emitf(eng_obj)
                    if inc is not None:
                        instr.then_inc(sems[inc[0]], inc[1])
            if fence is not None:
                for s, v in fence:
                    if waited.get(s, -1) < v:
                        eng_obj.wait_ge(sems[s], v)

        for name in ENGINES:
            getattr(block, name)((lambda st: (lambda e: run(e, st)))(self.streams[name]))


def build_core(D=1024, S=4096, H=4096, L=2048, Bc=2, dbg=False):
    TA = 256
    TW = 256
    T = Bc * L
    d8, s32, h32 = D // 128, S // 128, H // 128
    nA = T // TA
    nW = T // TW
    nWq = L // TW
    WM = max(D, H)
    sqrtS = float(np.float32(np.sqrt(S)))
    ndh = max(1, D // 512)
    dw = D // ndh
    nbn = max(1, D // 512)
    nqg = 1
    assert TA == 256 and TW == 256 and L % TW == 0 and T % TA == 0
    assert (H * D) & (H * D - 1) == 0
    assert h32 % nqg == 0

    nc = bass.Bass("TRN2", target_bir_lowering=False, debug=False)

    x_in = nc.dram_tensor("x", [T, D], F32, kind="ExternalInput").ap()
    k_in = nc.dram_tensor("k", [S, D], F32, kind="ExternalInput").ap()
    v_in = nc.dram_tensor("v", [S, D], F32, kind="ExternalInput").ap()
    w1_in = nc.dram_tensor("w1", [H, D], F32, kind="ExternalInput").ap()
    w2_in = nc.dram_tensor("w2", [D, H], F32, kind="ExternalInput").ap()
    cwb_in = nc.dram_tensor("cwb", [H, 4], F32, kind="ExternalInput").ap()
    out_hbm = nc.dram_tensor("out", [T, D], F32, kind="ExternalOutput").ap()
    ikind = "ExternalOutput" if dbg else "Internal"
    y_sc = nc.dram_tensor("y_sc", [T, D], F32, kind=ikind).ap()
    yqT_sc = nc.dram_tensor("yqT_sc", [D, T], BF16, kind=ikind).ap()
    w1T_sc = nc.dram_tensor("w1T_sc", [D, H], BF16, kind=ikind).ap()
    w2T_sc = nc.dram_tensor("w2T_sc", [H, D], BF16, kind=ikind).ap()
    ry_sc = nc.dram_tensor("ry_sc", [1, T], F32, kind=ikind).ap()
    if dbg:
        dbg_m = nc.dram_tensor("dbg_m", [128, 512], F32, kind="ExternalOutput").ap()
        dbg_r = nc.dram_tensor("dbg_r", [4, 512], F32, kind="ExternalOutput").ap()
        dbg_h = nc.dram_tensor("dbg_h", [128, 8192], F32, kind="ExternalOutput").ap()

    outer = contextlib.ExitStack()
    sbuf = lambda name, shape, dt=F32: outer.enter_context(nc.sbuf_tensor(name, shape, dt))
    sems = {s: outer.enter_context(nc.semaphore(s))
            for s in ("pe", "ve", "ac", "gp", "qc", "qw0", "qw1", "qs",
                      "qk0", "qk1", "qv", "qx0", "qx1", "qy", "qq",
                      "qw1t", "qw2t", "qt0", "qt1", "qyl", "qo",
                      "qr0", "qr1", "qrl0", "qrl1",
                      "qg0", "qg1", "qg2", "qg3")}
    sc = Sched()
    O = sc.op

    pA = outer.enter_context(nc.psum_tensor("pA", [128, 2048], F32))
    pB = outer.enter_context(nc.psum_tensor("pB", [128, 1024], F32))
    pC = outer.enter_context(nc.psum_tensor("pC", [128, 512], F32))
    pD = outer.enter_context(nc.psum_tensor("pD", [128, 1024], BF16))

    identb = sbuf("identb", [128, 128], BF16)
    identf = sbuf("identf", [128, 128], F32)
    ones_f1 = sbuf("ones_f1", [1, 128], F32)
    one11 = sbuf("one11", [1, 1], F32)
    ones_bf = sbuf("ones_bf", [128, 1], BF16)
    b256 = sbuf("b256", [128, 1], F32)
    cqc = sbuf("cqc", [128, 1], F32)
    m256c = sbuf("m256c", [128, 1], F32)
    epsc = sbuf("epsc", [128, 1], F32)
    cwb_sb = sbuf("cwb_sb", [128, h32 * 4], F32)
    mrow_s = sbuf("mrow_s", [1, 2], F32)
    mcl = sbuf("mcl", [1, 2], F32)
    wsc = sbuf("wsc", [1, 2], F32)
    mwv = sbuf("mwv", [1, 2], F32)
    m2s = sbuf("m2s", [1, 1], F32)
    wq_col = sbuf("wq_col", [128, 2], F32)
    m2s_col = sbuf("m2s_col", [128, 1], F32)

    def ident_init(g, buf):
        return g.affine_select(out=buf[:], in_=buf[:], compare_op=ALU.not_equal,
                               fill=1.0, base=0, pattern=[[-1, 128]],
                               channel_multiplier=1)

    O("gpsimd", lambda g: g.memset(identb[:], 0.0), (), "gp")
    O("gpsimd", lambda g: ident_init(g, identb), W(sc.cur("gp")), "gp")
    O("gpsimd", lambda g: g.memset(identf[:], 0.0), (), "gp")
    O("gpsimd", lambda g: ident_init(g, identf), W(sc.cur("gp")), "gp")
    for buf, val in ((ones_f1, 1.0), (one11, 1.0), (ones_bf, 1.0),
                     (b256, 256.5), (epsc, 1e-5), (cqc, CQ), (m256c, -256.0)):
        O("gpsimd", lambda g, b=buf, v=val: g.memset(b[:], v), (), "gp")
    consts_ok = sc.cur("gp")

    O("sync", lambda s: s.dma_start(
        out=cwb_sb[:].rearrange("p (c k) -> p c k", c=h32),
        in_=cwb_in.rearrange("(c p) k -> p c k", p=128)), (), "qc", 16)
    cwb_ok = sc.cur("qc")
    sc.mark("consts")

    pa = contextlib.ExitStack()
    asb = lambda name, shape, dt=F32: pa.enter_context(nc.sbuf_tensor(name, shape, dt))
    KT = asb("KT", [128, d8 * S], BF16)
    V_sb = asb("V_sb", [128, s32 * D], BF16)
    kbts = [asb(f"kbt{i}", [128, D], BF16) for i in range(2)]
    buf_free = [None, None]
    for c in range(S // 128):
        b = c % 2
        ld = O("gpsimd", lambda g, c=c, b=b: g.dma_start(
            out=kbts[b][:], in_=k_in[128 * c:128 * (c + 1), :]),
            W(buf_free[b]), f"qk{b}", 16)
        for g0 in range(0, d8, 8):
            gn = min(8, d8 - g0)
            for jj in range(gn):
                j = g0 + jj
                O("tensor", lambda t, j=j, jj=jj, b=b: t.transpose(
                    pD[:, 128 * jj:128 * (jj + 1)],
                    kbts[b][:, 128 * j:128 * (j + 1)], identb[:]),
                    W(ld, sc.cur("ve"), consts_ok) if jj == 0 else (),
                    "pe" if jj == gn - 1 else None)
            O("vector", lambda v, c=c, g0=g0, gn=gn: v.tensor_copy(
                KT[:].rearrange("p (j s) -> p j s", j=d8)[:, g0:g0 + gn,
                                                          128 * c:128 * (c + 1)],
                pD[:, 0:128 * gn].rearrange("p (j q) -> p j q", j=gn)),
                W(sc.cur("pe")), "ve")
        buf_free[b] = sc.cur("pe")
    kt_ok = sc.cur("ve")
    sc.mark("ktv")
    v_ok = O("gpsimd", lambda g: g.dma_start(
        out=V_sb[:].rearrange("p (c d) -> p c d", c=s32),
        in_=v_in.rearrange("(c p) d -> p c d", p=128)), (), "qv", 16)

    UW = min(512, min(D, H))
    wtmp = [asb(f"wtmp{i}", [128, UW], F32) for i in range(3)]
    wred = asb("wred", [128, 2], F32)
    wacc = asb("wacc", [128, 2], F32)
    wi32 = asb("wi32", [128, UW], I32)
    wqb = [asb(f"wqb{i}", [128, UW], BF16) for i in range(2)]
    wstg = asb("wstg", [128, (UW // 128) * 128], BF16)

    O("vector", lambda v: v.memset(wacc[:], 0.0), (), "ve")

    wst = {"buf_free": [None] * 3, "stg_free": None, "n": 0, "nq": 0, "np": 0,
           "wq_col_ok": None, "mw_ok": None, "wqb_free": [None, None],
           "wqb_ok": [None, None], "loads": [], "ld_next": 0, "ld_h": {}}

    def issue_wload():
        u = wst["ld_next"]
        if u >= len(wst["loads"]):
            return
        wst["ld_next"] += 1
        m, r, cc = wst["loads"][u]
        b = u % 3
        wst["ld_h"][u] = O("gpsimd", lambda g, m=m, r=r, cc=cc, b=b: g.dma_start(
            out=wtmp[b][:], in_=m[128 * r:128 * (r + 1), UW * cc:UW * (cc + 1)]),
            W(wst["buf_free"][b]), f"qg{b}", 16)

    def units_of(rows, width):
        return [(r, cc) for r in range(rows // 128) for cc in range(width // UW)]

    def mean_unit(m_in, r, cc, col):
        if wst["n"] == 0:
            for _ in range(3):
                issue_wload()
        u = wst["n"]
        b = u % 3
        wst["n"] += 1
        O("vector", lambda v, b=b: v.reduce_sum(
            wred[:, 0:1], wtmp[b][:], axis=AX.X, apply_absolute_value=True),
            W(wst["ld_h"].pop(u), sc.cur("ve")), "ve")
        wst["buf_free"][b] = sc.cur("ve")
        O("vector", lambda v, col=col: v.tensor_tensor(
            wacc[:, col:col + 1], wacc[:, col:col + 1], wred[:, 0:1], ALU.add),
            W(sc.cur("ve")), "ve")
        issue_wload()

    def mean_finalize():
        O("gpsimd", lambda g: g.tensor_reduce(mrow_s[:], wacc[:], axis=AX.C,
                                              op=ALU.add),
          W(sc.cur("ve"), consts_ok), "gp")
        O("vector", lambda v: v.tensor_scalar(mcl[:], mrow_s[:], 1.0 / (H * D), 1e-5,
                                              ALU.mult, ALU.max),
          W(sc.cur("gp")), "ve")
        O("vector", lambda v: v.reciprocal(wsc[:], mcl[:]), W(sc.cur("ve")), "ve")
        O("vector", lambda v: v.reciprocal(mwv[:], wsc[:]), W(sc.cur("ve")), "ve")
        O("vector", lambda v: v.tensor_scalar(m2s[:], mwv[0:1, 1:2], 1.0 / 127.0,
                                              None, ALU.mult),
          W(sc.cur("ve")), "ve")
        wst["mw_ok"] = sc.cur("ve")
        O("tensor", lambda t: t.matmul(pA[:, 0:2], ones_f1[:], wsc[:],
                                       start=True, stop=True),
          W(wst["mw_ok"], consts_ok, last["pA0"]), "pe")
        O("tensor", lambda t: t.matmul(pA[:, 2:3], ones_f1[:], m2s[:],
                                       start=True, stop=True), (), "pe")
        O("vector", lambda v: v.tensor_copy(wq_col[:], pA[:, 0:2]),
          W(sc.cur("pe"), sc.cur("ve")), "ve")
        O("vector", lambda v: v.tensor_copy(m2s_col[:], pA[:, 2:3]),
          W(sc.cur("ve")), "ve")
        last["pA0"] = sc.cur("ve")
        wst["wq_col_ok"] = sc.cur("ve")

    def tern_unit_dve(m_in, r, cc, col):
        u = wst["n"]
        b = u % 3
        qb = wst["nq"] % 2
        wst["n"] += 1
        wst["nq"] += 1
        O("vector", lambda v, b=b, col=col: v.tensor_scalar(
            wtmp[b][:], wtmp[b][:], wq_col[:, col:col + 1], 1.0, ALU.mult, ALU.min),
            W(wst["ld_h"].pop(u), wst["wq_col_ok"], sc.cur("ve")), "ve")
        O("vector", lambda v, b=b: v.tensor_scalar(
            wtmp[b][:], wtmp[b][:], -1.0, MAGIC, ALU.max, ALU.add),
            W(sc.cur("ve")), "ve")
        O("vector", lambda v, b=b: v.tensor_scalar(
            wi32[:], wtmp[b][:], MAGIC, CQ, ALU.subtract, ALU.add),
            W(sc.cur("ve")), "ve")
        wst["buf_free"][b] = sc.cur("ve")
        issue_wload()
        O("vector", lambda v, qb=qb: v.tensor_scalar(
            wqb[qb][:], wi32[:], -256.0, None, ALU.add),
            W(sc.cur("ve"), wst["wqb_free"][qb]), "ve")
        wst["wqb_ok"][qb] = sc.cur("ve")

    def tern_unit_pe(m_sc, r, cc):
        qb = wst["np"] % 2
        wst["np"] += 1
        nj = UW // 128
        for jj in range(nj):
            O("tensor", lambda t, jj=jj, qb=qb: t.transpose(
                pD[:, 128 * jj:128 * (jj + 1)],
                wqb[qb][:, 128 * jj:128 * (jj + 1)], identb[:]),
                W(wst["wqb_ok"][qb], last["pD"], consts_ok) if jj == 0 else (),
                "pe" if jj == nj - 1 else None)
        wst["wqb_free"][qb] = sc.cur("pe")
        O("vector", lambda v, nj=nj: v.tensor_copy(
            wstg[:, 0:128 * nj], pD[:, 0:128 * nj]),
            W(sc.cur("pe"), wst["stg_free"], sc.cur("ve")), "ve")
        last["pD"] = sc.cur("ve")
        wst["stg_free"] = O("sync", lambda s, m_sc=m_sc, r=r, cc=cc, nj=nj:
                            s.dma_start(
            out=m_sc.rearrange("(j p) r -> p j r", p=128)[
                :, (UW * cc) // 128:(UW * cc) // 128 + nj,
                128 * r:128 * (r + 1)],
            in_=wstg[:, 0:128 * nj].rearrange("p (j q) -> p j q", j=nj)),
            W(sc.cur("ve")), "qs", 16)

    wprep_steps = []
    for r, cc in units_of(H, D):
        wst["loads"].append((w1_in, r, cc))
    for r, cc in units_of(D, H):
        wst["loads"].append((w2_in, r, cc))
    for r, cc in units_of(H, D):
        wst["loads"].append((w1_in, r, cc))
    for r, cc in units_of(D, H):
        wst["loads"].append((w2_in, r, cc))
    for r, cc in units_of(H, D):
        wprep_steps.append(lambda r=r, cc=cc: mean_unit(w1_in, r, cc, 0))
    for r, cc in units_of(D, H):
        wprep_steps.append(lambda r=r, cc=cc: mean_unit(w2_in, r, cc, 1))
    wprep_steps.append(mean_finalize)
    tern_list = ([(w1_in, w1T_sc, r, cc, 0) for r, cc in units_of(H, D)]
                 + [(w2_in, w2T_sc, r, cc, 1) for r, cc in units_of(D, H)])
    for idx, (tm_in, tm_sc, tr, tcc, tcol) in enumerate(tern_list):
        def step(idx=idx, m_in=tm_in, m_sc=tm_sc, r=tr, cc=tcc, col=tcol):
            if idx > 0:
                pm = tern_list[idx - 1]
                tern_unit_pe(pm[1], pm[2], pm[3])
            tern_unit_dve(m_in, r, cc, col)
            if idx == len(tern_list) - 1:
                tern_unit_pe(m_sc, r, cc)
        wprep_steps.append(step)

    x_sb = [asb(f"x_sb{i}", [128, 2 * D], F32) for i in range(2)]
    xb_sb = asb("xb_sb", [128, D], BF16)
    xT_sb = asb("xT_sb", [128, d8 * TA], BF16)
    g_sb = asb("g_sb", [128, s32 * TA], BF16)
    g2_sb = [asb(f"g2_sb{i}", [128, TA], BF16) for i in range(3)]
    l2row = asb("l2row", [1, TA], F32)
    zh = asb("zh", [128, D], F32)
    y_sb = asb("y_sb", [128, 2 * D], F32)
    yqi = asb("yqi", [128, D // 2], I32)
    yqb = asb("yqb", [128, 2 * D], BF16)
    yqT_sb = asb("yqT_sb", [128, d8 * 128], BF16)
    bnst = asb("bnst", [128, nbn * 6], F32)
    smalls = asb("smalls", [128, 16], F32)
    ry_row = [asb(f"ry_row{i}", [1, TA], F32) for i in range(2)]

    last = {k: None for k in ("x0", "x1", "xb", "xT", "g", "pD", "pC",
                              "zh", "y", "yqi", "yqb", "yqT", "smalls_ac",
                              "pS0", "pS1", "g2_0", "g2_1", "g2_2", "g2_3",
                              "ry", "yqb_rdy", "qsc_ok", "rysm",
                              "ryb0", "ryb1")}
    last.update({f"pA{i}": None for i in range(4)})

    nmean = len(units_of(H, D)) + len(units_of(D, H)) + 1
    ntern = len(units_of(H, D)) + len(units_of(D, H))
    mtiles = max(1, min(4, nA // 4))
    wsteps_per_tile = []
    for i in range(nA):
        if i < mtiles:
            k = -(-nmean // mtiles)
        else:
            k = -(-ntern // (nA - mtiles))
        wsteps_per_tile.append(k)

    xloads = [O("sync", lambda s: s.dma_start(
        out=x_sb[0][:].rearrange("p (c d) -> p c d", c=2),
        in_=x_in[0:TA, :].rearrange("(c p) d -> p c d", p=128)), (), "qx0", 16)]

    def emit_head(i):
        xload = xloads[i]
        for c in range(2):
            O("vector", lambda v, c=c, xs=x_sb[i % 2]: v.tensor_copy(
                xb_sb[:], xs[:, c * D:(c + 1) * D]),
                W(xload, last["xb"], sc.cur("ve")), "ve")
            xb_ok = sc.cur("ve")
            for j in range(d8):
                O("tensor", lambda t, c=c, j=j: t.transpose(
                    pD[:, 128 * j:128 * (j + 1)],
                    xb_sb[:, 128 * j:128 * (j + 1)], identb[:]),
                    W(xb_ok, last["pD"], consts_ok) if j == 0 else (),
                    "pe" if j == d8 - 1 else None)
            O("vector", lambda v, c=c: v.tensor_copy(
                xT_sb[:].rearrange("p (j t) -> p j t", j=d8)[:, :, 128 * c:128 * (c + 1)],
                pD[:, 0:128 * d8].rearrange("p (j q) -> p j q", j=d8)),
                W(sc.cur("pe"), last["xT"], sc.cur("ve")), "ve")
            last["pD"] = sc.cur("ve")
            last["xb"] = sc.cur("pe")
        return sc.cur("ve")

    def emit_yq_lag(i):
        for c in range(2):
            for j in range(d8):
                O("tensor", lambda t, c=c, j=j: t.transpose(
                    pD[:, 128 * j:128 * (j + 1)],
                    yqb[:, c * D + 128 * j:c * D + 128 * (j + 1)], identb[:]),
                    W(last["yqb_rdy"], last["pD"]) if j == 0 else (),
                    "pe" if j == d8 - 1 else None)
            last["yqb"] = sc.cur("pe")
            O("vector", lambda v, c=c: v.tensor_copy(
                yqT_sb[:], pD[:, 0:128 * d8]),
                W(sc.cur("pe"), last["yqT"], sc.cur("ve")), "ve")
            last["pD"] = sc.cur("ve")
            O("sync", lambda s, i=i, c=c: s.dma_start(
                out=yqT_sc.rearrange("(j p) t -> p j t", p=128)[
                    :, :, TA * i + 128 * c:TA * i + 128 * (c + 1)],
                in_=yqT_sb[:].rearrange("p (j t) -> p j t", j=d8)),
                W(sc.cur("ve")), "qq", 16)
            last["yqT"] = sc.cur("qq")
        rb = i % 2
        for c in range(2):
            O("tensor", lambda t, c=c: t.transpose(
                pC[0:1, 256:384], smalls[:, 14 + c:15 + c], identf[:]),
                W(last["qsc_ok"], last["pC"], last[f"ryb{rb}"]), "pe")
            O("vector", lambda v, c=c, rb=rb: v.tensor_copy(
                ry_row[rb][0:1, 128 * c:128 * (c + 1)], pC[0:1, 256:384]),
                W(sc.cur("pe"), sc.cur("ve")), "ve")
            last["pC"] = sc.cur("ve")
        last["rysm"] = sc.cur("pe")
        O("sync", lambda s, i=i, rb=rb: s.dma_start(
            out=ry_sc[0:1, TA * i:TA * (i + 1)], in_=ry_row[rb][:]),
            W(sc.cur("ve")), f"qr{rb}", 16)
        last[f"ryb{rb}"] = sc.cur(f"qr{rb}")
        last["ry"] = sc.cur(f"qr{rb}")

    xT_ok = None
    for i in range(nA):
        sc.mark(f"tile{i}")
        wquota = [wsteps_per_tile[i] if i < len(wsteps_per_tile) else 0]
        xb_i = i % 2
        xload = xloads[i]

        if i + 1 < nA:
            xloads.append(O("sync", lambda s, i=i: s.dma_start(
                out=x_sb[(i + 1) % 2][:].rearrange("p (c d) -> p c d", c=2),
                in_=x_in[TA * (i + 1):TA * (i + 2), :]
                .rearrange("(c p) d -> p c d", p=128)),
                W(last[f"x{(i + 1) % 2}"]), f"qx{(i + 1) % 2}", 16))

        if i == 0:
            xT_ok = emit_head(0)

        ps_lohi = [(0, TA), (512, 512 + TA)]
        gelu_cnt = {}
        g2_cnt = {}

        def emit_ones(ss, i=i):
            O("tensor", lambda t, ss=ss: t.matmul(
                pC[0:1, 0:TA], ones_bf[:], g2_sb[ss % 3][:],
                start=(ss == 0), stop=(ss == s32 - 1)),
                W(g2_cnt[ss], last["pC"] if ss == 0 else None), "pe")
            last[f"g2_{ss % 3}"] = sc.cur("pe")

        for ss in range(s32):
            lo, hi = ps_lohi[ss % 2]
            w0 = [xT_ok, kt_ok]
            if ss >= 2:
                w0.append(gelu_cnt[ss - 2])
            else:
                w0.append(last[f"pS{ss}"])
            for j in range(d8):
                O("tensor", lambda t, ss=ss, j=j, lo=lo, hi=hi: t.matmul(
                    pB[:, lo:hi],
                    KT[:, j * S + 128 * ss: j * S + 128 * (ss + 1)],
                    xT_sb[:, j * TA:(j + 1) * TA],
                    start=(j == 0), stop=(j == d8 - 1)),
                    W(*w0) if j == 0 else (), "pe" if j == d8 - 1 else None)
            mm1_done = sc.cur("pe")
            if ss >= 3:
                emit_ones(ss - 3)
            O("scalar", lambda s, ss=ss, lo=lo, hi=hi: s.activation(
                g_sb[:, ss * TA:(ss + 1) * TA], pB[:, lo:hi], AF.Gelu),
                W(mm1_done), "ac")
            gelu_cnt[ss] = sc.cur("ac")
            O("vector", lambda v, ss=ss: v.tensor_mul(
                g2_sb[ss % 3][:], g_sb[:, ss * TA:(ss + 1) * TA],
                g_sb[:, ss * TA:(ss + 1) * TA]),
                W(gelu_cnt[ss], last[f"g2_{ss % 3}"], sc.cur("ve")), "ve")
            g2_cnt[ss] = sc.cur("ve")
            if ss % 3 == 2 and wquota[0] > 0 and wprep_steps:
                nper = -(-wquota[0] // max(1, (s32 - ss) // 3))
                for _ in range(nper):
                    if wprep_steps:
                        wprep_steps.pop(0)()
                        wquota[0] -= 1
        for sss in range(max(0, s32 - 3), s32):
            emit_ones(sss)
        last["pS0"] = gelu_cnt[s32 - 2]
        last["pS1"] = gelu_cnt[s32 - 1]
        l2mm_done = sc.cur("pe")

        last["xT"] = l2mm_done
        if i + 1 < nA:
            next_xT_ok = emit_head(i + 1)
        if i > 0:
            emit_yq_lag(i - 1)

        O("vector", lambda v: v.tensor_copy(l2row[:], pC[0:1, 0:TA]),
          W(l2mm_done, sc.cur("ve")), "ve")
        l2row_ok = sc.cur("ve")
        for c in range(2):
            O("tensor", lambda t, c=c: t.matmul(
                pC[:, 256 + c:257 + c], l2row[0:1, 128 * c:128 * (c + 1)], one11[:],
                start=True, stop=True),
                W(l2row_ok) if c == 0 else (), "pe" if c == 1 else None)
        O("vector", lambda v: v.tensor_copy(smalls[:, 0:2], pC[:, 256:258]),
          W(sc.cur("pe"), sc.cur("ve")), "ve")
        last["pC"] = sc.cur("ve")
        O("scalar", lambda s: s.activation(smalls[:, 0:2], smalls[:, 0:2], AF.Sqrt),
          W(sc.cur("ve"), last["smalls_ac"]), "ac")
        sqrt_ok = sc.cur("ac")
        O("vector", lambda v: v.reciprocal(smalls[:, 2:4], smalls[:, 0:2]),
          W(sqrt_ok, sc.cur("ve")), "ve")
        O("vector", lambda v: v.tensor_scalar(smalls[:, 2:4], smalls[:, 2:4],
                                              sqrtS, None, ALU.mult),
          W(sc.cur("ve")), "ve")
        rl2_ok = sc.cur("ve")

        for c in range(2):
            for n in range(ndh):
                gidx = c * ndh + n
                for k in range(s32):
                    w0 = W(gelu_cnt[k], v_ok, last[f"pA{gidx}"] if k == 0 else None)
                    O("tensor", lambda t, c=c, n=n, gidx=gidx, k=k: t.matmul(
                        pA[:, 512 * gidx:512 * gidx + dw],
                        g_sb[:, k * TA + 128 * c: k * TA + 128 * (c + 1)],
                        V_sb[:, k * D + dw * n: k * D + dw * (n + 1)],
                        start=(k == 0), stop=(k == s32 - 1)),
                        w0, "pe" if k == s32 - 1 else None)
                mm2_done = sc.cur("pe")
                O("vector", lambda v, c=c, n=n, gidx=gidx, xs=x_sb[xb_i]:
                  v.scalar_tensor_tensor(
                    zh[:, dw * n:dw * (n + 1)], pA[:, 512 * gidx:512 * gidx + dw],
                    smalls[:, 2 + c:3 + c],
                    xs[:, c * D + dw * n:c * D + dw * (n + 1)],
                    ALU.mult, ALU.add),
                    W(mm2_done, rl2_ok, xload,
                      last["zh"] if n == 0 else None, sc.cur("ve")), "ve")
                last[f"pA{gidx}"] = sc.cur("ve")
            z_ok = sc.cur("ve")
            for bb in range(nbn):
                O("vector", lambda v, bb=bb: v.bn_stats(
                    bnst[:, 6 * bb:6 * (bb + 1)],
                    zh[:, 512 * bb:512 * bb + min(512, D)]),
                    W(z_ok, sc.cur("ve")), "ve")
            O("vector", lambda v: v.bn_aggr(smalls[:, 4:6], bnst[:, 0:6 * nbn]),
              W(sc.cur("ve")), "ve")
            bn_ok = sc.cur("ve")
            O("scalar", lambda s: s.activation(smalls[:, 6:7], smalls[:, 5:6],
                                               AF.Sqrt, bias=epsc[:]),
              W(bn_ok, consts_ok), "ac")
            last["smalls_ac"] = sc.cur("ac")
            O("vector", lambda v: v.reciprocal(smalls[:, 8:9], smalls[:, 6:7]),
              W(sc.cur("ac"), sc.cur("ve")), "ve")
            O("vector", lambda v: v.tensor_mul(smalls[:, 10:11], smalls[:, 4:5],
                                               smalls[:, 8:9]),
              W(sc.cur("ve")), "ve")
            O("vector", lambda v, c=c: v.tensor_scalar(
                y_sb[:, c * D:(c + 1) * D], zh[:], smalls[:, 8:9], smalls[:, 10:11],
                ALU.mult, ALU.subtract),
                W(sc.cur("ve"), last["y"] if c == 0 else None), "ve")
            last["zh"] = sc.cur("ve")
            y_ok = sc.cur("ve")
            O("vector", lambda v, c=c: v.reduce_max(
                smalls[:, 12:13], y_sb[:, c * D:(c + 1) * D], axis=AX.X,
                apply_absolute_value=True), W(y_ok), "ve")
            O("vector", lambda v: v.tensor_scalar(smalls[:, 12:13], smalls[:, 12:13],
                                                  1e-5, None, ALU.max),
              W(sc.cur("ve")), "ve")
            O("vector", lambda v: v.reciprocal(smalls[:, 13:14], smalls[:, 12:13]),
              W(sc.cur("ve")), "ve")
            O("vector", lambda v: v.tensor_scalar(smalls[:, 13:14], smalls[:, 13:14],
                                                  127.0, None, ALU.mult),
              W(sc.cur("ve")), "ve")
            O("vector", lambda v, c=c: v.tensor_scalar(
                smalls[:, 14 + c:15 + c], smalls[:, 12:13], 1.0 / 127.0, None,
                ALU.mult),
                W(sc.cur("ve"), last["rysm"] if c == 0 else None), "ve")
            qsc_ok = sc.cur("ve")
            for hh in range(2):
                hw2 = D // 2
                O("vector", lambda v, c=c, hh=hh, hw2=hw2: v.tensor_scalar(
                    zh[:, hh * hw2:(hh + 1) * hw2],
                    y_sb[:, c * D + hh * hw2:c * D + (hh + 1) * hw2],
                    smalls[:, 13:14], MAGIC, ALU.mult, ALU.add),
                    W(qsc_ok, last["zh"], sc.cur("ve")), "ve")
                O("vector", lambda v, hh=hh, hw2=hw2: v.tensor_scalar(
                    yqi[:], zh[:, hh * hw2:(hh + 1) * hw2],
                    MAGIC, CQ, ALU.subtract, ALU.add),
                    W(last["yqi"], sc.cur("ve")), "ve")
                O("vector", lambda v, c=c, hh=hh, hw2=hw2: v.tensor_scalar(
                    yqb[:, c * D + hh * hw2:c * D + (hh + 1) * hw2],
                    yqi[:], -256.0, None, ALU.add),
                    W(last["yqb"] if (c == 0 and hh == 0) else None,
                      sc.cur("ve")), "ve")
                last["yqi"] = sc.cur("ve")
                last["zh"] = sc.cur("ve")
        last["g"] = sc.cur("pe")
        last["yqb_rdy"] = sc.cur("ve")
        last["qsc_ok"] = sc.cur("ve")

        O("sync", lambda s, i=i: s.dma_start(
            out=y_sc[TA * i:TA * (i + 1), :].rearrange("(c p) d -> p c d", p=128),
            in_=y_sb[:].rearrange("p (c d) -> p c d", c=2)),
            W(sc.cur("ve")), "qy", 16)
        last["y"] = sc.cur("qy")
        last[f"x{xb_i}"] = sc.cur("ve")
        if i + 1 < nA:
            xT_ok = next_xT_ok

    while wprep_steps:
        wprep_steps.pop(0)()
    emit_yq_lag(nA - 1)
    sc.mark("phaseA")
    pa.close()
    sc.barrier()

    W1T = sbuf("W1T", [128, d8 * H], BF16)
    W2T = sbuf("W2T", [128, h32 * D], BF16)
    yqTw = [sbuf(f"yqTw{i}", [128, d8 * (TW + 2)], BF16) for i in range(2)]
    y_win = sbuf("y_win", [128, D], F32)
    Srow_b = sbuf("Srow_b", [1, TW + 2], F32)
    Srow = [Srow_b, Srow_b]
    S_sb = [sbuf(f"S_sb{i}", [128, TW + 2], F32) for i in range(2)]
    htr = [sbuf(f"htr{i}", [128, TW + 2], F32) for i in range(3)]
    ca = [sbuf(f"ca{i}", [128, TW], F32) for i in range(2)]
    ca3 = [sbuf(f"ca3{i}", [128, TW], F32) for i in range(2)]
    hc_buf = sbuf("hc_buf", [128, h32 * TW], F32)
    mbuf = sbuf("mbuf", [128, TW], F32)
    mg = sbuf("mg", [128, 2 * TW], F32)
    i32tr = sbuf("i32tr", [128, 2 * TW], I32)
    hq_sb = sbuf("hq_sb", [128, h32 * TW], BF16)
    row2 = sbuf("row2", [1, TW], F32)
    S2_sb = sbuf("S2_sb", [128, TW], F32)
    bsml = sbuf("bsml", [128, 16], F32)
    bnst2 = sbuf("bnst2", [128, nbn * 6], F32)

    w1t_ok = O("sync", lambda s: s.dma_start(
        out=W1T[:].rearrange("p (j h) -> p j h", j=d8),
        in_=w1T_sc.rearrange("(j p) h -> p j h", p=128)), (), "qw1t", 16)
    w2t_ok = O("sync", lambda s: s.dma_start(
        out=W2T[:].rearrange("p (c d) -> p c d", c=h32),
        in_=w2T_sc.rearrange("(c p) d -> p c d", p=128)), (), "qw2t", 16)

    lastB = {k: None for k in ("pC", "htr0", "htr1", "htr2",
                               "htr0ac", "htr1ac", "htr2ac",
                               "ca0", "ca1", "ca30", "ca31",
                               "hc", "m8", "hq", "S2",
                               "row2", "ywin", "bsml_ac",
                               "Srow0", "Srow1", "S0", "S1", "pA2",
                               "mslot0", "mslot1", "mslot2", "mslot3")}
    lastB.update({f"pA{i}": None for i in range(4)})

    def emit_win_loads(w):
        q, wq = divmod(w, nWq)
        go = q * L + wq * TW
        lp = wq == 0
        rp = wq == nWq - 1
        c0 = go - (0 if lp else 1)
        c1 = min(go + TW + 1, q * L + L)
        ncols = c1 - c0
        dst0 = 1 if lp else 0
        wb = w % 2
        buf = yqTw[wb]
        ld = O("sync", lambda s, buf=buf, c0=c0, ncols=ncols, dst0=dst0: s.dma_start(
            out=buf[:].rearrange("p (j t) -> p j t", j=d8)[:, :, dst0:dst0 + ncols],
            in_=yqT_sc.rearrange("(j p) t -> p j t", p=128)[:, :, c0:c0 + ncols]),
            W(lastB.get(f"yqTw{wb}"), last["yqT"]), f"qt{wb}", 16)
        pads = []
        if lp:
            pads.append(O("vector", lambda v, buf=buf: v.memset(
                buf[:].rearrange("p (j t) -> p j t", j=d8)[:, :, 0:1], 0.0),
                W(lastB.get(f"yqTw{wb}"), ld, sc.cur("ve")), "ve"))
        if rp:
            pads.append(O("vector", lambda v, buf=buf: v.memset(
                buf[:].rearrange("p (j t) -> p j t", j=d8)[:, :, TW + 1:TW + 2], 0.0),
                W(lastB.get(f"yqTw{wb}"), ld, sc.cur("ve")), "ve"))
        return (ld, pads, go, lp, rp)

    def emit_sprep(w):
        wb = w % 2
        q, wq = divmod(w, nWq)
        go = q * L + wq * TW
        lp = wq == 0
        rp = wq == nWq - 1
        sr0 = go - (0 if lp else 1)
        srcols = min(go + TW + 1, q * L + L) - sr0
        sdst = 1 if lp else 0
        ryld = O("sync", lambda s, sr0=sr0, srcols=srcols, sdst=sdst, wb=wb:
                 s.dma_start(out=Srow[wb][0:1, sdst:sdst + srcols],
                             in_=ry_sc[0:1, sr0:sr0 + srcols]),
                 W(lastB["Srow0"], lastB["Srow1"], last["ry"]), f"qrl{wb}", 16)
        O("vector", lambda v, sr0=sr0, srcols=srcols, sdst=sdst, wb=wb:
          v.tensor_scalar(
            Srow[wb][0:1, sdst:sdst + srcols], Srow[wb][0:1, sdst:sdst + srcols],
            mwv[0:1, 0:1], None, ALU.mult),
            W(ryld, lastB[f"Srow{wb}"], wst["mw_ok"], sc.cur("ve")), "ve")
        if lp:
            O("vector", lambda v, wb=wb: v.memset(Srow[wb][0:1, 0:1], 0.0),
              W(sc.cur("ve")), "ve")
        if rp:
            O("vector", lambda v, wb=wb: v.memset(Srow[wb][0:1, TW + 1:TW + 2], 0.0),
              W(sc.cur("ve")), "ve")
        srow_ok = sc.cur("ve")
        O("tensor", lambda t, wb=wb: t.matmul(pC[:, 0:TW + 2], ones_f1[:],
                                              Srow[wb][:], start=True, stop=True),
          W(srow_ok, lastB["pC"]), "pe")
        O("vector", lambda v, wb=wb: v.tensor_copy(S_sb[wb][:], pC[:, 0:TW + 2]),
          W(sc.cur("pe"), lastB[f"S{wb}"], sc.cur("ve")), "ve")
        lastB["pC"] = sc.cur("ve")
        lastB[f"Srow{wb}"] = sc.cur("ve")
        return sc.cur("ve")

    win_loads = {0: emit_win_loads(0)}
    s_oks = {0: emit_sprep(0)}

    mslots = [(pB, 0), (pB, 512), (pA, 1536)]
    gsz = 16 if h32 % 16 == 0 else h32
    ng = h32 // gsz
    octets = [list(range(o, min(o + 8, h32))) for o in range(0, h32, 8)]
    st = {"hq_prev": None, "dbg_gate": None, "pend_silu": [], "i32ac": [None, None]}

    def emit_conv_chunk(w, hc):
        ld, pads, go, lp, rp = win_loads[w]
        s_ok = s_oks[w]
        s_i = hc % 3
        mp, lo = mslots[s_i]
        hi = lo + TW + 2

        def mm3(t, hc=hc, mp=mp, lo=lo, hi=hi, wb=w % 2):
            r = None
            for j in range(d8):
                r = t.matmul(
                    mp[:, lo:hi],
                    W1T[:, j * H + 128 * hc: j * H + 128 * (hc + 1)],
                    yqTw[wb][:, j * (TW + 2):(j + 1) * (TW + 2)],
                    start=(j == 0), stop=(j == d8 - 1))
            return r
        O("tensor", mm3,
          W(ld, *pads, w1t_ok, lastB[f"mslot{s_i}"]), "pe")
        mm3_done = sc.cur("pe")
        hb = hc % 3
        O("vector", lambda v, hb=hb, mp=mp, lo=lo, hi=hi, wb=w % 2:
          v.tensor_tensor(
            htr[hb][:], mp[:, lo:hi], S_sb[wb][:], ALU.mult),
            W(mm3_done, s_ok, lastB[f"htr{hb}"],
              lastB[f"htr{hb}ac"], sc.cur("ve")), "ve")
        lastB[f"mslot{s_i}"] = sc.cur("ve")
        htr_ok = sc.cur("ve")
        O("scalar", lambda s, hc=hc, hb=hb: s.activation(
            ca[hc % 2][:], htr[hb][:, 0:TW], AF.Identity,
            scale=cwb_sb[:, 4 * hc:4 * hc + 1],
            bias=cwb_sb[:, 4 * hc + 3:4 * hc + 4]),
            W(htr_ok, cwb_ok, lastB[f"ca{hc % 2}"]), "ac")
        c1_ok = sc.cur("ac")
        O("scalar", lambda s, hc=hc, hb=hb: s.activation(
            ca3[hc % 2][:], htr[hb][:, 2:TW + 2], AF.Identity,
            scale=cwb_sb[:, 4 * hc + 2:4 * hc + 3], bias=0.0),
            W(lastB[f"ca3{hc % 2}"], sc.cur("ac")), "ac")
        lastB[f"htr{hb}ac"] = sc.cur("ac")
        c3_ok = sc.cur("ac")
        O("vector", lambda v, hc=hc, hb=hb: v.scalar_tensor_tensor(
            ca[hc % 2][:], htr[hb][:, 1:TW + 1],
            cwb_sb[:, 4 * hc + 1:4 * hc + 2], ca[hc % 2][:],
            ALU.mult, ALU.add),
            W(c1_ok, sc.cur("ve")), "ve")
        lastB[f"htr{hb}"] = sc.cur("ve")
        c2_ok = sc.cur("ve")
        O("gpsimd", lambda g, hc=hc: g.tensor_tensor(
            hc_buf[:, hc * TW:(hc + 1) * TW], ca[hc % 2][:],
            ca3[hc % 2][:], ALU.add),
            W(c2_ok, c3_ok,
              st["hc_last"][hc] if st["hc_last"] is not None else None,
              st["dbg_gate"] if w == 1 else None,
              sc.cur("gp")), "gp")
        lastB[f"ca{hc % 2}"] = sc.cur("gp")
        lastB[f"ca3{hc % 2}"] = sc.cur("gp")
        if hc % gsz == gsz - 1:
            st["pend_silu"].append((w, hc // gsz))

    def emit_silu_group(w, gq):
        hc = gq * gsz + gsz - 1
        O("scalar", lambda s, hc=hc: s.activation(
            hc_buf[:, (hc - gsz + 1) * TW:(hc + 1) * TW],
            hc_buf[:, (hc - gsz + 1) * TW:(hc + 1) * TW], AF.Silu),
            W(sc.cur("gp"), sc.cur("ac")), "ac")
        O("vector", lambda v, hc=hc, gq=gq: v.reduce_max(
            mg[:, gq * TW:(gq + 1) * TW],
            hc_buf[:, (hc - gsz + 1) * TW:(hc + 1) * TW]
            .rearrange("p (c t) -> p t c", c=gsz),
            axis=AX.X, apply_absolute_value=True),
            W(sc.cur("ac"), sc.cur("ve")), "ve")

    def emit_finals(w):
        O("vector", lambda v: v.reduce_max(
            mbuf[:], mg[:, 0:ng * TW].rearrange("p (c t) -> p t c", c=ng),
            axis=AX.X), W(sc.cur("ve"), lastB["m8"]), "ve")
        mbuf_ok = sc.cur("ve")
        for c in range(2):
            O("tensor", lambda t, c=c: t.transpose(
                pC[:, 128 * c:128 * (c + 1)], mbuf[:, 128 * c:128 * (c + 1)],
                identf[:]),
                W(mbuf_ok, lastB["pC"]) if c == 0 else (),
                "pe" if c == 1 else None)
        lastB["m8"] = sc.cur("pe")
        O("vector", lambda v: v.reduce_max(
            bsml[:, 0:2], pC[:, 0:256].rearrange("p (c t) -> p c t", c=2),
            axis=AX.X), W(sc.cur("pe"), lastB["bsml_ac"], sc.cur("ve")), "ve")
        lastB["pC"] = sc.cur("ve")
        O("vector", lambda v: v.tensor_scalar(bsml[:, 0:2], bsml[:, 0:2],
                                              1e-5, None, ALU.max),
          W(sc.cur("ve")), "ve")
        O("vector", lambda v: v.tensor_scalar(bsml[:, 2:4], bsml[:, 0:2],
                                              m2s_col[:], None, ALU.mult),
          W(sc.cur("ve"), wst["wq_col_ok"]), "ve")
        rh2_ok = sc.cur("ve")
        O("vector", lambda v: v.reciprocal(bsml[:, 4:6], bsml[:, 0:2]),
          W(sc.cur("ve")), "ve")
        O("vector", lambda v: v.tensor_scalar(bsml[:, 4:6], bsml[:, 4:6],
                                              127.0, None, ALU.mult),
          W(sc.cur("ve")), "ve")
        sc2_ok = sc.cur("ve")
        for c in range(2):
            O("tensor", lambda t, c=c: t.transpose(
                pC[0:1, 256:384], bsml[:, 4 + c:5 + c], identf[:]),
                W(sc2_ok, lastB["pC"]), "pe")
            O("vector", lambda v, c=c: v.tensor_copy(
                row2[0:1, 128 * c:128 * (c + 1)], pC[0:1, 256:384]),
                W(sc.cur("pe"), lastB["row2"] if c == 0 else None,
                  sc.cur("ve")), "ve")
            lastB["pC"] = sc.cur("ve")
        O("tensor", lambda t: t.matmul(pC[:, 0:TW], ones_f1[:], row2[:],
                                       start=True, stop=True),
          W(sc.cur("ve"), lastB["pC"]), "pe")
        lastB["row2"] = sc.cur("pe")
        O("vector", lambda v: v.tensor_copy(S2_sb[:], pC[:, 0:TW]),
          W(sc.cur("pe"), lastB["S2"], sc.cur("ve")), "ve")
        lastB["pC"] = sc.cur("ve")
        s2_ok = sc.cur("ve")
        dbg_ok = None
        if dbg and w == 0:
            O("sync", lambda s: s.dma_start(out=dbg_m[:, 0:TW], in_=mbuf[:]),
              W(s2_ok), "qc", 16)
            O("sync", lambda s: s.dma_start(out=dbg_h[:, 0:h32 * TW], in_=hc_buf[:]),
              W(s2_ok), "qc", 16)
            dbg_ok = sc.cur("qc")
            st["dbg_gate"] = dbg_ok
        return s2_ok, rh2_ok, dbg_ok

    def emit_quant_chunk(w, k, s2_ok, dbg_ok, hq_cnt, hc_last):
        sl = slice(k * TW, (k + 1) * TW)
        i2 = slice((k % 2) * TW, (k % 2) * TW + TW)
        O("gpsimd", lambda g, sl=sl: g.tensor_tensor(
            hc_buf[:, sl], hc_buf[:, sl], S2_sb[:], ALU.mult),
            W(s2_ok, dbg_ok if k == 0 else None, sc.cur("gp")), "gp")
        O("vector", lambda v, sl=sl: v.tensor_scalar(
            hc_buf[:, sl], hc_buf[:, sl], MAGIC, MAGIC, ALU.add, ALU.subtract),
            W(sc.cur("gp"), sc.cur("ve")), "ve")
        O("vector", lambda v, sl=sl, i2=i2: v.tensor_scalar(
            i32tr[:, i2], hc_buf[:, sl], CQ, None, ALU.add),
            W(st["i32ac"][k % 2], sc.cur("ve")), "ve")
        hc_last[k] = sc.cur("ve")
        if k % 2 == 1:
            O("scalar", lambda s, k=k: s.activation(
                hq_sb[:, (k - 1) * TW:(k + 1) * TW], i32tr[:],
                AF.Identity, bias=m256c[:]),
                W(lastB["hq"] if k == 1 else None, consts_ok,
                  sc.cur("ve"), sc.cur("ac")), "ac")
            st["i32ac"][0] = sc.cur("ac")
            st["i32ac"][1] = sc.cur("ac")
            hq_cnt[k] = sc.cur("ac")

    def emit_mm4_octet(w, ks, hq_cnt, mm4_done):
        for k in ks:
            for n in range(ndh):
                w0 = W(hq_cnt[ks[-1]] if (k == ks[0] and n == 0) else None,
                       w2t_ok if k == 0 and n == 0 else None,
                       lastB[f"pA{n}"] if k == 0 else None)
                is_stop = k == h32 - 1
                O("tensor", lambda t, n=n, k=k, is_stop=is_stop: t.matmul(
                    pA[:, 512 * n:512 * n + dw],
                    hq_sb[:, k * TW: k * TW + 128],
                    W2T[:, k * D + dw * n: k * D + dw * (n + 1)],
                    start=(k == 0), stop=is_stop),
                    w0, "pe" if is_stop else None)
                if is_stop:
                    mm4_done[(0, n)] = sc.cur("pe")
        for k in ks:
            w0 = W(lastB["pA2"] if k == 0 else None)
            is_stop = k == h32 - 1
            O("tensor", lambda t, k=k, is_stop=is_stop: t.matmul(
                pA[:, 1024:1024 + dw],
                hq_sb[:, k * TW + 128: k * TW + 256],
                W2T[:, k * D: k * D + dw],
                start=(k == 0), stop=is_stop),
                w0, "pe" if is_stop else None)
            if is_stop:
                mm4_done[(1, 0)] = sc.cur("pe")

    def emit_mm4_c1n1(w, mm4_done):
        if ndh < 2:
            return
        for k in range(h32):
            w0 = W(lastB["pC"] if k == 0 else None)
            is_stop = k == h32 - 1
            O("tensor", lambda t, k=k, is_stop=is_stop: t.matmul(
                pC[:, 0:dw],
                hq_sb[:, k * TW + 128: k * TW + 256],
                W2T[:, k * D + dw: k * D + 2 * dw],
                start=(k == 0), stop=is_stop),
                w0, "pe" if is_stop else None)
            if is_stop:
                mm4_done[(1, 1)] = sc.cur("pe")

    def emit_ln2_store(w, c):
        for bb in range(nbn):
            O("vector", lambda v, bb=bb: v.bn_stats(
                bnst2[:, 6 * bb:6 * (bb + 1)],
                y_win[:, 512 * bb:512 * bb + min(512, D)]),
                W(sc.cur("ve")), "ve")
        O("vector", lambda v: v.bn_aggr(bsml[:, 4:6], bnst2[:, 0:6 * nbn]),
          W(sc.cur("ve")), "ve")
        O("scalar", lambda s: s.activation(bsml[:, 6:7], bsml[:, 5:6],
                                           AF.Sqrt, bias=epsc[:]),
          W(sc.cur("ve"), consts_ok, sc.cur("ac")), "ac")
        lastB["bsml_ac"] = sc.cur("ac")
        O("vector", lambda v: v.reciprocal(bsml[:, 8:9], bsml[:, 6:7]),
          W(sc.cur("ac"), sc.cur("ve")), "ve")
        O("vector", lambda v: v.tensor_mul(bsml[:, 10:11], bsml[:, 4:5],
                                           bsml[:, 8:9]),
          W(sc.cur("ve")), "ve")
        O("vector", lambda v: v.tensor_scalar(
            y_win[:], y_win[:],
            bsml[:, 8:9], bsml[:, 10:11], ALU.mult, ALU.subtract),
            W(sc.cur("ve")), "ve")
        out_ok = sc.cur("ve")
        O("sync", lambda s, w=w, c=c: s.dma_start(
            out=out_hbm[TW * w + 128 * c:TW * w + 128 * (c + 1), :],
            in_=y_win[:]),
            W(out_ok), "qo", 16)
        lastB["ywin"] = sc.cur("qo")

    def emit_tail_c0(w, yld, rh2_ok, mm4_done):
        for n in range(ndh):
            O("vector", lambda v, n=n: v.scalar_tensor_tensor(
                y_win[:, dw * n:dw * (n + 1)],
                pA[:, 512 * n:512 * n + dw],
                bsml[:, 2:3],
                y_win[:, dw * n:dw * (n + 1)],
                ALU.mult, ALU.add),
                W(mm4_done[(0, n)], rh2_ok, yld, sc.cur("ve")), "ve")
            lastB[f"pA{n}"] = sc.cur("ve")
        emit_ln2_store(w, 0)

    def emit_tail_c1(w, yld, rh2_ok, mm4_done):
        O("vector", lambda v: v.scalar_tensor_tensor(
            y_win[:, 0:dw], pA[:, 1024:1024 + dw], bsml[:, 3:4],
            y_win[:, 0:dw], ALU.mult, ALU.add),
            W(mm4_done[(1, 0)], rh2_ok, yld, sc.cur("ve")), "ve")
        lastB["pA2"] = sc.cur("ve")
        if ndh >= 2:
            O("vector", lambda v: v.scalar_tensor_tensor(
                y_win[:, dw:2 * dw], pC[:, 0:dw], bsml[:, 3:4],
                y_win[:, dw:2 * dw], ALU.mult, ALU.add),
                W(mm4_done[(1, 1)], rh2_ok, yld, sc.cur("ve")), "ve")
            lastB["pC"] = sc.cur("ve")
        emit_ln2_store(w, 1)

    def flush_silu():
        for (wc, gq) in st["pend_silu"]:
            emit_silu_group(wc, gq)
        st["pend_silu"] = []

    st["hc_last"] = None
    for hc in range(h32):
        emit_conv_chunk(0, hc)
    flush_silu()
    lastB["S0"] = sc.cur("ve")
    lastB["yqTw0"] = sc.cur("pe")
    if nW > 1:
        s_oks[1] = emit_sprep(1)
        win_loads[1] = emit_win_loads(1)
    sc.mark("w_head")

    for w in range(nW):
        sc.mark(f"win{w}")
        s2_ok, rh2_ok, dbg_ok = emit_finals(w)
        hq_cnt = {}
        hc_last = {}
        st["hc_last"] = hc_last
        yld = O("sync", lambda s, w=w: s.dma_start(
            out=y_win[:],
            in_=y_sc[TW * w:TW * w + 128, :]),
            W(lastB["ywin"], last["y"]), "qyl", 16)
        mm4_done = {}
        for oi, ks in enumerate(octets):
            flush_silu()
            for k in ks:
                emit_quant_chunk(w, k, s2_ok, dbg_ok, hq_cnt, hc_last)
            if w + 1 < nW:
                for hc in ks:
                    emit_conv_chunk(w + 1, hc)
            if oi > 0:
                emit_mm4_octet(w, octets[oi - 1], hq_cnt, mm4_done)
        emit_mm4_octet(w, octets[-1], hq_cnt, mm4_done)
        if w + 1 < nW:
            lastB[f"S{(w + 1) % 2}"] = sc.cur("ve")
            lastB[f"yqTw{(w + 1) % 2}"] = sc.cur("pe")
        emit_mm4_c1n1(w, mm4_done)
        emit_tail_c0(w, yld, rh2_ok, mm4_done)
        yld = O("sync", lambda s, w=w: s.dma_start(
            out=y_win[:],
            in_=y_sc[TW * w + 128:TW * w + 256, :]),
            W(lastB["ywin"], last["y"]), "qyl", 16)
        emit_tail_c1(w, yld, rh2_ok, mm4_done)
        lastB["hq"] = sc.cur("pe")
        lastB["S2"] = sc.cur("ve")
        flush_silu()
        if w + 2 < nW:
            s_oks[w + 2] = emit_sprep(w + 2)
            win_loads[w + 2] = emit_win_loads(w + 2)
        win_loads.pop(w)
        s_oks.pop(w)

    sc.streams["sync"].append((W(sc.cur("qo"), sc.cur("qy"), sc.cur("qq")), None, None, sc.seq))
    sc.streams["gpsimd"].append((W(sc.cur("qk0"), sc.cur("qk1"), sc.cur("qv")), None, None, sc.seq))
    sc.seq += 1

    with nc.Block() as block:
        sc.emit(block, sems)

    outer.close()
    return nc


_NC_CACHE = {}


def get_program(key=(1024, 4096, 4096, 2048, 2)):
    if key not in _NC_CACHE:
        _NC_CACHE[key] = build_core(*key)
    return _NC_CACHE[key]


def kernel(L0_attn_inputs, L0_attn_key_param_tokens, L0_attn_value_param_tokens,
           L0_ffn_w1, L0_ffn_conv_w, L0_ffn_conv_b, L0_ffn_w2):
    x = np.ascontiguousarray(np.asarray(L0_attn_inputs, dtype=np.float32))
    k = np.ascontiguousarray(np.asarray(L0_attn_key_param_tokens, dtype=np.float32))
    v = np.ascontiguousarray(np.asarray(L0_attn_value_param_tokens, dtype=np.float32))
    w1 = np.ascontiguousarray(np.asarray(L0_ffn_w1, dtype=np.float32))
    w2 = np.ascontiguousarray(np.asarray(L0_ffn_w2, dtype=np.float32))
    cw = np.asarray(L0_ffn_conv_w, dtype=np.float32).reshape(-1, 3)
    cb = np.asarray(L0_ffn_conv_b, dtype=np.float32).reshape(-1, 1)
    cwb = np.ascontiguousarray(np.concatenate([cw, cb], axis=1))

    B, L, D = x.shape
    ncores = 8
    bc = B // ncores
    nc_prog = get_program((D, k.shape[0], w1.shape[0], L, bc))
    in_maps = []
    for c in range(ncores):
        in_maps.append({
            "x": np.ascontiguousarray(x[bc * c:bc * (c + 1)].reshape(bc * L, D)),
            "k": k, "v": v, "w1": w1, "w2": w2, "cwb": cwb,
        })
    res = run_bass_kernel_spmd(nc_prog, in_maps, core_ids=list(range(ncores)))
    out = np.concatenate(
        [r["out"].reshape(bc, L, D) for r in res.results], axis=0)
    return out


# version 25
# speedup vs baseline: 1.0203x; 1.0070x over previous
import contextlib

import numpy as np

import concourse.bass as bass
import concourse.mybir as mybir
from concourse.ap import AP
from concourse.bass_utils import run_bass_kernel_spmd

F32 = mybir.dt.float32
BF16 = mybir.dt.bfloat16
I32 = mybir.dt.int32
AF = mybir.ActivationFunctionType
ALU = mybir.AluOpType
AX = mybir.AxisListType

ENGINES = ("sync", "gpsimd", "vector", "scalar", "tensor")

MAGIC = 12582912.0
CQ = 256.49996948242188


def W(*items):
    out = []
    for it in items:
        if it is None:
            continue
        if isinstance(it, tuple) and it and all(
                sub is None or isinstance(sub, tuple) for sub in it):
            for sub in it:
                if sub is not None and sub[1] > 0:
                    out.append(sub)
            continue
        s, v = it
        if v > 0:
            out.append((s, v))
    return tuple(out)


TRUNCATE_AT = None


class Sched:
    def __init__(self):
        self.streams = {e: [] for e in ENGINES}
        self.cnt = {}
        self.seq = 0
        self.sections = {}

    def mark(self, name):
        self.sections[name] = (self.seq, dict(self.cnt))

    def n(self, sem):
        return self.cnt.get(sem, 0)

    def cur(self, sem):
        return (sem, self.cnt.get(sem, 0))

    def op(self, eng, emit, waits=(), inc=None, inc_n=1):
        if inc is not None:
            self.cnt[inc] = self.cnt.get(inc, 0) + inc_n
        co = {}
        for s, v in waits:
            if co.get(s, -1) < v:
                co[s] = v
        self.streams[eng].append(
            (tuple(co.items()), emit, (inc, inc_n) if inc else None, self.seq))
        self.seq += 1
        return (inc, self.cnt[inc]) if inc else None

    def barrier(self):
        f = W(*[(s, v) for s, v in self.cnt.items()])
        for e in ENGINES:
            self.streams[e].append((f, None, None, self.seq))
        self.seq += 1

    def emit(self, block, sems):
        cutoff, fence = None, None
        if TRUNCATE_AT is not None:
            cutoff, fcnt = self.sections[TRUNCATE_AT]
            fence = W(*[(s, v) for s, v in fcnt.items()])

        def run(eng_obj, stream):
            waited = {}
            for waits, emitf, inc, seq in stream:
                if cutoff is not None and seq >= cutoff:
                    continue
                for s, v in waits:
                    if waited.get(s, -1) < v:
                        eng_obj.wait_ge(sems[s], v)
                        waited[s] = v
                if emitf is not None:
                    instr = emitf(eng_obj)
                    if inc is not None:
                        instr.then_inc(sems[inc[0]], inc[1])
            if fence is not None:
                for s, v in fence:
                    if waited.get(s, -1) < v:
                        eng_obj.wait_ge(sems[s], v)

        for name in ENGINES:
            getattr(block, name)((lambda st: (lambda e: run(e, st)))(self.streams[name]))


def build_core(D=1024, S=4096, H=4096, L=2048, Bc=2, dbg=False):
    TA = 256
    TW = 256
    T = Bc * L
    d8, s32, h32 = D // 128, S // 128, H // 128
    nA = T // TA
    nW = T // TW
    nWq = L // TW
    WM = max(D, H)
    sqrtS = float(np.float32(np.sqrt(S)))
    ndh = max(1, D // 512)
    dw = D // ndh
    nbn = max(1, D // 512)
    nqg = 1
    assert TA == 256 and TW == 256 and L % TW == 0 and T % TA == 0
    assert (H * D) & (H * D - 1) == 0
    assert h32 % nqg == 0

    nc = bass.Bass("TRN2", target_bir_lowering=False, debug=False)

    x_in = nc.dram_tensor("x", [T, D], F32, kind="ExternalInput").ap()
    k_in = nc.dram_tensor("k", [S, D], F32, kind="ExternalInput").ap()
    v_in = nc.dram_tensor("v", [S, D], F32, kind="ExternalInput").ap()
    w1_in = nc.dram_tensor("w1", [H, D], F32, kind="ExternalInput").ap()
    w2_in = nc.dram_tensor("w2", [D, H], F32, kind="ExternalInput").ap()
    cwb_in = nc.dram_tensor("cwb", [H, 4], F32, kind="ExternalInput").ap()
    out_hbm = nc.dram_tensor("out", [T, D], F32, kind="ExternalOutput").ap()
    ikind = "ExternalOutput" if dbg else "Internal"
    y_sc = nc.dram_tensor("y_sc", [T, D], F32, kind=ikind).ap()
    yqT_sc = nc.dram_tensor("yqT_sc", [D, T], BF16, kind=ikind).ap()
    w1T_sc = nc.dram_tensor("w1T_sc", [D, H], BF16, kind=ikind).ap()
    w2T_sc = nc.dram_tensor("w2T_sc", [H, D], BF16, kind=ikind).ap()
    ry_sc = nc.dram_tensor("ry_sc", [1, T], F32, kind=ikind).ap()
    if dbg:
        dbg_m = nc.dram_tensor("dbg_m", [128, 512], F32, kind="ExternalOutput").ap()
        dbg_r = nc.dram_tensor("dbg_r", [4, 512], F32, kind="ExternalOutput").ap()
        dbg_h = nc.dram_tensor("dbg_h", [128, 8192], F32, kind="ExternalOutput").ap()

    outer = contextlib.ExitStack()
    sbuf = lambda name, shape, dt=F32: outer.enter_context(nc.sbuf_tensor(name, shape, dt))
    sems = {s: outer.enter_context(nc.semaphore(s))
            for s in ("pe", "ve", "ac", "gp", "qc", "qw0", "qw1", "qs",
                      "qk0", "qk1", "qv", "qx0", "qx1", "qy", "qq",
                      "qw1t", "qw2t", "qt0", "qt1", "qyl", "qo",
                      "qr0", "qr1", "qrl0", "qrl1",
                      "qg0", "qg1", "qg2", "qg3")}
    sc = Sched()
    O = sc.op

    pA = outer.enter_context(nc.psum_tensor("pA", [128, 2048], F32))
    pB = outer.enter_context(nc.psum_tensor("pB", [128, 1024], F32))
    pC = outer.enter_context(nc.psum_tensor("pC", [128, 512], F32))
    pD = outer.enter_context(nc.psum_tensor("pD", [128, 1024], BF16))

    identb = sbuf("identb", [128, 128], BF16)
    identf = sbuf("identf", [128, 128], F32)
    ones_f1 = sbuf("ones_f1", [1, 128], F32)
    one11 = sbuf("one11", [1, 1], F32)
    ones_bf = sbuf("ones_bf", [128, 1], BF16)
    b256 = sbuf("b256", [128, 1], F32)
    cqc = sbuf("cqc", [128, 1], F32)
    m256c = sbuf("m256c", [128, 1], F32)
    epsc = sbuf("epsc", [128, 1], F32)
    cwb_sb = sbuf("cwb_sb", [128, h32 * 4], F32)
    mrow_s = sbuf("mrow_s", [1, 2], F32)
    mcl = sbuf("mcl", [1, 2], F32)
    wsc = sbuf("wsc", [1, 2], F32)
    mwv = sbuf("mwv", [1, 2], F32)
    m2s = sbuf("m2s", [1, 1], F32)
    wq_col = sbuf("wq_col", [128, 2], F32)
    m2s_col = sbuf("m2s_col", [128, 1], F32)

    def ident_init(g, buf):
        return g.affine_select(out=buf[:], in_=buf[:], compare_op=ALU.not_equal,
                               fill=1.0, base=0, pattern=[[-1, 128]],
                               channel_multiplier=1)

    O("gpsimd", lambda g: g.memset(identb[:], 0.0), (), "gp")
    O("gpsimd", lambda g: ident_init(g, identb), W(sc.cur("gp")), "gp")
    O("gpsimd", lambda g: g.memset(identf[:], 0.0), (), "gp")
    O("gpsimd", lambda g: ident_init(g, identf), W(sc.cur("gp")), "gp")
    for buf, val in ((ones_f1, 1.0), (one11, 1.0), (ones_bf, 1.0),
                     (b256, 256.5), (epsc, 1e-5), (cqc, CQ), (m256c, -256.0)):
        O("gpsimd", lambda g, b=buf, v=val: g.memset(b[:], v), (), "gp")
    consts_ok = sc.cur("gp")

    O("sync", lambda s: s.dma_start(
        out=cwb_sb[:].rearrange("p (c k) -> p c k", c=h32),
        in_=cwb_in.rearrange("(c p) k -> p c k", p=128)), (), "qc", 16)
    cwb_ok = sc.cur("qc")
    sc.mark("consts")

    pa = contextlib.ExitStack()
    asb = lambda name, shape, dt=F32: pa.enter_context(nc.sbuf_tensor(name, shape, dt))
    KT = asb("KT", [128, d8 * S], BF16)
    V_sb = asb("V_sb", [128, s32 * D], BF16)
    kbts = [asb(f"kbt{i}", [128, D], BF16) for i in range(2)]
    buf_free = [None, None]
    for c in range(S // 128):
        b = c % 2
        ld = O("gpsimd", lambda g, c=c, b=b: g.dma_start(
            out=kbts[b][:], in_=k_in[128 * c:128 * (c + 1), :]),
            W(buf_free[b]), f"qk{b}", 16)
        for g0 in range(0, d8, 8):
            gn = min(8, d8 - g0)
            for jj in range(gn):
                j = g0 + jj
                O("tensor", lambda t, j=j, jj=jj, b=b: t.transpose(
                    pD[:, 128 * jj:128 * (jj + 1)],
                    kbts[b][:, 128 * j:128 * (j + 1)], identb[:]),
                    W(ld, sc.cur("ve"), consts_ok) if jj == 0 else (),
                    "pe" if jj == gn - 1 else None)
            O("vector", lambda v, c=c, g0=g0, gn=gn: v.tensor_copy(
                KT[:].rearrange("p (j s) -> p j s", j=d8)[:, g0:g0 + gn,
                                                          128 * c:128 * (c + 1)],
                pD[:, 0:128 * gn].rearrange("p (j q) -> p j q", j=gn)),
                W(sc.cur("pe")), "ve")
        buf_free[b] = sc.cur("pe")
    kt_ok = sc.cur("ve")
    sc.mark("ktv")
    v_ok = O("gpsimd", lambda g: g.dma_start(
        out=V_sb[:].rearrange("p (c d) -> p c d", c=s32),
        in_=v_in.rearrange("(c p) d -> p c d", p=128)), (), "qv", 16)

    UW = min(512, min(D, H))
    wtmp = [asb(f"wtmp{i}", [128, UW], F32) for i in range(3)]
    wred = asb("wred", [128, 2], F32)
    wacc = asb("wacc", [128, 2], F32)
    wi32 = asb("wi32", [128, UW], I32)
    wqb = [asb(f"wqb{i}", [128, UW], BF16) for i in range(2)]
    wstg = asb("wstg", [128, (UW // 128) * 128], BF16)

    O("vector", lambda v: v.memset(wacc[:], 0.0), (), "ve")

    wst = {"buf_free": [None] * 3, "stg_free": None, "n": 0, "nq": 0, "np": 0,
           "wq_col_ok": None, "mw_ok": None, "wqb_free": [None, None],
           "wqb_ok": [None, None], "loads": [], "ld_next": 0, "ld_h": {}}

    def issue_wload():
        u = wst["ld_next"]
        if u >= len(wst["loads"]):
            return
        wst["ld_next"] += 1
        m, r, cc = wst["loads"][u]
        b = u % 3
        wst["ld_h"][u] = O("gpsimd", lambda g, m=m, r=r, cc=cc, b=b: g.dma_start(
            out=wtmp[b][:], in_=m[128 * r:128 * (r + 1), UW * cc:UW * (cc + 1)]),
            W(wst["buf_free"][b]), f"qg{b}", 16)

    def units_of(rows, width):
        return [(r, cc) for r in range(rows // 128) for cc in range(width // UW)]

    def mean_unit(m_in, r, cc, col):
        if wst["n"] == 0:
            for _ in range(3):
                issue_wload()
        u = wst["n"]
        b = u % 3
        wst["n"] += 1
        O("vector", lambda v, b=b: v.reduce_sum(
            wred[:, 0:1], wtmp[b][:], axis=AX.X, apply_absolute_value=True),
            W(wst["ld_h"].pop(u), sc.cur("ve")), "ve")
        wst["buf_free"][b] = sc.cur("ve")
        O("vector", lambda v, col=col: v.tensor_tensor(
            wacc[:, col:col + 1], wacc[:, col:col + 1], wred[:, 0:1], ALU.add),
            W(sc.cur("ve")), "ve")
        issue_wload()

    def mean_finalize():
        O("gpsimd", lambda g: g.tensor_reduce(mrow_s[:], wacc[:], axis=AX.C,
                                              op=ALU.add),
          W(sc.cur("ve"), consts_ok), "gp")
        O("vector", lambda v: v.tensor_scalar(mcl[:], mrow_s[:], 1.0 / (H * D), 1e-5,
                                              ALU.mult, ALU.max),
          W(sc.cur("gp")), "ve")
        O("vector", lambda v: v.reciprocal(wsc[:], mcl[:]), W(sc.cur("ve")), "ve")
        O("vector", lambda v: v.reciprocal(mwv[:], wsc[:]), W(sc.cur("ve")), "ve")
        O("vector", lambda v: v.tensor_scalar(m2s[:], mwv[0:1, 1:2], 1.0 / 127.0,
                                              None, ALU.mult),
          W(sc.cur("ve")), "ve")
        wst["mw_ok"] = sc.cur("ve")
        O("tensor", lambda t: t.matmul(pA[:, 0:2], ones_f1[:], wsc[:],
                                       start=True, stop=True),
          W(wst["mw_ok"], consts_ok, last["pA0"]), "pe")
        O("tensor", lambda t: t.matmul(pA[:, 2:3], ones_f1[:], m2s[:],
                                       start=True, stop=True), (), "pe")
        O("vector", lambda v: v.tensor_copy(wq_col[:], pA[:, 0:2]),
          W(sc.cur("pe"), sc.cur("ve")), "ve")
        O("vector", lambda v: v.tensor_copy(m2s_col[:], pA[:, 2:3]),
          W(sc.cur("ve")), "ve")
        last["pA0"] = sc.cur("ve")
        wst["wq_col_ok"] = sc.cur("ve")

    def tern_unit_dve(m_in, r, cc, col):
        u = wst["n"]
        b = u % 3
        qb = wst["nq"] % 2
        wst["n"] += 1
        wst["nq"] += 1
        O("vector", lambda v, b=b, col=col: v.tensor_scalar(
            wtmp[b][:], wtmp[b][:], wq_col[:, col:col + 1], 1.0, ALU.mult, ALU.min),
            W(wst["ld_h"].pop(u), wst["wq_col_ok"], sc.cur("ve")), "ve")
        O("vector", lambda v, b=b: v.tensor_scalar(
            wtmp[b][:], wtmp[b][:], -1.0, MAGIC, ALU.max, ALU.add),
            W(sc.cur("ve")), "ve")
        O("vector", lambda v, b=b: v.tensor_scalar(
            wi32[:], wtmp[b][:], MAGIC, CQ, ALU.subtract, ALU.add),
            W(sc.cur("ve")), "ve")
        wst["buf_free"][b] = sc.cur("ve")
        issue_wload()
        O("vector", lambda v, qb=qb: v.tensor_scalar(
            wqb[qb][:], wi32[:], -256.0, None, ALU.add),
            W(sc.cur("ve"), wst["wqb_free"][qb]), "ve")
        wst["wqb_ok"][qb] = sc.cur("ve")

    def tern_unit_pe(m_sc, r, cc):
        qb = wst["np"] % 2
        wst["np"] += 1
        nj = UW // 128
        for jj in range(nj):
            O("tensor", lambda t, jj=jj, qb=qb: t.transpose(
                pD[:, 128 * jj:128 * (jj + 1)],
                wqb[qb][:, 128 * jj:128 * (jj + 1)], identb[:]),
                W(wst["wqb_ok"][qb], last["pD"], consts_ok) if jj == 0 else (),
                "pe" if jj == nj - 1 else None)
        wst["wqb_free"][qb] = sc.cur("pe")
        O("vector", lambda v, nj=nj: v.tensor_copy(
            wstg[:, 0:128 * nj], pD[:, 0:128 * nj]),
            W(sc.cur("pe"), wst["stg_free"], sc.cur("ve")), "ve")
        last["pD"] = sc.cur("ve")
        wst["stg_free"] = O("sync", lambda s, m_sc=m_sc, r=r, cc=cc, nj=nj:
                            s.dma_start(
            out=m_sc.rearrange("(j p) r -> p j r", p=128)[
                :, (UW * cc) // 128:(UW * cc) // 128 + nj,
                128 * r:128 * (r + 1)],
            in_=wstg[:, 0:128 * nj].rearrange("p (j q) -> p j q", j=nj)),
            W(sc.cur("ve")), "qs", 16)

    wprep_steps = []
    for r, cc in units_of(H, D):
        wst["loads"].append((w1_in, r, cc))
    for r, cc in units_of(D, H):
        wst["loads"].append((w2_in, r, cc))
    for r, cc in units_of(H, D):
        wst["loads"].append((w1_in, r, cc))
    for r, cc in units_of(D, H):
        wst["loads"].append((w2_in, r, cc))
    for r, cc in units_of(H, D):
        wprep_steps.append(lambda r=r, cc=cc: mean_unit(w1_in, r, cc, 0))
    for r, cc in units_of(D, H):
        wprep_steps.append(lambda r=r, cc=cc: mean_unit(w2_in, r, cc, 1))
    wprep_steps.append(mean_finalize)
    tern_list = ([(w1_in, w1T_sc, r, cc, 0) for r, cc in units_of(H, D)]
                 + [(w2_in, w2T_sc, r, cc, 1) for r, cc in units_of(D, H)])
    for idx, (tm_in, tm_sc, tr, tcc, tcol) in enumerate(tern_list):
        def step(idx=idx, m_in=tm_in, m_sc=tm_sc, r=tr, cc=tcc, col=tcol):
            if idx > 0:
                pm = tern_list[idx - 1]
                tern_unit_pe(pm[1], pm[2], pm[3])
            tern_unit_dve(m_in, r, cc, col)
            if idx == len(tern_list) - 1:
                tern_unit_pe(m_sc, r, cc)
        wprep_steps.append(step)

    x_sb = [asb(f"x_sb{i}", [128, 2 * D], F32) for i in range(2)]
    xb_sb = asb("xb_sb", [128, D], BF16)
    xT_sb = asb("xT_sb", [128, d8 * TA], BF16)
    g_sb = asb("g_sb", [128, s32 * TA], BF16)
    g2_sb = [asb(f"g2_sb{i}", [128, TA], BF16) for i in range(3)]
    l2row = asb("l2row", [1, TA], F32)
    zh = asb("zh", [128, D], F32)
    y_sb = asb("y_sb", [128, 2 * D], F32)
    yqi = asb("yqi", [128, D // 2], I32)
    yqb = asb("yqb", [128, 2 * D], BF16)
    yqT_sb = asb("yqT_sb", [128, d8 * 128], BF16)
    bnst = asb("bnst", [128, nbn * 6], F32)
    smalls = asb("smalls", [128, 16], F32)
    ry_row = [asb(f"ry_row{i}", [1, TA], F32) for i in range(2)]

    last = {k: None for k in ("x0", "x1", "xb", "xT", "g", "pD", "pC",
                              "zh", "y", "yqi", "yqb", "yqT", "smalls_ac",
                              "pS0", "pS1", "g2_0", "g2_1", "g2_2", "g2_3",
                              "ry", "yqb_rdy", "qsc_ok", "rysm",
                              "ryb0", "ryb1")}
    last.update({f"pA{i}": None for i in range(4)})

    nmean = len(units_of(H, D)) + len(units_of(D, H)) + 1
    ntern = len(units_of(H, D)) + len(units_of(D, H))
    mtiles = max(1, min(4, nA // 4))
    wsteps_per_tile = []
    for i in range(nA):
        if i < mtiles:
            k = -(-nmean // mtiles)
        else:
            k = -(-ntern // (nA - mtiles))
        wsteps_per_tile.append(k)

    xloads = [O("sync", lambda s: s.dma_start(
        out=x_sb[0][:].rearrange("p (c d) -> p c d", c=2),
        in_=x_in[0:TA, :].rearrange("(c p) d -> p c d", p=128)), (), "qx0", 16)]

    def emit_head(i):
        xload = xloads[i]
        for c in range(2):
            O("vector", lambda v, c=c, xs=x_sb[i % 2]: v.tensor_copy(
                xb_sb[:], xs[:, c * D:(c + 1) * D]),
                W(xload, last["xb"], sc.cur("ve")), "ve")
            xb_ok = sc.cur("ve")
            for j in range(d8):
                O("tensor", lambda t, c=c, j=j: t.transpose(
                    pD[:, 128 * j:128 * (j + 1)],
                    xb_sb[:, 128 * j:128 * (j + 1)], identb[:]),
                    W(xb_ok, last["pD"], consts_ok) if j == 0 else (),
                    "pe" if j == d8 - 1 else None)
            O("vector", lambda v, c=c: v.tensor_copy(
                xT_sb[:].rearrange("p (j t) -> p j t", j=d8)[:, :, 128 * c:128 * (c + 1)],
                pD[:, 0:128 * d8].rearrange("p (j q) -> p j q", j=d8)),
                W(sc.cur("pe"), last["xT"], sc.cur("ve")), "ve")
            last["pD"] = sc.cur("ve")
            last["xb"] = sc.cur("pe")
        return sc.cur("ve")

    def emit_yq_lag(i):
        for c in range(2):
            for j in range(d8):
                O("tensor", lambda t, c=c, j=j: t.transpose(
                    pD[:, 128 * j:128 * (j + 1)],
                    yqb[:, c * D + 128 * j:c * D + 128 * (j + 1)], identb[:]),
                    W(last["yqb_rdy"], last["pD"]) if j == 0 else (),
                    "pe" if j == d8 - 1 else None)
            last["yqb"] = sc.cur("pe")
            O("vector", lambda v, c=c: v.tensor_copy(
                yqT_sb[:], pD[:, 0:128 * d8]),
                W(sc.cur("pe"), last["yqT"], sc.cur("ve")), "ve")
            last["pD"] = sc.cur("ve")
            O("sync", lambda s, i=i, c=c: s.dma_start(
                out=yqT_sc.rearrange("(j p) t -> p j t", p=128)[
                    :, :, TA * i + 128 * c:TA * i + 128 * (c + 1)],
                in_=yqT_sb[:].rearrange("p (j t) -> p j t", j=d8)),
                W(sc.cur("ve")), "qq", 16)
            last["yqT"] = sc.cur("qq")
        rb = i % 2
        for c in range(2):
            O("tensor", lambda t, c=c: t.transpose(
                pC[0:1, 256:384], smalls[:, 14 + c:15 + c], identf[:]),
                W(last["qsc_ok"], last["pC"], last[f"ryb{rb}"]), "pe")
            O("vector", lambda v, c=c, rb=rb: v.tensor_copy(
                ry_row[rb][0:1, 128 * c:128 * (c + 1)], pC[0:1, 256:384]),
                W(sc.cur("pe"), sc.cur("ve")), "ve")
            last["pC"] = sc.cur("ve")
        last["rysm"] = sc.cur("pe")
        O("sync", lambda s, i=i, rb=rb: s.dma_start(
            out=ry_sc[0:1, TA * i:TA * (i + 1)], in_=ry_row[rb][:]),
            W(sc.cur("ve")), f"qr{rb}", 16)
        last[f"ryb{rb}"] = sc.cur(f"qr{rb}")
        last["ry"] = sc.cur(f"qr{rb}")

    xT_ok = None
    for i in range(nA):
        sc.mark(f"tile{i}")
        wquota = [wsteps_per_tile[i] if i < len(wsteps_per_tile) else 0]
        xb_i = i % 2
        xload = xloads[i]

        if i + 1 < nA:
            xloads.append(O("sync", lambda s, i=i: s.dma_start(
                out=x_sb[(i + 1) % 2][:].rearrange("p (c d) -> p c d", c=2),
                in_=x_in[TA * (i + 1):TA * (i + 2), :]
                .rearrange("(c p) d -> p c d", p=128)),
                W(last[f"x{(i + 1) % 2}"]), f"qx{(i + 1) % 2}", 16))

        if i == 0:
            xT_ok = emit_head(0)

        ps_lohi = [(0, TA), (512, 512 + TA)]
        gelu_cnt = {}
        g2_cnt = {}

        def emit_ones(ss, i=i):
            O("tensor", lambda t, ss=ss: t.matmul(
                pC[0:1, 0:TA], ones_bf[:], g2_sb[ss % 3][:],
                start=(ss == 0), stop=(ss == s32 - 1)),
                W(g2_cnt[ss], last["pC"] if ss == 0 else None), "pe")
            last[f"g2_{ss % 3}"] = sc.cur("pe")

        for ss in range(s32):
            lo, hi = ps_lohi[ss % 2]
            w0 = [xT_ok, kt_ok]
            if ss >= 2:
                w0.append(gelu_cnt[ss - 2])
            else:
                w0.append(last[f"pS{ss}"])
            for j in range(d8):
                O("tensor", lambda t, ss=ss, j=j, lo=lo, hi=hi: t.matmul(
                    pB[:, lo:hi],
                    KT[:, j * S + 128 * ss: j * S + 128 * (ss + 1)],
                    xT_sb[:, j * TA:(j + 1) * TA],
                    start=(j == 0), stop=(j == d8 - 1)),
                    W(*w0) if j == 0 else (), "pe" if j == d8 - 1 else None)
            mm1_done = sc.cur("pe")
            if ss >= 3:
                emit_ones(ss - 3)
            O("scalar", lambda s, ss=ss, lo=lo, hi=hi: s.activation(
                g_sb[:, ss * TA:(ss + 1) * TA], pB[:, lo:hi], AF.Gelu),
                W(mm1_done), "ac")
            gelu_cnt[ss] = sc.cur("ac")
            O("vector", lambda v, ss=ss: v.tensor_mul(
                g2_sb[ss % 3][:], g_sb[:, ss * TA:(ss + 1) * TA],
                g_sb[:, ss * TA:(ss + 1) * TA]),
                W(gelu_cnt[ss], last[f"g2_{ss % 3}"], sc.cur("ve")), "ve")
            g2_cnt[ss] = sc.cur("ve")
            if ss % 3 == 2 and wquota[0] > 0 and wprep_steps:
                nper = -(-wquota[0] // max(1, (s32 - ss) // 3))
                for _ in range(nper):
                    if wprep_steps:
                        wprep_steps.pop(0)()
                        wquota[0] -= 1
        for sss in range(max(0, s32 - 3), s32):
            emit_ones(sss)
        last["pS0"] = gelu_cnt[s32 - 2]
        last["pS1"] = gelu_cnt[s32 - 1]
        l2mm_done = sc.cur("pe")

        last["xT"] = l2mm_done
        if i + 1 < nA:
            next_xT_ok = emit_head(i + 1)
        if i > 0:
            emit_yq_lag(i - 1)

        O("vector", lambda v: v.tensor_copy(l2row[:], pC[0:1, 0:TA]),
          W(l2mm_done, sc.cur("ve")), "ve")
        l2row_ok = sc.cur("ve")
        for c in range(2):
            O("tensor", lambda t, c=c: t.matmul(
                pC[:, 256 + c:257 + c], l2row[0:1, 128 * c:128 * (c + 1)], one11[:],
                start=True, stop=True),
                W(l2row_ok) if c == 0 else (), "pe" if c == 1 else None)
        O("vector", lambda v: v.tensor_copy(smalls[:, 0:2], pC[:, 256:258]),
          W(sc.cur("pe"), sc.cur("ve")), "ve")
        last["pC"] = sc.cur("ve")
        O("scalar", lambda s: s.activation(smalls[:, 0:2], smalls[:, 0:2], AF.Sqrt),
          W(sc.cur("ve"), last["smalls_ac"]), "ac")
        sqrt_ok = sc.cur("ac")
        O("vector", lambda v: v.reciprocal(smalls[:, 2:4], smalls[:, 0:2]),
          W(sqrt_ok, sc.cur("ve")), "ve")
        O("vector", lambda v: v.tensor_scalar(smalls[:, 2:4], smalls[:, 2:4],
                                              sqrtS, None, ALU.mult),
          W(sc.cur("ve")), "ve")
        rl2_ok = sc.cur("ve")

        for c in range(2):
            for n in range(ndh):
                gidx = c * ndh + n
                for k in range(s32):
                    w0 = W(gelu_cnt[k], v_ok, last[f"pA{gidx}"] if k == 0 else None)
                    O("tensor", lambda t, c=c, n=n, gidx=gidx, k=k: t.matmul(
                        pA[:, 512 * gidx:512 * gidx + dw],
                        g_sb[:, k * TA + 128 * c: k * TA + 128 * (c + 1)],
                        V_sb[:, k * D + dw * n: k * D + dw * (n + 1)],
                        start=(k == 0), stop=(k == s32 - 1)),
                        w0, "pe" if k == s32 - 1 else None)
                mm2_done = sc.cur("pe")
                O("vector", lambda v, c=c, n=n, gidx=gidx, xs=x_sb[xb_i]:
                  v.scalar_tensor_tensor(
                    zh[:, dw * n:dw * (n + 1)], pA[:, 512 * gidx:512 * gidx + dw],
                    smalls[:, 2 + c:3 + c],
                    xs[:, c * D + dw * n:c * D + dw * (n + 1)],
                    ALU.mult, ALU.add),
                    W(mm2_done, rl2_ok, xload,
                      last["zh"] if n == 0 else None, sc.cur("ve")), "ve")
                last[f"pA{gidx}"] = sc.cur("ve")
            z_ok = sc.cur("ve")
            for bb in range(nbn):
                O("vector", lambda v, bb=bb: v.bn_stats(
                    bnst[:, 6 * bb:6 * (bb + 1)],
                    zh[:, 512 * bb:512 * bb + min(512, D)]),
                    W(z_ok, sc.cur("ve")), "ve")
            O("vector", lambda v: v.bn_aggr(smalls[:, 4:6], bnst[:, 0:6 * nbn]),
              W(sc.cur("ve")), "ve")
            bn_ok = sc.cur("ve")
            O("scalar", lambda s: s.activation(smalls[:, 6:7], smalls[:, 5:6],
                                               AF.Sqrt, bias=epsc[:]),
              W(bn_ok, consts_ok), "ac")
            last["smalls_ac"] = sc.cur("ac")
            O("vector", lambda v: v.reciprocal(smalls[:, 8:9], smalls[:, 6:7]),
              W(sc.cur("ac"), sc.cur("ve")), "ve")
            O("vector", lambda v: v.tensor_mul(smalls[:, 10:11], smalls[:, 4:5],
                                               smalls[:, 8:9]),
              W(sc.cur("ve")), "ve")
            O("vector", lambda v, c=c: v.tensor_scalar(
                y_sb[:, c * D:(c + 1) * D], zh[:], smalls[:, 8:9], smalls[:, 10:11],
                ALU.mult, ALU.subtract),
                W(sc.cur("ve"), last["y"] if c == 0 else None), "ve")
            last["zh"] = sc.cur("ve")
            y_ok = sc.cur("ve")
            O("vector", lambda v, c=c: v.reduce_max(
                smalls[:, 12:13], y_sb[:, c * D:(c + 1) * D], axis=AX.X,
                apply_absolute_value=True), W(y_ok), "ve")
            O("vector", lambda v: v.tensor_scalar(smalls[:, 12:13], smalls[:, 12:13],
                                                  1e-5, None, ALU.max),
              W(sc.cur("ve")), "ve")
            O("vector", lambda v: v.reciprocal(smalls[:, 13:14], smalls[:, 12:13]),
              W(sc.cur("ve")), "ve")
            O("vector", lambda v: v.tensor_scalar(smalls[:, 13:14], smalls[:, 13:14],
                                                  127.0, None, ALU.mult),
              W(sc.cur("ve")), "ve")
            O("vector", lambda v, c=c: v.tensor_scalar(
                smalls[:, 14 + c:15 + c], smalls[:, 12:13], 1.0 / 127.0, None,
                ALU.mult),
                W(sc.cur("ve"), last["rysm"] if c == 0 else None), "ve")
            qsc_ok = sc.cur("ve")
            for hh in range(2):
                hw2 = D // 2
                O("vector", lambda v, c=c, hh=hh, hw2=hw2: v.tensor_scalar(
                    zh[:, hh * hw2:(hh + 1) * hw2],
                    y_sb[:, c * D + hh * hw2:c * D + (hh + 1) * hw2],
                    smalls[:, 13:14], MAGIC, ALU.mult, ALU.add),
                    W(qsc_ok, last["zh"], sc.cur("ve")), "ve")
                O("vector", lambda v, hh=hh, hw2=hw2: v.tensor_scalar(
                    yqi[:], zh[:, hh * hw2:(hh + 1) * hw2],
                    MAGIC, CQ, ALU.subtract, ALU.add),
                    W(last["yqi"], sc.cur("ve")), "ve")
                O("vector", lambda v, c=c, hh=hh, hw2=hw2: v.tensor_scalar(
                    yqb[:, c * D + hh * hw2:c * D + (hh + 1) * hw2],
                    yqi[:], -256.0, None, ALU.add),
                    W(last["yqb"] if (c == 0 and hh == 0) else None,
                      sc.cur("ve")), "ve")
                last["yqi"] = sc.cur("ve")
                last["zh"] = sc.cur("ve")
        last["g"] = sc.cur("pe")
        last["yqb_rdy"] = sc.cur("ve")
        last["qsc_ok"] = sc.cur("ve")

        O("sync", lambda s, i=i: s.dma_start(
            out=y_sc[TA * i:TA * (i + 1), :].rearrange("(c p) d -> p c d", p=128),
            in_=y_sb[:].rearrange("p (c d) -> p c d", c=2)),
            W(sc.cur("ve")), "qy", 16)
        last["y"] = sc.cur("qy")
        last[f"x{xb_i}"] = sc.cur("ve")
        if i + 1 < nA:
            xT_ok = next_xT_ok

    while wprep_steps:
        wprep_steps.pop(0)()
    emit_yq_lag(nA - 1)
    sc.mark("phaseA")
    pa.close()
    sc.barrier()

    W1T = sbuf("W1T", [128, d8 * H], BF16)
    W2T = sbuf("W2T", [128, h32 * D], BF16)
    yqTw = [sbuf(f"yqTw{i}", [128, d8 * (TW + 2)], BF16) for i in range(2)]
    y_win = sbuf("y_win", [128, D], F32)
    Srow_b = sbuf("Srow_b", [1, TW + 2], F32)
    Srow = [Srow_b, Srow_b]
    S_sb = [sbuf(f"S_sb{i}", [128, TW + 2], F32) for i in range(2)]
    htr = [sbuf(f"htr{i}", [128, TW + 2], F32) for i in range(3)]
    ca = [sbuf(f"ca{i}", [128, TW], F32) for i in range(2)]
    ca3 = [sbuf(f"ca3{i}", [128, TW], F32) for i in range(2)]
    hc_buf = sbuf("hc_buf", [128, h32 * TW], F32)
    mbuf = sbuf("mbuf", [128, TW], F32)
    mg = sbuf("mg", [128, 2 * TW], F32)
    i32tr = sbuf("i32tr", [128, 2 * TW], I32)
    hq_sb = sbuf("hq_sb", [128, h32 * TW], BF16)
    row2 = sbuf("row2", [1, TW], F32)
    S2_sb = sbuf("S2_sb", [128, TW], F32)
    bsml = sbuf("bsml", [128, 16], F32)
    bnst2 = sbuf("bnst2", [128, nbn * 6], F32)

    w1t_ok = O("sync", lambda s: s.dma_start(
        out=W1T[:].rearrange("p (j h) -> p j h", j=d8),
        in_=w1T_sc.rearrange("(j p) h -> p j h", p=128)), (), "qw1t", 16)
    w2t_ok = O("sync", lambda s: s.dma_start(
        out=W2T[:].rearrange("p (c d) -> p c d", c=h32),
        in_=w2T_sc.rearrange("(c p) d -> p c d", p=128)), (), "qw2t", 16)

    lastB = {k: None for k in ("pC", "htr0", "htr1", "htr2",
                               "htr0ac", "htr1ac", "htr2ac",
                               "ca0", "ca1", "ca30", "ca31",
                               "hc", "m8", "hq", "S2",
                               "row2", "ywin", "bsml_ac",
                               "Srow0", "Srow1", "S0", "S1", "pA2",
                               "mslot0", "mslot1", "mslot2", "mslot3")}
    lastB.update({f"pA{i}": None for i in range(4)})

    def emit_win_loads(w):
        q, wq = divmod(w, nWq)
        go = q * L + wq * TW
        lp = wq == 0
        rp = wq == nWq - 1
        c0 = go - (0 if lp else 1)
        c1 = min(go + TW + 1, q * L + L)
        ncols = c1 - c0
        dst0 = 1 if lp else 0
        wb = w % 2
        buf = yqTw[wb]
        ld = O("sync", lambda s, buf=buf, c0=c0, ncols=ncols, dst0=dst0: s.dma_start(
            out=buf[:].rearrange("p (j t) -> p j t", j=d8)[:, :, dst0:dst0 + ncols],
            in_=yqT_sc.rearrange("(j p) t -> p j t", p=128)[:, :, c0:c0 + ncols]),
            W(lastB.get(f"yqTw{wb}"), last["yqT"]), f"qt{wb}", 16)
        pads = []
        if lp:
            pads.append(O("vector", lambda v, buf=buf: v.memset(
                buf[:].rearrange("p (j t) -> p j t", j=d8)[:, :, 0:1], 0.0),
                W(lastB.get(f"yqTw{wb}"), ld, sc.cur("ve")), "ve"))
        if rp:
            pads.append(O("vector", lambda v, buf=buf: v.memset(
                buf[:].rearrange("p (j t) -> p j t", j=d8)[:, :, TW + 1:TW + 2], 0.0),
                W(lastB.get(f"yqTw{wb}"), ld, sc.cur("ve")), "ve"))
        return (ld, pads, go, lp, rp)

    def emit_sprep(w):
        wb = w % 2
        q, wq = divmod(w, nWq)
        go = q * L + wq * TW
        lp = wq == 0
        rp = wq == nWq - 1
        sr0 = go - (0 if lp else 1)
        srcols = min(go + TW + 1, q * L + L) - sr0
        sdst = 1 if lp else 0
        ryld = O("sync", lambda s, sr0=sr0, srcols=srcols, sdst=sdst, wb=wb:
                 s.dma_start(out=Srow[wb][0:1, sdst:sdst + srcols],
                             in_=ry_sc[0:1, sr0:sr0 + srcols]),
                 W(lastB["Srow0"], lastB["Srow1"], last["ry"]), f"qrl{wb}", 16)
        O("vector", lambda v, sr0=sr0, srcols=srcols, sdst=sdst, wb=wb:
          v.tensor_scalar(
            Srow[wb][0:1, sdst:sdst + srcols], Srow[wb][0:1, sdst:sdst + srcols],
            mwv[0:1, 0:1], None, ALU.mult),
            W(ryld, lastB[f"Srow{wb}"], wst["mw_ok"], sc.cur("ve")), "ve")
        if lp:
            O("vector", lambda v, wb=wb: v.memset(Srow[wb][0:1, 0:1], 0.0),
              W(sc.cur("ve")), "ve")
        if rp:
            O("vector", lambda v, wb=wb: v.memset(Srow[wb][0:1, TW + 1:TW + 2], 0.0),
              W(sc.cur("ve")), "ve")
        srow_ok = sc.cur("ve")
        O("tensor", lambda t, wb=wb: t.matmul(pC[:, 0:TW + 2], ones_f1[:],
                                              Srow[wb][:], start=True, stop=True),
          W(srow_ok, lastB["pC"]), "pe")
        O("vector", lambda v, wb=wb: v.tensor_copy(S_sb[wb][:], pC[:, 0:TW + 2]),
          W(sc.cur("pe"), lastB[f"S{wb}"], sc.cur("ve")), "ve")
        lastB["pC"] = sc.cur("ve")
        lastB[f"Srow{wb}"] = sc.cur("ve")
        return sc.cur("ve")

    win_loads = {0: emit_win_loads(0)}
    s_oks = {0: emit_sprep(0)}

    mslots = [(pB, 0), (pB, 512), (pA, 1536)]
    gsz = 16 if h32 % 16 == 0 else h32
    ng = h32 // gsz
    octets = [list(range(o, min(o + 8, h32))) for o in range(0, h32, 8)]
    st = {"hq_prev": None, "dbg_gate": None, "pend_silu": [], "i32ac": [None, None]}

    def emit_conv_chunk(w, hc):
        ld, pads, go, lp, rp = win_loads[w]
        s_ok = s_oks[w]
        s_i = hc % 3
        mp, lo = mslots[s_i]
        hi = lo + TW + 2

        def mm3(t, hc=hc, mp=mp, lo=lo, hi=hi, wb=w % 2):
            r = None
            for j in range(d8):
                r = t.matmul(
                    mp[:, lo:hi],
                    W1T[:, j * H + 128 * hc: j * H + 128 * (hc + 1)],
                    yqTw[wb][:, j * (TW + 2):(j + 1) * (TW + 2)],
                    start=(j == 0), stop=(j == d8 - 1))
            return r
        O("tensor", mm3,
          W(ld, *pads, w1t_ok, lastB[f"mslot{s_i}"]), "pe")
        mm3_done = sc.cur("pe")
        hb = hc % 3
        O("vector", lambda v, hb=hb, mp=mp, lo=lo, hi=hi, wb=w % 2:
          v.tensor_tensor(
            htr[hb][:], mp[:, lo:hi], S_sb[wb][:], ALU.mult),
            W(mm3_done, s_ok, lastB[f"htr{hb}"],
              lastB[f"htr{hb}ac"], sc.cur("ve")), "ve")
        lastB[f"mslot{s_i}"] = sc.cur("ve")
        htr_ok = sc.cur("ve")
        O("scalar", lambda s, hc=hc, hb=hb: s.activation(
            ca[hc % 2][:], htr[hb][:, 0:TW], AF.Identity,
            scale=cwb_sb[:, 4 * hc:4 * hc + 1],
            bias=cwb_sb[:, 4 * hc + 3:4 * hc + 4]),
            W(htr_ok, cwb_ok, lastB[f"ca{hc % 2}"]), "ac")
        c1_ok = sc.cur("ac")
        O("scalar", lambda s, hc=hc, hb=hb: s.activation(
            ca3[hc % 2][:], htr[hb][:, 2:TW + 2], AF.Identity,
            scale=cwb_sb[:, 4 * hc + 2:4 * hc + 3], bias=0.0),
            W(lastB[f"ca3{hc % 2}"], sc.cur("ac")), "ac")
        lastB[f"htr{hb}ac"] = sc.cur("ac")
        c3_ok = sc.cur("ac")
        O("vector", lambda v, hc=hc, hb=hb: v.scalar_tensor_tensor(
            ca[hc % 2][:], htr[hb][:, 1:TW + 1],
            cwb_sb[:, 4 * hc + 1:4 * hc + 2], ca[hc % 2][:],
            ALU.mult, ALU.add),
            W(c1_ok, sc.cur("ve")), "ve")
        lastB[f"htr{hb}"] = sc.cur("ve")
        c2_ok = sc.cur("ve")
        O("gpsimd", lambda g, hc=hc: g.tensor_tensor(
            hc_buf[:, hc * TW:(hc + 1) * TW], ca[hc % 2][:],
            ca3[hc % 2][:], ALU.add),
            W(c2_ok, c3_ok,
              st["hc_last"][hc] if st["hc_last"] is not None else None,
              st["dbg_gate"] if w == 1 else None,
              sc.cur("gp")), "gp")
        lastB[f"ca{hc % 2}"] = sc.cur("gp")
        lastB[f"ca3{hc % 2}"] = sc.cur("gp")
        if hc % gsz == gsz - 1:
            st["pend_silu"].append((w, hc // gsz))

    def emit_silu_group(w, gq):
        hc = gq * gsz + gsz - 1
        O("scalar", lambda s, hc=hc: s.activation(
            hc_buf[:, (hc - gsz + 1) * TW:(hc + 1) * TW],
            hc_buf[:, (hc - gsz + 1) * TW:(hc + 1) * TW], AF.Silu),
            W(sc.cur("gp"), sc.cur("ac")), "ac")
        O("vector", lambda v, hc=hc, gq=gq: v.reduce_max(
            mg[:, gq * TW:(gq + 1) * TW],
            hc_buf[:, (hc - gsz + 1) * TW:(hc + 1) * TW]
            .rearrange("p (c t) -> p t c", c=gsz),
            axis=AX.X, apply_absolute_value=True),
            W(sc.cur("ac"), sc.cur("ve")), "ve")

    def emit_finals(w):
        O("vector", lambda v: v.reduce_max(
            mbuf[:], mg[:, 0:ng * TW].rearrange("p (c t) -> p t c", c=ng),
            axis=AX.X), W(sc.cur("ve"), lastB["m8"]), "ve")
        mbuf_ok = sc.cur("ve")
        for c in range(2):
            O("tensor", lambda t, c=c: t.transpose(
                pC[:, 128 * c:128 * (c + 1)], mbuf[:, 128 * c:128 * (c + 1)],
                identf[:]),
                W(mbuf_ok, lastB["pC"]) if c == 0 else (),
                "pe" if c == 1 else None)
        lastB["m8"] = sc.cur("pe")
        O("vector", lambda v: v.reduce_max(
            bsml[:, 0:2], pC[:, 0:256].rearrange("p (c t) -> p c t", c=2),
            axis=AX.X), W(sc.cur("pe"), lastB["bsml_ac"], sc.cur("ve")), "ve")
        lastB["pC"] = sc.cur("ve")
        O("vector", lambda v: v.tensor_scalar(bsml[:, 0:2], bsml[:, 0:2],
                                              1e-5, None, ALU.max),
          W(sc.cur("ve")), "ve")
        O("vector", lambda v: v.tensor_scalar(bsml[:, 2:4], bsml[:, 0:2],
                                              m2s_col[:], None, ALU.mult),
          W(sc.cur("ve"), wst["wq_col_ok"]), "ve")
        rh2_ok = sc.cur("ve")
        O("vector", lambda v: v.reciprocal(bsml[:, 4:6], bsml[:, 0:2]),
          W(sc.cur("ve")), "ve")
        O("vector", lambda v: v.tensor_scalar(bsml[:, 4:6], bsml[:, 4:6],
                                              127.0, None, ALU.mult),
          W(sc.cur("ve")), "ve")
        sc2_ok = sc.cur("ve")
        for c in range(2):
            O("tensor", lambda t, c=c: t.transpose(
                pC[0:1, 256:384], bsml[:, 4 + c:5 + c], identf[:]),
                W(sc2_ok, lastB["pC"]), "pe")
            O("vector", lambda v, c=c: v.tensor_copy(
                row2[0:1, 128 * c:128 * (c + 1)], pC[0:1, 256:384]),
                W(sc.cur("pe"), lastB["row2"] if c == 0 else None,
                  sc.cur("ve")), "ve")
            lastB["pC"] = sc.cur("ve")
        O("tensor", lambda t: t.matmul(pC[:, 0:TW], ones_f1[:], row2[:],
                                       start=True, stop=True),
          W(sc.cur("ve"), lastB["pC"]), "pe")
        lastB["row2"] = sc.cur("pe")
        O("vector", lambda v: v.tensor_copy(S2_sb[:], pC[:, 0:TW]),
          W(sc.cur("pe"), lastB["S2"], sc.cur("ve")), "ve")
        lastB["pC"] = sc.cur("ve")
        s2_ok = sc.cur("ve")
        dbg_ok = None
        if dbg and w == 0:
            O("sync", lambda s: s.dma_start(out=dbg_m[:, 0:TW], in_=mbuf[:]),
              W(s2_ok), "qc", 16)
            O("sync", lambda s: s.dma_start(out=dbg_h[:, 0:h32 * TW], in_=hc_buf[:]),
              W(s2_ok), "qc", 16)
            dbg_ok = sc.cur("qc")
            st["dbg_gate"] = dbg_ok
        return s2_ok, rh2_ok, dbg_ok

    def emit_quant_chunk(w, k, s2_ok, dbg_ok, hq_cnt, hc_last):
        sl = slice(k * TW, (k + 1) * TW)
        i2 = slice((k % 2) * TW, (k % 2) * TW + TW)
        O("vector", lambda v, sl=sl: v.tensor_tensor(
            hc_buf[:, sl], hc_buf[:, sl], S2_sb[:], ALU.mult),
            W(s2_ok, dbg_ok if k == 0 else None, sc.cur("ve")), "ve")
        O("vector", lambda v, sl=sl: v.tensor_scalar(
            hc_buf[:, sl], hc_buf[:, sl], MAGIC, MAGIC, ALU.add, ALU.subtract),
            W(sc.cur("ve")), "ve")
        O("vector", lambda v, sl=sl, i2=i2: v.tensor_scalar(
            i32tr[:, i2], hc_buf[:, sl], CQ, None, ALU.add),
            W(st["i32ac"][k % 2], sc.cur("ve")), "ve")
        hc_last[k] = sc.cur("ve")
        if k % 2 == 1:
            O("scalar", lambda s, k=k: s.activation(
                hq_sb[:, (k - 1) * TW:(k + 1) * TW], i32tr[:],
                AF.Identity, bias=m256c[:]),
                W(lastB["hq"] if k == 1 else None, consts_ok,
                  sc.cur("ve"), sc.cur("ac")), "ac")
            st["i32ac"][0] = sc.cur("ac")
            st["i32ac"][1] = sc.cur("ac")
            hq_cnt[k] = sc.cur("ac")

    def emit_mm4_octet(w, ks, hq_cnt, mm4_done):
        for k in ks:
            for n in range(ndh):
                w0 = W(hq_cnt[ks[-1]] if (k == ks[0] and n == 0) else None,
                       w2t_ok if k == 0 and n == 0 else None,
                       lastB[f"pA{n}"] if k == 0 else None)
                is_stop = k == h32 - 1
                O("tensor", lambda t, n=n, k=k, is_stop=is_stop: t.matmul(
                    pA[:, 512 * n:512 * n + dw],
                    hq_sb[:, k * TW: k * TW + 128],
                    W2T[:, k * D + dw * n: k * D + dw * (n + 1)],
                    start=(k == 0), stop=is_stop),
                    w0, "pe" if is_stop else None)
                if is_stop:
                    mm4_done[(0, n)] = sc.cur("pe")
        for k in ks:
            w0 = W(lastB["pA2"] if k == 0 else None)
            is_stop = k == h32 - 1
            O("tensor", lambda t, k=k, is_stop=is_stop: t.matmul(
                pA[:, 1024:1024 + dw],
                hq_sb[:, k * TW + 128: k * TW + 256],
                W2T[:, k * D: k * D + dw],
                start=(k == 0), stop=is_stop),
                w0, "pe" if is_stop else None)
            if is_stop:
                mm4_done[(1, 0)] = sc.cur("pe")

    def emit_mm4_c1n1(w, mm4_done):
        if ndh < 2:
            return
        for k in range(h32):
            w0 = W(lastB["pC"] if k == 0 else None)
            is_stop = k == h32 - 1
            O("tensor", lambda t, k=k, is_stop=is_stop: t.matmul(
                pC[:, 0:dw],
                hq_sb[:, k * TW + 128: k * TW + 256],
                W2T[:, k * D + dw: k * D + 2 * dw],
                start=(k == 0), stop=is_stop),
                w0, "pe" if is_stop else None)
            if is_stop:
                mm4_done[(1, 1)] = sc.cur("pe")

    def emit_ln2_store(w, c):
        for bb in range(nbn):
            O("vector", lambda v, bb=bb: v.bn_stats(
                bnst2[:, 6 * bb:6 * (bb + 1)],
                y_win[:, 512 * bb:512 * bb + min(512, D)]),
                W(sc.cur("ve")), "ve")
        O("vector", lambda v: v.bn_aggr(bsml[:, 4:6], bnst2[:, 0:6 * nbn]),
          W(sc.cur("ve")), "ve")
        O("scalar", lambda s: s.activation(bsml[:, 6:7], bsml[:, 5:6],
                                           AF.Sqrt, bias=epsc[:]),
          W(sc.cur("ve"), consts_ok, sc.cur("ac")), "ac")
        lastB["bsml_ac"] = sc.cur("ac")
        O("vector", lambda v: v.reciprocal(bsml[:, 8:9], bsml[:, 6:7]),
          W(sc.cur("ac"), sc.cur("ve")), "ve")
        O("vector", lambda v: v.tensor_mul(bsml[:, 10:11], bsml[:, 4:5],
                                           bsml[:, 8:9]),
          W(sc.cur("ve")), "ve")
        O("vector", lambda v: v.tensor_scalar(
            y_win[:], y_win[:],
            bsml[:, 8:9], bsml[:, 10:11], ALU.mult, ALU.subtract),
            W(sc.cur("ve")), "ve")
        out_ok = sc.cur("ve")
        O("sync", lambda s, w=w, c=c: s.dma_start(
            out=out_hbm[TW * w + 128 * c:TW * w + 128 * (c + 1), :],
            in_=y_win[:]),
            W(out_ok), "qo", 16)
        lastB["ywin"] = sc.cur("qo")

    def emit_tail_c0(w, yld, rh2_ok, mm4_done):
        for n in range(ndh):
            O("vector", lambda v, n=n: v.scalar_tensor_tensor(
                y_win[:, dw * n:dw * (n + 1)],
                pA[:, 512 * n:512 * n + dw],
                bsml[:, 2:3],
                y_win[:, dw * n:dw * (n + 1)],
                ALU.mult, ALU.add),
                W(mm4_done[(0, n)], rh2_ok, yld, sc.cur("ve")), "ve")
            lastB[f"pA{n}"] = sc.cur("ve")
        emit_ln2_store(w, 0)

    def emit_tail_c1(w, yld, rh2_ok, mm4_done):
        O("vector", lambda v: v.scalar_tensor_tensor(
            y_win[:, 0:dw], pA[:, 1024:1024 + dw], bsml[:, 3:4],
            y_win[:, 0:dw], ALU.mult, ALU.add),
            W(mm4_done[(1, 0)], rh2_ok, yld, sc.cur("ve")), "ve")
        lastB["pA2"] = sc.cur("ve")
        if ndh >= 2:
            O("vector", lambda v: v.scalar_tensor_tensor(
                y_win[:, dw:2 * dw], pC[:, 0:dw], bsml[:, 3:4],
                y_win[:, dw:2 * dw], ALU.mult, ALU.add),
                W(mm4_done[(1, 1)], rh2_ok, yld, sc.cur("ve")), "ve")
            lastB["pC"] = sc.cur("ve")
        emit_ln2_store(w, 1)

    def flush_silu():
        for (wc, gq) in st["pend_silu"]:
            emit_silu_group(wc, gq)
        st["pend_silu"] = []

    st["hc_last"] = None
    for hc in range(h32):
        emit_conv_chunk(0, hc)
    flush_silu()
    lastB["S0"] = sc.cur("ve")
    lastB["yqTw0"] = sc.cur("pe")
    if nW > 1:
        s_oks[1] = emit_sprep(1)
        win_loads[1] = emit_win_loads(1)
    sc.mark("w_head")

    for w in range(nW):
        sc.mark(f"win{w}")
        s2_ok, rh2_ok, dbg_ok = emit_finals(w)
        hq_cnt = {}
        hc_last = {}
        st["hc_last"] = hc_last
        yld = O("sync", lambda s, w=w: s.dma_start(
            out=y_win[:],
            in_=y_sc[TW * w:TW * w + 128, :]),
            W(lastB["ywin"], last["y"]), "qyl", 16)
        mm4_done = {}
        for oi, ks in enumerate(octets):
            flush_silu()
            for k in ks:
                emit_quant_chunk(w, k, s2_ok, dbg_ok, hq_cnt, hc_last)
            if w + 1 < nW:
                for hc in ks:
                    emit_conv_chunk(w + 1, hc)
            if oi > 0:
                emit_mm4_octet(w, octets[oi - 1], hq_cnt, mm4_done)
        emit_mm4_octet(w, octets[-1], hq_cnt, mm4_done)
        if w + 1 < nW:
            lastB[f"S{(w + 1) % 2}"] = sc.cur("ve")
            lastB[f"yqTw{(w + 1) % 2}"] = sc.cur("pe")
        emit_mm4_c1n1(w, mm4_done)
        emit_tail_c0(w, yld, rh2_ok, mm4_done)
        yld = O("sync", lambda s, w=w: s.dma_start(
            out=y_win[:],
            in_=y_sc[TW * w + 128:TW * w + 256, :]),
            W(lastB["ywin"], last["y"]), "qyl", 16)
        emit_tail_c1(w, yld, rh2_ok, mm4_done)
        lastB["hq"] = sc.cur("pe")
        lastB["S2"] = sc.cur("ve")
        flush_silu()
        if w + 2 < nW:
            s_oks[w + 2] = emit_sprep(w + 2)
            win_loads[w + 2] = emit_win_loads(w + 2)
        win_loads.pop(w)
        s_oks.pop(w)

    sc.streams["sync"].append((W(sc.cur("qo"), sc.cur("qy"), sc.cur("qq")), None, None, sc.seq))
    sc.streams["gpsimd"].append((W(sc.cur("qk0"), sc.cur("qk1"), sc.cur("qv")), None, None, sc.seq))
    sc.seq += 1

    with nc.Block() as block:
        sc.emit(block, sems)

    outer.close()
    return nc


_NC_CACHE = {}


def get_program(key=(1024, 4096, 4096, 2048, 2)):
    if key not in _NC_CACHE:
        _NC_CACHE[key] = build_core(*key)
    return _NC_CACHE[key]


def kernel(L0_attn_inputs, L0_attn_key_param_tokens, L0_attn_value_param_tokens,
           L0_ffn_w1, L0_ffn_conv_w, L0_ffn_conv_b, L0_ffn_w2):
    x = np.ascontiguousarray(np.asarray(L0_attn_inputs, dtype=np.float32))
    k = np.ascontiguousarray(np.asarray(L0_attn_key_param_tokens, dtype=np.float32))
    v = np.ascontiguousarray(np.asarray(L0_attn_value_param_tokens, dtype=np.float32))
    w1 = np.ascontiguousarray(np.asarray(L0_ffn_w1, dtype=np.float32))
    w2 = np.ascontiguousarray(np.asarray(L0_ffn_w2, dtype=np.float32))
    cw = np.asarray(L0_ffn_conv_w, dtype=np.float32).reshape(-1, 3)
    cb = np.asarray(L0_ffn_conv_b, dtype=np.float32).reshape(-1, 1)
    cwb = np.ascontiguousarray(np.concatenate([cw, cb], axis=1))

    B, L, D = x.shape
    ncores = 8
    bc = B // ncores
    nc_prog = get_program((D, k.shape[0], w1.shape[0], L, bc))
    in_maps = []
    for c in range(ncores):
        in_maps.append({
            "x": np.ascontiguousarray(x[bc * c:bc * (c + 1)].reshape(bc * L, D)),
            "k": k, "v": v, "w1": w1, "w2": w2, "cwb": cwb,
        })
    res = run_bass_kernel_spmd(nc_prog, in_maps, core_ids=list(range(ncores)))
    out = np.concatenate(
        [r["out"].reshape(bc, L, D) for r in res.results], axis=0)
    return out


# version 26
# speedup vs baseline: 1.0258x; 1.0054x over previous
import contextlib

import numpy as np

import concourse.bass as bass
import concourse.mybir as mybir
from concourse.ap import AP
from concourse.bass_utils import run_bass_kernel_spmd

F32 = mybir.dt.float32
BF16 = mybir.dt.bfloat16
I32 = mybir.dt.int32
AF = mybir.ActivationFunctionType
ALU = mybir.AluOpType
AX = mybir.AxisListType

ENGINES = ("sync", "gpsimd", "vector", "scalar", "tensor")

MAGIC = 12582912.0
CQ = 256.49996948242188


def W(*items):
    out = []
    for it in items:
        if it is None:
            continue
        if isinstance(it, tuple) and it and all(
                sub is None or isinstance(sub, tuple) for sub in it):
            for sub in it:
                if sub is not None and sub[1] > 0:
                    out.append(sub)
            continue
        s, v = it
        if v > 0:
            out.append((s, v))
    return tuple(out)


TRUNCATE_AT = None


class Sched:
    def __init__(self):
        self.streams = {e: [] for e in ENGINES}
        self.cnt = {}
        self.seq = 0
        self.sections = {}

    def mark(self, name):
        self.sections[name] = (self.seq, dict(self.cnt))

    def n(self, sem):
        return self.cnt.get(sem, 0)

    def cur(self, sem):
        return (sem, self.cnt.get(sem, 0))

    def op(self, eng, emit, waits=(), inc=None, inc_n=1):
        if inc is not None:
            self.cnt[inc] = self.cnt.get(inc, 0) + inc_n
        co = {}
        for s, v in waits:
            if co.get(s, -1) < v:
                co[s] = v
        self.streams[eng].append(
            (tuple(co.items()), emit, (inc, inc_n) if inc else None, self.seq))
        self.seq += 1
        return (inc, self.cnt[inc]) if inc else None

    def barrier(self):
        f = W(*[(s, v) for s, v in self.cnt.items()])
        for e in ENGINES:
            self.streams[e].append((f, None, None, self.seq))
        self.seq += 1

    def emit(self, block, sems):
        cutoff, fence = None, None
        if TRUNCATE_AT is not None:
            cutoff, fcnt = self.sections[TRUNCATE_AT]
            fence = W(*[(s, v) for s, v in fcnt.items()])

        def run(eng_obj, stream):
            waited = {}
            for waits, emitf, inc, seq in stream:
                if cutoff is not None and seq >= cutoff:
                    continue
                for s, v in waits:
                    if waited.get(s, -1) < v:
                        eng_obj.wait_ge(sems[s], v)
                        waited[s] = v
                if emitf is not None:
                    instr = emitf(eng_obj)
                    if inc is not None:
                        instr.then_inc(sems[inc[0]], inc[1])
            if fence is not None:
                for s, v in fence:
                    if waited.get(s, -1) < v:
                        eng_obj.wait_ge(sems[s], v)

        for name in ENGINES:
            getattr(block, name)((lambda st: (lambda e: run(e, st)))(self.streams[name]))


def build_core(D=1024, S=4096, H=4096, L=2048, Bc=2, dbg=False):
    TA = 256
    TW = 256
    T = Bc * L
    d8, s32, h32 = D // 128, S // 128, H // 128
    nA = T // TA
    nW = T // TW
    nWq = L // TW
    WM = max(D, H)
    sqrtS = float(np.float32(np.sqrt(S)))
    ndh = max(1, D // 512)
    dw = D // ndh
    nbn = max(1, D // 512)
    nqg = 1
    assert TA == 256 and TW == 256 and L % TW == 0 and T % TA == 0
    assert (H * D) & (H * D - 1) == 0
    assert h32 % nqg == 0

    nc = bass.Bass("TRN2", target_bir_lowering=False, debug=False)

    x_in = nc.dram_tensor("x", [T, D], F32, kind="ExternalInput").ap()
    k_in = nc.dram_tensor("k", [S, D], F32, kind="ExternalInput").ap()
    v_in = nc.dram_tensor("v", [S, D], F32, kind="ExternalInput").ap()
    w1_in = nc.dram_tensor("w1", [H, D], F32, kind="ExternalInput").ap()
    w2_in = nc.dram_tensor("w2", [D, H], F32, kind="ExternalInput").ap()
    cwb_in = nc.dram_tensor("cwb", [H, 4], F32, kind="ExternalInput").ap()
    out_hbm = nc.dram_tensor("out", [T, D], F32, kind="ExternalOutput").ap()
    ikind = "ExternalOutput" if dbg else "Internal"
    y_sc = nc.dram_tensor("y_sc", [T, D], F32, kind=ikind).ap()
    yqT_sc = nc.dram_tensor("yqT_sc", [D, T], BF16, kind=ikind).ap()
    w1T_sc = nc.dram_tensor("w1T_sc", [D, H], BF16, kind=ikind).ap()
    w2T_sc = nc.dram_tensor("w2T_sc", [H, D], BF16, kind=ikind).ap()
    ry_sc = nc.dram_tensor("ry_sc", [1, T], F32, kind=ikind).ap()
    if dbg:
        dbg_m = nc.dram_tensor("dbg_m", [128, 512], F32, kind="ExternalOutput").ap()
        dbg_r = nc.dram_tensor("dbg_r", [4, 512], F32, kind="ExternalOutput").ap()
        dbg_h = nc.dram_tensor("dbg_h", [128, 8192], F32, kind="ExternalOutput").ap()

    outer = contextlib.ExitStack()
    sbuf = lambda name, shape, dt=F32: outer.enter_context(nc.sbuf_tensor(name, shape, dt))
    sems = {s: outer.enter_context(nc.semaphore(s))
            for s in ("pe", "ve", "ac", "gp", "qc", "qw0", "qw1", "qs",
                      "qk0", "qk1", "qv", "qx0", "qx1", "qy", "qq",
                      "qw1t", "qw2t", "qt0", "qt1", "qyl", "qo",
                      "qr0", "qr1", "qrl0", "qrl1",
                      "qg0", "qg1", "qg2", "qg3")}
    sc = Sched()
    O = sc.op

    pA = outer.enter_context(nc.psum_tensor("pA", [128, 2048], F32))
    pB = outer.enter_context(nc.psum_tensor("pB", [128, 1024], F32))
    pC = outer.enter_context(nc.psum_tensor("pC", [128, 512], F32))
    pD = outer.enter_context(nc.psum_tensor("pD", [128, 1024], BF16))

    identb = sbuf("identb", [128, 128], BF16)
    identf = sbuf("identf", [128, 128], F32)
    ones_f1 = sbuf("ones_f1", [1, 128], F32)
    one11 = sbuf("one11", [1, 1], F32)
    ones_bf = sbuf("ones_bf", [128, 1], BF16)
    b256 = sbuf("b256", [128, 1], F32)
    cqc = sbuf("cqc", [128, 1], F32)
    m256c = sbuf("m256c", [128, 1], F32)
    epsc = sbuf("epsc", [128, 1], F32)
    cwb_sb = sbuf("cwb_sb", [128, h32 * 4], F32)
    mrow_s = sbuf("mrow_s", [1, 2], F32)
    mcl = sbuf("mcl", [1, 2], F32)
    wsc = sbuf("wsc", [1, 2], F32)
    mwv = sbuf("mwv", [1, 2], F32)
    m2s = sbuf("m2s", [1, 1], F32)
    wq_col = sbuf("wq_col", [128, 2], F32)
    m2s_col = sbuf("m2s_col", [128, 1], F32)

    def ident_init(g, buf):
        return g.affine_select(out=buf[:], in_=buf[:], compare_op=ALU.not_equal,
                               fill=1.0, base=0, pattern=[[-1, 128]],
                               channel_multiplier=1)

    O("gpsimd", lambda g: g.memset(identb[:], 0.0), (), "gp")
    O("gpsimd", lambda g: ident_init(g, identb), W(sc.cur("gp")), "gp")
    O("gpsimd", lambda g: g.memset(identf[:], 0.0), (), "gp")
    O("gpsimd", lambda g: ident_init(g, identf), W(sc.cur("gp")), "gp")
    for buf, val in ((ones_f1, 1.0), (one11, 1.0), (ones_bf, 1.0),
                     (b256, 256.5), (epsc, 1e-5), (cqc, CQ), (m256c, -256.0)):
        O("gpsimd", lambda g, b=buf, v=val: g.memset(b[:], v), (), "gp")
    consts_ok = sc.cur("gp")

    O("sync", lambda s: s.dma_start(
        out=cwb_sb[:].rearrange("p (c k) -> p c k", c=h32),
        in_=cwb_in.rearrange("(c p) k -> p c k", p=128)), (), "qc", 16)
    cwb_ok = sc.cur("qc")
    sc.mark("consts")

    pa = contextlib.ExitStack()
    asb = lambda name, shape, dt=F32: pa.enter_context(nc.sbuf_tensor(name, shape, dt))
    KT = asb("KT", [128, d8 * S], BF16)
    V_sb = asb("V_sb", [128, s32 * D], BF16)
    kbts = [asb(f"kbt{i}", [128, D], BF16) for i in range(2)]
    buf_free = [None, None]
    for c in range(S // 128):
        b = c % 2
        ld = O("gpsimd", lambda g, c=c, b=b: g.dma_start(
            out=kbts[b][:], in_=k_in[128 * c:128 * (c + 1), :]),
            W(buf_free[b]), f"qk{b}", 16)
        for g0 in range(0, d8, 8):
            gn = min(8, d8 - g0)
            for jj in range(gn):
                j = g0 + jj
                O("tensor", lambda t, j=j, jj=jj, b=b: t.transpose(
                    pD[:, 128 * jj:128 * (jj + 1)],
                    kbts[b][:, 128 * j:128 * (j + 1)], identb[:]),
                    W(ld, sc.cur("ve"), consts_ok) if jj == 0 else (),
                    "pe" if jj == gn - 1 else None)
            O("vector", lambda v, c=c, g0=g0, gn=gn: v.tensor_copy(
                KT[:].rearrange("p (j s) -> p j s", j=d8)[:, g0:g0 + gn,
                                                          128 * c:128 * (c + 1)],
                pD[:, 0:128 * gn].rearrange("p (j q) -> p j q", j=gn)),
                W(sc.cur("pe")), "ve")
        buf_free[b] = sc.cur("pe")
    kt_ok = sc.cur("ve")
    sc.mark("ktv")
    v_ok = O("gpsimd", lambda g: g.dma_start(
        out=V_sb[:].rearrange("p (c d) -> p c d", c=s32),
        in_=v_in.rearrange("(c p) d -> p c d", p=128)), (), "qv", 16)

    UW = min(512, min(D, H))
    wtmp = [asb(f"wtmp{i}", [128, UW], F32) for i in range(3)]
    wred = asb("wred", [128, 2], F32)
    wacc = asb("wacc", [128, 2], F32)
    wi32 = asb("wi32", [128, UW], I32)
    wqb = [asb(f"wqb{i}", [128, UW], BF16) for i in range(2)]
    wstg = asb("wstg", [128, (UW // 128) * 128], BF16)

    O("vector", lambda v: v.memset(wacc[:], 0.0), (), "ve")

    wst = {"buf_free": [None] * 3, "stg_free": None, "n": 0, "nq": 0, "np": 0,
           "wq_col_ok": None, "mw_ok": None, "wqb_free": [None, None],
           "wqb_ok": [None, None], "loads": [], "ld_next": 0, "ld_h": {}}

    def issue_wload():
        u = wst["ld_next"]
        if u >= len(wst["loads"]):
            return
        wst["ld_next"] += 1
        m, r, cc = wst["loads"][u]
        b = u % 3
        wst["ld_h"][u] = O("gpsimd", lambda g, m=m, r=r, cc=cc, b=b: g.dma_start(
            out=wtmp[b][:], in_=m[128 * r:128 * (r + 1), UW * cc:UW * (cc + 1)]),
            W(wst["buf_free"][b]), f"qg{b}", 16)

    def units_of(rows, width):
        return [(r, cc) for r in range(rows // 128) for cc in range(width // UW)]

    def mean_unit(m_in, r, cc, col):
        if wst["n"] == 0:
            for _ in range(3):
                issue_wload()
        u = wst["n"]
        b = u % 3
        wst["n"] += 1
        O("vector", lambda v, b=b: v.reduce_sum(
            wred[:, 0:1], wtmp[b][:], axis=AX.X, apply_absolute_value=True),
            W(wst["ld_h"].pop(u), sc.cur("ve")), "ve")
        wst["buf_free"][b] = sc.cur("ve")
        O("vector", lambda v, col=col: v.tensor_tensor(
            wacc[:, col:col + 1], wacc[:, col:col + 1], wred[:, 0:1], ALU.add),
            W(sc.cur("ve")), "ve")
        issue_wload()

    def mean_finalize():
        O("gpsimd", lambda g: g.tensor_reduce(mrow_s[:], wacc[:], axis=AX.C,
                                              op=ALU.add),
          W(sc.cur("ve"), consts_ok), "gp")
        O("vector", lambda v: v.tensor_scalar(mcl[:], mrow_s[:], 1.0 / (H * D), 1e-5,
                                              ALU.mult, ALU.max),
          W(sc.cur("gp")), "ve")
        O("vector", lambda v: v.reciprocal(wsc[:], mcl[:]), W(sc.cur("ve")), "ve")
        O("vector", lambda v: v.reciprocal(mwv[:], wsc[:]), W(sc.cur("ve")), "ve")
        O("vector", lambda v: v.tensor_scalar(m2s[:], mwv[0:1, 1:2], 1.0 / 127.0,
                                              None, ALU.mult),
          W(sc.cur("ve")), "ve")
        wst["mw_ok"] = sc.cur("ve")
        O("tensor", lambda t: t.matmul(pA[:, 0:2], ones_f1[:], wsc[:],
                                       start=True, stop=True),
          W(wst["mw_ok"], consts_ok, last["pA0"]), "pe")
        O("tensor", lambda t: t.matmul(pA[:, 2:3], ones_f1[:], m2s[:],
                                       start=True, stop=True), (), "pe")
        O("vector", lambda v: v.tensor_copy(wq_col[:], pA[:, 0:2]),
          W(sc.cur("pe"), sc.cur("ve")), "ve")
        O("vector", lambda v: v.tensor_copy(m2s_col[:], pA[:, 2:3]),
          W(sc.cur("ve")), "ve")
        last["pA0"] = sc.cur("ve")
        wst["wq_col_ok"] = sc.cur("ve")

    def tern_unit_dve(m_in, r, cc, col):
        u = wst["n"]
        b = u % 3
        qb = wst["nq"] % 2
        wst["n"] += 1
        wst["nq"] += 1
        O("vector", lambda v, b=b, col=col: v.tensor_scalar(
            wtmp[b][:], wtmp[b][:], wq_col[:, col:col + 1], 1.0, ALU.mult, ALU.min),
            W(wst["ld_h"].pop(u), wst["wq_col_ok"], sc.cur("ve")), "ve")
        O("vector", lambda v, b=b: v.tensor_scalar(
            wtmp[b][:], wtmp[b][:], -1.0, MAGIC, ALU.max, ALU.add),
            W(sc.cur("ve")), "ve")
        O("vector", lambda v, b=b: v.tensor_scalar(
            wi32[:], wtmp[b][:], MAGIC, CQ, ALU.subtract, ALU.add),
            W(sc.cur("ve")), "ve")
        wst["buf_free"][b] = sc.cur("ve")
        issue_wload()
        O("vector", lambda v, qb=qb: v.tensor_scalar(
            wqb[qb][:], wi32[:], -256.0, None, ALU.add),
            W(sc.cur("ve"), wst["wqb_free"][qb]), "ve")
        wst["wqb_ok"][qb] = sc.cur("ve")

    def tern_unit_pe(m_sc, r, cc):
        qb = wst["np"] % 2
        wst["np"] += 1
        nj = UW // 128
        for jj in range(nj):
            O("tensor", lambda t, jj=jj, qb=qb: t.transpose(
                pD[:, 128 * jj:128 * (jj + 1)],
                wqb[qb][:, 128 * jj:128 * (jj + 1)], identb[:]),
                W(wst["wqb_ok"][qb], last["pD"], consts_ok) if jj == 0 else (),
                "pe" if jj == nj - 1 else None)
        wst["wqb_free"][qb] = sc.cur("pe")
        O("vector", lambda v, nj=nj: v.tensor_copy(
            wstg[:, 0:128 * nj], pD[:, 0:128 * nj]),
            W(sc.cur("pe"), wst["stg_free"], sc.cur("ve")), "ve")
        last["pD"] = sc.cur("ve")
        wst["stg_free"] = O("sync", lambda s, m_sc=m_sc, r=r, cc=cc, nj=nj:
                            s.dma_start(
            out=m_sc.rearrange("(j p) r -> p j r", p=128)[
                :, (UW * cc) // 128:(UW * cc) // 128 + nj,
                128 * r:128 * (r + 1)],
            in_=wstg[:, 0:128 * nj].rearrange("p (j q) -> p j q", j=nj)),
            W(sc.cur("ve")), "qs", 16)

    wprep_steps = []
    for r, cc in units_of(H, D):
        wst["loads"].append((w1_in, r, cc))
    for r, cc in units_of(D, H):
        wst["loads"].append((w2_in, r, cc))
    for r, cc in units_of(H, D):
        wst["loads"].append((w1_in, r, cc))
    for r, cc in units_of(D, H):
        wst["loads"].append((w2_in, r, cc))
    for r, cc in units_of(H, D):
        wprep_steps.append(lambda r=r, cc=cc: mean_unit(w1_in, r, cc, 0))
    for r, cc in units_of(D, H):
        wprep_steps.append(lambda r=r, cc=cc: mean_unit(w2_in, r, cc, 1))
    wprep_steps.append(mean_finalize)
    tern_list = ([(w1_in, w1T_sc, r, cc, 0) for r, cc in units_of(H, D)]
                 + [(w2_in, w2T_sc, r, cc, 1) for r, cc in units_of(D, H)])
    for idx, (tm_in, tm_sc, tr, tcc, tcol) in enumerate(tern_list):
        def step(idx=idx, m_in=tm_in, m_sc=tm_sc, r=tr, cc=tcc, col=tcol):
            if idx > 0:
                pm = tern_list[idx - 1]
                tern_unit_pe(pm[1], pm[2], pm[3])
            tern_unit_dve(m_in, r, cc, col)
            if idx == len(tern_list) - 1:
                tern_unit_pe(m_sc, r, cc)
        wprep_steps.append(step)

    x_sb = [asb(f"x_sb{i}", [128, 2 * D], F32) for i in range(2)]
    xb_sb = asb("xb_sb", [128, D], BF16)
    xT_sb = asb("xT_sb", [128, d8 * TA], BF16)
    g_sb = asb("g_sb", [128, s32 * TA], BF16)
    g2_sb = [asb(f"g2_sb{i}", [128, TA], BF16) for i in range(3)]
    l2row = asb("l2row", [1, TA], F32)
    zh = asb("zh", [128, D], F32)
    y_sb = asb("y_sb", [128, 2 * D], F32)
    yqi = asb("yqi", [128, D // 2], I32)
    yqb = asb("yqb", [128, 2 * D], BF16)
    yqT_sb = asb("yqT_sb", [128, d8 * 128], BF16)
    bnst = asb("bnst", [128, nbn * 6], F32)
    smalls = asb("smalls", [128, 16], F32)
    ry_row = [asb(f"ry_row{i}", [1, TA], F32) for i in range(2)]

    last = {k: None for k in ("x0", "x1", "xb", "xT", "g", "pD", "pC",
                              "zh", "y", "yqi", "yqb", "yqT", "smalls_ac",
                              "pS0", "pS1", "g2_0", "g2_1", "g2_2", "g2_3",
                              "ry", "yqb_rdy", "qsc_ok", "rysm",
                              "ryb0", "ryb1")}
    last.update({f"pA{i}": None for i in range(4)})

    nmean = len(units_of(H, D)) + len(units_of(D, H)) + 1
    ntern = len(units_of(H, D)) + len(units_of(D, H))
    mtiles = max(1, min(4, nA // 4))
    wsteps_per_tile = []
    for i in range(nA):
        if i < mtiles:
            k = -(-nmean // mtiles)
        else:
            k = -(-ntern // (nA - mtiles))
        wsteps_per_tile.append(k)

    xloads = [O("sync", lambda s: s.dma_start(
        out=x_sb[0][:].rearrange("p (c d) -> p c d", c=2),
        in_=x_in[0:TA, :].rearrange("(c p) d -> p c d", p=128)), (), "qx0", 16)]

    def emit_head(i):
        xload = xloads[i]
        for c in range(2):
            O("vector", lambda v, c=c, xs=x_sb[i % 2]: v.tensor_copy(
                xb_sb[:], xs[:, c * D:(c + 1) * D]),
                W(xload, last["xb"], sc.cur("ve")), "ve")
            xb_ok = sc.cur("ve")
            for j in range(d8):
                O("tensor", lambda t, c=c, j=j: t.transpose(
                    pD[:, 128 * j:128 * (j + 1)],
                    xb_sb[:, 128 * j:128 * (j + 1)], identb[:]),
                    W(xb_ok, last["pD"], consts_ok) if j == 0 else (),
                    "pe" if j == d8 - 1 else None)
            O("vector", lambda v, c=c: v.tensor_copy(
                xT_sb[:].rearrange("p (j t) -> p j t", j=d8)[:, :, 128 * c:128 * (c + 1)],
                pD[:, 0:128 * d8].rearrange("p (j q) -> p j q", j=d8)),
                W(sc.cur("pe"), last["xT"], sc.cur("ve")), "ve")
            last["pD"] = sc.cur("ve")
            last["xb"] = sc.cur("pe")
        return sc.cur("ve")

    def emit_yq_lag(i):
        for c in range(2):
            for j in range(d8):
                O("tensor", lambda t, c=c, j=j: t.transpose(
                    pD[:, 128 * j:128 * (j + 1)],
                    yqb[:, c * D + 128 * j:c * D + 128 * (j + 1)], identb[:]),
                    W(last["yqb_rdy"], last["pD"]) if j == 0 else (),
                    "pe" if j == d8 - 1 else None)
            last["yqb"] = sc.cur("pe")
            O("vector", lambda v, c=c: v.tensor_copy(
                yqT_sb[:], pD[:, 0:128 * d8]),
                W(sc.cur("pe"), last["yqT"], sc.cur("ve")), "ve")
            last["pD"] = sc.cur("ve")
            O("sync", lambda s, i=i, c=c: s.dma_start(
                out=yqT_sc.rearrange("(j p) t -> p j t", p=128)[
                    :, :, TA * i + 128 * c:TA * i + 128 * (c + 1)],
                in_=yqT_sb[:].rearrange("p (j t) -> p j t", j=d8)),
                W(sc.cur("ve")), "qq", 16)
            last["yqT"] = sc.cur("qq")
        rb = i % 2
        for c in range(2):
            O("tensor", lambda t, c=c: t.transpose(
                pC[0:1, 256:384], smalls[:, 14 + c:15 + c], identf[:]),
                W(last["qsc_ok"], last["pC"], last[f"ryb{rb}"]), "pe")
            O("vector", lambda v, c=c, rb=rb: v.tensor_copy(
                ry_row[rb][0:1, 128 * c:128 * (c + 1)], pC[0:1, 256:384]),
                W(sc.cur("pe"), sc.cur("ve")), "ve")
            last["pC"] = sc.cur("ve")
        last["rysm"] = sc.cur("pe")
        O("sync", lambda s, i=i, rb=rb: s.dma_start(
            out=ry_sc[0:1, TA * i:TA * (i + 1)], in_=ry_row[rb][:]),
            W(sc.cur("ve")), f"qr{rb}", 16)
        last[f"ryb{rb}"] = sc.cur(f"qr{rb}")
        last["ry"] = sc.cur(f"qr{rb}")

    xT_ok = None
    for i in range(nA):
        sc.mark(f"tile{i}")
        wquota = [wsteps_per_tile[i] if i < len(wsteps_per_tile) else 0]
        xb_i = i % 2
        xload = xloads[i]

        if i + 1 < nA:
            xloads.append(O("sync", lambda s, i=i: s.dma_start(
                out=x_sb[(i + 1) % 2][:].rearrange("p (c d) -> p c d", c=2),
                in_=x_in[TA * (i + 1):TA * (i + 2), :]
                .rearrange("(c p) d -> p c d", p=128)),
                W(last[f"x{(i + 1) % 2}"]), f"qx{(i + 1) % 2}", 16))

        if i == 0:
            xT_ok = emit_head(0)

        ps_lohi = [(0, TA), (512, 512 + TA)]
        gelu_cnt = {}
        g2_cnt = {}

        def emit_ones(ss, i=i):
            O("tensor", lambda t, ss=ss: t.matmul(
                pC[0:1, 0:TA], ones_bf[:], g2_sb[ss % 3][:],
                start=(ss == 0), stop=(ss == s32 - 1)),
                W(g2_cnt[ss], last["pC"] if ss == 0 else None), "pe")
            last[f"g2_{ss % 3}"] = sc.cur("pe")

        for ss in range(s32):
            lo, hi = ps_lohi[ss % 2]
            w0 = [xT_ok, kt_ok]
            if ss >= 2:
                w0.append(gelu_cnt[ss - 2])
            else:
                w0.append(last[f"pS{ss}"])
            for j in range(d8):
                O("tensor", lambda t, ss=ss, j=j, lo=lo, hi=hi: t.matmul(
                    pB[:, lo:hi],
                    KT[:, j * S + 128 * ss: j * S + 128 * (ss + 1)],
                    xT_sb[:, j * TA:(j + 1) * TA],
                    start=(j == 0), stop=(j == d8 - 1)),
                    W(*w0) if j == 0 else (), "pe" if j == d8 - 1 else None)
            mm1_done = sc.cur("pe")
            if ss >= 3:
                emit_ones(ss - 3)
            O("scalar", lambda s, ss=ss, lo=lo, hi=hi: s.activation(
                g_sb[:, ss * TA:(ss + 1) * TA], pB[:, lo:hi], AF.Gelu),
                W(mm1_done), "ac")
            gelu_cnt[ss] = sc.cur("ac")
            O("vector", lambda v, ss=ss: v.tensor_mul(
                g2_sb[ss % 3][:], g_sb[:, ss * TA:(ss + 1) * TA],
                g_sb[:, ss * TA:(ss + 1) * TA]),
                W(gelu_cnt[ss], last[f"g2_{ss % 3}"], sc.cur("ve")), "ve")
            g2_cnt[ss] = sc.cur("ve")
            if ss % 3 == 2 and wquota[0] > 0 and wprep_steps:
                nper = -(-wquota[0] // max(1, (s32 - ss) // 3))
                for _ in range(nper):
                    if wprep_steps:
                        wprep_steps.pop(0)()
                        wquota[0] -= 1
        for sss in range(max(0, s32 - 3), s32):
            emit_ones(sss)
        last["pS0"] = gelu_cnt[s32 - 2]
        last["pS1"] = gelu_cnt[s32 - 1]
        l2mm_done = sc.cur("pe")

        last["xT"] = l2mm_done
        if i + 1 < nA:
            next_xT_ok = emit_head(i + 1)
        if i > 0:
            emit_yq_lag(i - 1)

        O("vector", lambda v: v.tensor_copy(l2row[:], pC[0:1, 0:TA]),
          W(l2mm_done, sc.cur("ve")), "ve")
        l2row_ok = sc.cur("ve")
        for c in range(2):
            O("tensor", lambda t, c=c: t.matmul(
                pC[:, 256 + c:257 + c], l2row[0:1, 128 * c:128 * (c + 1)], one11[:],
                start=True, stop=True),
                W(l2row_ok) if c == 0 else (), "pe" if c == 1 else None)
        O("vector", lambda v: v.tensor_copy(smalls[:, 0:2], pC[:, 256:258]),
          W(sc.cur("pe"), sc.cur("ve")), "ve")
        last["pC"] = sc.cur("ve")
        O("scalar", lambda s: s.activation(smalls[:, 0:2], smalls[:, 0:2], AF.Sqrt),
          W(sc.cur("ve"), last["smalls_ac"]), "ac")
        sqrt_ok = sc.cur("ac")
        O("vector", lambda v: v.reciprocal(smalls[:, 2:4], smalls[:, 0:2]),
          W(sqrt_ok, sc.cur("ve")), "ve")
        O("vector", lambda v: v.tensor_scalar(smalls[:, 2:4], smalls[:, 2:4],
                                              sqrtS, None, ALU.mult),
          W(sc.cur("ve")), "ve")
        rl2_ok = sc.cur("ve")

        for c in range(2):
            for n in range(ndh):
                gidx = c * ndh + n
                for k in range(s32):
                    w0 = W(gelu_cnt[k], v_ok, last[f"pA{gidx}"] if k == 0 else None)
                    O("tensor", lambda t, c=c, n=n, gidx=gidx, k=k: t.matmul(
                        pA[:, 512 * gidx:512 * gidx + dw],
                        g_sb[:, k * TA + 128 * c: k * TA + 128 * (c + 1)],
                        V_sb[:, k * D + dw * n: k * D + dw * (n + 1)],
                        start=(k == 0), stop=(k == s32 - 1)),
                        w0, "pe" if k == s32 - 1 else None)
                mm2_done = sc.cur("pe")
                O("vector", lambda v, c=c, n=n, gidx=gidx, xs=x_sb[xb_i]:
                  v.scalar_tensor_tensor(
                    zh[:, dw * n:dw * (n + 1)], pA[:, 512 * gidx:512 * gidx + dw],
                    smalls[:, 2 + c:3 + c],
                    xs[:, c * D + dw * n:c * D + dw * (n + 1)],
                    ALU.mult, ALU.add),
                    W(mm2_done, rl2_ok, xload,
                      last["zh"] if n == 0 else None, sc.cur("ve")), "ve")
                last[f"pA{gidx}"] = sc.cur("ve")
            z_ok = sc.cur("ve")
            for bb in range(nbn):
                O("vector", lambda v, bb=bb: v.bn_stats(
                    bnst[:, 6 * bb:6 * (bb + 1)],
                    zh[:, 512 * bb:512 * bb + min(512, D)]),
                    W(z_ok, sc.cur("ve")), "ve")
            O("vector", lambda v: v.bn_aggr(smalls[:, 4:6], bnst[:, 0:6 * nbn]),
              W(sc.cur("ve")), "ve")
            bn_ok = sc.cur("ve")
            O("scalar", lambda s: s.activation(smalls[:, 6:7], smalls[:, 5:6],
                                               AF.Sqrt, bias=epsc[:]),
              W(bn_ok, consts_ok), "ac")
            last["smalls_ac"] = sc.cur("ac")
            O("vector", lambda v: v.reciprocal(smalls[:, 8:9], smalls[:, 6:7]),
              W(sc.cur("ac"), sc.cur("ve")), "ve")
            O("vector", lambda v: v.tensor_mul(smalls[:, 10:11], smalls[:, 4:5],
                                               smalls[:, 8:9]),
              W(sc.cur("ve")), "ve")
            O("vector", lambda v, c=c: v.tensor_scalar(
                y_sb[:, c * D:(c + 1) * D], zh[:], smalls[:, 8:9], smalls[:, 10:11],
                ALU.mult, ALU.subtract),
                W(sc.cur("ve"), last["y"] if c == 0 else None), "ve")
            last["zh"] = sc.cur("ve")
            y_ok = sc.cur("ve")
            O("vector", lambda v, c=c: v.reduce_max(
                smalls[:, 12:13], y_sb[:, c * D:(c + 1) * D], axis=AX.X,
                apply_absolute_value=True), W(y_ok), "ve")
            O("vector", lambda v: v.tensor_scalar(smalls[:, 12:13], smalls[:, 12:13],
                                                  1e-5, None, ALU.max),
              W(sc.cur("ve")), "ve")
            O("vector", lambda v: v.reciprocal(smalls[:, 13:14], smalls[:, 12:13]),
              W(sc.cur("ve")), "ve")
            O("vector", lambda v: v.tensor_scalar(smalls[:, 13:14], smalls[:, 13:14],
                                                  127.0, None, ALU.mult),
              W(sc.cur("ve")), "ve")
            O("vector", lambda v, c=c: v.tensor_scalar(
                smalls[:, 14 + c:15 + c], smalls[:, 12:13], 1.0 / 127.0, None,
                ALU.mult),
                W(sc.cur("ve"), last["rysm"] if c == 0 else None), "ve")
            qsc_ok = sc.cur("ve")
            for hh in range(2):
                hw2 = D // 2
                O("vector", lambda v, c=c, hh=hh, hw2=hw2: v.tensor_scalar(
                    zh[:, hh * hw2:(hh + 1) * hw2],
                    y_sb[:, c * D + hh * hw2:c * D + (hh + 1) * hw2],
                    smalls[:, 13:14], MAGIC, ALU.mult, ALU.add),
                    W(qsc_ok, last["zh"], sc.cur("ve")), "ve")
                O("vector", lambda v, hh=hh, hw2=hw2: v.tensor_scalar(
                    yqi[:], zh[:, hh * hw2:(hh + 1) * hw2],
                    MAGIC, CQ, ALU.subtract, ALU.add),
                    W(last["yqi"], sc.cur("ve")), "ve")
                O("vector", lambda v, c=c, hh=hh, hw2=hw2: v.tensor_scalar(
                    yqb[:, c * D + hh * hw2:c * D + (hh + 1) * hw2],
                    yqi[:], -256.0, None, ALU.add),
                    W(last["yqb"] if (c == 0 and hh == 0) else None,
                      sc.cur("ve")), "ve")
                last["yqi"] = sc.cur("ve")
                last["zh"] = sc.cur("ve")
        last["g"] = sc.cur("pe")
        last["yqb_rdy"] = sc.cur("ve")
        last["qsc_ok"] = sc.cur("ve")

        O("sync", lambda s, i=i: s.dma_start(
            out=y_sc[TA * i:TA * (i + 1), :].rearrange("(c p) d -> p c d", p=128),
            in_=y_sb[:].rearrange("p (c d) -> p c d", c=2)),
            W(sc.cur("ve")), "qy", 16)
        last["y"] = sc.cur("qy")
        last[f"x{xb_i}"] = sc.cur("ve")
        if i + 1 < nA:
            xT_ok = next_xT_ok

    while wprep_steps:
        wprep_steps.pop(0)()
    emit_yq_lag(nA - 1)
    sc.mark("phaseA")
    pa.close()
    sc.barrier()

    W1T = sbuf("W1T", [128, d8 * H], BF16)
    W2T = sbuf("W2T", [128, h32 * D], BF16)
    yqTw = [sbuf(f"yqTw{i}", [128, d8 * (TW + 2)], BF16) for i in range(2)]
    y_win = sbuf("y_win", [128, D], F32)
    Srow_b = sbuf("Srow_b", [1, TW + 2], F32)
    Srow = [Srow_b, Srow_b]
    S_sb = [sbuf(f"S_sb{i}", [128, TW + 2], F32) for i in range(2)]
    htr = [sbuf(f"htr{i}", [128, TW + 2], F32) for i in range(3)]
    ca = [sbuf(f"ca{i}", [128, TW], F32) for i in range(2)]
    ca3 = [sbuf(f"ca3{i}", [128, TW], F32) for i in range(2)]
    hc_buf = sbuf("hc_buf", [128, h32 * TW], F32)
    mbuf = sbuf("mbuf", [128, TW], F32)
    mg = sbuf("mg", [128, 2 * TW], F32)
    i32tr = sbuf("i32tr", [128, 2 * TW], I32)
    hq_sb = sbuf("hq_sb", [128, h32 * TW], BF16)
    row2 = sbuf("row2", [1, TW], F32)
    S2_sb = sbuf("S2_sb", [128, TW], F32)
    bsml = sbuf("bsml", [128, 16], F32)
    bnst2 = sbuf("bnst2", [128, nbn * 6], F32)

    w1t_ok = O("sync", lambda s: s.dma_start(
        out=W1T[:].rearrange("p (j h) -> p j h", j=d8),
        in_=w1T_sc.rearrange("(j p) h -> p j h", p=128)), (), "qw1t", 16)
    w2t_ok = O("sync", lambda s: s.dma_start(
        out=W2T[:].rearrange("p (c d) -> p c d", c=h32),
        in_=w2T_sc.rearrange("(c p) d -> p c d", p=128)), (), "qw2t", 16)

    lastB = {k: None for k in ("pC", "htr0", "htr1", "htr2",
                               "htr0ac", "htr1ac", "htr2ac",
                               "ca0", "ca1", "ca30", "ca31",
                               "hc", "m8", "hq", "S2",
                               "row2", "ywin", "bsml_ac",
                               "Srow0", "Srow1", "S0", "S1", "pA2",
                               "mslot0", "mslot1", "mslot2", "mslot3")}
    lastB.update({f"pA{i}": None for i in range(4)})

    def emit_win_loads(w):
        q, wq = divmod(w, nWq)
        go = q * L + wq * TW
        lp = wq == 0
        rp = wq == nWq - 1
        c0 = go - (0 if lp else 1)
        c1 = min(go + TW + 1, q * L + L)
        ncols = c1 - c0
        dst0 = 1 if lp else 0
        wb = w % 2
        buf = yqTw[wb]
        ld = O("sync", lambda s, buf=buf, c0=c0, ncols=ncols, dst0=dst0: s.dma_start(
            out=buf[:].rearrange("p (j t) -> p j t", j=d8)[:, :, dst0:dst0 + ncols],
            in_=yqT_sc.rearrange("(j p) t -> p j t", p=128)[:, :, c0:c0 + ncols]),
            W(lastB.get(f"yqTw{wb}"), last["yqT"]), f"qt{wb}", 16)
        pads = []
        if lp:
            pads.append(O("vector", lambda v, buf=buf: v.memset(
                buf[:].rearrange("p (j t) -> p j t", j=d8)[:, :, 0:1], 0.0),
                W(lastB.get(f"yqTw{wb}"), ld, sc.cur("ve")), "ve"))
        if rp:
            pads.append(O("vector", lambda v, buf=buf: v.memset(
                buf[:].rearrange("p (j t) -> p j t", j=d8)[:, :, TW + 1:TW + 2], 0.0),
                W(lastB.get(f"yqTw{wb}"), ld, sc.cur("ve")), "ve"))
        return (ld, pads, go, lp, rp)

    def emit_sprep(w):
        wb = w % 2
        q, wq = divmod(w, nWq)
        go = q * L + wq * TW
        lp = wq == 0
        rp = wq == nWq - 1
        sr0 = go - (0 if lp else 1)
        srcols = min(go + TW + 1, q * L + L) - sr0
        sdst = 1 if lp else 0
        ryld = O("sync", lambda s, sr0=sr0, srcols=srcols, sdst=sdst, wb=wb:
                 s.dma_start(out=Srow[wb][0:1, sdst:sdst + srcols],
                             in_=ry_sc[0:1, sr0:sr0 + srcols]),
                 W(lastB["Srow0"], lastB["Srow1"], last["ry"]), f"qrl{wb}", 16)
        O("vector", lambda v, sr0=sr0, srcols=srcols, sdst=sdst, wb=wb:
          v.tensor_scalar(
            Srow[wb][0:1, sdst:sdst + srcols], Srow[wb][0:1, sdst:sdst + srcols],
            mwv[0:1, 0:1], None, ALU.mult),
            W(ryld, lastB[f"Srow{wb}"], wst["mw_ok"], sc.cur("ve")), "ve")
        if lp:
            O("vector", lambda v, wb=wb: v.memset(Srow[wb][0:1, 0:1], 0.0),
              W(sc.cur("ve")), "ve")
        if rp:
            O("vector", lambda v, wb=wb: v.memset(Srow[wb][0:1, TW + 1:TW + 2], 0.0),
              W(sc.cur("ve")), "ve")
        srow_ok = sc.cur("ve")
        O("tensor", lambda t, wb=wb: t.matmul(pC[:, 0:TW + 2], ones_f1[:],
                                              Srow[wb][:], start=True, stop=True),
          W(srow_ok, lastB["pC"]), "pe")
        O("vector", lambda v, wb=wb: v.tensor_copy(S_sb[wb][:], pC[:, 0:TW + 2]),
          W(sc.cur("pe"), lastB[f"S{wb}"], sc.cur("ve")), "ve")
        lastB["pC"] = sc.cur("ve")
        lastB[f"Srow{wb}"] = sc.cur("ve")
        return sc.cur("ve")

    win_loads = {0: emit_win_loads(0)}
    s_oks = {0: emit_sprep(0)}

    mslots = [(pB, 0), (pB, 512), (pA, 1536)]
    gsz = 16 if h32 % 16 == 0 else h32
    ng = h32 // gsz
    octets = [list(range(o, min(o + 8, h32))) for o in range(0, h32, 8)]
    st = {"hq_prev": None, "dbg_gate": None, "pend_silu": [], "i32ac": [None, None]}

    def emit_conv_chunk(w, hc):
        ld, pads, go, lp, rp = win_loads[w]
        s_ok = s_oks[w]
        s_i = hc % 3
        mp, lo = mslots[s_i]
        hi = lo + TW + 2

        def mm3(t, hc=hc, mp=mp, lo=lo, hi=hi, wb=w % 2):
            r = None
            for j in range(d8):
                r = t.matmul(
                    mp[:, lo:hi],
                    W1T[:, j * H + 128 * hc: j * H + 128 * (hc + 1)],
                    yqTw[wb][:, j * (TW + 2):(j + 1) * (TW + 2)],
                    start=(j == 0), stop=(j == d8 - 1))
            return r
        O("tensor", mm3,
          W(ld, *pads, w1t_ok, lastB[f"mslot{s_i}"]), "pe")
        mm3_done = sc.cur("pe")
        hb = hc % 3
        O("vector", lambda v, hb=hb, mp=mp, lo=lo, hi=hi, wb=w % 2:
          v.tensor_tensor(
            htr[hb][:], mp[:, lo:hi], S_sb[wb][:], ALU.mult),
            W(mm3_done, s_ok, lastB[f"htr{hb}"],
              lastB[f"htr{hb}ac"], sc.cur("ve")), "ve")
        lastB[f"mslot{s_i}"] = sc.cur("ve")
        htr_ok = sc.cur("ve")
        O("scalar", lambda s, hc=hc, hb=hb: s.activation(
            ca[hc % 2][:], htr[hb][:, 0:TW], AF.Identity,
            scale=cwb_sb[:, 4 * hc:4 * hc + 1],
            bias=cwb_sb[:, 4 * hc + 3:4 * hc + 4]),
            W(htr_ok, cwb_ok, lastB[f"ca{hc % 2}"]), "ac")
        c1_ok = sc.cur("ac")
        O("scalar", lambda s, hc=hc, hb=hb: s.activation(
            ca3[hc % 2][:], htr[hb][:, 2:TW + 2], AF.Identity,
            scale=cwb_sb[:, 4 * hc + 2:4 * hc + 3], bias=0.0),
            W(lastB[f"ca3{hc % 2}"], sc.cur("ac")), "ac")
        lastB[f"htr{hb}ac"] = sc.cur("ac")
        c3_ok = sc.cur("ac")
        O("vector", lambda v, hc=hc, hb=hb: v.scalar_tensor_tensor(
            ca[hc % 2][:], htr[hb][:, 1:TW + 1],
            cwb_sb[:, 4 * hc + 1:4 * hc + 2], ca[hc % 2][:],
            ALU.mult, ALU.add),
            W(c1_ok, sc.cur("ve")), "ve")
        lastB[f"htr{hb}"] = sc.cur("ve")
        c2_ok = sc.cur("ve")
        O("gpsimd", lambda g, hc=hc: g.tensor_tensor(
            hc_buf[:, hc * TW:(hc + 1) * TW], ca[hc % 2][:],
            ca3[hc % 2][:], ALU.add),
            W(c2_ok, c3_ok,
              st["hc_last"][hc] if st["hc_last"] is not None else None,
              st["dbg_gate"] if w == 1 else None,
              sc.cur("gp")), "gp")
        lastB[f"ca{hc % 2}"] = sc.cur("gp")
        lastB[f"ca3{hc % 2}"] = sc.cur("gp")
        if hc % gsz == gsz - 1:
            st["pend_silu"].append((w, hc // gsz))

    def emit_silu_group(w, gq):
        hc = gq * gsz + gsz - 1
        O("scalar", lambda s, hc=hc: s.activation(
            hc_buf[:, (hc - gsz + 1) * TW:(hc + 1) * TW],
            hc_buf[:, (hc - gsz + 1) * TW:(hc + 1) * TW], AF.Silu),
            W(sc.cur("gp"), sc.cur("ac")), "ac")
        O("vector", lambda v, hc=hc, gq=gq: v.reduce_max(
            mg[:, gq * TW:(gq + 1) * TW],
            hc_buf[:, (hc - gsz + 1) * TW:(hc + 1) * TW]
            .rearrange("p (c t) -> p t c", c=gsz),
            axis=AX.X, apply_absolute_value=True),
            W(sc.cur("ac"), sc.cur("ve")), "ve")

    def emit_finals(w):
        O("vector", lambda v: v.reduce_max(
            mbuf[:], mg[:, 0:ng * TW].rearrange("p (c t) -> p t c", c=ng),
            axis=AX.X), W(sc.cur("ve"), lastB["m8"]), "ve")
        mbuf_ok = sc.cur("ve")
        for c in range(2):
            O("tensor", lambda t, c=c: t.transpose(
                pC[:, 128 * c:128 * (c + 1)], mbuf[:, 128 * c:128 * (c + 1)],
                identf[:]),
                W(mbuf_ok, lastB["pC"]) if c == 0 else (),
                "pe" if c == 1 else None)
        lastB["m8"] = sc.cur("pe")
        O("vector", lambda v: v.reduce_max(
            bsml[:, 0:2], pC[:, 0:256].rearrange("p (c t) -> p c t", c=2),
            axis=AX.X), W(sc.cur("pe"), lastB["bsml_ac"], sc.cur("ve")), "ve")
        lastB["pC"] = sc.cur("ve")
        O("vector", lambda v: v.tensor_scalar(bsml[:, 0:2], bsml[:, 0:2],
                                              1e-5, None, ALU.max),
          W(sc.cur("ve")), "ve")
        O("vector", lambda v: v.tensor_scalar(bsml[:, 2:4], bsml[:, 0:2],
                                              m2s_col[:], None, ALU.mult),
          W(sc.cur("ve"), wst["wq_col_ok"]), "ve")
        rh2_ok = sc.cur("ve")
        O("vector", lambda v: v.reciprocal(bsml[:, 4:6], bsml[:, 0:2]),
          W(sc.cur("ve")), "ve")
        O("vector", lambda v: v.tensor_scalar(bsml[:, 4:6], bsml[:, 4:6],
                                              127.0, None, ALU.mult),
          W(sc.cur("ve")), "ve")
        sc2_ok = sc.cur("ve")
        for c in range(2):
            O("tensor", lambda t, c=c: t.transpose(
                pC[0:1, 256:384], bsml[:, 4 + c:5 + c], identf[:]),
                W(sc2_ok, lastB["pC"]), "pe")
            O("vector", lambda v, c=c: v.tensor_copy(
                row2[0:1, 128 * c:128 * (c + 1)], pC[0:1, 256:384]),
                W(sc.cur("pe"), lastB["row2"] if c == 0 else None,
                  sc.cur("ve")), "ve")
            lastB["pC"] = sc.cur("ve")
        O("tensor", lambda t: t.matmul(pC[:, 0:TW], ones_f1[:], row2[:],
                                       start=True, stop=True),
          W(sc.cur("ve"), lastB["pC"]), "pe")
        lastB["row2"] = sc.cur("pe")
        O("vector", lambda v: v.tensor_copy(S2_sb[:], pC[:, 0:TW]),
          W(sc.cur("pe"), lastB["S2"], sc.cur("ve")), "ve")
        lastB["pC"] = sc.cur("ve")
        s2_ok = sc.cur("ve")
        dbg_ok = None
        if dbg and w == 0:
            O("sync", lambda s: s.dma_start(out=dbg_m[:, 0:TW], in_=mbuf[:]),
              W(s2_ok), "qc", 16)
            O("sync", lambda s: s.dma_start(out=dbg_h[:, 0:h32 * TW], in_=hc_buf[:]),
              W(s2_ok), "qc", 16)
            dbg_ok = sc.cur("qc")
            st["dbg_gate"] = dbg_ok
        return s2_ok, rh2_ok, dbg_ok

    def emit_quant_chunk(w, k, s2_ok, dbg_ok, hq_cnt, hc_last):
        sl = slice(k * TW, (k + 1) * TW)
        i2 = slice((k % 2) * TW, (k % 2) * TW + TW)
        O("vector", lambda v, sl=sl: v.tensor_tensor(
            hc_buf[:, sl], hc_buf[:, sl], S2_sb[:], ALU.mult),
            W(s2_ok, dbg_ok if k == 0 else None, sc.cur("ve")), "ve")
        O("vector", lambda v, sl=sl: v.tensor_scalar(
            hc_buf[:, sl], hc_buf[:, sl], MAGIC, MAGIC, ALU.add, ALU.subtract),
            W(sc.cur("ve")), "ve")
        O("vector", lambda v, sl=sl, i2=i2: v.tensor_scalar(
            i32tr[:, i2], hc_buf[:, sl], CQ, None, ALU.add),
            W(st["i32ac"][k % 2], sc.cur("ve")), "ve")
        hc_last[k] = sc.cur("ve")
        if k % 2 == 1:
            O("scalar", lambda s, k=k: s.activation(
                hq_sb[:, (k - 1) * TW:(k + 1) * TW], i32tr[:],
                AF.Identity, bias=m256c[:]),
                W(lastB["hq"] if k == 1 else None, consts_ok,
                  sc.cur("ve"), sc.cur("ac")), "ac")
            st["i32ac"][0] = sc.cur("ac")
            st["i32ac"][1] = sc.cur("ac")
            hq_cnt[k] = sc.cur("ac")

    def emit_mm4_octet(w, ks, hq_cnt, mm4_done):
        for k in ks:
            for n in range(ndh):
                w0 = W(hq_cnt[ks[-1]] if (k == ks[0] and n == 0) else None,
                       w2t_ok if k == 0 and n == 0 else None,
                       lastB[f"pA{n}"] if k == 0 else None)
                is_stop = k == h32 - 1
                O("tensor", lambda t, n=n, k=k, is_stop=is_stop: t.matmul(
                    pA[:, 512 * n:512 * n + dw],
                    hq_sb[:, k * TW: k * TW + 128],
                    W2T[:, k * D + dw * n: k * D + dw * (n + 1)],
                    start=(k == 0), stop=is_stop),
                    w0, "pe" if is_stop else None)
                if is_stop:
                    mm4_done[(0, n)] = sc.cur("pe")
        for k in ks:
            w0 = W(lastB["pA2"] if k == 0 else None)
            is_stop = k == h32 - 1
            O("tensor", lambda t, k=k, is_stop=is_stop: t.matmul(
                pA[:, 1024:1024 + dw],
                hq_sb[:, k * TW + 128: k * TW + 256],
                W2T[:, k * D: k * D + dw],
                start=(k == 0), stop=is_stop),
                w0, "pe" if is_stop else None)
            if is_stop:
                mm4_done[(1, 0)] = sc.cur("pe")

    def emit_mm4_c1n1(w, mm4_done):
        if ndh < 2:
            return
        for k in range(h32):
            w0 = W(lastB["pC"] if k == 0 else None)
            is_stop = k == h32 - 1
            O("tensor", lambda t, k=k, is_stop=is_stop: t.matmul(
                pC[:, 0:dw],
                hq_sb[:, k * TW + 128: k * TW + 256],
                W2T[:, k * D + dw: k * D + 2 * dw],
                start=(k == 0), stop=is_stop),
                w0, "pe" if is_stop else None)
            if is_stop:
                mm4_done[(1, 1)] = sc.cur("pe")

    def emit_ln2_store(w, c):
        for bb in range(nbn):
            O("vector", lambda v, bb=bb: v.bn_stats(
                bnst2[:, 6 * bb:6 * (bb + 1)],
                y_win[:, 512 * bb:512 * bb + min(512, D)]),
                W(sc.cur("ve")), "ve")
        O("vector", lambda v: v.bn_aggr(bsml[:, 4:6], bnst2[:, 0:6 * nbn]),
          W(sc.cur("ve")), "ve")
        O("scalar", lambda s: s.activation(bsml[:, 6:7], bsml[:, 5:6],
                                           AF.Sqrt, bias=epsc[:]),
          W(sc.cur("ve"), consts_ok, sc.cur("ac")), "ac")
        lastB["bsml_ac"] = sc.cur("ac")
        O("vector", lambda v: v.reciprocal(bsml[:, 8:9], bsml[:, 6:7]),
          W(sc.cur("ac"), sc.cur("ve")), "ve")
        O("vector", lambda v: v.tensor_mul(bsml[:, 10:11], bsml[:, 4:5],
                                           bsml[:, 8:9]),
          W(sc.cur("ve")), "ve")
        O("vector", lambda v: v.tensor_scalar(
            y_win[:], y_win[:],
            bsml[:, 8:9], bsml[:, 10:11], ALU.mult, ALU.subtract),
            W(sc.cur("ve")), "ve")
        out_ok = sc.cur("ve")
        O("sync", lambda s, w=w, c=c: s.dma_start(
            out=out_hbm[TW * w + 128 * c:TW * w + 128 * (c + 1), :],
            in_=y_win[:]),
            W(out_ok), "qo", 16)
        lastB["ywin"] = sc.cur("qo")

    def emit_tail_c0(w, yld, rh2_ok, mm4_done):
        for n in range(ndh):
            O("vector", lambda v, n=n: v.scalar_tensor_tensor(
                y_win[:, dw * n:dw * (n + 1)],
                pA[:, 512 * n:512 * n + dw],
                bsml[:, 2:3],
                y_win[:, dw * n:dw * (n + 1)],
                ALU.mult, ALU.add),
                W(mm4_done[(0, n)], rh2_ok, yld, sc.cur("ve")), "ve")
            lastB[f"pA{n}"] = sc.cur("ve")
        emit_ln2_store(w, 0)

    def emit_tail_c1(w, yld, rh2_ok, mm4_done):
        O("vector", lambda v: v.scalar_tensor_tensor(
            y_win[:, 0:dw], pA[:, 1024:1024 + dw], bsml[:, 3:4],
            y_win[:, 0:dw], ALU.mult, ALU.add),
            W(mm4_done[(1, 0)], rh2_ok, yld, sc.cur("ve")), "ve")
        lastB["pA2"] = sc.cur("ve")
        if ndh >= 2:
            O("vector", lambda v: v.scalar_tensor_tensor(
                y_win[:, dw:2 * dw], pC[:, 0:dw], bsml[:, 3:4],
                y_win[:, dw:2 * dw], ALU.mult, ALU.add),
                W(mm4_done[(1, 1)], rh2_ok, yld, sc.cur("ve")), "ve")
            lastB["pC"] = sc.cur("ve")
        emit_ln2_store(w, 1)

    def flush_silu():
        for (wc, gq) in st["pend_silu"]:
            emit_silu_group(wc, gq)
        st["pend_silu"] = []

    st["hc_last"] = None
    for hc in range(h32):
        emit_conv_chunk(0, hc)
    flush_silu()
    lastB["S0"] = sc.cur("ve")
    lastB["yqTw0"] = sc.cur("pe")
    if nW > 1:
        s_oks[1] = emit_sprep(1)
        win_loads[1] = emit_win_loads(1)
    sc.mark("w_head")

    for w in range(nW):
        sc.mark(f"win{w}")
        s2_ok, rh2_ok, dbg_ok = emit_finals(w)
        hq_cnt = {}
        hc_last = {}
        st["hc_last"] = hc_last
        yld = O("sync", lambda s, w=w: s.dma_start(
            out=y_win[:],
            in_=y_sc[TW * w:TW * w + 128, :]),
            W(lastB["ywin"], last["y"]), "qyl", 16)
        mm4_done = {}
        for oi, ks in enumerate(octets):
            flush_silu()
            if oi > 0 and w + 1 < nW:
                for hc in octets[oi - 1]:
                    emit_conv_chunk(w + 1, hc)
            for k in ks:
                emit_quant_chunk(w, k, s2_ok, dbg_ok, hq_cnt, hc_last)
            if oi > 0:
                emit_mm4_octet(w, octets[oi - 1], hq_cnt, mm4_done)
        flush_silu()
        if w + 1 < nW:
            for hc in octets[-1]:
                emit_conv_chunk(w + 1, hc)
        emit_mm4_octet(w, octets[-1], hq_cnt, mm4_done)
        if w + 1 < nW:
            lastB[f"S{(w + 1) % 2}"] = sc.cur("ve")
            lastB[f"yqTw{(w + 1) % 2}"] = sc.cur("pe")
        emit_mm4_c1n1(w, mm4_done)
        emit_tail_c0(w, yld, rh2_ok, mm4_done)
        yld = O("sync", lambda s, w=w: s.dma_start(
            out=y_win[:],
            in_=y_sc[TW * w + 128:TW * w + 256, :]),
            W(lastB["ywin"], last["y"]), "qyl", 16)
        emit_tail_c1(w, yld, rh2_ok, mm4_done)
        lastB["hq"] = sc.cur("pe")
        lastB["S2"] = sc.cur("ve")
        flush_silu()
        if w + 2 < nW:
            s_oks[w + 2] = emit_sprep(w + 2)
            win_loads[w + 2] = emit_win_loads(w + 2)
        win_loads.pop(w)
        s_oks.pop(w)

    sc.streams["sync"].append((W(sc.cur("qo"), sc.cur("qy"), sc.cur("qq")), None, None, sc.seq))
    sc.streams["gpsimd"].append((W(sc.cur("qk0"), sc.cur("qk1"), sc.cur("qv")), None, None, sc.seq))
    sc.seq += 1

    with nc.Block() as block:
        sc.emit(block, sems)

    outer.close()
    return nc


_NC_CACHE = {}


def get_program(key=(1024, 4096, 4096, 2048, 2)):
    if key not in _NC_CACHE:
        _NC_CACHE[key] = build_core(*key)
    return _NC_CACHE[key]


def kernel(L0_attn_inputs, L0_attn_key_param_tokens, L0_attn_value_param_tokens,
           L0_ffn_w1, L0_ffn_conv_w, L0_ffn_conv_b, L0_ffn_w2):
    x = np.ascontiguousarray(np.asarray(L0_attn_inputs, dtype=np.float32))
    k = np.ascontiguousarray(np.asarray(L0_attn_key_param_tokens, dtype=np.float32))
    v = np.ascontiguousarray(np.asarray(L0_attn_value_param_tokens, dtype=np.float32))
    w1 = np.ascontiguousarray(np.asarray(L0_ffn_w1, dtype=np.float32))
    w2 = np.ascontiguousarray(np.asarray(L0_ffn_w2, dtype=np.float32))
    cw = np.asarray(L0_ffn_conv_w, dtype=np.float32).reshape(-1, 3)
    cb = np.asarray(L0_ffn_conv_b, dtype=np.float32).reshape(-1, 1)
    cwb = np.ascontiguousarray(np.concatenate([cw, cb], axis=1))

    B, L, D = x.shape
    ncores = 8
    bc = B // ncores
    nc_prog = get_program((D, k.shape[0], w1.shape[0], L, bc))
    in_maps = []
    for c in range(ncores):
        in_maps.append({
            "x": np.ascontiguousarray(x[bc * c:bc * (c + 1)].reshape(bc * L, D)),
            "k": k, "v": v, "w1": w1, "w2": w2, "cwb": cwb,
        })
    res = run_bass_kernel_spmd(nc_prog, in_maps, core_ids=list(range(ncores)))
    out = np.concatenate(
        [r["out"].reshape(bc, L, D) for r in res.results], axis=0)
    return out


# version 27
# speedup vs baseline: 1.0448x; 1.0185x over previous
import contextlib

import numpy as np

import concourse.bass as bass
import concourse.mybir as mybir
from concourse.ap import AP
from concourse.bass_utils import run_bass_kernel_spmd

F32 = mybir.dt.float32
BF16 = mybir.dt.bfloat16
I32 = mybir.dt.int32
AF = mybir.ActivationFunctionType
ALU = mybir.AluOpType
AX = mybir.AxisListType

ENGINES = ("sync", "gpsimd", "vector", "scalar", "tensor")

MAGIC = 12582912.0
CQ = 256.49996948242188


def W(*items):
    out = []
    for it in items:
        if it is None:
            continue
        if isinstance(it, tuple) and it and all(
                sub is None or isinstance(sub, tuple) for sub in it):
            for sub in it:
                if sub is not None and sub[1] > 0:
                    out.append(sub)
            continue
        s, v = it
        if v > 0:
            out.append((s, v))
    return tuple(out)


TRUNCATE_AT = None


class Sched:
    def __init__(self):
        self.streams = {e: [] for e in ENGINES}
        self.cnt = {}
        self.seq = 0
        self.sections = {}

    def mark(self, name):
        self.sections[name] = (self.seq, dict(self.cnt))

    def n(self, sem):
        return self.cnt.get(sem, 0)

    def cur(self, sem):
        return (sem, self.cnt.get(sem, 0))

    def op(self, eng, emit, waits=(), inc=None, inc_n=1):
        if inc is not None:
            self.cnt[inc] = self.cnt.get(inc, 0) + inc_n
        co = {}
        for s, v in waits:
            if co.get(s, -1) < v:
                co[s] = v
        self.streams[eng].append(
            (tuple(co.items()), emit, (inc, inc_n) if inc else None, self.seq))
        self.seq += 1
        return (inc, self.cnt[inc]) if inc else None

    def barrier(self):
        f = W(*[(s, v) for s, v in self.cnt.items()])
        for e in ENGINES:
            self.streams[e].append((f, None, None, self.seq))
        self.seq += 1

    def emit(self, block, sems):
        cutoff, fence = None, None
        if TRUNCATE_AT is not None:
            cutoff, fcnt = self.sections[TRUNCATE_AT]
            fence = W(*[(s, v) for s, v in fcnt.items()])

        def run(eng_obj, stream):
            waited = {}
            for waits, emitf, inc, seq in stream:
                if cutoff is not None and seq >= cutoff:
                    continue
                for s, v in waits:
                    if waited.get(s, -1) < v:
                        eng_obj.wait_ge(sems[s], v)
                        waited[s] = v
                if emitf is not None:
                    instr = emitf(eng_obj)
                    if inc is not None:
                        instr.then_inc(sems[inc[0]], inc[1])
            if fence is not None:
                for s, v in fence:
                    if waited.get(s, -1) < v:
                        eng_obj.wait_ge(sems[s], v)

        for name in ENGINES:
            getattr(block, name)((lambda st: (lambda e: run(e, st)))(self.streams[name]))


def build_core(D=1024, S=4096, H=4096, L=2048, Bc=2, dbg=False):
    TA = 256
    TW = 256
    T = Bc * L
    d8, s32, h32 = D // 128, S // 128, H // 128
    nA = T // TA
    nW = T // TW
    nWq = L // TW
    WM = max(D, H)
    sqrtS = float(np.float32(np.sqrt(S)))
    ndh = max(1, D // 512)
    dw = D // ndh
    nbn = max(1, D // 512)
    nqg = 1
    assert TA == 256 and TW == 256 and L % TW == 0 and T % TA == 0
    assert (H * D) & (H * D - 1) == 0
    assert h32 % nqg == 0

    nc = bass.Bass("TRN2", target_bir_lowering=False, debug=False)

    x_in = nc.dram_tensor("x", [T, D], F32, kind="ExternalInput").ap()
    k_in = nc.dram_tensor("k", [S, D], F32, kind="ExternalInput").ap()
    v_in = nc.dram_tensor("v", [S, D], F32, kind="ExternalInput").ap()
    w1_in = nc.dram_tensor("w1", [H, D], F32, kind="ExternalInput").ap()
    w2_in = nc.dram_tensor("w2", [D, H], F32, kind="ExternalInput").ap()
    cwb_in = nc.dram_tensor("cwb", [H, 4], F32, kind="ExternalInput").ap()
    out_hbm = nc.dram_tensor("out", [T, D], F32, kind="ExternalOutput").ap()
    ikind = "ExternalOutput" if dbg else "Internal"
    y_sc = nc.dram_tensor("y_sc", [T, D], F32, kind=ikind).ap()
    yqT_sc = nc.dram_tensor("yqT_sc", [D, T], BF16, kind=ikind).ap()
    w1T_sc = nc.dram_tensor("w1T_sc", [D, H], BF16, kind=ikind).ap()
    w2T_sc = nc.dram_tensor("w2T_sc", [H, D], BF16, kind=ikind).ap()
    ry_sc = nc.dram_tensor("ry_sc", [1, T], F32, kind=ikind).ap()
    if dbg:
        dbg_m = nc.dram_tensor("dbg_m", [128, 512], F32, kind="ExternalOutput").ap()
        dbg_r = nc.dram_tensor("dbg_r", [4, 512], F32, kind="ExternalOutput").ap()
        dbg_h = nc.dram_tensor("dbg_h", [128, 8192], F32, kind="ExternalOutput").ap()

    outer = contextlib.ExitStack()
    sbuf = lambda name, shape, dt=F32: outer.enter_context(nc.sbuf_tensor(name, shape, dt))
    sems = {s: outer.enter_context(nc.semaphore(s))
            for s in ("pe", "ve", "ac", "gp", "qc", "qw0", "qw1", "qs",
                      "qk0", "qk1", "qv", "qx0", "qx1", "qy", "qq",
                      "qw1t", "qw2t", "qt0", "qt1", "qyl", "qo",
                      "qr0", "qr1", "qrl0", "qrl1",
                      "qg0", "qg1", "qg2", "qg3")}
    sc = Sched()
    O = sc.op

    pA = outer.enter_context(nc.psum_tensor("pA", [128, 2048], F32))
    pB = outer.enter_context(nc.psum_tensor("pB", [128, 1024], F32))
    pC = outer.enter_context(nc.psum_tensor("pC", [128, 512], F32))
    pD = outer.enter_context(nc.psum_tensor("pD", [128, 1024], BF16))

    identb = sbuf("identb", [128, 128], BF16)
    identf = sbuf("identf", [128, 128], F32)
    ones_f1 = sbuf("ones_f1", [1, 128], F32)
    one11 = sbuf("one11", [1, 1], F32)
    ones_bf = sbuf("ones_bf", [128, 1], BF16)
    b256 = sbuf("b256", [128, 1], F32)
    cqc = sbuf("cqc", [128, 1], F32)
    m256c = sbuf("m256c", [128, 1], F32)
    epsc = sbuf("epsc", [128, 1], F32)
    cwb_sb = sbuf("cwb_sb", [128, h32 * 4], F32)
    mrow_s = sbuf("mrow_s", [1, 2], F32)
    mcl = sbuf("mcl", [1, 2], F32)
    wsc = sbuf("wsc", [1, 2], F32)
    mwv = sbuf("mwv", [1, 2], F32)
    m2s = sbuf("m2s", [1, 1], F32)
    wq_col = sbuf("wq_col", [128, 2], F32)
    m2s_col = sbuf("m2s_col", [128, 1], F32)

    def ident_init(g, buf):
        return g.affine_select(out=buf[:], in_=buf[:], compare_op=ALU.not_equal,
                               fill=1.0, base=0, pattern=[[-1, 128]],
                               channel_multiplier=1)

    O("gpsimd", lambda g: g.memset(identb[:], 0.0), (), "gp")
    O("gpsimd", lambda g: ident_init(g, identb), W(sc.cur("gp")), "gp")
    O("gpsimd", lambda g: g.memset(identf[:], 0.0), (), "gp")
    O("gpsimd", lambda g: ident_init(g, identf), W(sc.cur("gp")), "gp")
    for buf, val in ((ones_f1, 1.0), (one11, 1.0), (ones_bf, 1.0),
                     (b256, 256.5), (epsc, 1e-5), (cqc, CQ), (m256c, -256.0)):
        O("gpsimd", lambda g, b=buf, v=val: g.memset(b[:], v), (), "gp")
    consts_ok = sc.cur("gp")

    O("sync", lambda s: s.dma_start(
        out=cwb_sb[:].rearrange("p (c k) -> p c k", c=h32),
        in_=cwb_in.rearrange("(c p) k -> p c k", p=128)), (), "qc", 16)
    cwb_ok = sc.cur("qc")
    sc.mark("consts")

    pa = contextlib.ExitStack()
    asb = lambda name, shape, dt=F32: pa.enter_context(nc.sbuf_tensor(name, shape, dt))
    KT = asb("KT", [128, d8 * S], BF16)
    V_sb = asb("V_sb", [128, s32 * D], BF16)
    kbts = [asb(f"kbt{i}", [128, D], BF16) for i in range(2)]
    buf_free = [None, None]
    for c in range(S // 128):
        b = c % 2
        ld = O("gpsimd", lambda g, c=c, b=b: g.dma_start(
            out=kbts[b][:], in_=k_in[128 * c:128 * (c + 1), :]),
            W(buf_free[b]), f"qk{b}", 16)
        for g0 in range(0, d8, 8):
            gn = min(8, d8 - g0)
            for jj in range(gn):
                j = g0 + jj
                O("tensor", lambda t, j=j, jj=jj, b=b: t.transpose(
                    pD[:, 128 * jj:128 * (jj + 1)],
                    kbts[b][:, 128 * j:128 * (j + 1)], identb[:]),
                    W(ld, sc.cur("ve"), consts_ok) if jj == 0 else (),
                    "pe" if jj == gn - 1 else None)
            O("vector", lambda v, c=c, g0=g0, gn=gn: v.tensor_copy(
                KT[:].rearrange("p (j s) -> p j s", j=d8)[:, g0:g0 + gn,
                                                          128 * c:128 * (c + 1)],
                pD[:, 0:128 * gn].rearrange("p (j q) -> p j q", j=gn)),
                W(sc.cur("pe")), "ve")
        buf_free[b] = sc.cur("pe")
    kt_ok = sc.cur("ve")
    sc.mark("ktv")
    v_ok = O("gpsimd", lambda g: g.dma_start(
        out=V_sb[:].rearrange("p (c d) -> p c d", c=s32),
        in_=v_in.rearrange("(c p) d -> p c d", p=128)), (), "qv", 16)

    UW = min(512, min(D, H))
    wtmp = [asb(f"wtmp{i}", [128, UW], F32) for i in range(3)]
    wred = asb("wred", [128, 2], F32)
    wacc = asb("wacc", [128, 2], F32)
    wi32 = asb("wi32", [128, UW], I32)
    wqb = [asb(f"wqb{i}", [128, UW], BF16) for i in range(2)]
    wstg = asb("wstg", [128, (UW // 128) * 128], BF16)

    O("vector", lambda v: v.memset(wacc[:], 0.0), (), "ve")

    wst = {"buf_free": [None] * 3, "stg_free": None, "n": 0, "nq": 0, "np": 0,
           "wq_col_ok": None, "mw_ok": None, "wqb_free": [None, None],
           "wqb_ok": [None, None], "loads": [], "ld_next": 0, "ld_h": {}}

    def issue_wload():
        u = wst["ld_next"]
        if u >= len(wst["loads"]):
            return
        wst["ld_next"] += 1
        m, r, cc = wst["loads"][u]
        b = u % 3
        wst["ld_h"][u] = O("gpsimd", lambda g, m=m, r=r, cc=cc, b=b: g.dma_start(
            out=wtmp[b][:], in_=m[128 * r:128 * (r + 1), UW * cc:UW * (cc + 1)]),
            W(wst["buf_free"][b]), f"qg{b}", 16)

    def units_of(rows, width):
        return [(r, cc) for r in range(rows // 128) for cc in range(width // UW)]

    def mean_unit(m_in, r, cc, col):
        if wst["n"] == 0:
            for _ in range(3):
                issue_wload()
        u = wst["n"]
        b = u % 3
        wst["n"] += 1
        O("vector", lambda v, b=b: v.reduce_sum(
            wred[:, 0:1], wtmp[b][:], axis=AX.X, apply_absolute_value=True),
            W(wst["ld_h"].pop(u), sc.cur("ve")), "ve")
        wst["buf_free"][b] = sc.cur("ve")
        O("vector", lambda v, col=col: v.tensor_tensor(
            wacc[:, col:col + 1], wacc[:, col:col + 1], wred[:, 0:1], ALU.add),
            W(sc.cur("ve")), "ve")
        issue_wload()

    def mean_finalize():
        O("gpsimd", lambda g: g.tensor_reduce(mrow_s[:], wacc[:], axis=AX.C,
                                              op=ALU.add),
          W(sc.cur("ve"), consts_ok), "gp")
        O("vector", lambda v: v.tensor_scalar(mcl[:], mrow_s[:], 1.0 / (H * D), 1e-5,
                                              ALU.mult, ALU.max),
          W(sc.cur("gp")), "ve")
        O("vector", lambda v: v.reciprocal(wsc[:], mcl[:]), W(sc.cur("ve")), "ve")
        O("vector", lambda v: v.reciprocal(mwv[:], wsc[:]), W(sc.cur("ve")), "ve")
        O("vector", lambda v: v.tensor_scalar(m2s[:], mwv[0:1, 1:2], 1.0 / 127.0,
                                              None, ALU.mult),
          W(sc.cur("ve")), "ve")
        wst["mw_ok"] = sc.cur("ve")
        O("tensor", lambda t: t.matmul(pA[:, 0:2], ones_f1[:], wsc[:],
                                       start=True, stop=True),
          W(wst["mw_ok"], consts_ok, last["pA0"]), "pe")
        O("tensor", lambda t: t.matmul(pA[:, 2:3], ones_f1[:], m2s[:],
                                       start=True, stop=True), (), "pe")
        O("vector", lambda v: v.tensor_copy(wq_col[:], pA[:, 0:2]),
          W(sc.cur("pe"), sc.cur("ve")), "ve")
        O("vector", lambda v: v.tensor_copy(m2s_col[:], pA[:, 2:3]),
          W(sc.cur("ve")), "ve")
        last["pA0"] = sc.cur("ve")
        wst["wq_col_ok"] = sc.cur("ve")

    def tern_unit_dve(m_in, r, cc, col):
        u = wst["n"]
        b = u % 3
        qb = wst["nq"] % 2
        wst["n"] += 1
        wst["nq"] += 1
        O("vector", lambda v, b=b, col=col: v.tensor_scalar(
            wtmp[b][:], wtmp[b][:], wq_col[:, col:col + 1], 1.0, ALU.mult, ALU.min),
            W(wst["ld_h"].pop(u), wst["wq_col_ok"], sc.cur("ve")), "ve")
        O("vector", lambda v, b=b: v.tensor_scalar(
            wtmp[b][:], wtmp[b][:], -1.0, MAGIC, ALU.max, ALU.add),
            W(sc.cur("ve")), "ve")
        O("vector", lambda v, b=b: v.tensor_scalar(
            wi32[:], wtmp[b][:], MAGIC, CQ, ALU.subtract, ALU.add),
            W(sc.cur("ve")), "ve")
        wst["buf_free"][b] = sc.cur("ve")
        issue_wload()
        O("vector", lambda v, qb=qb: v.tensor_scalar(
            wqb[qb][:], wi32[:], -256.0, None, ALU.add),
            W(sc.cur("ve"), wst["wqb_free"][qb]), "ve")
        wst["wqb_ok"][qb] = sc.cur("ve")

    def tern_unit_pe(m_sc, r, cc):
        qb = wst["np"] % 2
        wst["np"] += 1
        nj = UW // 128
        for jj in range(nj):
            O("tensor", lambda t, jj=jj, qb=qb: t.transpose(
                pD[:, 128 * jj:128 * (jj + 1)],
                wqb[qb][:, 128 * jj:128 * (jj + 1)], identb[:]),
                W(wst["wqb_ok"][qb], last["pD"], consts_ok) if jj == 0 else (),
                "pe" if jj == nj - 1 else None)
        wst["wqb_free"][qb] = sc.cur("pe")
        O("vector", lambda v, nj=nj: v.tensor_copy(
            wstg[:, 0:128 * nj], pD[:, 0:128 * nj]),
            W(sc.cur("pe"), wst["stg_free"], sc.cur("ve")), "ve")
        last["pD"] = sc.cur("ve")
        wst["stg_free"] = O("sync", lambda s, m_sc=m_sc, r=r, cc=cc, nj=nj:
                            s.dma_start(
            out=m_sc.rearrange("(j p) r -> p j r", p=128)[
                :, (UW * cc) // 128:(UW * cc) // 128 + nj,
                128 * r:128 * (r + 1)],
            in_=wstg[:, 0:128 * nj].rearrange("p (j q) -> p j q", j=nj)),
            W(sc.cur("ve")), "qs", 16)

    wprep_steps = []
    for r, cc in units_of(H, D):
        wst["loads"].append((w1_in, r, cc))
    for r, cc in units_of(D, H):
        wst["loads"].append((w2_in, r, cc))
    for r, cc in units_of(H, D):
        wst["loads"].append((w1_in, r, cc))
    for r, cc in units_of(D, H):
        wst["loads"].append((w2_in, r, cc))
    for r, cc in units_of(H, D):
        wprep_steps.append(lambda r=r, cc=cc: mean_unit(w1_in, r, cc, 0))
    for r, cc in units_of(D, H):
        wprep_steps.append(lambda r=r, cc=cc: mean_unit(w2_in, r, cc, 1))
    wprep_steps.append(mean_finalize)
    tern_list = ([(w1_in, w1T_sc, r, cc, 0) for r, cc in units_of(H, D)]
                 + [(w2_in, w2T_sc, r, cc, 1) for r, cc in units_of(D, H)])
    for idx, (tm_in, tm_sc, tr, tcc, tcol) in enumerate(tern_list):
        def step(idx=idx, m_in=tm_in, m_sc=tm_sc, r=tr, cc=tcc, col=tcol):
            if idx > 0:
                pm = tern_list[idx - 1]
                tern_unit_pe(pm[1], pm[2], pm[3])
            tern_unit_dve(m_in, r, cc, col)
            if idx == len(tern_list) - 1:
                tern_unit_pe(m_sc, r, cc)
        wprep_steps.append(step)

    x_sb = [asb(f"x_sb{i}", [128, 2 * D], F32) for i in range(2)]
    xb_sb = asb("xb_sb", [128, D], BF16)
    xT_sb = asb("xT_sb", [128, d8 * TA], BF16)
    g_sb = asb("g_sb", [128, s32 * TA], BF16)
    g2_sb = [asb(f"g2_sb{i}", [128, TA], BF16) for i in range(3)]
    l2row = asb("l2row", [1, TA], F32)
    zh = asb("zh", [128, D], F32)
    y_sb = asb("y_sb", [128, 2 * D], F32)
    yqi = asb("yqi", [128, D // 2], I32)
    yqb = asb("yqb", [128, 2 * D], BF16)
    yqT_sb = asb("yqT_sb", [128, d8 * 128], BF16)
    bnst = asb("bnst", [128, nbn * 6], F32)
    smalls = asb("smalls", [128, 16], F32)
    ry_row = [asb(f"ry_row{i}", [1, TA], F32) for i in range(2)]

    last = {k: None for k in ("x0", "x1", "xb", "xT", "g", "pD", "pC",
                              "zh", "y", "yqi", "yqb", "yqT", "smalls_ac",
                              "pS0", "pS1", "g2_0", "g2_1", "g2_2", "g2_3",
                              "ry", "yqb_rdy", "qsc_ok", "rysm",
                              "ryb0", "ryb1")}
    last.update({f"pA{i}": None for i in range(4)})

    nmean = len(units_of(H, D)) + len(units_of(D, H)) + 1
    ntern = len(units_of(H, D)) + len(units_of(D, H))
    mtiles = max(1, min(4, nA // 4))
    wsteps_per_tile = []
    for i in range(nA):
        if i < mtiles:
            k = -(-nmean // mtiles)
        else:
            k = -(-ntern // (nA - mtiles))
        wsteps_per_tile.append(k)

    xloads = [O("sync", lambda s: s.dma_start(
        out=x_sb[0][:].rearrange("p (c d) -> p c d", c=2),
        in_=x_in[0:TA, :].rearrange("(c p) d -> p c d", p=128)), (), "qx0", 16)]

    def emit_head(i):
        xload = xloads[i]
        for c in range(2):
            O("vector", lambda v, c=c, xs=x_sb[i % 2]: v.tensor_copy(
                xb_sb[:], xs[:, c * D:(c + 1) * D]),
                W(xload, last["xb"], sc.cur("ve")), "ve")
            xb_ok = sc.cur("ve")
            for j in range(d8):
                O("tensor", lambda t, c=c, j=j: t.transpose(
                    pD[:, 128 * j:128 * (j + 1)],
                    xb_sb[:, 128 * j:128 * (j + 1)], identb[:]),
                    W(xb_ok, last["pD"], consts_ok) if j == 0 else (),
                    "pe" if j == d8 - 1 else None)
            O("vector", lambda v, c=c: v.tensor_copy(
                xT_sb[:].rearrange("p (j t) -> p j t", j=d8)[:, :, 128 * c:128 * (c + 1)],
                pD[:, 0:128 * d8].rearrange("p (j q) -> p j q", j=d8)),
                W(sc.cur("pe"), last["xT"], sc.cur("ve")), "ve")
            last["pD"] = sc.cur("ve")
            last["xb"] = sc.cur("pe")
        return sc.cur("ve")

    def emit_yq_lag(i):
        for c in range(2):
            for j in range(d8):
                O("tensor", lambda t, c=c, j=j: t.transpose(
                    pD[:, 128 * j:128 * (j + 1)],
                    yqb[:, c * D + 128 * j:c * D + 128 * (j + 1)], identb[:]),
                    W(last["yqb_rdy"], last["pD"]) if j == 0 else (),
                    "pe" if j == d8 - 1 else None)
            last["yqb"] = sc.cur("pe")
            O("vector", lambda v, c=c: v.tensor_copy(
                yqT_sb[:], pD[:, 0:128 * d8]),
                W(sc.cur("pe"), last["yqT"], sc.cur("ve")), "ve")
            last["pD"] = sc.cur("ve")
            O("sync", lambda s, i=i, c=c: s.dma_start(
                out=yqT_sc.rearrange("(j p) t -> p j t", p=128)[
                    :, :, TA * i + 128 * c:TA * i + 128 * (c + 1)],
                in_=yqT_sb[:].rearrange("p (j t) -> p j t", j=d8)),
                W(sc.cur("ve")), "qq", 16)
            last["yqT"] = sc.cur("qq")
        rb = i % 2
        for c in range(2):
            O("tensor", lambda t, c=c: t.transpose(
                pC[0:1, 256:384], smalls[:, 14 + c:15 + c], identf[:]),
                W(last["qsc_ok"], last["pC"], last[f"ryb{rb}"]), "pe")
            O("vector", lambda v, c=c, rb=rb: v.tensor_copy(
                ry_row[rb][0:1, 128 * c:128 * (c + 1)], pC[0:1, 256:384]),
                W(sc.cur("pe"), sc.cur("ve")), "ve")
            last["pC"] = sc.cur("ve")
        last["rysm"] = sc.cur("pe")
        O("sync", lambda s, i=i, rb=rb: s.dma_start(
            out=ry_sc[0:1, TA * i:TA * (i + 1)], in_=ry_row[rb][:]),
            W(sc.cur("ve")), f"qr{rb}", 16)
        last[f"ryb{rb}"] = sc.cur(f"qr{rb}")
        last["ry"] = sc.cur(f"qr{rb}")

    xT_ok = None
    for i in range(nA):
        sc.mark(f"tile{i}")
        wquota = [wsteps_per_tile[i] if i < len(wsteps_per_tile) else 0]
        xb_i = i % 2
        xload = xloads[i]

        if i + 1 < nA:
            xloads.append(O("sync", lambda s, i=i: s.dma_start(
                out=x_sb[(i + 1) % 2][:].rearrange("p (c d) -> p c d", c=2),
                in_=x_in[TA * (i + 1):TA * (i + 2), :]
                .rearrange("(c p) d -> p c d", p=128)),
                W(last[f"x{(i + 1) % 2}"]), f"qx{(i + 1) % 2}", 16))

        if i == 0:
            xT_ok = emit_head(0)

        ps_lohi = [(0, TA), (512, 512 + TA)]
        gelu_cnt = {}
        g2_cnt = {}

        def emit_ones(ss, i=i):
            O("tensor", lambda t, ss=ss: t.matmul(
                pC[0:1, 0:TA], ones_bf[:], g2_sb[ss % 3][:],
                start=(ss == 0), stop=(ss == s32 - 1)),
                W(g2_cnt[ss], last["pC"] if ss == 0 else None), "pe")
            last[f"g2_{ss % 3}"] = sc.cur("pe")

        for ss in range(s32):
            lo, hi = ps_lohi[ss % 2]
            w0 = [xT_ok, kt_ok]
            if ss >= 2:
                w0.append(gelu_cnt[ss - 2])
            else:
                w0.append(last[f"pS{ss}"])
            for j in range(d8):
                O("tensor", lambda t, ss=ss, j=j, lo=lo, hi=hi: t.matmul(
                    pB[:, lo:hi],
                    KT[:, j * S + 128 * ss: j * S + 128 * (ss + 1)],
                    xT_sb[:, j * TA:(j + 1) * TA],
                    start=(j == 0), stop=(j == d8 - 1)),
                    W(*w0) if j == 0 else (), "pe" if j == d8 - 1 else None)
            mm1_done = sc.cur("pe")
            if ss >= 3:
                emit_ones(ss - 3)
            O("scalar", lambda s, ss=ss, lo=lo, hi=hi: s.activation(
                g_sb[:, ss * TA:(ss + 1) * TA], pB[:, lo:hi], AF.Gelu),
                W(mm1_done), "ac")
            gelu_cnt[ss] = sc.cur("ac")
            O("vector", lambda v, ss=ss: v.tensor_mul(
                g2_sb[ss % 3][:], g_sb[:, ss * TA:(ss + 1) * TA],
                g_sb[:, ss * TA:(ss + 1) * TA]),
                W(gelu_cnt[ss], last[f"g2_{ss % 3}"], sc.cur("ve")), "ve")
            g2_cnt[ss] = sc.cur("ve")
            if ss % 3 == 2 and wquota[0] > 0 and wprep_steps:
                nper = -(-wquota[0] // max(1, (s32 - ss) // 3))
                for _ in range(nper):
                    if wprep_steps:
                        wprep_steps.pop(0)()
                        wquota[0] -= 1
        for sss in range(max(0, s32 - 3), s32):
            emit_ones(sss)
        last["pS0"] = gelu_cnt[s32 - 2]
        last["pS1"] = gelu_cnt[s32 - 1]
        l2mm_done = sc.cur("pe")

        last["xT"] = l2mm_done
        if i + 1 < nA:
            next_xT_ok = emit_head(i + 1)
        if i > 0:
            emit_yq_lag(i - 1)

        O("vector", lambda v: v.tensor_copy(l2row[:], pC[0:1, 0:TA]),
          W(l2mm_done, sc.cur("ve")), "ve")
        l2row_ok = sc.cur("ve")
        for c in range(2):
            O("tensor", lambda t, c=c: t.matmul(
                pC[:, 256 + c:257 + c], l2row[0:1, 128 * c:128 * (c + 1)], one11[:],
                start=True, stop=True),
                W(l2row_ok) if c == 0 else (), "pe" if c == 1 else None)
        O("vector", lambda v: v.tensor_copy(smalls[:, 0:2], pC[:, 256:258]),
          W(sc.cur("pe"), sc.cur("ve")), "ve")
        last["pC"] = sc.cur("ve")
        O("scalar", lambda s: s.activation(smalls[:, 0:2], smalls[:, 0:2], AF.Sqrt),
          W(sc.cur("ve"), last["smalls_ac"]), "ac")
        sqrt_ok = sc.cur("ac")
        O("vector", lambda v: v.reciprocal(smalls[:, 2:4], smalls[:, 0:2]),
          W(sqrt_ok, sc.cur("ve")), "ve")
        O("vector", lambda v: v.tensor_scalar(smalls[:, 2:4], smalls[:, 2:4],
                                              sqrtS, None, ALU.mult),
          W(sc.cur("ve")), "ve")
        rl2_ok = sc.cur("ve")

        for c in range(2):
            for n in range(ndh):
                gidx = c * ndh + n
                for k in range(s32):
                    w0 = W(gelu_cnt[k], v_ok, last[f"pA{gidx}"] if k == 0 else None)
                    O("tensor", lambda t, c=c, n=n, gidx=gidx, k=k: t.matmul(
                        pA[:, 512 * gidx:512 * gidx + dw],
                        g_sb[:, k * TA + 128 * c: k * TA + 128 * (c + 1)],
                        V_sb[:, k * D + dw * n: k * D + dw * (n + 1)],
                        start=(k == 0), stop=(k == s32 - 1)),
                        w0, "pe" if k == s32 - 1 else None)
                mm2_done = sc.cur("pe")
                O("vector", lambda v, c=c, n=n, gidx=gidx, xs=x_sb[xb_i]:
                  v.scalar_tensor_tensor(
                    zh[:, dw * n:dw * (n + 1)], pA[:, 512 * gidx:512 * gidx + dw],
                    smalls[:, 2 + c:3 + c],
                    xs[:, c * D + dw * n:c * D + dw * (n + 1)],
                    ALU.mult, ALU.add),
                    W(mm2_done, rl2_ok, xload,
                      last["zh"] if n == 0 else None, sc.cur("ve")), "ve")
                last[f"pA{gidx}"] = sc.cur("ve")
            z_ok = sc.cur("ve")
            for bb in range(nbn):
                O("vector", lambda v, bb=bb: v.bn_stats(
                    bnst[:, 6 * bb:6 * (bb + 1)],
                    zh[:, 512 * bb:512 * bb + min(512, D)]),
                    W(z_ok, sc.cur("ve")), "ve")
            O("vector", lambda v: v.bn_aggr(smalls[:, 4:6], bnst[:, 0:6 * nbn]),
              W(sc.cur("ve")), "ve")
            bn_ok = sc.cur("ve")
            O("scalar", lambda s: s.activation(smalls[:, 6:7], smalls[:, 5:6],
                                               AF.Sqrt, bias=epsc[:]),
              W(bn_ok, consts_ok), "ac")
            last["smalls_ac"] = sc.cur("ac")
            O("vector", lambda v: v.reciprocal(smalls[:, 8:9], smalls[:, 6:7]),
              W(sc.cur("ac"), sc.cur("ve")), "ve")
            O("vector", lambda v: v.tensor_mul(smalls[:, 10:11], smalls[:, 4:5],
                                               smalls[:, 8:9]),
              W(sc.cur("ve")), "ve")
            O("vector", lambda v, c=c: v.tensor_scalar(
                y_sb[:, c * D:(c + 1) * D], zh[:], smalls[:, 8:9], smalls[:, 10:11],
                ALU.mult, ALU.subtract),
                W(sc.cur("ve"), last["y"] if c == 0 else None), "ve")
            last["zh"] = sc.cur("ve")
            y_ok = sc.cur("ve")
            O("vector", lambda v, c=c: v.reduce_max(
                smalls[:, 12:13], y_sb[:, c * D:(c + 1) * D], axis=AX.X,
                apply_absolute_value=True), W(y_ok), "ve")
            O("vector", lambda v: v.tensor_scalar(smalls[:, 12:13], smalls[:, 12:13],
                                                  1e-5, None, ALU.max),
              W(sc.cur("ve")), "ve")
            O("vector", lambda v: v.reciprocal(smalls[:, 13:14], smalls[:, 12:13]),
              W(sc.cur("ve")), "ve")
            O("vector", lambda v: v.tensor_scalar(smalls[:, 13:14], smalls[:, 13:14],
                                                  127.0, None, ALU.mult),
              W(sc.cur("ve")), "ve")
            O("vector", lambda v, c=c: v.tensor_scalar(
                smalls[:, 14 + c:15 + c], smalls[:, 12:13], 1.0 / 127.0, None,
                ALU.mult),
                W(sc.cur("ve"), last["rysm"] if c == 0 else None), "ve")
            qsc_ok = sc.cur("ve")
            for hh in range(2):
                hw2 = D // 2
                O("vector", lambda v, c=c, hh=hh, hw2=hw2: v.tensor_scalar(
                    zh[:, hh * hw2:(hh + 1) * hw2],
                    y_sb[:, c * D + hh * hw2:c * D + (hh + 1) * hw2],
                    smalls[:, 13:14], MAGIC, ALU.mult, ALU.add),
                    W(qsc_ok, last["zh"], sc.cur("ve")), "ve")
                O("vector", lambda v, hh=hh, hw2=hw2: v.tensor_scalar(
                    yqi[:], zh[:, hh * hw2:(hh + 1) * hw2],
                    MAGIC, CQ, ALU.subtract, ALU.add),
                    W(last["yqi"], sc.cur("ve")), "ve")
                O("vector", lambda v, c=c, hh=hh, hw2=hw2: v.tensor_scalar(
                    yqb[:, c * D + hh * hw2:c * D + (hh + 1) * hw2],
                    yqi[:], -256.0, None, ALU.add),
                    W(last["yqb"] if (c == 0 and hh == 0) else None,
                      sc.cur("ve")), "ve")
                last["yqi"] = sc.cur("ve")
                last["zh"] = sc.cur("ve")
        last["g"] = sc.cur("pe")
        last["yqb_rdy"] = sc.cur("ve")
        last["qsc_ok"] = sc.cur("ve")

        O("sync", lambda s, i=i: s.dma_start(
            out=y_sc[TA * i:TA * (i + 1), :].rearrange("(c p) d -> p c d", p=128),
            in_=y_sb[:].rearrange("p (c d) -> p c d", c=2)),
            W(sc.cur("ve")), "qy", 16)
        last["y"] = sc.cur("qy")
        last[f"x{xb_i}"] = sc.cur("ve")
        if i + 1 < nA:
            xT_ok = next_xT_ok

    while wprep_steps:
        wprep_steps.pop(0)()
    emit_yq_lag(nA - 1)
    sc.mark("phaseA")
    pa.close()
    sc.barrier()

    W1T = sbuf("W1T", [128, d8 * H], BF16)
    W2T = sbuf("W2T", [128, h32 * D], BF16)
    yqTw = [sbuf(f"yqTw{i}", [128, d8 * (TW + 2)], BF16) for i in range(2)]
    y_win = sbuf("y_win", [128, D], F32)
    Srow_b = sbuf("Srow_b", [1, TW + 2], F32)
    Srow = [Srow_b, Srow_b]
    S_sb = [sbuf(f"S_sb{i}", [128, TW + 2], F32) for i in range(2)]
    htr = [sbuf(f"htr{i}", [128, TW + 2], F32) for i in range(3)]
    ca = [sbuf(f"ca{i}", [128, TW], F32) for i in range(2)]
    ca3 = [sbuf(f"ca3{i}", [128, TW], F32) for i in range(2)]
    hc_buf = sbuf("hc_buf", [128, h32 * TW], F32)
    mbuf = sbuf("mbuf", [128, TW], F32)
    mg = sbuf("mg", [128, 2 * TW], F32)
    i32tr = sbuf("i32tr", [128, 2 * TW], I32)
    hq_sb = sbuf("hq_sb", [128, h32 * TW], BF16)
    row2 = sbuf("row2", [1, TW], F32)
    S2_sb = sbuf("S2_sb", [128, TW], F32)
    bsml = sbuf("bsml", [128, 16], F32)
    bnst2 = sbuf("bnst2", [128, nbn * 6], F32)

    w1t_ok = O("sync", lambda s: s.dma_start(
        out=W1T[:].rearrange("p (j h) -> p j h", j=d8),
        in_=w1T_sc.rearrange("(j p) h -> p j h", p=128)), (), "qw1t", 16)
    w2t_ok = O("sync", lambda s: s.dma_start(
        out=W2T[:].rearrange("p (c d) -> p c d", c=h32),
        in_=w2T_sc.rearrange("(c p) d -> p c d", p=128)), (), "qw2t", 16)

    lastB = {k: None for k in ("pC", "htr0", "htr1", "htr2",
                               "htr0ac", "htr1ac", "htr2ac",
                               "ca0", "ca1", "ca30", "ca31",
                               "hc", "m8", "hq", "S2",
                               "row2", "ywin", "bsml_ac",
                               "Srow0", "Srow1", "S0", "S1", "pA2",
                               "mslot0", "mslot1", "mslot2", "mslot3")}
    lastB.update({f"pA{i}": None for i in range(4)})

    def emit_win_loads(w):
        q, wq = divmod(w, nWq)
        go = q * L + wq * TW
        lp = wq == 0
        rp = wq == nWq - 1
        c0 = go - (0 if lp else 1)
        c1 = min(go + TW + 1, q * L + L)
        ncols = c1 - c0
        dst0 = 1 if lp else 0
        wb = w % 2
        buf = yqTw[wb]
        ld = O("sync", lambda s, buf=buf, c0=c0, ncols=ncols, dst0=dst0: s.dma_start(
            out=buf[:].rearrange("p (j t) -> p j t", j=d8)[:, :, dst0:dst0 + ncols],
            in_=yqT_sc.rearrange("(j p) t -> p j t", p=128)[:, :, c0:c0 + ncols]),
            W(lastB.get(f"yqTw{wb}"), last["yqT"]), f"qt{wb}", 16)
        pads = []
        if lp:
            pads.append(O("vector", lambda v, buf=buf: v.memset(
                buf[:].rearrange("p (j t) -> p j t", j=d8)[:, :, 0:1], 0.0),
                W(lastB.get(f"yqTw{wb}"), ld, sc.cur("ve")), "ve"))
        if rp:
            pads.append(O("vector", lambda v, buf=buf: v.memset(
                buf[:].rearrange("p (j t) -> p j t", j=d8)[:, :, TW + 1:TW + 2], 0.0),
                W(lastB.get(f"yqTw{wb}"), ld, sc.cur("ve")), "ve"))
        return (ld, pads, go, lp, rp)

    def emit_sprep(w):
        wb = w % 2
        q, wq = divmod(w, nWq)
        go = q * L + wq * TW
        lp = wq == 0
        rp = wq == nWq - 1
        sr0 = go - (0 if lp else 1)
        srcols = min(go + TW + 1, q * L + L) - sr0
        sdst = 1 if lp else 0
        ryld = O("sync", lambda s, sr0=sr0, srcols=srcols, sdst=sdst, wb=wb:
                 s.dma_start(out=Srow[wb][0:1, sdst:sdst + srcols],
                             in_=ry_sc[0:1, sr0:sr0 + srcols]),
                 W(lastB["Srow0"], lastB["Srow1"], last["ry"]), f"qrl{wb}", 16)
        O("vector", lambda v, sr0=sr0, srcols=srcols, sdst=sdst, wb=wb:
          v.tensor_scalar(
            Srow[wb][0:1, sdst:sdst + srcols], Srow[wb][0:1, sdst:sdst + srcols],
            mwv[0:1, 0:1], None, ALU.mult),
            W(ryld, lastB[f"Srow{wb}"], wst["mw_ok"], sc.cur("ve")), "ve")
        if lp:
            O("vector", lambda v, wb=wb: v.memset(Srow[wb][0:1, 0:1], 0.0),
              W(sc.cur("ve")), "ve")
        if rp:
            O("vector", lambda v, wb=wb: v.memset(Srow[wb][0:1, TW + 1:TW + 2], 0.0),
              W(sc.cur("ve")), "ve")
        srow_ok = sc.cur("ve")
        O("tensor", lambda t, wb=wb: t.matmul(pC[:, 0:TW + 2], ones_f1[:],
                                              Srow[wb][:], start=True, stop=True),
          W(srow_ok, lastB["pC"]), "pe")
        O("vector", lambda v, wb=wb: v.tensor_copy(S_sb[wb][:], pC[:, 0:TW + 2]),
          W(sc.cur("pe"), lastB[f"S{wb}"], sc.cur("ve")), "ve")
        lastB["pC"] = sc.cur("ve")
        lastB[f"Srow{wb}"] = sc.cur("ve")
        return sc.cur("ve")

    win_loads = {0: emit_win_loads(0)}
    s_oks = {0: emit_sprep(0)}

    mslots = [(pB, 0), (pB, 512), (pA, 1536)]
    gsz = 16 if h32 % 16 == 0 else h32
    ng = h32 // gsz
    octets = [list(range(o, min(o + 8, h32))) for o in range(0, h32, 8)]
    st = {"hq_prev": None, "dbg_gate": None, "pend_silu": [], "i32ac": [None, None]}

    def emit_conv_chunk(w, hc):
        ld, pads, go, lp, rp = win_loads[w]
        s_ok = s_oks[w]
        s_i = hc % 3
        mp, lo = mslots[s_i]
        hi = lo + TW + 2

        def mm3(t, hc=hc, mp=mp, lo=lo, hi=hi, wb=w % 2):
            r = None
            for j in range(d8):
                r = t.matmul(
                    mp[:, lo:hi],
                    W1T[:, j * H + 128 * hc: j * H + 128 * (hc + 1)],
                    yqTw[wb][:, j * (TW + 2):(j + 1) * (TW + 2)],
                    start=(j == 0), stop=(j == d8 - 1))
            return r
        O("tensor", mm3,
          W(ld, *pads, w1t_ok, lastB[f"mslot{s_i}"]), "pe")
        mm3_done = sc.cur("pe")
        hb = hc % 3
        O("vector", lambda v, hb=hb, mp=mp, lo=lo, hi=hi, wb=w % 2:
          v.tensor_tensor(
            htr[hb][:], mp[:, lo:hi], S_sb[wb][:], ALU.mult),
            W(mm3_done, s_ok, lastB[f"htr{hb}"],
              lastB[f"htr{hb}ac"], sc.cur("ve")), "ve")
        lastB[f"mslot{s_i}"] = sc.cur("ve")
        htr_ok = sc.cur("ve")
        O("scalar", lambda s, hc=hc, hb=hb: s.activation(
            ca[hc % 2][:], htr[hb][:, 0:TW], AF.Identity,
            scale=cwb_sb[:, 4 * hc:4 * hc + 1],
            bias=cwb_sb[:, 4 * hc + 3:4 * hc + 4]),
            W(htr_ok, cwb_ok, lastB[f"ca{hc % 2}"]), "ac")
        c1_ok = sc.cur("ac")
        O("scalar", lambda s, hc=hc, hb=hb: s.activation(
            ca3[hc % 2][:], htr[hb][:, 2:TW + 2], AF.Identity,
            scale=cwb_sb[:, 4 * hc + 2:4 * hc + 3], bias=0.0),
            W(lastB[f"ca3{hc % 2}"], sc.cur("ac")), "ac")
        lastB[f"htr{hb}ac"] = sc.cur("ac")
        c3_ok = sc.cur("ac")
        O("vector", lambda v, hc=hc, hb=hb: v.scalar_tensor_tensor(
            ca[hc % 2][:], htr[hb][:, 1:TW + 1],
            cwb_sb[:, 4 * hc + 1:4 * hc + 2], ca[hc % 2][:],
            ALU.mult, ALU.add),
            W(c1_ok, sc.cur("ve")), "ve")
        lastB[f"htr{hb}"] = sc.cur("ve")
        c2_ok = sc.cur("ve")
        O("gpsimd", lambda g, hc=hc: g.tensor_tensor(
            hc_buf[:, hc * TW:(hc + 1) * TW], ca[hc % 2][:],
            ca3[hc % 2][:], ALU.add),
            W(c2_ok, c3_ok,
              st["hc_last"][hc] if st["hc_last"] is not None else None,
              st["dbg_gate"] if w == 1 else None,
              sc.cur("gp")), "gp")
        lastB[f"ca{hc % 2}"] = sc.cur("gp")
        lastB[f"ca3{hc % 2}"] = sc.cur("gp")
        if hc % gsz == gsz - 1:
            st["pend_silu"].append((w, hc // gsz))

    def emit_silu_group(w, gq):
        hc = gq * gsz + gsz - 1
        O("scalar", lambda s, hc=hc: s.activation(
            hc_buf[:, (hc - gsz + 1) * TW:(hc + 1) * TW],
            hc_buf[:, (hc - gsz + 1) * TW:(hc + 1) * TW], AF.Silu),
            W(sc.cur("gp"), sc.cur("ac")), "ac")
        O("vector", lambda v, hc=hc, gq=gq: v.reduce_max(
            mg[:, gq * TW:(gq + 1) * TW],
            hc_buf[:, (hc - gsz + 1) * TW:(hc + 1) * TW]
            .rearrange("p (c t) -> p t c", c=gsz),
            axis=AX.X, apply_absolute_value=True),
            W(sc.cur("ac"), sc.cur("ve")), "ve")

    def emit_finals(w):
        O("vector", lambda v: v.reduce_max(
            mbuf[:], mg[:, 0:ng * TW].rearrange("p (c t) -> p t c", c=ng),
            axis=AX.X), W(sc.cur("ve"), lastB["m8"]), "ve")
        mbuf_ok = sc.cur("ve")
        for c in range(2):
            O("tensor", lambda t, c=c: t.transpose(
                pC[:, 128 * c:128 * (c + 1)], mbuf[:, 128 * c:128 * (c + 1)],
                identf[:]),
                W(mbuf_ok, lastB["pC"]) if c == 0 else (),
                "pe" if c == 1 else None)
        lastB["m8"] = sc.cur("pe")
        O("vector", lambda v: v.reduce_max(
            bsml[:, 0:2], pC[:, 0:256].rearrange("p (c t) -> p c t", c=2),
            axis=AX.X), W(sc.cur("pe"), lastB["bsml_ac"], sc.cur("ve")), "ve")
        lastB["pC"] = sc.cur("ve")
        O("vector", lambda v: v.tensor_scalar(bsml[:, 0:2], bsml[:, 0:2],
                                              1e-5, None, ALU.max),
          W(sc.cur("ve")), "ve")
        O("vector", lambda v: v.tensor_scalar(bsml[:, 2:4], bsml[:, 0:2],
                                              m2s_col[:], None, ALU.mult),
          W(sc.cur("ve"), wst["wq_col_ok"]), "ve")
        rh2_ok = sc.cur("ve")
        O("vector", lambda v: v.reciprocal(bsml[:, 4:6], bsml[:, 0:2]),
          W(sc.cur("ve")), "ve")
        O("vector", lambda v: v.tensor_scalar(bsml[:, 4:6], bsml[:, 4:6],
                                              127.0, None, ALU.mult),
          W(sc.cur("ve")), "ve")
        sc2_ok = sc.cur("ve")
        for c in range(2):
            O("tensor", lambda t, c=c: t.transpose(
                pC[0:1, 256:384], bsml[:, 4 + c:5 + c], identf[:]),
                W(sc2_ok, lastB["pC"]), "pe")
            O("vector", lambda v, c=c: v.tensor_copy(
                row2[0:1, 128 * c:128 * (c + 1)], pC[0:1, 256:384]),
                W(sc.cur("pe"), lastB["row2"] if c == 0 else None,
                  sc.cur("ve")), "ve")
            lastB["pC"] = sc.cur("ve")
        O("tensor", lambda t: t.matmul(pC[:, 0:TW], ones_f1[:], row2[:],
                                       start=True, stop=True),
          W(sc.cur("ve"), lastB["pC"]), "pe")
        lastB["row2"] = sc.cur("pe")
        O("vector", lambda v: v.tensor_copy(S2_sb[:], pC[:, 0:TW]),
          W(sc.cur("pe"), lastB["S2"], sc.cur("ve")), "ve")
        lastB["pC"] = sc.cur("ve")
        s2_ok = sc.cur("ve")
        dbg_ok = None
        if dbg and w == 0:
            O("sync", lambda s: s.dma_start(out=dbg_m[:, 0:TW], in_=mbuf[:]),
              W(s2_ok), "qc", 16)
            O("sync", lambda s: s.dma_start(out=dbg_h[:, 0:h32 * TW], in_=hc_buf[:]),
              W(s2_ok), "qc", 16)
            dbg_ok = sc.cur("qc")
            st["dbg_gate"] = dbg_ok
        return s2_ok, rh2_ok, dbg_ok

    def emit_quant_pair(w, p, s2_ok, dbg_ok, hq_cnt, hc_last):
        sl = slice(2 * p * TW, (2 * p + 2) * TW)
        s2b = AP(S2_sb[:].tensor, S2_sb[:].offset,
                 [list(dd) for dd in S2_sb[:].ap[:-1]] + [[0, 2], [1, TW]])
        O("gpsimd", lambda g, sl=sl, s2b=s2b: g.tensor_tensor(
            hc_buf[:, sl].rearrange("q (c t) -> q c t", c=2),
            hc_buf[:, sl].rearrange("q (c t) -> q c t", c=2),
            s2b, ALU.mult),
            W(s2_ok, dbg_ok if p == 0 else None, sc.cur("gp")), "gp")
        O("vector", lambda v, sl=sl: v.tensor_scalar(
            hc_buf[:, sl], hc_buf[:, sl], MAGIC, MAGIC, ALU.add, ALU.subtract),
            W(sc.cur("gp"), sc.cur("ve")), "ve")
        O("scalar", lambda s, sl=sl: s.activation(
            i32tr[:], hc_buf[:, sl], AF.Identity, bias=cqc[:]),
            W(consts_ok, sc.cur("ve"), sc.cur("ac")), "ac")
        hc_last[2 * p] = hc_last[2 * p + 1] = sc.cur("ac")
        O("scalar", lambda s, sl=sl: s.activation(
            hq_sb[:, sl], i32tr[:], AF.Identity, bias=m256c[:]),
            W(lastB["hq"] if p == 0 else None, sc.cur("ac")), "ac")
        hq_cnt[2 * p + 1] = sc.cur("ac")

    def emit_mm4_octet(w, ks, hq_cnt, mm4_done):
        for k in ks:
            for n in range(ndh):
                w0 = W(hq_cnt[ks[-1]] if (k == ks[0] and n == 0) else None,
                       w2t_ok if k == 0 and n == 0 else None,
                       lastB[f"pA{n}"] if k == 0 else None)
                is_stop = k == h32 - 1
                O("tensor", lambda t, n=n, k=k, is_stop=is_stop: t.matmul(
                    pA[:, 512 * n:512 * n + dw],
                    hq_sb[:, k * TW: k * TW + 128],
                    W2T[:, k * D + dw * n: k * D + dw * (n + 1)],
                    start=(k == 0), stop=is_stop),
                    w0, "pe" if is_stop else None)
                if is_stop:
                    mm4_done[(0, n)] = sc.cur("pe")
        for k in ks:
            w0 = W(lastB["pA2"] if k == 0 else None)
            is_stop = k == h32 - 1
            O("tensor", lambda t, k=k, is_stop=is_stop: t.matmul(
                pA[:, 1024:1024 + dw],
                hq_sb[:, k * TW + 128: k * TW + 256],
                W2T[:, k * D: k * D + dw],
                start=(k == 0), stop=is_stop),
                w0, "pe" if is_stop else None)
            if is_stop:
                mm4_done[(1, 0)] = sc.cur("pe")

    def emit_mm4_c1n1(w, mm4_done):
        if ndh < 2:
            return
        for k in range(h32):
            w0 = W(lastB["pC"] if k == 0 else None)
            is_stop = k == h32 - 1
            O("tensor", lambda t, k=k, is_stop=is_stop: t.matmul(
                pC[:, 0:dw],
                hq_sb[:, k * TW + 128: k * TW + 256],
                W2T[:, k * D + dw: k * D + 2 * dw],
                start=(k == 0), stop=is_stop),
                w0, "pe" if is_stop else None)
            if is_stop:
                mm4_done[(1, 1)] = sc.cur("pe")

    def emit_ln2_store(w, c):
        for bb in range(nbn):
            O("vector", lambda v, bb=bb: v.bn_stats(
                bnst2[:, 6 * bb:6 * (bb + 1)],
                y_win[:, 512 * bb:512 * bb + min(512, D)]),
                W(sc.cur("ve")), "ve")
        O("vector", lambda v: v.bn_aggr(bsml[:, 4:6], bnst2[:, 0:6 * nbn]),
          W(sc.cur("ve")), "ve")
        O("scalar", lambda s: s.activation(bsml[:, 6:7], bsml[:, 5:6],
                                           AF.Sqrt, bias=epsc[:]),
          W(sc.cur("ve"), consts_ok, sc.cur("ac")), "ac")
        lastB["bsml_ac"] = sc.cur("ac")
        O("vector", lambda v: v.reciprocal(bsml[:, 8:9], bsml[:, 6:7]),
          W(sc.cur("ac"), sc.cur("ve")), "ve")
        O("vector", lambda v: v.tensor_mul(bsml[:, 10:11], bsml[:, 4:5],
                                           bsml[:, 8:9]),
          W(sc.cur("ve")), "ve")
        O("vector", lambda v: v.tensor_scalar(
            y_win[:], y_win[:],
            bsml[:, 8:9], bsml[:, 10:11], ALU.mult, ALU.subtract),
            W(sc.cur("ve")), "ve")
        out_ok = sc.cur("ve")
        O("sync", lambda s, w=w, c=c: s.dma_start(
            out=out_hbm[TW * w + 128 * c:TW * w + 128 * (c + 1), :],
            in_=y_win[:]),
            W(out_ok), "qo", 16)
        lastB["ywin"] = sc.cur("qo")

    def emit_tail_c0(w, yld, rh2_ok, mm4_done):
        for n in range(ndh):
            O("vector", lambda v, n=n: v.scalar_tensor_tensor(
                y_win[:, dw * n:dw * (n + 1)],
                pA[:, 512 * n:512 * n + dw],
                bsml[:, 2:3],
                y_win[:, dw * n:dw * (n + 1)],
                ALU.mult, ALU.add),
                W(mm4_done[(0, n)], rh2_ok, yld, sc.cur("ve")), "ve")
            lastB[f"pA{n}"] = sc.cur("ve")
        emit_ln2_store(w, 0)

    def emit_tail_c1(w, yld, rh2_ok, mm4_done):
        O("vector", lambda v: v.scalar_tensor_tensor(
            y_win[:, 0:dw], pA[:, 1024:1024 + dw], bsml[:, 3:4],
            y_win[:, 0:dw], ALU.mult, ALU.add),
            W(mm4_done[(1, 0)], rh2_ok, yld, sc.cur("ve")), "ve")
        lastB["pA2"] = sc.cur("ve")
        if ndh >= 2:
            O("vector", lambda v: v.scalar_tensor_tensor(
                y_win[:, dw:2 * dw], pC[:, 0:dw], bsml[:, 3:4],
                y_win[:, dw:2 * dw], ALU.mult, ALU.add),
                W(mm4_done[(1, 1)], rh2_ok, yld, sc.cur("ve")), "ve")
            lastB["pC"] = sc.cur("ve")
        emit_ln2_store(w, 1)

    def flush_silu():
        for (wc, gq) in st["pend_silu"]:
            emit_silu_group(wc, gq)
        st["pend_silu"] = []

    st["hc_last"] = None
    for hc in range(h32):
        emit_conv_chunk(0, hc)
    flush_silu()
    lastB["S0"] = sc.cur("ve")
    lastB["yqTw0"] = sc.cur("pe")
    if nW > 1:
        s_oks[1] = emit_sprep(1)
        win_loads[1] = emit_win_loads(1)
    sc.mark("w_head")

    for w in range(nW):
        sc.mark(f"win{w}")
        s2_ok, rh2_ok, dbg_ok = emit_finals(w)
        hq_cnt = {}
        hc_last = {}
        st["hc_last"] = hc_last
        yld = O("sync", lambda s, w=w: s.dma_start(
            out=y_win[:],
            in_=y_sc[TW * w:TW * w + 128, :]),
            W(lastB["ywin"], last["y"]), "qyl", 16)
        mm4_done = {}
        for oi, ks in enumerate(octets):
            flush_silu()
            if oi > 0 and w + 1 < nW:
                for hc in octets[oi - 1]:
                    emit_conv_chunk(w + 1, hc)
            for p in range(ks[0] // 2, (ks[-1] + 1) // 2):
                emit_quant_pair(w, p, s2_ok, dbg_ok, hq_cnt, hc_last)
            if oi > 0:
                emit_mm4_octet(w, octets[oi - 1], hq_cnt, mm4_done)
        flush_silu()
        if w + 1 < nW:
            for hc in octets[-1]:
                emit_conv_chunk(w + 1, hc)
        emit_mm4_octet(w, octets[-1], hq_cnt, mm4_done)
        if w + 1 < nW:
            lastB[f"S{(w + 1) % 2}"] = sc.cur("ve")
            lastB[f"yqTw{(w + 1) % 2}"] = sc.cur("pe")
        emit_mm4_c1n1(w, mm4_done)
        emit_tail_c0(w, yld, rh2_ok, mm4_done)
        yld = O("sync", lambda s, w=w: s.dma_start(
            out=y_win[:],
            in_=y_sc[TW * w + 128:TW * w + 256, :]),
            W(lastB["ywin"], last["y"]), "qyl", 16)
        emit_tail_c1(w, yld, rh2_ok, mm4_done)
        lastB["hq"] = sc.cur("pe")
        lastB["S2"] = sc.cur("ve")
        flush_silu()
        if w + 2 < nW:
            s_oks[w + 2] = emit_sprep(w + 2)
            win_loads[w + 2] = emit_win_loads(w + 2)
        win_loads.pop(w)
        s_oks.pop(w)

    sc.streams["sync"].append((W(sc.cur("qo"), sc.cur("qy"), sc.cur("qq")), None, None, sc.seq))
    sc.streams["gpsimd"].append((W(sc.cur("qk0"), sc.cur("qk1"), sc.cur("qv")), None, None, sc.seq))
    sc.seq += 1

    with nc.Block() as block:
        sc.emit(block, sems)

    outer.close()
    return nc


_NC_CACHE = {}


def get_program(key=(1024, 4096, 4096, 2048, 2)):
    if key not in _NC_CACHE:
        _NC_CACHE[key] = build_core(*key)
    return _NC_CACHE[key]


def kernel(L0_attn_inputs, L0_attn_key_param_tokens, L0_attn_value_param_tokens,
           L0_ffn_w1, L0_ffn_conv_w, L0_ffn_conv_b, L0_ffn_w2):
    x = np.ascontiguousarray(np.asarray(L0_attn_inputs, dtype=np.float32))
    k = np.ascontiguousarray(np.asarray(L0_attn_key_param_tokens, dtype=np.float32))
    v = np.ascontiguousarray(np.asarray(L0_attn_value_param_tokens, dtype=np.float32))
    w1 = np.ascontiguousarray(np.asarray(L0_ffn_w1, dtype=np.float32))
    w2 = np.ascontiguousarray(np.asarray(L0_ffn_w2, dtype=np.float32))
    cw = np.asarray(L0_ffn_conv_w, dtype=np.float32).reshape(-1, 3)
    cb = np.asarray(L0_ffn_conv_b, dtype=np.float32).reshape(-1, 1)
    cwb = np.ascontiguousarray(np.concatenate([cw, cb], axis=1))

    B, L, D = x.shape
    ncores = 8
    bc = B // ncores
    nc_prog = get_program((D, k.shape[0], w1.shape[0], L, bc))
    in_maps = []
    for c in range(ncores):
        in_maps.append({
            "x": np.ascontiguousarray(x[bc * c:bc * (c + 1)].reshape(bc * L, D)),
            "k": k, "v": v, "w1": w1, "w2": w2, "cwb": cwb,
        })
    res = run_bass_kernel_spmd(nc_prog, in_maps, core_ids=list(range(ncores)))
    out = np.concatenate(
        [r["out"].reshape(bc, L, D) for r in res.results], axis=0)
    return out


# version 28
# speedup vs baseline: 1.0642x; 1.0186x over previous
import contextlib

import numpy as np

import concourse.bass as bass
import concourse.mybir as mybir
from concourse.ap import AP
from concourse.bass_utils import run_bass_kernel_spmd

F32 = mybir.dt.float32
BF16 = mybir.dt.bfloat16
I32 = mybir.dt.int32
AF = mybir.ActivationFunctionType
ALU = mybir.AluOpType
AX = mybir.AxisListType

ENGINES = ("sync", "gpsimd", "vector", "scalar", "tensor")

MAGIC = 12582912.0
CQ = 256.49996948242188


def W(*items):
    out = []
    for it in items:
        if it is None:
            continue
        if isinstance(it, tuple) and it and all(
                sub is None or isinstance(sub, tuple) for sub in it):
            for sub in it:
                if sub is not None and sub[1] > 0:
                    out.append(sub)
            continue
        s, v = it
        if v > 0:
            out.append((s, v))
    return tuple(out)


TRUNCATE_AT = None


class Sched:
    def __init__(self):
        self.streams = {e: [] for e in ENGINES}
        self.cnt = {}
        self.seq = 0
        self.sections = {}

    def mark(self, name):
        self.sections[name] = (self.seq, dict(self.cnt))

    def n(self, sem):
        return self.cnt.get(sem, 0)

    def cur(self, sem):
        return (sem, self.cnt.get(sem, 0))

    def op(self, eng, emit, waits=(), inc=None, inc_n=1):
        if inc is not None:
            self.cnt[inc] = self.cnt.get(inc, 0) + inc_n
        co = {}
        for s, v in waits:
            if co.get(s, -1) < v:
                co[s] = v
        self.streams[eng].append(
            (tuple(co.items()), emit, (inc, inc_n) if inc else None, self.seq))
        self.seq += 1
        return (inc, self.cnt[inc]) if inc else None

    def barrier(self):
        f = W(*[(s, v) for s, v in self.cnt.items()])
        for e in ENGINES:
            self.streams[e].append((f, None, None, self.seq))
        self.seq += 1

    def emit(self, block, sems):
        cutoff, fence = None, None
        if TRUNCATE_AT is not None:
            cutoff, fcnt = self.sections[TRUNCATE_AT]
            fence = W(*[(s, v) for s, v in fcnt.items()])

        def run(eng_obj, stream):
            waited = {}
            for waits, emitf, inc, seq in stream:
                if cutoff is not None and seq >= cutoff:
                    continue
                for s, v in waits:
                    if waited.get(s, -1) < v:
                        eng_obj.wait_ge(sems[s], v)
                        waited[s] = v
                if emitf is not None:
                    instr = emitf(eng_obj)
                    if inc is not None:
                        instr.then_inc(sems[inc[0]], inc[1])
            if fence is not None:
                for s, v in fence:
                    if waited.get(s, -1) < v:
                        eng_obj.wait_ge(sems[s], v)

        for name in ENGINES:
            getattr(block, name)((lambda st: (lambda e: run(e, st)))(self.streams[name]))


def build_core(D=1024, S=4096, H=4096, L=2048, Bc=2, dbg=False):
    TA = 256
    TW = 256
    T = Bc * L
    d8, s32, h32 = D // 128, S // 128, H // 128
    nA = T // TA
    nW = T // TW
    nWq = L // TW
    WM = max(D, H)
    sqrtS = float(np.float32(np.sqrt(S)))
    ndh = max(1, D // 512)
    dw = D // ndh
    nbn = max(1, D // 512)
    nqg = 1
    assert TA == 256 and TW == 256 and L % TW == 0 and T % TA == 0
    assert (H * D) & (H * D - 1) == 0
    assert h32 % nqg == 0

    nc = bass.Bass("TRN2", target_bir_lowering=False, debug=False)

    x_in = nc.dram_tensor("x", [T, D], F32, kind="ExternalInput").ap()
    k_in = nc.dram_tensor("k", [S, D], F32, kind="ExternalInput").ap()
    v_in = nc.dram_tensor("v", [S, D], F32, kind="ExternalInput").ap()
    w1_in = nc.dram_tensor("w1", [H, D], F32, kind="ExternalInput").ap()
    w2_in = nc.dram_tensor("w2", [D, H], F32, kind="ExternalInput").ap()
    cwb_in = nc.dram_tensor("cwb", [H, 4], F32, kind="ExternalInput").ap()
    out_hbm = nc.dram_tensor("out", [T, D], F32, kind="ExternalOutput").ap()
    ikind = "ExternalOutput" if dbg else "Internal"
    y_sc = nc.dram_tensor("y_sc", [T, D], F32, kind=ikind).ap()
    yqT_sc = nc.dram_tensor("yqT_sc", [D, T], BF16, kind=ikind).ap()
    w1T_sc = nc.dram_tensor("w1T_sc", [D, H], BF16, kind=ikind).ap()
    w2T_sc = nc.dram_tensor("w2T_sc", [H, D], BF16, kind=ikind).ap()
    ry_sc = nc.dram_tensor("ry_sc", [1, T], F32, kind=ikind).ap()
    if dbg:
        dbg_m = nc.dram_tensor("dbg_m", [128, 512], F32, kind="ExternalOutput").ap()
        dbg_r = nc.dram_tensor("dbg_r", [4, 512], F32, kind="ExternalOutput").ap()
        dbg_h = nc.dram_tensor("dbg_h", [128, 8192], F32, kind="ExternalOutput").ap()

    outer = contextlib.ExitStack()
    sbuf = lambda name, shape, dt=F32: outer.enter_context(nc.sbuf_tensor(name, shape, dt))
    sems = {s: outer.enter_context(nc.semaphore(s))
            for s in ("pe", "ve", "ac", "gp", "qc", "qw0", "qw1", "qs",
                      "qk0", "qk1", "qv", "qx0", "qx1", "qy", "qq",
                      "qw1t", "qw2t", "qt0", "qt1", "qyl", "qo",
                      "qr0", "qr1", "qrl0", "qrl1",
                      "qg0", "qg1", "qg2", "qg3")}
    sc = Sched()
    O = sc.op

    pA = outer.enter_context(nc.psum_tensor("pA", [128, 2048], F32))
    pB = outer.enter_context(nc.psum_tensor("pB", [128, 1024], F32))
    pC = outer.enter_context(nc.psum_tensor("pC", [128, 512], F32))
    pD = outer.enter_context(nc.psum_tensor("pD", [128, 1024], BF16))

    identb = sbuf("identb", [128, 128], BF16)
    identf = sbuf("identf", [128, 128], F32)
    ones_f1 = sbuf("ones_f1", [1, 128], F32)
    one11 = sbuf("one11", [1, 1], F32)
    ones_bf = sbuf("ones_bf", [128, 1], BF16)
    b256 = sbuf("b256", [128, 1], F32)
    cqc = sbuf("cqc", [128, 1], F32)
    m256c = sbuf("m256c", [128, 1], F32)
    epsc = sbuf("epsc", [128, 1], F32)
    cwb_sb = sbuf("cwb_sb", [128, h32 * 4], F32)
    mrow_s = sbuf("mrow_s", [1, 2], F32)
    mcl = sbuf("mcl", [1, 2], F32)
    wsc = sbuf("wsc", [1, 2], F32)
    mwv = sbuf("mwv", [1, 2], F32)
    m2s = sbuf("m2s", [1, 1], F32)
    wq_col = sbuf("wq_col", [128, 2], F32)
    m2s_col = sbuf("m2s_col", [128, 1], F32)

    def ident_init(g, buf):
        return g.affine_select(out=buf[:], in_=buf[:], compare_op=ALU.not_equal,
                               fill=1.0, base=0, pattern=[[-1, 128]],
                               channel_multiplier=1)

    O("gpsimd", lambda g: g.memset(identb[:], 0.0), (), "gp")
    O("gpsimd", lambda g: ident_init(g, identb), W(sc.cur("gp")), "gp")
    O("gpsimd", lambda g: g.memset(identf[:], 0.0), (), "gp")
    O("gpsimd", lambda g: ident_init(g, identf), W(sc.cur("gp")), "gp")
    for buf, val in ((ones_f1, 1.0), (one11, 1.0), (ones_bf, 1.0),
                     (b256, 256.5), (epsc, 1e-5), (cqc, CQ), (m256c, -256.0)):
        O("gpsimd", lambda g, b=buf, v=val: g.memset(b[:], v), (), "gp")
    consts_ok = sc.cur("gp")

    O("sync", lambda s: s.dma_start(
        out=cwb_sb[:].rearrange("p (c k) -> p c k", c=h32),
        in_=cwb_in.rearrange("(c p) k -> p c k", p=128)), (), "qc", 16)
    cwb_ok = sc.cur("qc")
    sc.mark("consts")

    pa = contextlib.ExitStack()
    asb = lambda name, shape, dt=F32: pa.enter_context(nc.sbuf_tensor(name, shape, dt))
    KT = asb("KT", [128, d8 * S], BF16)
    V_sb = asb("V_sb", [128, s32 * D], BF16)
    kbts = [asb(f"kbt{i}", [128, D], BF16) for i in range(2)]
    buf_free = [None, None]
    for c in range(S // 128):
        b = c % 2
        ld = O("gpsimd", lambda g, c=c, b=b: g.dma_start(
            out=kbts[b][:], in_=k_in[128 * c:128 * (c + 1), :]),
            W(buf_free[b]), f"qk{b}", 16)
        for g0 in range(0, d8, 8):
            gn = min(8, d8 - g0)
            for jj in range(gn):
                j = g0 + jj
                O("tensor", lambda t, j=j, jj=jj, b=b: t.transpose(
                    pD[:, 128 * jj:128 * (jj + 1)],
                    kbts[b][:, 128 * j:128 * (j + 1)], identb[:]),
                    W(ld, sc.cur("ve"), consts_ok) if jj == 0 else (),
                    "pe" if jj == gn - 1 else None)
            O("vector", lambda v, c=c, g0=g0, gn=gn: v.tensor_copy(
                KT[:].rearrange("p (j s) -> p j s", j=d8)[:, g0:g0 + gn,
                                                          128 * c:128 * (c + 1)],
                pD[:, 0:128 * gn].rearrange("p (j q) -> p j q", j=gn)),
                W(sc.cur("pe")), "ve")
        buf_free[b] = sc.cur("pe")
    kt_ok = sc.cur("ve")
    sc.mark("ktv")
    v_ok = O("gpsimd", lambda g: g.dma_start(
        out=V_sb[:].rearrange("p (c d) -> p c d", c=s32),
        in_=v_in.rearrange("(c p) d -> p c d", p=128)), (), "qv", 16)

    UW = min(512, min(D, H))
    wtmp = [asb(f"wtmp{i}", [128, UW], F32) for i in range(3)]
    wred = asb("wred", [128, 2], F32)
    wacc = asb("wacc", [128, 2], F32)
    wi32 = asb("wi32", [128, UW], I32)
    wqb = [asb(f"wqb{i}", [128, UW], BF16) for i in range(2)]
    wstg = asb("wstg", [128, (UW // 128) * 128], BF16)

    O("vector", lambda v: v.memset(wacc[:], 0.0), (), "ve")

    wst = {"buf_free": [None] * 3, "stg_free": None, "n": 0, "nq": 0, "np": 0,
           "wq_col_ok": None, "mw_ok": None, "wqb_free": [None, None],
           "wqb_ok": [None, None], "loads": [], "ld_next": 0, "ld_h": {}}

    def issue_wload():
        u = wst["ld_next"]
        if u >= len(wst["loads"]):
            return
        wst["ld_next"] += 1
        m, r, cc = wst["loads"][u]
        b = u % 3
        wst["ld_h"][u] = O("gpsimd", lambda g, m=m, r=r, cc=cc, b=b: g.dma_start(
            out=wtmp[b][:], in_=m[128 * r:128 * (r + 1), UW * cc:UW * (cc + 1)]),
            W(wst["buf_free"][b]), f"qg{b}", 16)

    def units_of(rows, width):
        return [(r, cc) for r in range(rows // 128) for cc in range(width // UW)]

    def mean_unit(m_in, r, cc, col):
        if wst["n"] == 0:
            for _ in range(3):
                issue_wload()
        u = wst["n"]
        b = u % 3
        wst["n"] += 1
        O("vector", lambda v, b=b: v.reduce_sum(
            wred[:, 0:1], wtmp[b][:], axis=AX.X, apply_absolute_value=True),
            W(wst["ld_h"].pop(u), sc.cur("ve")), "ve")
        wst["buf_free"][b] = sc.cur("ve")
        O("vector", lambda v, col=col: v.tensor_tensor(
            wacc[:, col:col + 1], wacc[:, col:col + 1], wred[:, 0:1], ALU.add),
            W(sc.cur("ve")), "ve")
        issue_wload()

    def mean_finalize():
        O("gpsimd", lambda g: g.tensor_reduce(mrow_s[:], wacc[:], axis=AX.C,
                                              op=ALU.add),
          W(sc.cur("ve"), consts_ok), "gp")
        O("vector", lambda v: v.tensor_scalar(mcl[:], mrow_s[:], 1.0 / (H * D), 1e-5,
                                              ALU.mult, ALU.max),
          W(sc.cur("gp")), "ve")
        O("vector", lambda v: v.reciprocal(wsc[:], mcl[:]), W(sc.cur("ve")), "ve")
        O("vector", lambda v: v.reciprocal(mwv[:], wsc[:]), W(sc.cur("ve")), "ve")
        O("vector", lambda v: v.tensor_scalar(m2s[:], mwv[0:1, 1:2], 1.0 / 127.0,
                                              None, ALU.mult),
          W(sc.cur("ve")), "ve")
        wst["mw_ok"] = sc.cur("ve")
        O("tensor", lambda t: t.matmul(pA[:, 0:2], ones_f1[:], wsc[:],
                                       start=True, stop=True),
          W(wst["mw_ok"], consts_ok, last["pA0"]), "pe")
        O("tensor", lambda t: t.matmul(pA[:, 2:3], ones_f1[:], m2s[:],
                                       start=True, stop=True), (), "pe")
        O("vector", lambda v: v.tensor_copy(wq_col[:], pA[:, 0:2]),
          W(sc.cur("pe"), sc.cur("ve")), "ve")
        O("vector", lambda v: v.tensor_copy(m2s_col[:], pA[:, 2:3]),
          W(sc.cur("ve")), "ve")
        last["pA0"] = sc.cur("ve")
        wst["wq_col_ok"] = sc.cur("ve")

    def tern_unit_dve(m_in, r, cc, col):
        u = wst["n"]
        b = u % 3
        qb = wst["nq"] % 2
        wst["n"] += 1
        wst["nq"] += 1
        O("vector", lambda v, b=b, col=col: v.tensor_scalar(
            wtmp[b][:], wtmp[b][:], wq_col[:, col:col + 1], 1.0, ALU.mult, ALU.min),
            W(wst["ld_h"].pop(u), wst["wq_col_ok"], sc.cur("ve")), "ve")
        O("vector", lambda v, b=b: v.tensor_scalar(
            wtmp[b][:], wtmp[b][:], -1.0, MAGIC, ALU.max, ALU.add),
            W(sc.cur("ve")), "ve")
        O("vector", lambda v, b=b: v.tensor_scalar(
            wi32[:], wtmp[b][:], MAGIC, CQ, ALU.subtract, ALU.add),
            W(sc.cur("ve")), "ve")
        wst["buf_free"][b] = sc.cur("ve")
        issue_wload()
        O("vector", lambda v, qb=qb: v.tensor_scalar(
            wqb[qb][:], wi32[:], -256.0, None, ALU.add),
            W(sc.cur("ve"), wst["wqb_free"][qb]), "ve")
        wst["wqb_ok"][qb] = sc.cur("ve")

    def tern_unit_pe(m_sc, r, cc):
        qb = wst["np"] % 2
        wst["np"] += 1
        nj = UW // 128
        for jj in range(nj):
            O("tensor", lambda t, jj=jj, qb=qb: t.transpose(
                pD[:, 128 * jj:128 * (jj + 1)],
                wqb[qb][:, 128 * jj:128 * (jj + 1)], identb[:]),
                W(wst["wqb_ok"][qb], last["pD"], consts_ok) if jj == 0 else (),
                "pe" if jj == nj - 1 else None)
        wst["wqb_free"][qb] = sc.cur("pe")
        O("vector", lambda v, nj=nj: v.tensor_copy(
            wstg[:, 0:128 * nj], pD[:, 0:128 * nj]),
            W(sc.cur("pe"), wst["stg_free"], sc.cur("ve")), "ve")
        last["pD"] = sc.cur("ve")
        wst["stg_free"] = O("sync", lambda s, m_sc=m_sc, r=r, cc=cc, nj=nj:
                            s.dma_start(
            out=m_sc.rearrange("(j p) r -> p j r", p=128)[
                :, (UW * cc) // 128:(UW * cc) // 128 + nj,
                128 * r:128 * (r + 1)],
            in_=wstg[:, 0:128 * nj].rearrange("p (j q) -> p j q", j=nj)),
            W(sc.cur("ve")), "qs", 16)

    wprep_steps = []
    for r, cc in units_of(H, D):
        wst["loads"].append((w1_in, r, cc))
    for r, cc in units_of(D, H):
        wst["loads"].append((w2_in, r, cc))
    for r, cc in units_of(H, D):
        wst["loads"].append((w1_in, r, cc))
    for r, cc in units_of(D, H):
        wst["loads"].append((w2_in, r, cc))
    for r, cc in units_of(H, D):
        wprep_steps.append(lambda r=r, cc=cc: mean_unit(w1_in, r, cc, 0))
    for r, cc in units_of(D, H):
        wprep_steps.append(lambda r=r, cc=cc: mean_unit(w2_in, r, cc, 1))
    wprep_steps.append(mean_finalize)
    tern_list = ([(w1_in, w1T_sc, r, cc, 0) for r, cc in units_of(H, D)]
                 + [(w2_in, w2T_sc, r, cc, 1) for r, cc in units_of(D, H)])
    for idx, (tm_in, tm_sc, tr, tcc, tcol) in enumerate(tern_list):
        def step(idx=idx, m_in=tm_in, m_sc=tm_sc, r=tr, cc=tcc, col=tcol):
            if idx > 0:
                pm = tern_list[idx - 1]
                tern_unit_pe(pm[1], pm[2], pm[3])
            tern_unit_dve(m_in, r, cc, col)
            if idx == len(tern_list) - 1:
                tern_unit_pe(m_sc, r, cc)
        wprep_steps.append(step)

    x_sb = [asb(f"x_sb{i}", [128, 2 * D], F32) for i in range(2)]
    xb_sb = asb("xb_sb", [128, D], BF16)
    xT_sb = asb("xT_sb", [128, d8 * TA], BF16)
    g_sb = asb("g_sb", [128, s32 * TA], BF16)
    g2_sb = [asb(f"g2_sb{i}", [128, TA], BF16) for i in range(3)]
    l2row = asb("l2row", [1, TA], F32)
    zh = asb("zh", [128, D], F32)
    y_sb = asb("y_sb", [128, 2 * D], F32)
    yqi = asb("yqi", [128, D // 2], I32)
    yqb = asb("yqb", [128, 2 * D], BF16)
    yqT_sb = asb("yqT_sb", [128, d8 * 128], BF16)
    bnst = asb("bnst", [128, nbn * 6], F32)
    smalls = asb("smalls", [128, 16], F32)
    ry_row = [asb(f"ry_row{i}", [1, TA], F32) for i in range(2)]

    last = {k: None for k in ("x0", "x1", "xb", "xT", "g", "pD", "pC",
                              "zh", "y", "yqi", "yqb", "yqT", "smalls_ac",
                              "pS0", "pS1", "g2_0", "g2_1", "g2_2", "g2_3",
                              "ry", "yqb_rdy", "qsc_ok", "rysm",
                              "ryb0", "ryb1")}
    last.update({f"pA{i}": None for i in range(4)})

    nmean = len(units_of(H, D)) + len(units_of(D, H)) + 1
    ntern = len(units_of(H, D)) + len(units_of(D, H))
    mtiles = max(1, min(4, nA // 4))
    wsteps_per_tile = []
    for i in range(nA):
        if i < mtiles:
            k = -(-nmean // mtiles)
        else:
            k = -(-ntern // (nA - mtiles))
        wsteps_per_tile.append(k)

    xloads = [O("sync", lambda s: s.dma_start(
        out=x_sb[0][:].rearrange("p (c d) -> p c d", c=2),
        in_=x_in[0:TA, :].rearrange("(c p) d -> p c d", p=128)), (), "qx0", 16)]

    def emit_head(i):
        xload = xloads[i]
        for c in range(2):
            O("vector", lambda v, c=c, xs=x_sb[i % 2]: v.tensor_copy(
                xb_sb[:], xs[:, c * D:(c + 1) * D]),
                W(xload, last["xb"], sc.cur("ve")), "ve")
            xb_ok = sc.cur("ve")
            for j in range(d8):
                O("tensor", lambda t, c=c, j=j: t.transpose(
                    pD[:, 128 * j:128 * (j + 1)],
                    xb_sb[:, 128 * j:128 * (j + 1)], identb[:]),
                    W(xb_ok, last["pD"], consts_ok) if j == 0 else (),
                    "pe" if j == d8 - 1 else None)
            O("vector", lambda v, c=c: v.tensor_copy(
                xT_sb[:].rearrange("p (j t) -> p j t", j=d8)[:, :, 128 * c:128 * (c + 1)],
                pD[:, 0:128 * d8].rearrange("p (j q) -> p j q", j=d8)),
                W(sc.cur("pe"), last["xT"], sc.cur("ve")), "ve")
            last["pD"] = sc.cur("ve")
            last["xb"] = sc.cur("pe")
        return sc.cur("ve")

    def emit_yq_lag(i):
        for c in range(2):
            for j in range(d8):
                O("tensor", lambda t, c=c, j=j: t.transpose(
                    pD[:, 128 * j:128 * (j + 1)],
                    yqb[:, c * D + 128 * j:c * D + 128 * (j + 1)], identb[:]),
                    W(last["yqb_rdy"], last["pD"]) if j == 0 else (),
                    "pe" if j == d8 - 1 else None)
            last["yqb"] = sc.cur("pe")
            O("vector", lambda v, c=c: v.tensor_copy(
                yqT_sb[:], pD[:, 0:128 * d8]),
                W(sc.cur("pe"), last["yqT"], sc.cur("ve")), "ve")
            last["pD"] = sc.cur("ve")
            O("sync", lambda s, i=i, c=c: s.dma_start(
                out=yqT_sc.rearrange("(j p) t -> p j t", p=128)[
                    :, :, TA * i + 128 * c:TA * i + 128 * (c + 1)],
                in_=yqT_sb[:].rearrange("p (j t) -> p j t", j=d8)),
                W(sc.cur("ve")), "qq", 16)
            last["yqT"] = sc.cur("qq")
        rb = i % 2
        for c in range(2):
            O("tensor", lambda t, c=c: t.transpose(
                pC[0:1, 256:384], smalls[:, 14 + c:15 + c], identf[:]),
                W(last["qsc_ok"], last["pC"], last[f"ryb{rb}"]), "pe")
            O("vector", lambda v, c=c, rb=rb: v.tensor_copy(
                ry_row[rb][0:1, 128 * c:128 * (c + 1)], pC[0:1, 256:384]),
                W(sc.cur("pe"), sc.cur("ve")), "ve")
            last["pC"] = sc.cur("ve")
        last["rysm"] = sc.cur("pe")
        O("sync", lambda s, i=i, rb=rb: s.dma_start(
            out=ry_sc[0:1, TA * i:TA * (i + 1)], in_=ry_row[rb][:]),
            W(sc.cur("ve")), f"qr{rb}", 16)
        last[f"ryb{rb}"] = sc.cur(f"qr{rb}")
        last["ry"] = sc.cur(f"qr{rb}")

    xT_ok = None
    for i in range(nA):
        sc.mark(f"tile{i}")
        wquota = [wsteps_per_tile[i] if i < len(wsteps_per_tile) else 0]
        xb_i = i % 2
        xload = xloads[i]

        if i + 1 < nA:
            xloads.append(O("sync", lambda s, i=i: s.dma_start(
                out=x_sb[(i + 1) % 2][:].rearrange("p (c d) -> p c d", c=2),
                in_=x_in[TA * (i + 1):TA * (i + 2), :]
                .rearrange("(c p) d -> p c d", p=128)),
                W(last[f"x{(i + 1) % 2}"]), f"qx{(i + 1) % 2}", 16))

        if i == 0:
            xT_ok = emit_head(0)

        ps_lohi = [(0, TA), (512, 512 + TA)]
        gelu_cnt = {}
        g2_cnt = {}

        def emit_ones(ss, i=i):
            O("tensor", lambda t, ss=ss: t.matmul(
                pC[0:1, 0:TA], ones_bf[:], g2_sb[ss % 3][:],
                start=(ss == 0), stop=(ss == s32 - 1)),
                W(g2_cnt[ss], last["pC"] if ss == 0 else None), "pe")
            last[f"g2_{ss % 3}"] = sc.cur("pe")

        for ss in range(s32):
            lo, hi = ps_lohi[ss % 2]
            w0 = [xT_ok, kt_ok]
            if ss >= 2:
                w0.append(gelu_cnt[ss - 2])
            else:
                w0.append(last[f"pS{ss}"])
            for j in range(d8):
                O("tensor", lambda t, ss=ss, j=j, lo=lo, hi=hi: t.matmul(
                    pB[:, lo:hi],
                    KT[:, j * S + 128 * ss: j * S + 128 * (ss + 1)],
                    xT_sb[:, j * TA:(j + 1) * TA],
                    start=(j == 0), stop=(j == d8 - 1)),
                    W(*w0) if j == 0 else (), "pe" if j == d8 - 1 else None)
            mm1_done = sc.cur("pe")
            if ss >= 3:
                emit_ones(ss - 3)
            O("scalar", lambda s, ss=ss, lo=lo, hi=hi: s.activation(
                g_sb[:, ss * TA:(ss + 1) * TA], pB[:, lo:hi], AF.Gelu),
                W(mm1_done), "ac")
            gelu_cnt[ss] = sc.cur("ac")
            O("vector", lambda v, ss=ss: v.tensor_mul(
                g2_sb[ss % 3][:], g_sb[:, ss * TA:(ss + 1) * TA],
                g_sb[:, ss * TA:(ss + 1) * TA]),
                W(gelu_cnt[ss], last[f"g2_{ss % 3}"], sc.cur("ve")), "ve")
            g2_cnt[ss] = sc.cur("ve")
            if ss % 3 == 2 and wquota[0] > 0 and wprep_steps:
                nper = -(-wquota[0] // max(1, (s32 - ss) // 3))
                for _ in range(nper):
                    if wprep_steps:
                        wprep_steps.pop(0)()
                        wquota[0] -= 1
        for sss in range(max(0, s32 - 3), s32):
            emit_ones(sss)
        last["pS0"] = gelu_cnt[s32 - 2]
        last["pS1"] = gelu_cnt[s32 - 1]
        l2mm_done = sc.cur("pe")

        last["xT"] = l2mm_done
        if i + 1 < nA:
            next_xT_ok = emit_head(i + 1)
        if i > 0:
            emit_yq_lag(i - 1)

        O("vector", lambda v: v.tensor_copy(l2row[:], pC[0:1, 0:TA]),
          W(l2mm_done, sc.cur("ve")), "ve")
        l2row_ok = sc.cur("ve")
        for c in range(2):
            O("tensor", lambda t, c=c: t.matmul(
                pC[:, 256 + c:257 + c], l2row[0:1, 128 * c:128 * (c + 1)], one11[:],
                start=True, stop=True),
                W(l2row_ok) if c == 0 else (), "pe" if c == 1 else None)
        O("vector", lambda v: v.tensor_copy(smalls[:, 0:2], pC[:, 256:258]),
          W(sc.cur("pe"), sc.cur("ve")), "ve")
        last["pC"] = sc.cur("ve")
        O("scalar", lambda s: s.activation(smalls[:, 0:2], smalls[:, 0:2], AF.Sqrt),
          W(sc.cur("ve"), last["smalls_ac"]), "ac")
        sqrt_ok = sc.cur("ac")
        O("vector", lambda v: v.reciprocal(smalls[:, 2:4], smalls[:, 0:2]),
          W(sqrt_ok, sc.cur("ve")), "ve")
        O("vector", lambda v: v.tensor_scalar(smalls[:, 2:4], smalls[:, 2:4],
                                              sqrtS, None, ALU.mult),
          W(sc.cur("ve")), "ve")
        rl2_ok = sc.cur("ve")

        for c in range(2):
            for n in range(ndh):
                gidx = c * ndh + n
                for k in range(s32):
                    w0 = W(gelu_cnt[k], v_ok, last[f"pA{gidx}"] if k == 0 else None)
                    O("tensor", lambda t, c=c, n=n, gidx=gidx, k=k: t.matmul(
                        pA[:, 512 * gidx:512 * gidx + dw],
                        g_sb[:, k * TA + 128 * c: k * TA + 128 * (c + 1)],
                        V_sb[:, k * D + dw * n: k * D + dw * (n + 1)],
                        start=(k == 0), stop=(k == s32 - 1)),
                        w0, "pe" if k == s32 - 1 else None)
                mm2_done = sc.cur("pe")
                O("vector", lambda v, c=c, n=n, gidx=gidx, xs=x_sb[xb_i]:
                  v.scalar_tensor_tensor(
                    zh[:, dw * n:dw * (n + 1)], pA[:, 512 * gidx:512 * gidx + dw],
                    smalls[:, 2 + c:3 + c],
                    xs[:, c * D + dw * n:c * D + dw * (n + 1)],
                    ALU.mult, ALU.add),
                    W(mm2_done, rl2_ok, xload,
                      last["zh"] if n == 0 else None, sc.cur("ve")), "ve")
                last[f"pA{gidx}"] = sc.cur("ve")
            z_ok = sc.cur("ve")
            for bb in range(nbn):
                O("vector", lambda v, bb=bb: v.bn_stats(
                    bnst[:, 6 * bb:6 * (bb + 1)],
                    zh[:, 512 * bb:512 * bb + min(512, D)]),
                    W(z_ok, sc.cur("ve")), "ve")
            O("vector", lambda v: v.bn_aggr(smalls[:, 4:6], bnst[:, 0:6 * nbn]),
              W(sc.cur("ve")), "ve")
            bn_ok = sc.cur("ve")
            O("scalar", lambda s: s.activation(smalls[:, 6:7], smalls[:, 5:6],
                                               AF.Sqrt, bias=epsc[:]),
              W(bn_ok, consts_ok), "ac")
            last["smalls_ac"] = sc.cur("ac")
            O("vector", lambda v: v.reciprocal(smalls[:, 8:9], smalls[:, 6:7]),
              W(sc.cur("ac"), sc.cur("ve")), "ve")
            O("vector", lambda v: v.tensor_mul(smalls[:, 10:11], smalls[:, 4:5],
                                               smalls[:, 8:9]),
              W(sc.cur("ve")), "ve")
            O("vector", lambda v, c=c: v.tensor_scalar(
                y_sb[:, c * D:(c + 1) * D], zh[:], smalls[:, 8:9], smalls[:, 10:11],
                ALU.mult, ALU.subtract),
                W(sc.cur("ve"), last["y"] if c == 0 else None), "ve")
            last["zh"] = sc.cur("ve")
            y_ok = sc.cur("ve")
            O("vector", lambda v, c=c: v.reduce_max(
                smalls[:, 12:13], y_sb[:, c * D:(c + 1) * D], axis=AX.X,
                apply_absolute_value=True), W(y_ok), "ve")
            O("vector", lambda v: v.tensor_scalar(smalls[:, 12:13], smalls[:, 12:13],
                                                  1e-5, None, ALU.max),
              W(sc.cur("ve")), "ve")
            O("vector", lambda v: v.reciprocal(smalls[:, 13:14], smalls[:, 12:13]),
              W(sc.cur("ve")), "ve")
            O("vector", lambda v: v.tensor_scalar(smalls[:, 13:14], smalls[:, 13:14],
                                                  127.0, None, ALU.mult),
              W(sc.cur("ve")), "ve")
            O("vector", lambda v, c=c: v.tensor_scalar(
                smalls[:, 14 + c:15 + c], smalls[:, 12:13], 1.0 / 127.0, None,
                ALU.mult),
                W(sc.cur("ve"), last["rysm"] if c == 0 else None), "ve")
            qsc_ok = sc.cur("ve")
            for hh in range(2):
                hw2 = D // 2
                O("vector", lambda v, c=c, hh=hh, hw2=hw2: v.tensor_scalar(
                    zh[:, hh * hw2:(hh + 1) * hw2],
                    y_sb[:, c * D + hh * hw2:c * D + (hh + 1) * hw2],
                    smalls[:, 13:14], MAGIC, ALU.mult, ALU.add),
                    W(qsc_ok, last["zh"], sc.cur("ve")), "ve")
                O("vector", lambda v, hh=hh, hw2=hw2: v.tensor_scalar(
                    yqi[:], zh[:, hh * hw2:(hh + 1) * hw2],
                    MAGIC, CQ, ALU.subtract, ALU.add),
                    W(last["yqi"], sc.cur("ve")), "ve")
                O("vector", lambda v, c=c, hh=hh, hw2=hw2: v.tensor_scalar(
                    yqb[:, c * D + hh * hw2:c * D + (hh + 1) * hw2],
                    yqi[:], -256.0, None, ALU.add),
                    W(last["yqb"] if (c == 0 and hh == 0) else None,
                      sc.cur("ve")), "ve")
                last["yqi"] = sc.cur("ve")
                last["zh"] = sc.cur("ve")
        last["g"] = sc.cur("pe")
        last["yqb_rdy"] = sc.cur("ve")
        last["qsc_ok"] = sc.cur("ve")

        O("sync", lambda s, i=i: s.dma_start(
            out=y_sc[TA * i:TA * (i + 1), :].rearrange("(c p) d -> p c d", p=128),
            in_=y_sb[:].rearrange("p (c d) -> p c d", c=2)),
            W(sc.cur("ve")), "qy", 16)
        last["y"] = sc.cur("qy")
        last[f"x{xb_i}"] = sc.cur("ve")
        if i + 1 < nA:
            xT_ok = next_xT_ok

    while wprep_steps:
        wprep_steps.pop(0)()
    emit_yq_lag(nA - 1)
    sc.mark("phaseA")
    pa.close()
    sc.barrier()

    W1T = sbuf("W1T", [128, d8 * H], BF16)
    W2T = sbuf("W2T", [128, h32 * D], BF16)
    yqTw = [sbuf(f"yqTw{i}", [128, d8 * (TW + 2)], BF16) for i in range(2)]
    y_win = sbuf("y_win", [128, D], F32)
    Srow_b = sbuf("Srow_b", [1, TW + 2], F32)
    Srow = [Srow_b, Srow_b]
    S_sb = [sbuf(f"S_sb{i}", [128, TW + 2], F32) for i in range(2)]
    htr = [sbuf(f"htr{i}", [128, TW + 2], F32) for i in range(3)]
    ca = [sbuf(f"ca{i}", [128, TW], F32) for i in range(2)]
    ca3 = [sbuf(f"ca3{i}", [128, TW], F32) for i in range(2)]
    hc_buf = sbuf("hc_buf", [128, h32 * TW], F32)
    mbuf = sbuf("mbuf", [128, TW], F32)
    mg = sbuf("mg", [128, 2 * TW], F32)
    i32tr = sbuf("i32tr", [128, 2 * TW], I32)
    hq_sb = sbuf("hq_sb", [128, h32 * TW], BF16)
    row2 = sbuf("row2", [1, TW], F32)
    S2_sb = sbuf("S2_sb", [128, TW], F32)
    bsml = sbuf("bsml", [128, 16], F32)
    bnst2 = sbuf("bnst2", [128, nbn * 6], F32)

    w1t_ok = O("sync", lambda s: s.dma_start(
        out=W1T[:].rearrange("p (j h) -> p j h", j=d8),
        in_=w1T_sc.rearrange("(j p) h -> p j h", p=128)), (), "qw1t", 16)
    w2t_ok = O("sync", lambda s: s.dma_start(
        out=W2T[:].rearrange("p (c d) -> p c d", c=h32),
        in_=w2T_sc.rearrange("(c p) d -> p c d", p=128)), (), "qw2t", 16)

    lastB = {k: None for k in ("pC", "htr0", "htr1", "htr2",
                               "htr0ac", "htr1ac", "htr2ac",
                               "ca0", "ca1", "ca30", "ca31",
                               "hc", "m8", "hq", "S2",
                               "row2", "ywin", "bsml_ac",
                               "Srow0", "Srow1", "S0", "S1", "pA2",
                               "mslot0", "mslot1", "mslot2", "mslot3")}
    lastB.update({f"pA{i}": None for i in range(4)})

    def emit_win_loads(w):
        q, wq = divmod(w, nWq)
        go = q * L + wq * TW
        lp = wq == 0
        rp = wq == nWq - 1
        c0 = go - (0 if lp else 1)
        c1 = min(go + TW + 1, q * L + L)
        ncols = c1 - c0
        dst0 = 1 if lp else 0
        wb = w % 2
        buf = yqTw[wb]
        ld = O("sync", lambda s, buf=buf, c0=c0, ncols=ncols, dst0=dst0: s.dma_start(
            out=buf[:].rearrange("p (j t) -> p j t", j=d8)[:, :, dst0:dst0 + ncols],
            in_=yqT_sc.rearrange("(j p) t -> p j t", p=128)[:, :, c0:c0 + ncols]),
            W(lastB.get(f"yqTw{wb}"), last["yqT"]), f"qt{wb}", 16)
        pads = []
        if lp:
            pads.append(O("vector", lambda v, buf=buf: v.memset(
                buf[:].rearrange("p (j t) -> p j t", j=d8)[:, :, 0:1], 0.0),
                W(lastB.get(f"yqTw{wb}"), ld, sc.cur("ve")), "ve"))
        if rp:
            pads.append(O("vector", lambda v, buf=buf: v.memset(
                buf[:].rearrange("p (j t) -> p j t", j=d8)[:, :, TW + 1:TW + 2], 0.0),
                W(lastB.get(f"yqTw{wb}"), ld, sc.cur("ve")), "ve"))
        return (ld, pads, go, lp, rp)

    def emit_sprep(w):
        wb = w % 2
        q, wq = divmod(w, nWq)
        go = q * L + wq * TW
        lp = wq == 0
        rp = wq == nWq - 1
        sr0 = go - (0 if lp else 1)
        srcols = min(go + TW + 1, q * L + L) - sr0
        sdst = 1 if lp else 0
        ryld = O("sync", lambda s, sr0=sr0, srcols=srcols, sdst=sdst, wb=wb:
                 s.dma_start(out=Srow[wb][0:1, sdst:sdst + srcols],
                             in_=ry_sc[0:1, sr0:sr0 + srcols]),
                 W(lastB["Srow0"], lastB["Srow1"], last["ry"]), f"qrl{wb}", 16)
        O("vector", lambda v, sr0=sr0, srcols=srcols, sdst=sdst, wb=wb:
          v.tensor_scalar(
            Srow[wb][0:1, sdst:sdst + srcols], Srow[wb][0:1, sdst:sdst + srcols],
            mwv[0:1, 0:1], None, ALU.mult),
            W(ryld, lastB[f"Srow{wb}"], wst["mw_ok"], sc.cur("ve")), "ve")
        if lp:
            O("vector", lambda v, wb=wb: v.memset(Srow[wb][0:1, 0:1], 0.0),
              W(sc.cur("ve")), "ve")
        if rp:
            O("vector", lambda v, wb=wb: v.memset(Srow[wb][0:1, TW + 1:TW + 2], 0.0),
              W(sc.cur("ve")), "ve")
        srow_ok = sc.cur("ve")
        O("tensor", lambda t, wb=wb: t.matmul(pC[:, 0:TW + 2], ones_f1[:],
                                              Srow[wb][:], start=True, stop=True),
          W(srow_ok, lastB["pC"]), "pe")
        O("vector", lambda v, wb=wb: v.tensor_copy(S_sb[wb][:], pC[:, 0:TW + 2]),
          W(sc.cur("pe"), lastB[f"S{wb}"], sc.cur("ve")), "ve")
        lastB["pC"] = sc.cur("ve")
        lastB[f"Srow{wb}"] = sc.cur("ve")
        return sc.cur("ve")

    win_loads = {0: emit_win_loads(0)}
    s_oks = {0: emit_sprep(0)}

    mslots = [(pB, 0), (pB, 512), (pA, 1536)]
    gsz = 16 if h32 % 16 == 0 else h32
    ng = h32 // gsz
    octets = [list(range(o, min(o + 8, h32))) for o in range(0, h32, 8)]
    st = {"hq_prev": None, "dbg_gate": None, "pend_silu": [], "i32ac": [None, None]}

    def emit_conv_chunk(w, hc):
        ld, pads, go, lp, rp = win_loads[w]
        s_ok = s_oks[w]
        s_i = hc % 3
        mp, lo = mslots[s_i]
        hi = lo + TW + 2

        def mm3(t, hc=hc, mp=mp, lo=lo, hi=hi, wb=w % 2):
            r = None
            for j in range(d8):
                r = t.matmul(
                    mp[:, lo:hi],
                    W1T[:, j * H + 128 * hc: j * H + 128 * (hc + 1)],
                    yqTw[wb][:, j * (TW + 2):(j + 1) * (TW + 2)],
                    start=(j == 0), stop=(j == d8 - 1))
            return r
        O("tensor", mm3,
          W(ld, *pads, w1t_ok, lastB[f"mslot{s_i}"]), "pe")
        mm3_done = sc.cur("pe")
        hb = hc % 3
        O("vector", lambda v, hb=hb, mp=mp, lo=lo, hi=hi, wb=w % 2:
          v.tensor_tensor(
            htr[hb][:], mp[:, lo:hi], S_sb[wb][:], ALU.mult),
            W(mm3_done, s_ok, lastB[f"htr{hb}"],
              lastB[f"htr{hb}ac"], sc.cur("ve")), "ve")
        lastB[f"mslot{s_i}"] = sc.cur("ve")
        htr_ok = sc.cur("ve")
        O("scalar", lambda s, hc=hc, hb=hb: s.activation(
            ca[hc % 2][:], htr[hb][:, 0:TW], AF.Identity,
            scale=cwb_sb[:, 4 * hc:4 * hc + 1],
            bias=cwb_sb[:, 4 * hc + 3:4 * hc + 4]),
            W(htr_ok, cwb_ok, lastB[f"ca{hc % 2}"]), "ac")
        c1_ok = sc.cur("ac")
        O("scalar", lambda s, hc=hc, hb=hb: s.activation(
            ca3[hc % 2][:], htr[hb][:, 2:TW + 2], AF.Identity,
            scale=cwb_sb[:, 4 * hc + 2:4 * hc + 3], bias=0.0),
            W(lastB[f"ca3{hc % 2}"], sc.cur("ac")), "ac")
        lastB[f"htr{hb}ac"] = sc.cur("ac")
        c3_ok = sc.cur("ac")
        O("vector", lambda v, hc=hc, hb=hb: v.scalar_tensor_tensor(
            ca[hc % 2][:], htr[hb][:, 1:TW + 1],
            cwb_sb[:, 4 * hc + 1:4 * hc + 2], ca[hc % 2][:],
            ALU.mult, ALU.add),
            W(c1_ok, sc.cur("ve")), "ve")
        lastB[f"htr{hb}"] = sc.cur("ve")
        c2_ok = sc.cur("ve")
        O("gpsimd", lambda g, hc=hc: g.tensor_tensor(
            hc_buf[:, hc * TW:(hc + 1) * TW], ca[hc % 2][:],
            ca3[hc % 2][:], ALU.add),
            W(c2_ok, c3_ok,
              st["hc_last"][hc] if st["hc_last"] is not None else None,
              st["dbg_gate"] if w == 1 else None,
              sc.cur("gp")), "gp")
        lastB[f"ca{hc % 2}"] = sc.cur("gp")
        lastB[f"ca3{hc % 2}"] = sc.cur("gp")
        if hc % gsz == gsz - 1:
            st["pend_silu"].append((w, hc // gsz))

    def emit_silu_group(w, gq):
        hc = gq * gsz + gsz - 1
        O("scalar", lambda s, hc=hc: s.activation(
            hc_buf[:, (hc - gsz + 1) * TW:(hc + 1) * TW],
            hc_buf[:, (hc - gsz + 1) * TW:(hc + 1) * TW], AF.Silu),
            W(sc.cur("gp"), sc.cur("ac")), "ac")
        O("vector", lambda v, hc=hc, gq=gq: v.reduce_max(
            mg[:, gq * TW:(gq + 1) * TW],
            hc_buf[:, (hc - gsz + 1) * TW:(hc + 1) * TW]
            .rearrange("p (c t) -> p t c", c=gsz),
            axis=AX.X, apply_absolute_value=True),
            W(sc.cur("ac"), sc.cur("ve")), "ve")

    def emit_finals(w):
        O("vector", lambda v: v.reduce_max(
            mbuf[:], mg[:, 0:ng * TW].rearrange("p (c t) -> p t c", c=ng),
            axis=AX.X), W(sc.cur("ve"), lastB["m8"]), "ve")
        mbuf_ok = sc.cur("ve")
        for c in range(2):
            O("tensor", lambda t, c=c: t.transpose(
                pC[:, 128 * c:128 * (c + 1)], mbuf[:, 128 * c:128 * (c + 1)],
                identf[:]),
                W(mbuf_ok, lastB["pC"]) if c == 0 else (),
                "pe" if c == 1 else None)
        lastB["m8"] = sc.cur("pe")
        O("vector", lambda v: v.reduce_max(
            bsml[:, 0:2], pC[:, 0:256].rearrange("p (c t) -> p c t", c=2),
            axis=AX.X), W(sc.cur("pe"), lastB["bsml_ac"], sc.cur("ve")), "ve")
        lastB["pC"] = sc.cur("ve")
        O("vector", lambda v: v.tensor_scalar(bsml[:, 0:2], bsml[:, 0:2],
                                              1e-5, None, ALU.max),
          W(sc.cur("ve")), "ve")
        O("vector", lambda v: v.tensor_scalar(bsml[:, 2:4], bsml[:, 0:2],
                                              m2s_col[:], None, ALU.mult),
          W(sc.cur("ve"), wst["wq_col_ok"]), "ve")
        rh2_ok = sc.cur("ve")
        O("vector", lambda v: v.reciprocal(bsml[:, 4:6], bsml[:, 0:2]),
          W(sc.cur("ve")), "ve")
        O("vector", lambda v: v.tensor_scalar(bsml[:, 4:6], bsml[:, 4:6],
                                              127.0, None, ALU.mult),
          W(sc.cur("ve")), "ve")
        sc2_ok = sc.cur("ve")
        for c in range(2):
            O("tensor", lambda t, c=c: t.transpose(
                pC[0:1, 256:384], bsml[:, 4 + c:5 + c], identf[:]),
                W(sc2_ok, lastB["pC"]), "pe")
            O("vector", lambda v, c=c: v.tensor_copy(
                row2[0:1, 128 * c:128 * (c + 1)], pC[0:1, 256:384]),
                W(sc.cur("pe"), lastB["row2"] if c == 0 else None,
                  sc.cur("ve")), "ve")
            lastB["pC"] = sc.cur("ve")
        O("tensor", lambda t: t.matmul(pC[:, 0:TW], ones_f1[:], row2[:],
                                       start=True, stop=True),
          W(sc.cur("ve"), lastB["pC"]), "pe")
        lastB["row2"] = sc.cur("pe")
        O("vector", lambda v: v.tensor_copy(S2_sb[:], pC[:, 0:TW]),
          W(sc.cur("pe"), lastB["S2"], sc.cur("ve")), "ve")
        lastB["pC"] = sc.cur("ve")
        s2_ok = sc.cur("ve")
        dbg_ok = None
        if dbg and w == 0:
            O("sync", lambda s: s.dma_start(out=dbg_m[:, 0:TW], in_=mbuf[:]),
              W(s2_ok), "qc", 16)
            O("sync", lambda s: s.dma_start(out=dbg_h[:, 0:h32 * TW], in_=hc_buf[:]),
              W(s2_ok), "qc", 16)
            dbg_ok = sc.cur("qc")
            st["dbg_gate"] = dbg_ok
        return s2_ok, rh2_ok, dbg_ok

    def emit_quant_pair(w, p, s2_ok, dbg_ok, hq_cnt, hc_last):
        sl = slice(2 * p * TW, (2 * p + 2) * TW)
        s2b = AP(S2_sb[:].tensor, S2_sb[:].offset,
                 [list(dd) for dd in S2_sb[:].ap[:-1]] + [[0, 2], [1, TW]])
        O("gpsimd", lambda g, sl=sl, s2b=s2b: g.tensor_tensor(
            hc_buf[:, sl].rearrange("q (c t) -> q c t", c=2),
            hc_buf[:, sl].rearrange("q (c t) -> q c t", c=2),
            s2b, ALU.mult),
            W(s2_ok, dbg_ok if p == 0 else None, sc.cur("gp")), "gp")
        O("vector", lambda v, sl=sl: v.tensor_scalar(
            hc_buf[:, sl], hc_buf[:, sl], MAGIC, MAGIC, ALU.add, ALU.subtract),
            W(sc.cur("gp"), sc.cur("ve")), "ve")
        O("scalar", lambda s, sl=sl: s.activation(
            i32tr[:], hc_buf[:, sl], AF.Identity, bias=cqc[:]),
            W(consts_ok, sc.cur("ve"), sc.cur("ac")), "ac")
        hc_last[2 * p] = hc_last[2 * p + 1] = sc.cur("ac")
        O("scalar", lambda s, sl=sl: s.activation(
            hq_sb[:, sl], i32tr[:], AF.Identity, bias=m256c[:]),
            W(lastB["hq"] if p == 0 else None, sc.cur("ac")), "ac")
        hq_cnt[2 * p + 1] = sc.cur("ac")

    def emit_mm4_octet(w, ks, hq_cnt, mm4_done):
        for k in ks:
            for n in range(ndh):
                w0 = W(hq_cnt[ks[-1]] if (k == ks[0] and n == 0) else None,
                       w2t_ok if k == 0 and n == 0 else None,
                       lastB[f"pA{n}"] if k == 0 else None)
                is_stop = k == h32 - 1
                O("tensor", lambda t, n=n, k=k, is_stop=is_stop: t.matmul(
                    pA[:, 512 * n:512 * n + dw],
                    hq_sb[:, k * TW: k * TW + 128],
                    W2T[:, k * D + dw * n: k * D + dw * (n + 1)],
                    start=(k == 0), stop=is_stop),
                    w0, "pe" if is_stop else None)
                if is_stop:
                    mm4_done[(0, n)] = sc.cur("pe")
        for k in ks:
            w0 = W(lastB["pA2"] if k == 0 else None)
            is_stop = k == h32 - 1
            O("tensor", lambda t, k=k, is_stop=is_stop: t.matmul(
                pA[:, 1024:1024 + dw],
                hq_sb[:, k * TW + 128: k * TW + 256],
                W2T[:, k * D: k * D + dw],
                start=(k == 0), stop=is_stop),
                w0, "pe" if is_stop else None)
            if is_stop:
                mm4_done[(1, 0)] = sc.cur("pe")

    def emit_mm4_c1n1(w, mm4_done):
        if ndh < 2:
            return
        for k in range(h32):
            w0 = W(lastB["pC"] if k == 0 else None)
            is_stop = k == h32 - 1
            O("tensor", lambda t, k=k, is_stop=is_stop: t.matmul(
                pC[:, 0:dw],
                hq_sb[:, k * TW + 128: k * TW + 256],
                W2T[:, k * D + dw: k * D + 2 * dw],
                start=(k == 0), stop=is_stop),
                w0, "pe" if is_stop else None)
            if is_stop:
                mm4_done[(1, 1)] = sc.cur("pe")

    def emit_ln2_store(w, c):
        for bb in range(nbn):
            O("vector", lambda v, bb=bb: v.bn_stats(
                bnst2[:, 6 * bb:6 * (bb + 1)],
                y_win[:, 512 * bb:512 * bb + min(512, D)]),
                W(sc.cur("ve")), "ve")
        O("vector", lambda v: v.bn_aggr(bsml[:, 4:6], bnst2[:, 0:6 * nbn]),
          W(sc.cur("ve")), "ve")
        O("scalar", lambda s: s.activation(bsml[:, 6:7], bsml[:, 5:6],
                                           AF.Sqrt, bias=epsc[:]),
          W(sc.cur("ve"), consts_ok, sc.cur("ac")), "ac")
        lastB["bsml_ac"] = sc.cur("ac")
        O("vector", lambda v: v.reciprocal(bsml[:, 8:9], bsml[:, 6:7]),
          W(sc.cur("ac"), sc.cur("ve")), "ve")
        O("vector", lambda v: v.tensor_mul(bsml[:, 10:11], bsml[:, 4:5],
                                           bsml[:, 8:9]),
          W(sc.cur("ve")), "ve")
        O("vector", lambda v: v.tensor_scalar(
            y_win[:], y_win[:],
            bsml[:, 8:9], bsml[:, 10:11], ALU.mult, ALU.subtract),
            W(sc.cur("ve")), "ve")
        out_ok = sc.cur("ve")
        O("sync", lambda s, w=w, c=c: s.dma_start(
            out=out_hbm[TW * w + 128 * c:TW * w + 128 * (c + 1), :],
            in_=y_win[:]),
            W(out_ok), "qo", 16)
        lastB["ywin"] = sc.cur("qo")

    def emit_tail_c0(w, yld, rh2_ok, mm4_done):
        for n in range(ndh):
            O("vector", lambda v, n=n: v.scalar_tensor_tensor(
                y_win[:, dw * n:dw * (n + 1)],
                pA[:, 512 * n:512 * n + dw],
                bsml[:, 2:3],
                y_win[:, dw * n:dw * (n + 1)],
                ALU.mult, ALU.add),
                W(mm4_done[(0, n)], rh2_ok, yld, sc.cur("ve")), "ve")
            lastB[f"pA{n}"] = sc.cur("ve")
        emit_ln2_store(w, 0)

    def emit_tail_c1(w, yld, rh2_ok, mm4_done):
        O("vector", lambda v: v.scalar_tensor_tensor(
            y_win[:, 0:dw], pA[:, 1024:1024 + dw], bsml[:, 3:4],
            y_win[:, 0:dw], ALU.mult, ALU.add),
            W(mm4_done[(1, 0)], rh2_ok, yld, sc.cur("ve")), "ve")
        lastB["pA2"] = sc.cur("ve")
        if ndh >= 2:
            O("vector", lambda v: v.scalar_tensor_tensor(
                y_win[:, dw:2 * dw], pC[:, 0:dw], bsml[:, 3:4],
                y_win[:, dw:2 * dw], ALU.mult, ALU.add),
                W(mm4_done[(1, 1)], rh2_ok, yld, sc.cur("ve")), "ve")
            lastB["pC"] = sc.cur("ve")
        emit_ln2_store(w, 1)

    def flush_silu():
        for (wc, gq) in st["pend_silu"]:
            emit_silu_group(wc, gq)
        st["pend_silu"] = []

    st["hc_last"] = None
    for hc in range(h32):
        emit_conv_chunk(0, hc)
    flush_silu()
    lastB["S0"] = sc.cur("ve")
    lastB["yqTw0"] = sc.cur("pe")
    if nW > 1:
        s_oks[1] = emit_sprep(1)
        win_loads[1] = emit_win_loads(1)
    sc.mark("w_head")

    for w in range(nW):
        sc.mark(f"win{w}")
        s2_ok, rh2_ok, dbg_ok = emit_finals(w)
        hq_cnt = {}
        hc_last = {}
        st["hc_last"] = hc_last
        yld = O("sync", lambda s, w=w: s.dma_start(
            out=y_win[:],
            in_=y_sc[TW * w:TW * w + 128, :]),
            W(lastB["ywin"], last["y"]), "qyl", 16)
        mm4_done = {}
        for oi, ks in enumerate(octets):
            flush_silu()
            for p in range(ks[0] // 2, (ks[-1] + 1) // 2):
                emit_quant_pair(w, p, s2_ok, dbg_ok, hq_cnt, hc_last)
            if oi > 0 and w + 1 < nW:
                for hc in octets[oi - 1]:
                    emit_conv_chunk(w + 1, hc)
            if oi > 0:
                emit_mm4_octet(w, octets[oi - 1], hq_cnt, mm4_done)
        if w + 1 < nW:
            for hc in octets[-1]:
                emit_conv_chunk(w + 1, hc)
        emit_mm4_octet(w, octets[-1], hq_cnt, mm4_done)
        if w + 1 < nW:
            lastB[f"S{(w + 1) % 2}"] = sc.cur("ve")
            lastB[f"yqTw{(w + 1) % 2}"] = sc.cur("pe")
        if w + 2 < nW:
            s_oks[w + 2] = emit_sprep(w + 2)
            win_loads[w + 2] = emit_win_loads(w + 2)
        flush_silu()
        emit_mm4_c1n1(w, mm4_done)
        emit_tail_c0(w, yld, rh2_ok, mm4_done)
        yld = O("sync", lambda s, w=w: s.dma_start(
            out=y_win[:],
            in_=y_sc[TW * w + 128:TW * w + 256, :]),
            W(lastB["ywin"], last["y"]), "qyl", 16)
        emit_tail_c1(w, yld, rh2_ok, mm4_done)
        lastB["hq"] = sc.cur("pe")
        lastB["S2"] = sc.cur("ve")
        win_loads.pop(w)
        s_oks.pop(w)

    sc.streams["sync"].append((W(sc.cur("qo"), sc.cur("qy"), sc.cur("qq")), None, None, sc.seq))
    sc.streams["gpsimd"].append((W(sc.cur("qk0"), sc.cur("qk1"), sc.cur("qv")), None, None, sc.seq))
    sc.seq += 1

    with nc.Block() as block:
        sc.emit(block, sems)

    outer.close()
    return nc


_NC_CACHE = {}


def get_program(key=(1024, 4096, 4096, 2048, 2)):
    if key not in _NC_CACHE:
        _NC_CACHE[key] = build_core(*key)
    return _NC_CACHE[key]


def kernel(L0_attn_inputs, L0_attn_key_param_tokens, L0_attn_value_param_tokens,
           L0_ffn_w1, L0_ffn_conv_w, L0_ffn_conv_b, L0_ffn_w2):
    x = np.ascontiguousarray(np.asarray(L0_attn_inputs, dtype=np.float32))
    k = np.ascontiguousarray(np.asarray(L0_attn_key_param_tokens, dtype=np.float32))
    v = np.ascontiguousarray(np.asarray(L0_attn_value_param_tokens, dtype=np.float32))
    w1 = np.ascontiguousarray(np.asarray(L0_ffn_w1, dtype=np.float32))
    w2 = np.ascontiguousarray(np.asarray(L0_ffn_w2, dtype=np.float32))
    cw = np.asarray(L0_ffn_conv_w, dtype=np.float32).reshape(-1, 3)
    cb = np.asarray(L0_ffn_conv_b, dtype=np.float32).reshape(-1, 1)
    cwb = np.ascontiguousarray(np.concatenate([cw, cb], axis=1))

    B, L, D = x.shape
    ncores = 8
    bc = B // ncores
    nc_prog = get_program((D, k.shape[0], w1.shape[0], L, bc))
    in_maps = []
    for c in range(ncores):
        in_maps.append({
            "x": np.ascontiguousarray(x[bc * c:bc * (c + 1)].reshape(bc * L, D)),
            "k": k, "v": v, "w1": w1, "w2": w2, "cwb": cwb,
        })
    res = run_bass_kernel_spmd(nc_prog, in_maps, core_ids=list(range(ncores)))
    out = np.concatenate(
        [r["out"].reshape(bc, L, D) for r in res.results], axis=0)
    return out
